# version 26
# baseline (speedup 1.0000x reference)
import numpy as np
import ml_dtypes
from contextlib import ExitStack

import concourse.bass as bass
import concourse.mybir as mybir
from concourse.bass_utils import run_bass_kernel_spmd

F32 = mybir.dt.float32
BF16 = mybir.dt.bfloat16
I32 = mybir.dt.int32
U8 = mybir.dt.uint8
AF = mybir.ActivationFunctionType
OP = mybir.AluOpType
BF = ml_dtypes.bfloat16

ENGS = ("sync", "tensor", "vector", "scalar", "gpsimd")
SELF_SYNC = True
DMAQS = ("sync", "scalar", "gpsimd")

NCORES = 8
T = 2048
D = 1024



class View:
    __slots__ = ("ap", "regs")

    def __init__(self, ap, regs):
        self.ap = ap
        self.regs = regs


class Pool:

    def __init__(self, prog, name, tensor, cols, dtype, parts, is_psum=False):
        self.prog = prog
        self.name = name
        self.t = tensor
        self.cols = cols
        self.dtype = dtype
        self.parts = parts
        self.is_psum = is_psum
        self.off = 0
        self.bank = 2048 // (2 if dtype == BF16 else 4)

    def v(self, col0, ncols, part0=0, nparts=None):
        nparts = nparts if nparts is not None else self.parts
        ap = self.t[part0:part0 + nparts, col0:col0 + ncols]
        lo, hi = col0, col0 + ncols
        if self.is_psum:
            lo = (lo // self.bank) * self.bank
            hi = ceil_div(hi, self.bank) * self.bank
        return View(ap, [(self.name, lo, hi)])

    def all(self):
        return self.v(0, self.cols)


class SubPool:

    def __init__(self, parent, off, cols):
        self.parent = parent
        self.off = parent.off + off
        self.cols = cols
        self.t = parent.t
        self.name = parent.name
        base = parent
        while isinstance(base, SubPool):
            base = base.parent
        self._base = base

    def v(self, col0, ncols, part0=0, nparts=None):
        return self._base.v(self.off + col0, ncols, part0, nparts)


class Prog:
    def __init__(self, nc):
        self.nc = nc
        self.eng_ops = {e: [] for e in ENGS}
        self.dma_n = {q: 0 for q in DMAQS}
        self.cc_n = 0
        self.needed = set()
        self.segs = {}
        self.pools = {}
        self._es = None

    def sbuf(self, name, cols, dtype, parts=128):
        t = self.nc.sbuf_tensor(name, [parts, cols], dtype)
        return self._reg_pool(name, t, cols, dtype, parts)

    def psum(self, name, cols, dtype, parts=128):
        t = self.nc.psum_tensor(name, [parts, cols], dtype)
        return self._reg_pool(name, t, cols, dtype, parts, True)

    def _reg_pool(self, name, tctx, cols, dtype, parts, is_psum=False):
        if self._es is None:
            self._es = ExitStack()
        t = self._es.enter_context(tctx)
        p = Pool(self, name, t, cols, dtype, parts, is_psum)
        self.pools[name] = p
        return p

    def dram(self, name, shape, dtype, kind="Internal"):
        h = self.nc.dram_tensor(name, shape, dtype, kind=kind)
        ap = h.ap()
        return View(ap, [("dram:" + name, 0, 1 << 60)])

    def _dep_read(self, space, lo, hi, deps):
        for s in self.segs.get(space, ()):
            if s[0] < hi and lo < s[1] and s[2] is not None:
                deps.add(s[2])

    def _dep_write(self, space, lo, hi, deps):
        for s in self.segs.get(space, ()):
            if s[0] < hi and lo < s[1]:
                if s[2] is not None:
                    deps.add(s[2])
                deps.update(s[3])

    def _commit_read(self, space, lo, hi, ev):
        for s in self.segs.get(space, ()):
            if s[0] < hi and lo < s[1]:
                s[3].append(ev)

    def _commit_write(self, space, lo, hi, ev):
        segs = self.segs.setdefault(space, [])
        out = []
        for s in segs:
            if s[0] < hi and lo < s[1]:
                if s[0] < lo:
                    out.append([s[0], lo, s[2], list(s[3])])
                if hi < s[1]:
                    out.append([hi, s[1], s[2], list(s[3])])
            else:
                out.append(s)
        out.append([lo, hi, ev, []])
        self.segs[space] = out

    def op(self, eng, fn, reads=(), writes=(), dma=False, cc=False, tag=""):
        deps = set()
        for v in reads:
            for (sp, lo, hi) in v.regs:
                self._dep_read(sp, lo, hi, deps)
        for v in writes:
            for (sp, lo, hi) in v.regs:
                self._dep_write(sp, lo, hi, deps)
        i = len(self.eng_ops[eng])
        if not SELF_SYNC or eng == "tensor":
            deps = {d for d in deps if not (d[0] == "c" and d[1] == eng)}
        if dma:
            q = eng
            assert q in DMAQS, f"dma on engine {eng}"
            self.dma_n[q] += 1
            ev = ("d", q, self.dma_n[q])
        elif cc:
            self.cc_n += 1
            ev = ("cc", self.cc_n)
        else:
            ev = ("c", eng, i)
        for d in deps:
            if d[0] == "c":
                self.needed.add(d)
        self.eng_ops[eng].append(dict(fn=fn, deps=deps, ev=ev, tag=tag))
        for v in reads:
            for (sp, lo, hi) in v.regs:
                self._commit_read(sp, lo, hi, ev)
        for v in writes:
            for (sp, lo, hi) in v.regs:
                self._commit_write(sp, lo, hi, ev)
        return ev

    def barrier_ev(self, eng):
        ops = self.eng_ops[eng]
        if not ops:
            return None
        ev = ops[-1]["ev"]
        if ev[0] == "c":
            self.needed.add(ev)
        return ev

    def finalize(self):
        nc = self.nc
        es = self._es if self._es is not None else ExitStack()
        self._es = es
        sems = {}
        for e in ENGS:
            sems[("c", e)] = es.enter_context(nc.semaphore("sem_" + e))
        for q in DMAQS:
            sems[("d", q)] = es.enter_context(nc.semaphore("dma_" + q))
        sems[("cc",)] = es.enter_context(nc.semaphore("sem_cc"))

        val = {}
        inc_at = {e: {} for e in ENGS}
        for e in ENGS:
            idxs = sorted(i for (k, ee, i) in self.needed if k == "c" and ee == e)
            for rank, i in enumerate(idxs):
                val[("c", e, i)] = rank + 1
                inc_at[e][i] = rank + 1

        def ev_to_wait(ev):
            if ev[0] == "c":
                return (("c", ev[1]), val[ev])
            if ev[0] == "d":
                return (("d", ev[1]), 16 * ev[2])
            return (("cc",), ev[1])

        block = es.enter_context(nc.Block())
        engobjs = {}

        def make_section(e):
            ops = self.eng_ops[e]

            def body(engine):
                done = {}
                for i, o in enumerate(ops):
                    for dep in sorted(o["deps"], key=str):
                        semk, v = ev_to_wait(dep)
                        if done.get(semk, 0) < v:
                            engine.wait_ge(sems[semk], v)
                            done[semk] = v
                    try:
                        inst = o["fn"](engine)
                    except Exception as ex:
                        raise RuntimeError(f"op tag={o['tag']} eng={e} i={i}: {ex}") from ex
                    ev = o["ev"]
                    if ev[0] == "d":
                        inst.then_inc(sems[("d", e)], 16)
                    elif ev[0] == "cc":
                        inst.then_inc(sems[("cc",)])
                    elif i in inc_at[e]:
                        inst.then_inc(sems[("c", e)], 1)
                if e in DMAQS and self.dma_n[e] > 0:
                    engine.wait_ge(sems[("d", e)], 16 * self.dma_n[e])
            return body

        for e in ENGS:
            if not self.eng_ops[e]:
                continue
            dec = getattr(block, e if e != "sync" else "sync")
            dec(make_section(e))
        es.close()
        return nc



def ceil_div(a, b):
    return (a + b - 1) // b


def PCOLS(pool):
    p = pool
    while isinstance(p, SubPool):
        p = p.parent
    return p.cols


class Ctx:

    def __init__(self, prog):
        self.p = prog
        self.const = {}
        self.uid = 0
        self.fconst_vals = []
        self._fc_pool = None

    def name(self, base):
        self.uid += 1
        return f"{base}{self.uid}"

    def fconst(self, value):
        value = float(value)
        if self._fc_pool is None:
            self._fc_dram = self.p.dram("fconsts", [128, 16], F32, kind="ExternalInput")
            self._fc_pool = self.p.sbuf("FC", 16, F32)
            emit_dma_load(self, "sync", self._fc_pool.v(0, 16), self._fc_dram, tag="fconsts")
        if value in self.fconst_vals:
            j = self.fconst_vals.index(value)
        else:
            self.fconst_vals.append(value)
            j = len(self.fconst_vals) - 1
            assert j < 16
        return self._fc_pool.v(j, 1)

    def host_fconsts(self):
        a = np.zeros((128, 16), np.float32)
        for j, v in enumerate(self.fconst_vals):
            a[:, j] = v
        return a


def emit_dma_load(ctx, eng, dst_view, src_ap_view, tag=""):
    p = ctx.p
    return p.op(eng, lambda q, o=dst_view.ap, i=src_ap_view.ap: q.dma_start(out=o, in_=i),
                reads=[src_ap_view], writes=[dst_view], dma=True, tag=tag)


def emit_dma_store(ctx, eng, dst_ap_view, src_view, tag=""):
    p = ctx.p
    return p.op(eng, lambda q, o=dst_ap_view.ap, i=src_view.ap: q.dma_start(out=o, in_=i),
                reads=[src_view], writes=[dst_ap_view], dma=True, tag=tag)


def xview(xt, f, c0, n):
    sp = xt.regs[0][0]
    return View(xt.ap[128 * f:128 * (f + 1), c0:c0 + n],
                [(sp, 2048 * f + c0, 2048 * f + c0 + n)])


def emit_xadd(ctx, f, c0, n, inv, scalar=None, op0=None, tag="xadd"):
    p = ctx.p
    xr = ctx.XR.v(0, n)
    src = xview(ctx.x_src, f, c0, n)
    emit_dma_load(ctx, "scalar", xr, src, tag=tag + ".ld")
    xo = ctx.XR.v(512, n)
    if scalar is None:
        emit_tt(ctx, xo, inv, xr, OP.add, tag=tag + ".add")
    else:
        emit_stt(ctx, xo, inv, scalar, xr, op0, OP.add, tag=tag + ".stt")
    dst = xview(ctx.x_dst, f, c0, n)
    emit_dma_store(ctx, "scalar", dst, xo, tag=tag + ".st")



def host_constants():
    c = {}
    c["IDB"] = np.eye(128).astype(BF)
    c["ID32"] = np.eye(128, dtype=np.float32)
    c["ONESC_B"] = np.ones((128, 1), BF)
    c["ONESC_F"] = np.ones((128, 1), np.float32)
    c["ONESR_B"] = np.ones((1, 128), BF)
    c["ONESR_F"] = np.ones((1, 128), np.float32)
    j = np.arange(128)[:, None]
    t = np.arange(128)[None, :]
    c["TRILT_B"] = (j <= t).astype(BF)
    m = np.zeros((128, 4, 512), np.float32)
    for a in range(4):
        m[:, a, :] = (128 * a + j) <= np.arange(512)[None, :]
    c["CMASK_B"] = m.reshape(128, 2048).astype(BF)
    e16 = np.zeros((128, 16, 16), np.float32)
    for mm_ in range(16):
        e16[:, mm_, mm_] = 1.0
    c["EHS16"] = e16.reshape(128, 256).astype(BF)
    c["DIFFT"] = np.maximum(t - j, 0).astype(np.float32)
    c["REVT"] = np.broadcast_to(127.0 - np.arange(128, dtype=np.float32)[None, :],
                                (128, 128)).copy()
    c["DECT"] = np.broadcast_to(np.arange(128, dtype=np.float32)[None, :] + 1.0,
                                (128, 128)).copy()
    c["REVC"] = (127.0 - np.arange(128, dtype=np.float32))[:, None].copy()
    invf = 1.0 / (10000.0 ** (np.arange(0, 64, 2, dtype=np.float32) / 64.0))
    tt_ = np.arange(2048, dtype=np.float32).reshape(16, 128)
    f_ = tt_[:, :, None] * invf[None, None, :]
    c["COSTOK"] = np.cos(f_).transpose(1, 0, 2).reshape(128, 512).astype(BF)
    c["SINTOK"] = np.sin(f_).transpose(1, 0, 2).reshape(128, 512).astype(BF)
    return c


def emit_load_consts(ctx, names):
    p = ctx.p
    hc = host_constants()
    sel = {n: hc[n] for n in names}
    total = 0
    metas = []
    for n, a in sel.items():
        assert a.shape[0] <= 128
        cols = a.shape[1] * (2 if a.dtype == np.dtype(BF) else 4) // 4
        metas.append((n, a))
    cb = sum(a.shape[1] for n, a in metas if a.dtype == np.dtype(BF))
    cf = sum(a.shape[1] for n, a in metas if a.dtype == np.float32)
    pb = p.sbuf("constB", max(cb, 1), BF16)
    pf = p.sbuf("constF", max(cf, 1), F32)
    ob = of = 0
    host_vals = {}
    for n, a in metas:
        dt = BF16 if a.dtype == np.dtype(BF) else F32
        dv = p.dram("c_" + n, list(a.shape), dt, kind="ExternalInput")
        host_vals["c_" + n] = a
        if dt == BF16:
            v = pb.v(ob, a.shape[1], 0, a.shape[0]); ob += a.shape[1]
        else:
            v = pf.v(of, a.shape[1], 0, a.shape[0]); of += a.shape[1]
        emit_dma_load(ctx, "sync", v, dv, tag="const:" + n)
        ctx.const[n] = v
    ctx.host_const_vals = host_vals
    return host_vals



def emit_mm(ctx, psv, lhsT, rhs, start, stop, tag=""):
    ctx.p.op("tensor",
             lambda t, o=psv.ap, l=lhsT.ap, r=rhs.ap, s0=start, s1=stop:
             t.matmul(o, l, r, start=s0, stop=s1),
             reads=[lhsT, rhs], writes=[psv], tag=tag)


def emit_transpose(ctx, psv, inv, ident, tag=""):
    np_ = inv.ap.shape[0]
    b0 = inv.ap.base_partition() if callable(inv.ap.base_partition) else inv.ap.base_partition
    dap = ident.ap
    if dap.shape[0] != np_ or b0 != 0:
        dap = ident.ap.tensor[b0:b0 + np_, 0:np_]
    ctx.p.op("tensor",
             lambda t, o=psv.ap, i=inv.ap, d=dap: t.transpose(o, i, d),
             reads=[inv, ident], writes=[psv], tag=tag)


def emit_copy(ctx, eng, dst, src, tag=""):
    fn = (lambda q, o=dst.ap, i=src.ap: q.tensor_copy(o, i)) if eng != "scalar" else \
         (lambda q, o=dst.ap, i=src.ap: q.copy(o, i))
    ctx.p.op(eng, fn, reads=[src], writes=[dst], tag=tag)


def emit_tt(ctx, dst, a, b, op, tag="", eng="vector"):
    ctx.p.op(eng, lambda q, o=dst.ap, x=a.ap, y=b.ap, oo=op: q.tensor_tensor(o, x, y, oo),
             reads=[a, b], writes=[dst], tag=tag)


def emit_ts(ctx, dst, a, s1, s2, op0, op1=None, tag="", eng="vector", accum=None):
    reads = [a]
    s1a = s1
    if isinstance(s1, View):
        reads.append(s1); s1a = s1.ap
    s2a = s2
    if isinstance(s2, View):
        reads.append(s2); s2a = s2.ap
    writes = [dst]
    aa = None
    if accum is not None:
        writes.append(accum); aa = accum.ap
    if op1 is None:
        ctx.p.op(eng, lambda q, o=dst.ap, x=a.ap, u=s1a, w=s2a, o0=op0, ac=aa:
                 q.tensor_scalar(o, x, u, w, o0, accum_out=ac),
                 reads=reads, writes=writes, tag=tag)
    else:
        ctx.p.op(eng, lambda q, o=dst.ap, x=a.ap, u=s1a, w=s2a, o0=op0, o1=op1, ac=aa:
                 q.tensor_scalar(o, x, u, w, o0, o1, accum_out=ac),
                 reads=reads, writes=writes, tag=tag)


def emit_stt(ctx, dst, in0, scal, in1, op0, op1, tag=""):
    reads = [in0, in1]
    sa = scal
    if isinstance(scal, View):
        reads.append(scal); sa = scal.ap
    ctx.p.op("vector", lambda q, o=dst.ap, a=in0.ap, s=sa, b=in1.ap, o0=op0, o1=op1:
             q.scalar_tensor_tensor(o, a, s, b, o0, o1),
             reads=reads, writes=[dst], tag=tag)


def emit_act(ctx, dst, src, func, bias=0.0, scale=1.0, accum=None, tag=""):
    if isinstance(bias, float) and bias != 0.0:
        bias = ctx.fconst(bias)
    if isinstance(bias, View) and bias.ap.shape[0] != src.ap.shape[0]:
        b0 = src.ap.base_partition
        b0 = b0() if callable(b0) else b0
        j = bias.regs[0][1]
        bias = View(bias.ap.tensor[b0:b0 + src.ap.shape[0], j:j + 1], bias.regs)
    reads = [src]
    ba = bias
    if isinstance(bias, View):
        reads.append(bias); ba = bias.ap
    sa = scale
    if isinstance(scale, View):
        reads.append(scale); sa = scale.ap
    writes = [dst]
    aa = None
    if accum is not None:
        writes.append(accum); aa = accum.ap
    if aa is None:
        ctx.p.op("scalar", lambda q, o=dst.ap, i=src.ap, f=func, b=ba, s=sa:
                 q.activation(o, i, f, bias=b, scale=s),
                 reads=reads, writes=writes, tag=tag)
    else:
        ctx.p.op("scalar", lambda q, o=dst.ap, i=src.ap, f=func, b=ba, s=sa, ac=aa:
                 q.activation(o, i, f, bias=b, scale=s, accum_out=ac),
                 reads=reads, writes=writes, tag=tag)


def load_weight_bf16(ctx, dram_ap, regs, rows, cols, dst_pool, dst_col0, tag="w"):
    nrt = ceil_div(rows, 128)
    for k in range(nrt):
        r0 = 128 * k
        nr = min(128, rows - r0)
        dv = View(dram_ap[r0:r0 + nr, :], regs)
        sv = dst_pool.v(dst_col0 + k * cols, cols, 0, nr)
        emit_dma_load(ctx, "gpsimd", sv, dv, tag=tag + f".load{k}")
    return nrt


def wtile(pool, col0, cols, k, nr=128, c0=0, ncol=None):
    ncol = cols if ncol is None else ncol
    base = pool.off + col0 + k * cols + c0
    return View(pool.t[0:nr, base: base + ncol],
                [(pool.name, base, base + ncol)])


def emit_transpose_tiles(ctx, src_pool, src_col0, rows, cols, dst_pool, dst_col0,
                         pst, tag="wt", row0=0, nrows=None):
    nrows = rows - row0 if nrows is None else nrows
    nrt = ceil_div(nrows, 128)
    nct = ceil_div(cols, 128)
    idb = ctx.const["IDB"]
    for c in range(nct):
        c0 = 128 * c
        ncol = min(128, cols - c0)
        for k0 in range(0, nrt, 8):
            kn = min(nrt, k0 + 8)
            for k in range(k0, kn):
                nr = min(128, nrows - 128 * k)
                svv = wtile(src_pool, src_col0, cols, k, nr, c0, ncol)
                ps = View(pst.t[0:ncol, 128 * (k - k0):128 * (k - k0) + nr],
                          [(pst.name, 128 * (k - k0), 128 * (k - k0) + nr)])
                emit_transpose(ctx, ps, svv, idb, tag=tag + f".t{c}.{k}")
            w = min(nrows, 128 * kn) - 128 * k0
            pse = View(pst.t[0:ncol, 0:w], [(pst.name, 0, w)])
            b0 = dst_pool.off + dst_col0 + c * rows + row0 + 128 * k0
            dv = View(dst_pool.t[0:ncol, b0:b0 + w], [(dst_pool.name, b0, b0 + w)])
            emit_copy(ctx, "vector", dv, pse, tag=tag + f".ev{c}.{k0}")


def wt_tile(pool, col0, rows, c, ncol=128, r0=0, nr=None):
    nr = rows if nr is None else nr
    base = pool.off + col0 + c * rows + r0
    return View(pool.t[0:ncol, base:base + nr], [(pool.name, base, base + nr)])



class ColVec:
    def __init__(self, pool, col0, n):
        self.pool = pool
        self.col0 = col0
        self.n = n

    def col(self, j, part0=0, nparts=128):
        return self.pool.v(self.col0 + j, 1, part0, nparts)

    def cols(self, j, n, part0=0, nparts=128):
        return self.pool.v(self.col0 + j, n, part0, nparts)


def declare_colvec(ctx, name, ncols, nparts=128):
    p = ctx.p
    dv = p.dram("cv_" + name, [nparts, ncols], F32, kind="ExternalInput")
    col0 = ctx.smf_off
    ctx.smf_off += ncols
    v = ctx.SMF.v(col0, ncols, 0, nparts)
    emit_dma_load(ctx, "sync", v, dv, tag="cv." + name)
    cv = ColVec(ctx.SMF, col0, ncols)
    ctx.colvecs[name] = cv
    return cv




def emit_l0aa(ctx):
    XN = ctx.XN
    PSA, PSB, PSC, PST = ctx.PSA, ctx.PSB, ctx.PSC, ctx.PST
    W, AB = ctx.W, ctx.ACTB
    idb = ctx.const["IDB"]
    trilt = ctx.const["TRILT_B"]
    qkvb = ctx.colvecs["l0aa_qkvb"]
    projb = ctx.colvecs["l0aa_projb"]

    load_weight_bf16(ctx, ctx.win["L0_attn_a_qkv_w"].ap, ctx.win["L0_attn_a_qkv_w"].regs,
                     192, 1024, W, 0, tag="qkvw")
    emit_transpose_tiles(ctx, W, 0, 192, 1024, W, 2048, PST, tag="qkvwT")
    load_weight_bf16(ctx, ctx.win["L0_attn_a_proj_w"].ap, ctx.win["L0_attn_a_proj_w"].regs,
                     1024, 1024, W, 4096, tag="projw")

    for c in range(4):
        for k in range(8):
            lA = wt_tile(W, 2048, 192, k, 128, 0, 128)
            lB = wt_tile(W, 2048, 192, k, 128, 128, 64)
            emit_mm(ctx, PSA.v(0, 512), lA, XN.v(2048 * k + 512 * c, 512), k == 0, k == 7, tag="qkv.A")
            emit_mm(ctx, PSB.v(0, 512, 0, 64), lB, XN.v(2048 * k + 512 * c, 512), k == 0, k == 7, tag="qkv.B")
        emit_ts(ctx, AB.v(512 * c, 512, 0, 64), PSA.v(0, 512, 0, 64),
                qkvb.col(0, 0, 64), None, OP.add, tag="qkv.evq")
        emit_ts(ctx, AB.v(2048 + 512 * c, 512, 0, 64), PSA.v(0, 512, 64, 64),
                qkvb.col(0, 64, 64), None, OP.add, tag="qkv.evk")
        emit_ts(ctx, AB.v(4096 + 512 * c, 512, 0, 64), PSB.v(0, 512, 0, 64),
                qkvb.col(1, 0, 64), None, OP.add, tag="qkv.evv")

    qT = lambda c0, n: AB.v(c0, n, 0, 64)
    kTv = lambda c0, n: AB.v(2048 + c0, n, 0, 64)
    vT = lambda c0, n: AB.v(4096 + c0, n, 0, 64)

    state_ps = PSC.v(0, 64, 0, 64)
    for c in range(16):
        c0 = 128 * c
        ktok = AB.v(8192, 64)
        vtok = AB.v(8256, 64)
        emit_transpose(ctx, PST.v(0, 64), kTv(c0, 128), idb, tag="laa.tk")
        emit_transpose(ctx, PST.v(64, 64), vT(c0, 128), idb, tag="laa.tv")
        emit_copy(ctx, "scalar", ktok, PST.v(0, 64), tag="laa.ek")
        emit_copy(ctx, "scalar", vtok, PST.v(64, 64), tag="laa.ev")
        psS = PSA.v(0, 128)
        emit_mm(ctx, psS, kTv(c0, 128), qT(c0, 128), True, True, tag="laa.S")
        smask = AB.v(8320, 128)
        emit_tt(ctx, smask, psS, trilt, OP.mult, tag="laa.mask")
        psO = PSB.v(512, 128, 0, 64)
        state_sb = AB.v(8448, 64, 0, 64)
        emit_mm(ctx, psO, vtok, smask, True, c == 0, tag="laa.intra")
        if c > 0:
            emit_mm(ctx, psO, state_sb, qT(c0, 128), False, True, tag="laa.inter")
        oT = AB.v(6144 + c0, 128, 0, 64)
        emit_copy(ctx, "vector", oT, psO, tag="laa.eo")
        emit_mm(ctx, state_ps, ktok, vtok, c == 0, c == 15, tag="laa.state")
        emit_copy(ctx, "vector", state_sb, state_ps, tag="laa.es")

    for k in range(8):
        b = 4096 + 1024 * k
        for w in (512, 256, 128, 64):
            a0 = W.v(b, w)
            a1 = W.v(b + w, w)
            emit_tt(ctx, a0, a0, a1, OP.add, tag="weff.sum")
    for k in range(8):
        b = 4096 + 1024 * k
        emit_transpose(ctx, PST.v(0, 128, 0, 64), W.v(b, 64), idb, tag="weff.t")
        emit_copy(ctx, "vector", W.v(12288 + 128 * k, 128, 0, 64), PST.v(0, 128, 0, 64), tag="weff.ev")

    for f in range(8):
        weffT = W.v(12288 + 128 * f, 128, 0, 64)
        for c in range(4):
            ps = PSA.v(512 + 512 * (c % 2), 512)
            emit_mm(ctx, ps, weffT, AB.v(6144 + 512 * c, 512, 0, 64), True, True, tag="out.mm")
            emit_xadd(ctx, f, 512 * c, 512, ps, projb.col(f), OP.add, tag="l0aa.res")



CONST_NAMES = ["IDB", "ID32", "ONESC_B", "ONESC_F", "ONESR_B", "ONESR_F",
               "TRILT_B", "CMASK_B", "EHS16"]


def make_base_ctx(nc, w_cols=14336, ab_cols=8704, actf_cols=2048, consts=None, sm_pool=True, no_x=False):
    p = Prog(nc)
    ctx = Ctx(p)
    ctx.win = {}
    ctx.colvecs = {}
    ctx.smf_off = 0
    ctx.xr_slot = 0
    ctx.XN = p.sbuf("XN", 8 * 2048, BF16)
    ctx.W = p.sbuf("Wp", w_cols, BF16)
    if ab_cols:
        ctx.ACTB = p.sbuf("ACTB", ab_cols, BF16)
    ctx.ACTF = p.sbuf("ACTF", max(actf_cols, 2048), F32)
    ctx.XS = p.sbuf("XSp", 2048, F32)
    ctx.HCF = p.sbuf("HCFp", 2048, F32)
    ctx.XR = p.sbuf("XRp", 1024, F32)
    ctx.LNSQ = p.sbuf("LNSQ", 512, BF16)
    ctx.SMF = p.sbuf("SMF", 256, F32)
    ctx.PSA = p.psum("PSA", 2048, F32)
    ctx.PSB = p.psum("PSB", 1024, F32)
    ctx.PSC = p.psum("PSC", 512, F32)
    ctx.PST = p.psum("PST", 1024, BF16)
    emit_load_consts(ctx, consts if consts is not None else CONST_NAMES)
    return ctx


def declare_weights(ctx, names_shapes_dtypes):
    for name, shape, dt in names_shapes_dtypes:
        ctx.win[name] = ctx.p.dram(name, list(shape), dt, kind="ExternalInput")


LAST_CTX = None


def build_prog_l0aa():
    global LAST_CTX
    nc = bass.Bass("TRN2", target_bir_lowering=False)
    ctx = make_base_ctx(nc)
    LAST_CTX = ctx
    p = ctx.p
    x_in = p.dram("x_in", [1024, 2048], F32, kind="ExternalInput")
    x_out = p.dram("x_out", [1024, 2048], F32, kind="ExternalOutput")
    declare_weights(ctx, [("L0_attn_a_qkv_w", (192, 1024), F32),
                          ("L0_attn_a_proj_w", (1024, 1024), F32)])
    declare_colvec(ctx, "l0aa_qkvb", 2)
    declare_colvec(ctx, "l0aa_projb", 8)
    ctx.x_src, ctx.x_dst = x_in, x_out
    emit_ln_hbm(ctx, x_in)
    emit_l0aa(ctx)
    p.finalize()
    return nc


def host_l0aa_inputs(inputs):
    qkvb = np.zeros((128, 2), np.float32)
    qkvb[0:64, 0] = inputs["L0_attn_a_qkv_b"][0:64]
    qkvb[64:128, 0] = inputs["L0_attn_a_qkv_b"][64:128]
    qkvb[0:64, 1] = inputs["L0_attn_a_qkv_b"][128:192]
    projb = np.ascontiguousarray(inputs["L0_attn_a_proj_b"].reshape(8, 128).T)
    return {"L0_attn_a_qkv_w": inputs["L0_attn_a_qkv_w"],
            "L0_attn_a_proj_w": inputs["L0_attn_a_proj_w"],
            "cv_l0aa_qkvb": qkvb, "cv_l0aa_projb": projb}


def host_const_inputs():
    return {"c_" + n: a for n, a in host_constants().items()}



def emit_l0ffn_phaseW(ctx, W):
    p = ctx.p
    PST = ctx.PST
    idb = ctx.const["IDB"]
    wguT = p.dram("sc_wguT", [24, 128, 2048], BF16)
    wdT = p.dram("sc_wdT", [8, 128, 3072], BF16)
    ctx.ffn_cache = (wguT, wdT)
    for half, dv in ((0, ctx.win["L0_ffn_w_gate"]),
                     (1, ctx.win["L0_ffn_w_up"])):
        for m in range(24):
            raw = W.v(0, 1024)
            src = View(dv.ap[128 * m:128 * (m + 1), :], dv.regs)
            emit_dma_load(ctx, "gpsimd", raw, src, tag=f"ffn.w{half}{m}")
            for c in range(8):
                emit_transpose(ctx, PST.v(128 * c, 128), W.v(128 * c, 128), idb,
                               tag="ffn.wT")
            wt = W.v(1024, 1024)
            emit_copy(ctx, "vector", wt, PST.v(0, 1024), tag="ffn.wev")
            dst = View(wguT.ap[m][:, 1024 * half:1024 * (half + 1)], wguT.regs)
            emit_dma_store(ctx, "sync", dst, wt, tag="ffn.wsp")
    for f in range(8):
        raw = W.v(0, 3072)
        src = View(ctx.win["L0_ffn_w_down"].ap[128 * f:128 * (f + 1), :],
                   ctx.win["L0_ffn_w_down"].regs)
        emit_dma_load(ctx, "gpsimd", raw, src, tag=f"ffn.wd{f}")
        for mb in range(3):
            for j in range(8):
                emit_transpose(ctx, PST.v(128 * j, 128),
                               W.v(128 * (8 * mb + j), 128), idb, tag="ffn.wdT")
            wt = W.v(3072, 1024)
            emit_copy(ctx, "vector", wt, PST.v(0, 1024), tag="ffn.wdev")
            dst = View(wdT.ap[f, :, 1024 * mb:1024 * (mb + 1)], wdT.regs)
            emit_dma_store(ctx, "sync", dst, wt, tag="ffn.wdsp")


def emit_l0ffn(ctx):
    p = ctx.p
    XN = ctx.XN
    PSA, PSB, PST = ctx.PSA, ctx.PSB, ctx.PST
    W, W2, H = ctx.W, ctx.W2, ctx.H
    idb = ctx.const["IDB"]

    if getattr(ctx, "ffn_cache", None) is None:
        emit_l0ffn_phaseW(ctx, W)
    wguT, wdT = ctx.ffn_cache

    for tc in range(4):
        t0 = 512 * tc
        for m in range(24):
            s = m % 3
            vgu = W2.v(2048 * s, 2048)
            emit_dma_load(ctx, "sync", vgu, View(wguT.ap[m], wguT.regs), tag="ffn.lg")
            pg = PSA.v(0, 512)
            pu = PSA.v(512, 512)
            for k in range(8):
                emit_mm(ctx, pg, W2.v(2048 * s + 128 * k, 128),
                        XN.v(2048 * k + t0, 512), k == 0, k == 7, tag="ffn.g")
                emit_mm(ctx, pu, W2.v(2048 * s + 1024 + 128 * k, 128),
                        XN.v(2048 * k + t0, 512), k == 0, k == 7, tag="ffn.u")
            sg = ctx.ACTB.v(0, 512)
            emit_act(ctx, sg, pg, AF.Silu, scale=1.0 / 127.0, tag="ffn.silu")
            emit_tt(ctx, H.v(512 * m, 512), sg, pu, OP.mult, tag="ffn.h")
        for f in range(8):
            s = f % 2
            wv = W2.v(6144 + 3072 * s, 3072)
            emit_dma_load(ctx, "sync", wv, View(wdT.ap[f], wdT.regs), tag="ffn.lwd")
            pd = PSB.v(0, 512)
            for m in range(24):
                lhsT = W2.v(6144 + 3072 * s + 128 * m, 128)
                emit_mm(ctx, pd, lhsT, H.v(512 * m, 512),
                        m == 0, m == 23, tag="ffn.d")
            emit_xadd(ctx, f, t0, 512, pd, 1.0 / (127.0 * 127.0),
                      OP.mult, tag="ffn.res")


def build_prog_l0ffn():
    global LAST_CTX
    nc = bass.Bass("TRN2", target_bir_lowering=False)
    ctx = make_base_ctx(nc, w_cols=4096, ab_cols=512, actf_cols=4096)
    LAST_CTX = ctx
    p = ctx.p
    ctx.W2 = p.sbuf("W2p", 12288, BF16)
    ctx.H = p.sbuf("Hp", 12288, BF16)
    x_in = p.dram("x_in", [1024, 2048], F32, kind="ExternalInput")
    x_out = p.dram("x_out", [1024, 2048], F32, kind="ExternalOutput")
    declare_weights(ctx, [("L0_ffn_w_gate", (3072, 1024), I32),
                          ("L0_ffn_w_up", (3072, 1024), I32),
                          ("L0_ffn_w_down", (1024, 3072), I32)])
    ctx.x_src, ctx.x_dst = x_in, x_out
    emit_ln_hbm(ctx, x_in)
    emit_l0ffn(ctx)
    p.finalize()
    return nc


def host_l0ffn_inputs(inputs):
    return {k: inputs[k] for k in ("L0_ffn_w_gate", "L0_ffn_w_up", "L0_ffn_w_down")}



def emit_l0ab(ctx, M0=16.0):
    p = ctx.p
    XN = ctx.XN
    PSA, PSB, PST = ctx.PSA, ctx.PSB, ctx.PST
    W = ctx.W
    QH, KH, VAUG, YH = ctx.QH, ctx.KH, ctx.VAUG, ctx.YH
    ATTN = ctx.ATTN
    idb = ctx.const["IDB"]
    cmask = ctx.const["CMASK_B"]
    onesr = ctx.const["ONESR_F"]
    onesb = ctx.const["ONESC_B"]

    for g in range(2):
        for wi, (wname, dstpool) in enumerate((("L0_attn_b_Wq", QH), ("L0_attn_b_Wk", KH))):
            sl = (2 * g + wi) % 2
            dv = ctx.win[wname]
            for r in range(4):
                src = View(dv.ap[512 * g + 128 * r:512 * g + 128 * (r + 1), :], dv.regs)
                emit_dma_load(ctx, "gpsimd", W.v(8192 * sl + 1024 * r, 1024), src, tag="ab.wl")
            emit_transpose_tiles(ctx, W, 8192 * sl, 512, 1024, W, 8192 * sl + 4096, PST,
                                 tag="ab.wT")
            for i in range(4):
                for c in range(4):
                    ps = PSA.v(512 * (c % 2), 512)
                    for k in range(8):
                        lhsT = wt_tile(W, 8192 * sl + 4096, 512, k, 128, 128 * i, 128)
                        emit_mm(ctx, ps, lhsT, XN.v(2048 * k + 512 * c, 512),
                                k == 0, k == 7, tag="ab.qk")
                    emit_copy(ctx, "vector", dstpool.v(2048 * i + 512 * c, 512), ps, tag="ab.qkev")
        dv = ctx.win["L0_attn_b_Wv"]
        for r in range(4):
            src = View(dv.ap[512 * g + 128 * r:512 * g + 128 * (r + 1), :], dv.regs)
            emit_dma_load(ctx, "gpsimd", W.v(1024 * r, 1024), src, tag="ab.wvl")
        emit_transpose_tiles(ctx, W, 0, 512, 1024, W, 4096, PST, tag="ab.wvT")
        ones_dst = View(VAUG.t[0:128, VAUG.off:VAUG.off + 520 * 16]
                        .rearrange("p (j h o) -> p j h o", h=8, o=65)[:, :, :, 64:65],
                        [(VAUG.name, VAUG.off, VAUG.off + 520 * 16)])
        ones_src = View(onesb.ap.unsqueeze(1).to_broadcast([128, 16, 8, 1]), onesb.regs)
        p.op("vector", lambda q, o=ones_dst.ap, i=ones_src.ap: q.tensor_copy(o, i),
             reads=[ones_src], writes=[ones_dst], tag="ab.vones")
        for tj in range(16):
            ps = PSA.v(0, 512)
            for k in range(8):
                rhs = wt_tile(W, 4096, 512, k, 128, 0, 512)
                emit_mm(ctx, ps, XN.v(2048 * k + 128 * tj, 128), rhs,
                        k == 0, k == 7, tag="ab.v")
            dst = View(VAUG.t[0:128, VAUG.off + 520 * tj:VAUG.off + 520 * tj + 520]
                       .rearrange("p (h o) -> p h o", h=8)[:, :, 0:64],
                       [(VAUG.name, VAUG.off + 520 * tj, VAUG.off + 520 * tj + 520)])
            srcps = View(PSA.t[0:128, 0:512].rearrange("p (h o) -> p h o", h=8),
                         [(PSA.name, 0, 512)])
            p.op("vector", lambda q, o=dst.ap, i=srcps.ap: q.tensor_copy(o, i),
                 reads=[srcps], writes=[dst], tag="ab.vev")
        dv = ctx.win["L0_attn_b_Wo"]
        for r in range(8):
            src = View(dv.ap[128 * r:128 * (r + 1), 512 * g:512 * (g + 1)], dv.regs)
            emit_dma_load(ctx, "gpsimd", W.v(512 * r, 512), src, tag="ab.wol")
        emit_transpose_tiles(ctx, W, 0, 1024, 512, W, 4096, PST, tag="ab.woT")
        for c in range(4):
            for hp in range(4):
                hpair = (2 * hp, 2 * hp + 1)
                nj = 4 * c + 4

                def atslot(j, i):
                    return (ATTN.v(512 * (j % 2), 512) if i == 0
                            else W.v(512 * (j % 2), 512))

                def a_front(j, i, c=c):
                    hh = hpair[i]
                    kblk, kbase = hh // 2, 64 * (hh % 2)
                    psS = PSA.v(512 * (2 * i + j % 2), 512)
                    emit_mm(ctx, psS,
                            KH.v(2048 * kblk + 128 * j, 128, kbase, 64),
                            QH.v(2048 * kblk + 512 * c, 512, kbase, 64),
                            True, True, tag="ab.s")
                    at = atslot(j, i)
                    emit_act(ctx, at, psS, AF.Exp, bias=-M0, scale=0.125, tag="ab.exp")
                    if j >= 4 * c:
                        emit_tt(ctx, at, at,
                                View(cmask.ap.tensor[0:128, cmask.regs[0][1] + 512 * (j - 4 * c):
                                     cmask.regs[0][1] + 512 * (j - 4 * c + 1)], cmask.regs),
                                OP.mult, tag="ab.mask")

                def a_av(j, i, nj=nj):
                    hh = hpair[i]
                    emit_mm(ctx, PSB.v(512 * i, 512, 0, 65),
                            VAUG.v(520 * j + 65 * hh, 65), atslot(j, i),
                            j == 0, j == nj - 1, tag="ab.av")

                for j in range(nj):
                    a_front(j, 0)
                    a_front(j, 1)
                    if j > 0:
                        a_av(j - 1, 0)
                        a_av(j - 1, 1)
                a_av(nj - 1, 0)
                a_av(nj - 1, 1)
                for i in range(2):
                    hh = hpair[i]
                    kblk, kbase = hh // 2, 64 * (hh % 2)
                    rl = ctx.XS.v(512 * i, 512, 0, 1)
                    p.op("vector", lambda q, o=rl.ap, i_=PSB.t[64:65, 512 * i:512 * i + 512]:
                         q.reciprocal(o, i_),
                         reads=[PSB.v(512 * i, 512, 64, 1)], writes=[rl], tag="ab.rl")
                    psb2 = (ctx.PSC.v(0, 512, 0, 64) if i == 0
                            else PSA.v(0, 512, 0, 64))
                    emit_mm(ctx, psb2, View(onesr.ap[0:1, 0:64], onesr.regs), rl,
                            True, True, tag="ab.bc")
                    rlb = (ATTN.v(1024, 512, 0, 64) if i == 0
                           else W.v(1024, 512, 0, 64))
                    emit_copy(ctx, "scalar", rlb, psb2, tag="ab.rlbev")
                    emit_tt(ctx, YH.v(512 * kblk, 512, kbase, 64),
                            PSB.v(512 * i, 512, 0, 64), rlb, OP.mult, tag="ab.y")
            for f in range(8):
                ps = PSA.v(1024, 512)
                for i in range(4):
                    lhsT = wt_tile(W, 4096, 1024, i, 128, 128 * f, 128)
                    emit_mm(ctx, ps, lhsT, YH.v(512 * i, 512), i == 0, i == 3, tag="ab.o")
                emit_xadd(ctx, f, 512 * c, 512, ps, tag="l0ab.res")


def build_prog_l0ab():
    global LAST_CTX
    nc = bass.Bass("TRN2", target_bir_lowering=False)
    ctx = make_base_ctx(nc, w_cols=16384, ab_cols=512, actf_cols=4096)
    LAST_CTX = ctx
    p = ctx.p
    ctx.QH = p.sbuf("QH", 8192, BF16)
    ctx.KH = p.sbuf("KH", 8192, BF16)
    ctx.VAUG = p.sbuf("VAUGp", 520 * 16, BF16)
    ctx.YH = p.sbuf("YHp", 2048, BF16)
    ctx.ATTN = p.sbuf("ATTNp", 1536, BF16)
    x_in = p.dram("x_in", [1024, 2048], F32, kind="ExternalInput")
    x_out = p.dram("x_out", [1024, 2048], F32, kind="ExternalOutput")
    declare_weights(ctx, [("L0_attn_b_Wq", (1024, 1024), F32),
                          ("L0_attn_b_Wk", (1024, 1024), F32),
                          ("L0_attn_b_Wv", (1024, 1024), F32),
                          ("L0_attn_b_Wo", (1024, 1024), F32)])
    ctx.x_src, ctx.x_dst = x_in, x_out
    emit_ln_hbm(ctx, x_in)
    emit_l0ab(ctx)
    p.finalize()
    return nc


def host_l0ab_inputs(inputs):
    return {k: inputs[k] for k in ("L0_attn_b_Wq", "L0_attn_b_Wk",
                                   "L0_attn_b_Wv", "L0_attn_b_Wo")}



def grouped(v, g=16):
    return View(v.ap.rearrange("p (g k) -> p g k", g=g), v.regs)


def gslice(v, k0, k1, g=16):
    ap = v.ap.rearrange("p (g k) -> p g k", g=g)
    return View(ap[:, :, k0:k1], v.regs)


def emit_l1ffn(ctx, idx):
    p = ctx.p
    XN = ctx.XN
    PSA, PSB, PSC, PST = ctx.PSA, ctx.PSB, ctx.PSC, ctx.PST
    W, SCOR, QB, F2, SF, RN, ACC = (ctx.W, ctx.SCOR, ctx.QB, ctx.F2, ctx.SF,
                                    ctx.RN, ctx.ACC)
    idb = ctx.const["IDB"]
    id32 = ctx.const["ID32"]
    ehs = ctx.const["EHS16"]
    bq = ctx.colvecs["l1ffn_bq"]
    gam = ctx.colvecs["l1ffn_gam"]
    bet = ctx.colvecs["l1ffn_bet"]

    scr = getattr(ctx, "l1ffn_scr", None)
    if scr is None:
        scr = p.dram("sc_qT", [16, 128, 2048], BF16)
        ctx.l1ffn_scr = scr
    cc_in = p.dram(f"cc_in{idx}", [128, 128], F32)
    cc_out = p.dram(f"cc_out{idx}", [128, 128], F32)
    ccv_in = View(cc_in.ap, [(f"dram:cc_in{idx}", 0, 128)])
    ccv_out = View(cc_out.ap, [(f"dram:cc_out{idx}", 0, 128)])

    load_weight_bf16(ctx, ctx.win["L1_ffn_Wq"].ap, ctx.win["L1_ffn_Wq"].regs,
                     1024, 2048, W, 0, tag="ff.wq")
    for m in range(16):
        for ch in range(2):
            for c2 in range(2):
                c = 2 * ch + c2
                ps = PSA.v(1024 * ch + 512 * c2, 512)
                for k in range(8):
                    lhsT = wtile(W, 0, 2048, k, 128, 128 * m, 128)
                    emit_mm(ctx, ps, lhsT, XN.v(2048 * k + 512 * c, 512),
                            k == 0, k == 7, tag="ff.q")
            qb = QB.v(2048 * ch + 1024 * (m % 2), 1024)
            emit_act(ctx, qb, PSA.v(1024 * ch, 1024), AF.Identity, bias=bq.col(m),
                     accum=ACC.v(m + 16 * ch, 1), tag="ff.qev")
            emit_act(ctx, SF.v(1024 * ch, 1024), PSA.v(1024 * ch, 1024), AF.Square,
                     bias=bq.col(m), accum=ACC.v(32 + m + 16 * ch, 1), tag="ff.qsq")
            emit_dma_store(ctx, "sync",
                           View(scr.ap[m][:, 1024 * ch:1024 * (ch + 1)], scr.regs),
                           qb, tag="ff.spill")
    emit_tt(ctx, ACC.v(88, 16), ACC.v(0, 16), ACC.v(16, 16), OP.add, tag="ff.cmb")
    emit_tt(ctx, ACC.v(104, 16), ACC.v(32, 16), ACC.v(48, 16), OP.add, tag="ff.cmb2")
    for j, (base, par) in enumerate(((88, 0), (88, 1), (104, 0), (104, 1))):
        srcap = bass.AP(ACC.t, ACC.off + base + par, [[PCOLS(ACC), 128], [2, 8]])
        srcv = View(srcap, [(ACC.name, ACC.off + base, ACC.off + base + 16)])
        dstv = ACC.v(64 + j, 1)
        p.op("vector", lambda q, o=dstv.ap, i=srcap:
             q.reduce_sum(o, i, mybir.AxisListType.X),
             reads=[srcv], writes=[dstv], tag="ff.red")
    emit_dma_store(ctx, "gpsimd", View(cc_in.ap[:, 0:4], ccv_in.regs),
                   ACC.v(64, 4), tag="ff.ccst")
    p.op("gpsimd", lambda q, a=cc_in.ap, b=cc_out.ap: q.collective_compute(
        "AllReduce", OP.add, replica_groups=[list(range(NCORES))], ins=[a], outs=[b]),
        reads=[ccv_in], writes=[ccv_out], cc=True, tag="ff.cc")
    emit_dma_load(ctx, "gpsimd", ACC.v(68, 4), View(cc_out.ap[:, 0:4], ccv_out.regs), tag="ff.ccld")
    NTOT = 131072.0
    mean = ACC.v(72, 2)
    emit_ts(ctx, mean, ACC.v(68, 2), 1.0 / NTOT, None, OP.mult, tag="ff.mean")
    msq = ACC.v(74, 2)
    emit_tt(ctx, msq, mean, mean, OP.mult, tag="ff.msq")
    var = ACC.v(76, 2)
    emit_ts(ctx, var, ACC.v(70, 2), 1.0 / NTOT, None, OP.mult, tag="ff.var0")
    emit_tt(ctx, var, var, msq, OP.subtract, tag="ff.var")
    sd = ACC.v(78, 2)
    emit_act(ctx, sd, var, AF.Sqrt, bias=1e-5, tag="ff.sd")
    rsd = ACC.v(80, 2)
    p.op("vector", lambda q, o=rsd.ap, i=sd.ap: q.reciprocal(o, i),
         reads=[sd], writes=[rsd], tag="ff.rsd")
    bnsc = ACC.v(82, 2)
    emit_tt(ctx, bnsc, rsd, gam.cols(0, 2), OP.mult, tag="ff.bnsc")
    bnsh = ACC.v(84, 2)
    emit_stt(ctx, bnsh, mean, -1.0, bnsc, OP.mult, OP.mult, tag="ff.bnsh0")
    emit_tt(ctx, bnsh, bnsh, bet.cols(0, 2), OP.add, tag="ff.bnsh")

    for j, nm in enumerate(("L1_ffn_sub_key0", "L1_ffn_sub_key1")):
        kb = F2.v(8704 + 128 * j, 128, 0, 32)
        emit_dma_load(ctx, "gpsimd", kb, ctx.win[nm], tag="ff.sk")
        ksq = SF.v(0, 128, 0, 32)
        emit_act(ctx, ksq, kb, AF.Square, tag="ff.ksq")
        kss = SF.v(128, 1, 0, 32)
        p.op("vector", lambda q, o=kss.ap, i=ksq.ap: q.reduce_sum(o, i, mybir.AxisListType.X),
             reads=[ksq], writes=[kss], tag="ff.kss")
        ksd = SF.v(129, 1, 0, 32)
        emit_act(ctx, ksd, kss, AF.Sqrt, tag="ff.ksd")
        emit_ts(ctx, ksd, ksd, 1e-12, None, OP.max, tag="ff.kmx")
        krn = SF.v(130, 1, 0, 32)
        p.op("vector", lambda q, o=krn.ap, i=ksd.ap: q.reciprocal(o, i),
             reads=[ksd], writes=[krn], tag="ff.krc")
        emit_ts(ctx, kb, kb, krn, None, OP.mult, tag="ff.kn")
        emit_transpose(ctx, PST.v(0, 32), kb, idb, tag="ff.kT")
        emit_copy(ctx, "vector", F2.v(8960 + 32 * j, 32), PST.v(0, 32), tag="ff.kTev")
    k0T = F2.v(8960, 32)
    k1T = F2.v(8992, 32)

    for m in range(16):
        qb = QB.v(0, 2048)
        emit_dma_load(ctx, "sync", qb, View(scr.ap[m], scr.regs), tag="ff.qld")
        par = m % 2
        qbn = QB.v(0, 2048)
        emit_ts(ctx, qbn, qb, ACC.v(82 + par, 1), ACC.v(84 + par, 1), OP.mult, OP.add, tag="ff.bn")
        emit_act(ctx, F2.v(0, 2048), qbn, AF.Square, tag="ff.sq")
        for c in range(4):
            lhsT = View(ehs.ap.tensor[0:128, ehs.regs[0][1] + 16 * m:
                                      ehs.regs[0][1] + 16 * (m + 1)], ehs.regs)
            emit_mm(ctx, PSA.v(512 * c, 512, 0, 16), lhsT, F2.v(512 * c, 512),
                    m == 0, m == 15, tag="ff.l2")
        for tc in range(16):
            b = 512 * (tc % 2)
            sv = View(PSB.t[0:128, b:b + 32], [(PSB.name, b, b + 512)])
            emit_mm(ctx, sv, QB.v(128 * tc, 128), (k0T if par == 0 else k1T),
                    True, True, tag="ff.s")
            emit_copy(ctx, "vector", SCOR.v(512 * tc + 32 * m, 32), sv, tag="ff.sev")
    emit_copy(ctx, "vector", SF.v(0, 2048, 0, 16), PSA.v(0, 2048, 0, 16), tag="ff.ss")
    emit_act(ctx, SF.v(0, 2048, 0, 16), SF.v(0, 2048, 0, 16), AF.Sqrt, tag="ff.ssd")
    emit_ts(ctx, SF.v(0, 2048, 0, 16), SF.v(0, 2048, 0, 16), 1e-12, None, OP.max, tag="ff.smx")
    p.op("vector", lambda q, o=SF.v(2048, 2048, 0, 16).ap, i=SF.v(0, 2048, 0, 16).ap:
         q.reciprocal(o, i), reads=[SF.v(0, 2048, 0, 16)],
         writes=[SF.v(2048, 2048, 0, 16)], tag="ff.rn")
    for tc in range(16):
        ps = PSC.v(0, 16)
        emit_transpose(ctx, ps, SF.v(2048 + 128 * tc, 128, 0, 16), id32, tag="ff.rT")
        emit_copy(ctx, "vector", RN.v(16 * tc, 16), ps, tag="ff.rTev")

    emit_l1ffn_phaseB(ctx)


def emit_l1ffn_phaseB(ctx):
    p = ctx.p
    XN = ctx.XN
    PSA, PSB, PSC, PST = ctx.PSA, ctx.PSB, ctx.PSC, ctx.PST
    W, SCOR, F2, SF, RN = ctx.W, ctx.SCOR, ctx.F2, ctx.SF, ctx.RN
    idb = ctx.const["IDB"]

    load_weight_bf16(ctx, ctx.win["L1_ffn_expert_down"].ap,
                     ctx.win["L1_ffn_expert_down"].regs, 1024, 1024, W, 8192, tag="ff.dn")
    emit_transpose_tiles(ctx, W, 8192, 1024, 1024, W, 0, PST, tag="ff.dnT")
    load_weight_bf16(ctx, ctx.win["L1_ffn_expert_up"].ap,
                     ctx.win["L1_ffn_expert_up"].regs, 1024, 1024, W, 8192, tag="ff.up")

    def ffb_front(tc):
        t0 = 128 * tc
        o2 = 4608 * (tc % 2)
        for dc in range(2):
            ps = PSA.v(1024 * (tc % 2) + 512 * dc, 512)
            for k in range(8):
                rhs = wt_tile(W, 0, 1024, k, 128, 512 * dc, 512)
                emit_mm(ctx, ps, XN.v(2048 * k + t0, 128), rhs, k == 0, k == 7, tag="fb.eh")
            emit_act(ctx, F2.v(o2 + 2560 + 512 * dc, 512), ps, AF.Gelu, tag="fb.gelu")

    def ffb_chain(tc):
        o2 = 4608 * (tc % 2)
        s = SF.v(0, 512)
        emit_copy(ctx, "vector", s, SCOR.v(512 * tc, 512), tag="fb.s")
        wk = SF.v(512, 512)
        emit_copy(ctx, "vector", wk, SCOR.v(512 * tc, 512), tag="fb.wk")
        m1 = SF.v(1536, 16)
        for r in range(4):
            cur, curw = wk, 32
            dsts = [SF.v(1024, 256), SF.v(1280, 128), SF.v(1024, 64), SF.v(1280, 32),
                    (m1 if r == 3 else SF.v(1024, 16))]
            for step in range(5):
                half = curw // 2
                dst = dsts[step]
                emit_tt(ctx, grouped(dst), gslice(cur, 0, half), gslice(cur, half, curw),
                        OP.max, tag="fb.tm")
                cur, curw = dst, half
            if r < 3:
                mb = View(cur.ap.unsqueeze(2).to_broadcast([128, 16, 32]), cur.regs)
                eq = SF.v(1552, 512)
                emit_tt(ctx, grouped(eq), grouped(wk), mb, OP.is_ge, tag="fb.eq")
                pen = SF.v(2064, 512)
                emit_ts(ctx, pen, eq, 1e30, None, OP.mult, tag="fb.pen")
                emit_tt(ctx, wk, wk, pen, OP.subtract, tag="fb.msk")
        t4b = View(m1.ap.unsqueeze(2).to_broadcast([128, 16, 32]), m1.regs)
        ge = SF.v(1552, 512)
        emit_tt(ctx, grouped(ge), grouped(s), t4b, OP.is_ge, tag="fb.ge")
        for m in range(16):
            emit_act(ctx, SF.v(2576 + 32 * m, 32), SF.v(32 * m, 32), AF.Exp,
                     scale=RN.v(16 * tc + m, 1), tag="fb.exp")
        e = SF.v(2576, 512)
        z = SF.v(3088, 512)
        emit_tt(ctx, z, e, ge, OP.mult, tag="fb.z")
        cur, curw = z, 32
        for step, dst in enumerate([SF.v(3600, 256), SF.v(3856, 128), SF.v(3600, 64),
                                    SF.v(3856, 32), SF.v(3600, 16)]):
            half = curw // 2
            emit_tt(ctx, grouped(dst), gslice(cur, 0, half), gslice(cur, half, curw),
                    OP.add, tag="fb.zs")
            cur, curw = dst, half
        rz = SF.v(3984, 16)
        p.op("vector", lambda q, o=rz.ap, i=cur.ap: q.reciprocal(o, i),
             reads=[cur], writes=[rz], tag="fb.rz")
        pb = z
        rzb = View(rz.ap.unsqueeze(2).to_broadcast([128, 16, 32]), rz.regs)
        emit_tt(ctx, grouped(pb), grouped(z), rzb, OP.mult, tag="fb.p")
        Wb = SF.v(1024, 1024)
        tmp = SF.v(2064, 1024)
        for h in range(8):
            p0 = View(bass.AP(SF.t, SF.off + 3088 + 64 * h,
                              [[PCOLS(SF), 128], [1, 32], [0, 32]]), pb.regs)
            p1 = View(bass.AP(SF.t, SF.off + 3088 + 64 * h + 32,
                              [[PCOLS(SF), 128], [0, 32], [1, 32]]), pb.regs)
            dst = Wb if h == 0 else tmp
            emit_tt(ctx, View(dst.ap.rearrange("p (a b) -> p a b", a=32), dst.regs),
                    p0, p1, OP.mult, tag="fb.w")
            if h > 0:
                emit_tt(ctx, Wb, Wb, tmp, OP.add, tag="fb.wa")
        cb = F2.v(o2 + 3584, 1024)
        emit_tt(ctx, cb, Wb, F2.v(o2 + 2560, 1024), OP.mult, tag="fb.c")

    def ffb_tail(tc):
        o2 = 4608 * (tc % 2)
        cq = tc % 4
        for e8 in range(8):
            emit_transpose(ctx, PST.v(128 * e8, 128), F2.v(o2 + 3584 + 128 * e8, 128),
                           idb, tag="fb.cT")
        for e8 in range(8):
            emit_copy(ctx, "vector", F2.v(9216 + 512 * e8 + 128 * cq, 128),
                      PST.v(128 * e8, 128), tag="fb.cTev")
        if cq == 3:
            tg0 = 512 * (tc // 4)
            for f in range(8):
                ps = PSB.v(0, 512)
                for k in range(8):
                    lhsT = wtile(W, 8192, 1024, k, 128, 128 * f, 128)
                    emit_mm(ctx, ps, lhsT, F2.v(9216 + 512 * k, 512), k == 0, k == 7,
                            tag="fb.up")
                emit_xadd(ctx, f, tg0, 512, ps, tag="l1ffn.res")

    for tc in range(16):
        ffb_front(tc)
        if tc > 0:
            ffb_tail(tc - 1)
        ffb_chain(tc)
    ffb_tail(15)


def build_prog_l1ffn(debug=False):
    global LAST_CTX
    nc = bass.Bass("TRN2", target_bir_lowering=False)
    ctx = make_base_ctx(nc, w_cols=16384, ab_cols=0, actf_cols=0, sm_pool=False,
                        consts=["IDB", "ID32", "ONESC_B", "ONESC_F", "ONESR_F", "EHS16"])
    LAST_CTX = ctx
    p = ctx.p
    ctx.SCOR = p.sbuf("SCOR", 8192, F32)
    ctx.QB = p.sbuf("QBp", 4096, BF16)
    ctx.F2 = p.sbuf("F2p", 13312, BF16)
    ctx.SF = p.sbuf("SFp", 4096, F32)
    ctx.RN = p.sbuf("RNp", 256, F32)
    ctx.ACC = p.sbuf("ACCp", 128, F32)
    x_in = p.dram("x_in", [1024, 2048], F32, kind="ExternalInput")
    x_out = p.dram("x_out", [1024, 2048], F32, kind="ExternalOutput")
    declare_weights(ctx, [("L1_ffn_Wq", (1024, 2048), F32),
                          ("L1_ffn_sub_key0", (32, 128), F32),
                          ("L1_ffn_sub_key1", (32, 128), F32),
                          ("L1_ffn_expert_down", (1024, 1024), F32),
                          ("L1_ffn_expert_up", (1024, 1024), F32)])
    declare_colvec(ctx, "l1ffn_bq", 16)
    declare_colvec(ctx, "l1ffn_gam", 2)
    declare_colvec(ctx, "l1ffn_bet", 2)
    ctx.x_src, ctx.x_dst = x_in, x_out
    emit_ln_hbm(ctx, x_in)
    emit_l1ffn(ctx, 0)
    if debug:
        dbg1 = p.dram("dbg_acc", [128, 96], F32, kind="ExternalOutput")
        emit_dma_store(ctx, "scalar", dbg1, ctx.ACC.v(0, 96), tag="dbg")
        dbg2 = p.dram("dbg_scor", [128, 8192], F32, kind="ExternalOutput")
        emit_dma_store(ctx, "scalar", dbg2, ctx.SCOR.v(0, 8192), tag="dbg")
        dbg3 = p.dram("dbg_rn", [128, 256], F32, kind="ExternalOutput")
        emit_dma_store(ctx, "scalar", dbg3, ctx.RN.v(0, 256), tag="dbg")
        dbg4 = p.dram("dbg_xn", [128, 2048], BF16, kind="ExternalOutput")
        emit_dma_store(ctx, "scalar", dbg4, ctx.XN.v(0, 2048), tag="dbg")
    p.finalize()
    return nc


def host_l1ffn_inputs(inputs):
    out = {k: inputs[k] for k in ("L1_ffn_Wq", "L1_ffn_sub_key0", "L1_ffn_sub_key1",
                                  "L1_ffn_expert_down", "L1_ffn_expert_up")}
    out["cv_l1ffn_bq"] = np.ascontiguousarray(inputs["L1_ffn_bq"].reshape(16, 128).T)
    out["cv_l1ffn_gam"] = np.ascontiguousarray(inputs["L1_ffn_bn_gamma"].reshape(2, 128).T)
    out["cv_l1ffn_bet"] = np.ascontiguousarray(inputs["L1_ffn_bn_beta"].reshape(2, 128).T)
    return out



def emit_l1aa(ctx):
    p = ctx.p
    XN = ctx.XN
    PSA, PSB, PSC, PST = ctx.PSA, ctx.PSB, ctx.PSC, ctx.PST
    W, QF, KF1, KF2, VT, STV, STF = (ctx.W, ctx.QF, ctx.KF1, ctx.KF2, ctx.VT,
                                     ctx.STV, ctx.STF)
    UN, MTOKB, MTOKF, FM, RB = ctx.UN, ctx.MTOKB, ctx.MTOKF, ctx.FM, ctx.RB
    idb = ctx.const["IDB"]
    trilt = ctx.const["TRILT_B"]
    onesr_b = ctx.const["ONESR_B"]
    onesc_f = ctx.const["ONESC_F"]
    nw = ctx.colvecs["l1aa_nw"]

    hbm_o = p.dram(ctx.name("aa_o"), [8, 128, 2048], F32)
    hbm_st = p.dram(ctx.name("aa_st"), [8, 128, 1024], F32)
    ov_o = View(hbm_o.ap, hbm_o.regs)

    for j, nm in enumerate(("L1_attn_a_fmq1_w", "L1_attn_a_fmq2_w",
                            "L1_attn_a_fmk1_w", "L1_attn_a_fmk2_w")):
        raw = W.v(0, 128)
        emit_dma_load(ctx, "gpsimd", raw, ctx.win[nm], tag="aa.fml")
        emit_transpose(ctx, PST.v(0, 128), raw, idb, tag="aa.fmT")
        emit_copy(ctx, "vector", FM.v(128 * j, 128), PST.v(0, 128), tag="aa.fmev")
    emit_ts(ctx, FM.v(0, 128), FM.v(0, 128), float(128.0 ** -0.5), None, OP.mult, tag="aa.fms")

    raw = W.v(0, 1024, 0, 8)
    emit_dma_load(ctx, "gpsimd", raw, ctx.win["L1_attn_a_gate_w"], tag="aa.gl")
    for k in range(8):
        emit_transpose(ctx, PST.v(8 * k, 8), W.v(128 * k, 128, 0, 8), idb, tag="aa.gT")
    emit_copy(ctx, "vector", W.v(1024, 64), PST.v(0, 64), tag="aa.gev")
    for tc in range(16):
        ps = PSC.v(0, 8)
        for k in range(8):
            emit_mm(ctx, ps, XN.v(2048 * k + 128 * tc, 128), W.v(1024 + 8 * k, 8),
                    k == 0, k == 7, tag="aa.logit")
        s = UN.v(0, 8)
        emit_copy(ctx, "vector", s, ps, tag="aa.lev")
        emit_tt(ctx, UN.v(8, 4), View(s.ap[:, 0:4], s.regs), View(s.ap[:, 4:8], s.regs), OP.max, tag="aa.m")
        emit_tt(ctx, UN.v(12, 2), UN.v(8, 2), UN.v(10, 2), OP.max, tag="aa.m")
        m1 = UN.v(16, 1)
        emit_tt(ctx, m1, UN.v(12, 1), UN.v(13, 1), OP.max, tag="aa.m")
        eq1 = UN.v(24, 8)
        emit_ts(ctx, eq1, s, m1, None, OP.is_ge, tag="aa.eq1")
        pen = UN.v(32, 8)
        emit_ts(ctx, pen, eq1, 1e30, None, OP.mult, tag="aa.pen")
        s2 = UN.v(40, 8)
        emit_tt(ctx, s2, s, pen, OP.subtract, tag="aa.s2")
        emit_tt(ctx, UN.v(8, 4), View(s2.ap[:, 0:4], s2.regs), View(s2.ap[:, 4:8], s2.regs), OP.max, tag="aa.m2")
        emit_tt(ctx, UN.v(12, 2), UN.v(8, 2), UN.v(10, 2), OP.max, tag="aa.m2")
        m2 = UN.v(17, 1)
        emit_tt(ctx, m2, UN.v(12, 1), UN.v(13, 1), OP.max, tag="aa.m2")
        eq2 = UN.v(32, 8)
        emit_ts(ctx, eq2, s2, m2, None, OP.is_ge, tag="aa.eq2")
        d12 = UN.v(18, 1)
        emit_tt(ctx, d12, m1, m2, OP.subtract, tag="aa.d")
        rw1 = UN.v(19, 1)
        emit_act(ctx, rw1, d12, AF.Sigmoid, tag="aa.rw1")
        rw2 = UN.v(20, 1)
        emit_act(ctx, rw2, d12, AF.Sigmoid, scale=-1.0, tag="aa.rw2")
        t1 = UN.v(40, 8)
        emit_ts(ctx, t1, eq1, rw1, None, OP.mult, tag="aa.t1")
        t2 = UN.v(48, 8)
        emit_ts(ctx, t2, eq2, rw2, None, OP.mult, tag="aa.t2")
        emit_tt(ctx, MTOKB.v(8 * tc, 8), t1, t2, OP.add, tag="aa.rw")
        emit_tt(ctx, MTOKF.v(8 * tc, 8), eq1, eq2, OP.add, tag="aa.mk")

    for g in range(2):
        t0 = 1024 * g
        for h in range(8):
            qwr = W.v(0, 1024)
            src = View(ctx.win["L1_attn_a_q_w"].ap[128 * h:128 * (h + 1), :],
                       ctx.win["L1_attn_a_q_w"].regs)
            emit_dma_load(ctx, "gpsimd", qwr, src, tag="aa.qwl")
            for fj in range(2):
                for dt in range(8):
                    psn = PSA.v(512 * fj + 128 * (dt % 4), 128)
                    emit_mm(ctx, psn, W.v(128 * dt, 128), FM.v(128 * fj, 128),
                            dt % 4 == 0, dt % 4 == 3, tag="aa.qc")
                    if dt % 4 == 3:
                        emit_copy(ctx, "vector", W.v(1024 + 1024 * fj + 128 * (dt - 3), 512),
                                  PSA.v(512 * fj, 512), tag="aa.qcev")
            for c in range(2):
                for fj in range(2):
                    ps = PSB.v(512 * fj, 512)
                    for k in range(8):
                        emit_mm(ctx, ps, W.v(1024 + 1024 * fj + 128 * k, 128),
                                XN.v(2048 * k + t0 + 512 * c, 512), k == 0, k == 7, tag="aa.qf")
                tmp = RB.v(1280, 512)
                emit_copy(ctx, "scalar", tmp, PSB.v(0, 512), tag="aa.qp1")
                emit_tt(ctx, QF.v(1024 * h + 512 * c, 512), tmp, PSB.v(512, 512),
                        OP.mult, tag="aa.qprod")
        for e in range(8):
            for h in range(8):
                kwr = W.v(0, 1024)
                src = View(ctx.win["L1_attn_a_k_w"].ap[e, 128 * h:128 * (h + 1), :],
                           ctx.win["L1_attn_a_k_w"].regs)
                emit_dma_load(ctx, "gpsimd", kwr, src, tag="aa.kwl")
                for fj in range(2):
                    for dt in range(8):
                        psn = PSA.v(512 * fj + 128 * (dt % 4), 128)
                        emit_mm(ctx, psn, W.v(128 * dt, 128), FM.v(256 + 128 * fj, 128),
                                dt % 4 == 0, dt % 4 == 3, tag="aa.kc")
                        if dt % 4 == 3:
                            emit_copy(ctx, "vector", W.v(1024 + 1024 * fj + 128 * (dt - 3), 512),
                                      PSA.v(512 * fj, 512), tag="aa.kcev")
                for c in range(2):
                    for fj in range(2):
                        ps = PSB.v(512 * fj, 512)
                        for k in range(8):
                            emit_mm(ctx, ps, W.v(1024 + 1024 * fj + 128 * k, 128),
                                    XN.v(2048 * k + t0 + 512 * c, 512), k == 0, k == 7, tag="aa.kf")
                    tmp = RB.v(1280, 512)
                    emit_copy(ctx, "scalar", tmp, PSB.v(0, 512), tag="aa.kp1")
                    emit_tt(ctx, KF1.v(1024 * h + 512 * c, 512), tmp, PSB.v(512, 512),
                            OP.mult, tag="aa.kprod")
            for cc in range(8):
                tcg = 8 * g + cc
                psr = View(PSA.t[0:1, 128 * cc:128 * (cc + 1)],
                           [(PSA.name, 512 * (cc // 4), 512 * (cc // 4) + 512)])
                emit_mm(ctx, psr, MTOKB.v(8 * tcg + e, 1), idb,
                        cc % 4 == 0, cc % 4 == 3, tag="aa.rrow")
            rrow = RB.v(1792, 1024, 0, 1)
            emit_copy(ctx, "vector", rrow, PSA.v(0, 1024, 0, 1), tag="aa.rrev")
            for c in range(2):
                psb = PSA.v(512 * (c % 2), 512)
                emit_mm(ctx, psb, onesr_b, RB.v(1792 + 512 * c, 512, 0, 1), True, True, tag="aa.rwb")
                emit_copy(ctx, "scalar", RB.v(512 * c, 512), psb, tag="aa.rwbev")
            for half in range(2):
                for r in range(4):
                    src = View(ctx.win["L1_attn_a_v_w"].ap[e, 512 * half + 128 * r:512 * half + 128 * (r + 1), :],
                               ctx.win["L1_attn_a_v_w"].regs)
                    emit_dma_load(ctx, "gpsimd", W.v(1024 * r, 1024), src, tag="aa.vwl")
                emit_transpose_tiles(ctx, W, 0, 1024, 1024, KF2, 0, PST,
                                     tag="aa.vwT", row0=512 * half, nrows=512)
            for tj in range(8):
                for n2 in range(2):
                    ps = PSB.v(512 * n2, 512)
                    for k in range(8):
                        rhs = wt_tile(KF2, 0, 1024, k, 128, 512 * n2, 512)
                        emit_mm(ctx, ps, XN.v(2048 * k + t0 + 128 * tj, 128), rhs,
                                k == 0, k == 7, tag="aa.v")
                    emit_ts(ctx, VT.v(1024 * tj + 512 * n2, 512), ps,
                            MTOKF.v(8 * (8 * g + tj) + e, 1), None, OP.mult, tag="aa.vev")
            if g == 1:
                emit_dma_load(ctx, "sync", STF.v(0, 1024),
                              View(hbm_st.ap[e], hbm_st.regs), tag="aa.stld")
                emit_copy(ctx, "vector", STV.v(0, 1024), STF.v(0, 1024), tag="aa.stcast0")
            KT = ctx.KT
            for h in range(8):
                for c8 in range(8):
                    srcv = KF1.v(1024 * h + 128 * c8, 128)
                    dstv = KT.v(1024 * h + 128 * c8, 128)
                    q = "sync" if (h + c8) % 2 == 0 else "scalar"
                    p.op(q, lambda qq, o=dstv.ap, i=srcv.ap:
                         qq.dma_start(out=o, in_=i, transpose=True),
                         reads=[srcv], writes=[dstv], dma=True, tag="aa.ktd")
            for grp in range(2):
                for ci in range(4):
                    c = 4 * grp + ci
                    c0 = 128 * c
                    first = (g == 0 and c == 0)

                    def bundle(hp, first=first, c=c, c0=c0, ci=ci, grp=grp):
                        stv = STV.v(128 * hp, 128)
                        stf = STF.v(128 * hp, 128)
                        ov = PSA.v(1024 + 512 * (hp % 2), 128)
                        vchunk = VT.v(1024 * c + 128 * hp, 128)
                        smask = RB.v(1024 + 128 * (hp % 4), 128)
                        emit_mm(ctx, ov, vchunk, smask, True, first, tag="aa.intra")
                        if not first:
                            emit_mm(ctx, ov, stv, QF.v(1024 * hp + c0, 128),
                                    False, True, tag="aa.inter")
                        stv_ps = PSB.v(512 * (hp % 2), 128)
                        emit_mm(ctx, stv_ps, KT.v(1024 * hp + c0, 128), vchunk,
                                True, True, tag="aa.state")
                        if first:
                            emit_copy(ctx, "vector", stf, stv_ps, tag="aa.stcp")
                        else:
                            emit_tt(ctx, stf, stf, stv_ps, OP.add, tag="aa.stadd")
                        emit_copy(ctx, "scalar", stv, stf, tag="aa.stcast")
                        emit_tt(ctx, UN.v(512 * hp + 128 * ci, 128), ov,
                                RB.v(512 * grp + 128 * ci, 128), OP.mult, tag="aa.ob")

                    for h in range(8):
                        psS = PSA.v(512 * (h % 2), 128)
                        emit_mm(ctx, psS, KF1.v(1024 * h + c0, 128), QF.v(1024 * h + c0, 128),
                                True, True, tag="aa.S")
                        emit_tt(ctx, RB.v(1024 + 128 * (h % 4), 128), psS, trilt,
                                OP.mult, tag="aa.smask")
                        if h >= 2:
                            bundle(h - 2)
                    bundle(6)
                    bundle(7)
                dst = View(hbm_o.ap[:, :, t0 + 512 * grp:t0 + 512 * (grp + 1)].rearrange("h p t -> p h t"),
                           hbm_o.regs)
                srcv = View(UN.t[0:128, UN.off:UN.off + 4096].rearrange("p (h t) -> p h t", h=8),
                            [(UN.name, UN.off, UN.off + 4096)])
                if e == 0:
                    p.op("gpsimd", lambda q, o=dst.ap, i=srcv.ap: q.dma_start(out=o, in_=i),
                         reads=[srcv], writes=[dst], dma=True, tag="aa.ost0")
                else:
                    p.op("gpsimd", lambda q, o=dst.ap, i=srcv.ap:
                         q.dma_start(out=o, in_=i, accum_op=OP.add),
                         reads=[srcv], writes=[dst], dma=True, tag="aa.ost")
            if g == 0:
                emit_dma_store(ctx, "sync", View(hbm_st.ap[e], hbm_st.regs),
                               STF.v(0, 1024), tag="aa.stsp")
        for r in range(2):
            for rr_ in range(4):
                src = View(ctx.win["L1_attn_a_o_w"].ap[512 * r + 128 * rr_:512 * r + 128 * (rr_ + 1), :],
                           ctx.win["L1_attn_a_o_w"].regs)
                emit_dma_load(ctx, "gpsimd", W.v(1024 * rr_, 1024), src, tag="aa.owl")
            emit_transpose_tiles(ctx, W, 0, 1024, 1024, KF2, 0, PST,
                                 tag="aa.owT", row0=512 * r, nrows=512)
        for h in range(8):
            ot = UN.v(0, 1024)
            emit_dma_load(ctx, "sync", ot,
                          View(hbm_o.ap[h, :, t0:t0 + 1024], hbm_o.regs), tag="aa.old")
            sq = UN.v(1024, 1024)
            emit_act(ctx, sq, ot, AF.Square, tag="aa.osq")
            for c in range(2):
                ps = View(PSA.t[0:1, 512 * c:512 * (c + 1)], [(PSA.name, 512 * c, 512 * (c + 1))])
                emit_mm(ctx, ps, onesc_f, UN.v(1024 + 512 * c, 512), True, True, tag="aa.oss")
            rr2 = UN.v(2048, 1024, 0, 1)
            emit_copy(ctx, "vector", rr2, PSA.v(0, 1024, 0, 1), tag="aa.ossev")
            emit_act(ctx, rr2, rr2, AF.Ln, scale=1.0 / 128.0, bias=1e-5, tag="aa.oln")
            emit_act(ctx, rr2, rr2, AF.Exp, scale=-0.5, tag="aa.orst")
            for c in range(2):
                psb = PSA.v(1024 + 512 * (c % 2), 512)
                emit_mm(ctx, psb, ctx.const["ONESR_F"], UN.v(2048 + 512 * c, 512, 0, 1),
                        True, True, tag="aa.rbc")
                emit_stt(ctx, QF.v(1024 * h + 512 * c, 512), UN.v(512 * c, 512),
                         nw.col(0), psb, OP.mult, OP.mult, tag="aa.onrm")
        for f in range(8):
            for c in range(2):
                ps = PSB.v(0, 512)
                for k in range(8):
                    lhsT = wt_tile(KF2, 0, 1024, k, 128, 128 * f, 128)
                    emit_mm(ctx, ps, lhsT, QF.v(1024 * k + 512 * c, 512), k == 0, k == 7, tag="aa.ow")
                emit_xadd(ctx, f, t0 + 512 * c, 512, ps, tag="l1aa.res")


def build_prog_l1aa():
    global LAST_CTX
    nc = bass.Bass("TRN2", target_bir_lowering=False)
    ctx = make_base_ctx(nc, w_cols=4096, ab_cols=0, actf_cols=0, sm_pool=False,
                        consts=["IDB", "TRILT_B", "ONESR_B", "ONESC_B", "ONESC_F", "ONESR_F"])
    LAST_CTX = ctx
    p = ctx.p
    ctx.QF = p.sbuf("QFp", 8192, BF16)
    ctx.KF1 = p.sbuf("KF1p", 8192, BF16)
    ctx.KF2 = p.sbuf("KF2p", 8192, BF16)
    ctx.VT = p.sbuf("VTp", 8192, BF16)
    ctx.STV = p.sbuf("STVp", 1024, BF16)
    ctx.STF = p.sbuf("STFp", 1024, F32)
    ctx.UN = p.sbuf("UNp", 6144, F32)
    ctx.MTOKB = p.sbuf("MTOKBp", 128, BF16)
    ctx.MTOKF = p.sbuf("MTOKFp", 128, F32)
    ctx.FM = p.sbuf("FMp", 512, BF16)
    ctx.RB = p.sbuf("RBp", 2816, BF16)
    ctx.KT = p.sbuf("KTp", 8192, BF16)

    class _Shift:
        def __init__(self, pool, off):
            self.pool, self.off = pool, off
        def v(self, c0, n, p0=0, np_=None):
            return self.pool.v(self.off + c0, n, p0, np_ if np_ is not None else 128)
    ctx.ACTF = _Shift(ctx.UN, 4096)
    x_in = p.dram("x_in", [1024, 2048], F32, kind="ExternalInput")
    x_out = p.dram("x_out", [1024, 2048], F32, kind="ExternalOutput")
    declare_weights(ctx, [("L1_attn_a_gate_w", (8, 1024), F32),
                          ("L1_attn_a_q_w", (1024, 1024), F32),
                          ("L1_attn_a_k_w", (8, 1024, 1024), F32),
                          ("L1_attn_a_v_w", (8, 1024, 1024), F32),
                          ("L1_attn_a_fmq1_w", (128, 128), F32),
                          ("L1_attn_a_fmq2_w", (128, 128), F32),
                          ("L1_attn_a_fmk1_w", (128, 128), F32),
                          ("L1_attn_a_fmk2_w", (128, 128), F32),
                          ("L1_attn_a_o_w", (1024, 1024), F32)])
    declare_colvec(ctx, "l1aa_nw", 1)
    ctx.x_src, ctx.x_dst = x_in, x_out
    emit_ln_hbm(ctx, x_in)
    emit_l1aa(ctx)
    p.finalize()
    return nc


def host_l1aa_inputs(inputs):
    out = {k: inputs[k] for k in ("L1_attn_a_gate_w", "L1_attn_a_q_w",
                                  "L1_attn_a_k_w", "L1_attn_a_v_w",
                                  "L1_attn_a_fmq1_w", "L1_attn_a_fmq2_w",
                                  "L1_attn_a_fmk1_w", "L1_attn_a_fmk2_w",
                                  "L1_attn_a_o_w")}
    out["cv_l1aa_nw"] = np.ascontiguousarray(inputs["L1_attn_a_norm_w"].reshape(128, 1))
    return out



class RePack:

    def __init__(self, scr, S, nk, ng):
        self.scr, self.S, self.nk, self.ng = scr, S, nk, ng


def emit_repack(ctx, dram_view, rows, cols, col0, ncols, tag, S=4, row0=0, nrows=None,
                qs=("scalar", "gpsimd")):
    p = ctx.p
    nrows = rows if nrows is None else nrows
    nk = nrows // 128
    nm = ncols // 128
    S = min(S, nm)
    assert nm % S == 0
    ng = nm // S
    CW = 128 * S
    scr = p.dram(ctx.name("rp"), [ng, 128, nk * CW], BF16)
    for k in range(nk):
        src = View(dram_view.ap[row0 + 128 * k:row0 + 128 * (k + 1),
                                col0:col0 + ncols], dram_view.regs)
        buf = ctx.WS.v(0, ncols)
        emit_dma_load(ctx, "gpsimd", buf, src, tag=tag + ".l")
        splits = np.linspace(0, ng, 3).astype(int)
        for qi in range(2):
            g0, g1 = int(splits[qi]), int(splits[qi + 1])
            if g0 == g1:
                continue
            dst = View(scr.ap[g0:g1, :, CW * k:CW * (k + 1)].rearrange("g p c -> p g c"),
                       scr.regs)
            sv = View(ctx.WS.t[0:128, ctx.WS.off + CW * g0:ctx.WS.off + CW * g1]
                      .rearrange("p (g c) -> p g c", c=CW),
                      [(ctx.WS.name, ctx.WS.off + CW * g0, ctx.WS.off + CW * g1)])
            emit_dma_store(ctx, qs[qi], dst, sv, tag=tag + ".s")
    return RePack(scr, S, nk, ng)


def emit_wload_g(ctx, rp, g, slot_col, tag="wl", queue="sync"):
    w = rp.nk * 128 * rp.S
    wb0 = ctx.WB.off + slot_col
    dv = View(ctx.WB.t[0:128, wb0:wb0 + w], [(ctx.WB.name, wb0, wb0 + w)])
    emit_dma_load(ctx, queue, dv, View(rp.scr.ap[g], rp.scr.regs), tag=tag)
    S = rp.S

    def tile(k, mj=0, ncol=128):
        c0 = wb0 + 128 * S * k + 128 * mj
        return View(ctx.WB.t[0:128, c0:c0 + ncol], [(ctx.WB.name, c0, c0 + ncol)])
    return tile


def emit_rms_chunk(ctx, src_pool, src_base, t0, dst_pool, tag):
    p = ctx.p
    onescb = ctx.const["ONESC_B"]
    onesr = ctx.const["ONESR_F"]
    XS = ctx.XS
    psr = View(ctx.PSC.t[0:1, 0:512], [(ctx.PSC.name, 0, 512)])
    for i in range(8):
        sv = src_pool.v(src_base + 2048 * i + t0, 512)
        emit_act(ctx, ctx.LNSQ.v(0, 512), sv, AF.Square, tag=tag + ".sq")
        emit_mm(ctx, psr, onescb, ctx.LNSQ.v(0, 512), i == 0, i == 7, tag=tag + ".ss")
    row = XS.v(512, 512, 0, 1)
    emit_ts(ctx, row, psr, 1.0 / 1024.0, None, OP.mult, tag=tag + ".ms")
    emit_act(ctx, row, row, AF.Ln, bias=1.1920929e-07, tag=tag + ".ln")
    emit_act(ctx, row, row, AF.Exp, scale=-0.5, tag=tag + ".ex")
    bc = ctx.PSC.v(0, 512)
    emit_mm(ctx, bc, onesr, row, True, True, tag=tag + ".bc")
    for i in range(8):
        sv = src_pool.v(src_base + 2048 * i + t0, 512)
        emit_tt(ctx, dst_pool.v(512 * i, 512), sv, bc, OP.mult, tag=tag + ".ap")


def emit_l1ab_repacks(ctx, ws_pool, batch):
    win = ctx.win
    saved_ws = getattr(ctx, "WS", None)
    ctx.WS = ws_pool
    if not hasattr(ctx, "ab_rps"):
        ctx.ab_rps = [None] * 13
    r = ctx.ab_rps
    if batch == 0:
        q0 = ("sync", "gpsimd")
        r[0] = emit_repack(ctx, win["L1_attn_b_in_proj_w"], 1024, 4096, 2048, 2048,
                           "ab.rg", qs=q0)
        r[1] = emit_repack(ctx, win["L1_attn_b_in_proj_w"], 1024, 4096, 0, 2048,
                           "ab.rv", qs=q0)
        r[2] = emit_repack(ctx, win["L1_attn_b_B_proj_w"], 1024, 1024, 0, 1024,
                           "ab.rB", qs=q0)
        r[3] = emit_repack(ctx, win["L1_attn_b_C_proj_w"], 1024, 1024, 0, 1024,
                           "ab.rC", qs=q0)
    elif batch == 1:
        qg = getattr(ctx, "rp_store_qs", ("scalar", "gpsimd"))
        r[4] = emit_repack(ctx, win["L1_attn_b_out_proj_w"], 2048, 1024, 0, 1024,
                           "ab.rop", S=2, qs=qg)
        r[5] = emit_repack(ctx, win["L1_attn_b_m_fc_w"], 1024, 4096, 0, 4096, "ab.rmf", qs=qg)
        r[6] = [emit_repack(ctx, win["L1_attn_b_m_proj_w"], 4096, 1024, 0, 1024,
                            f"ab.rmp{q}", S=2, row0=1024 * q, nrows=1024, qs=qg)
                for q in range(4)]
    else:
        qg = getattr(ctx, "rp_store_qs", ("scalar", "gpsimd"))
        r[7] = emit_repack(ctx, win["L1_attn_b_a_fc_w"], 1024, 4096, 0, 4096, "ab.raf", qs=qg)
        r[8] = [emit_repack(ctx, win["L1_attn_b_a_proj_w"], 4096, 1024, 0, 1024,
                            f"ab.rap{q}", S=2, row0=1024 * q, nrows=1024, qs=qg)
                for q in range(4)]
        r[9] = emit_repack(ctx, win["L1_attn_b_q_w"], 1024, 1024, 0, 1024, "ab.rq", qs=qg)
        r[10] = emit_repack(ctx, win["L1_attn_b_k_w"], 1024, 256, 0, 256, "ab.rk", S=2, qs=qg)
        r[11] = emit_repack(ctx, win["L1_attn_b_v_w"], 1024, 256, 0, 256, "ab.rvw", S=2, qs=qg)
        r[12] = emit_repack(ctx, win["L1_attn_b_o_w"], 1024, 1024, 0, 1024, "ab.ro", qs=qg)
    if saved_ws is not None:
        ctx.WS = saved_ws


def emit_l1ab(ctx):
    p = ctx.p
    XN, XM = ctx.XN, ctx.XM
    PSA, PSB, PSC, PST = ctx.PSA, ctx.PSB, ctx.PSC, ctx.PST
    NB, VB, VACT, SG, BCM, XS, SCB = (ctx.NB, ctx.VB, ctx.VACT, ctx.SG,
                                      ctx.BCM, ctx.XS, ctx.SCB)
    idb = ctx.const["IDB"]
    trilt = ctx.const["TRILT_B"]
    cv = ctx.colvecs
    win = ctx.win

    lnr = p.dram(ctx.name("ab_lnr"), [2, 2048], F32)
    emit_dma_store(ctx, "scalar", View(lnr.ap[0:1, :], lnr.regs),
                   ctx.ACTF.v(0, 2048, 0, 1), tag="ab.lnra")
    emit_dma_store(ctx, "scalar", View(lnr.ap[1:2, :], lnr.regs),
                   XS.v(0, 2048, 0, 1), tag="ab.lnrb")
    sp_sg = p.dram(ctx.name("ab_sg"), [4, 16, 128, 512], BF16)
    sp_h = p.dram(ctx.name("ab_h"), [4, 32, 128, 512], BF16)
    sp_q = p.dram(ctx.name("ab_q"), [16, 128, 1024], BF16)

    if not hasattr(ctx, "ab_rps"):
        emit_l1ab_repacks(ctx, ctx.WS, 0)
        emit_l1ab_repacks(ctx, ctx.WS, 1)
        emit_l1ab_repacks(ctx, ctx.WS, 2)
    (rp_gate, rp_val, rp_B, rp_C, rp_op, rp_mfc, rp_mpr, rp_afc, rp_apr,
     rp_qw, rp_kw, rp_vw, rp_ow) = ctx.ab_rps

    rmc = XS.v(1024, 8, 0, 128)
    emit_tt(ctx, rmc, cv["ab_rm0"].cols(0, 8), cv["ab_rm1"].cols(0, 8), OP.add, tag="ab.rm")
    for i in range(8):
        emit_ts(ctx, XM.v(2048 * i, 2048), XN.v(2048 * i, 2048),
                View(rmc.ap[:, i:i + 1], rmc.regs), None, OP.mult, tag="ab.xm")

    U = lambda c, n, p0=0, np_=128: XS.v(1032 + c, n, p0, np_)
    dtb = cv["ab_dtb"].cols(0, 16)
    alog = cv["ab_alog"].cols(0, 16)
    ax = U(0, 16)
    emit_act(ctx, ax, dtb, AF.Abs, tag="ab.abs")
    emit_act(ctx, ax, ax, AF.Exp, scale=-1.0, tag="ab.e")
    emit_act(ctx, ax, ax, AF.Ln, bias=1.0, tag="ab.l")
    mx = U(16, 16)
    emit_ts(ctx, mx, dtb, 0.0, None, OP.max, tag="ab.mx")
    dt = U(32, 16)
    emit_tt(ctx, dt, mx, ax, OP.add, tag="ab.dt")
    a_ = U(48, 16)
    emit_act(ctx, a_, alog, AF.Exp, tag="ab.A0")
    emit_ts(ctx, a_, a_, -1.0, None, OP.mult, tag="ab.A")
    adt = U(64, 16)
    emit_tt(ctx, adt, a_, dt, OP.mult, tag="ab.adt")
    lndt = U(80, 16)
    emit_act(ctx, lndt, dt, AF.Ln, tag="ab.lndt")
    cdec = U(96, 16)
    emit_ts(ctx, cdec, adt, 128.0, None, OP.mult, tag="ab.cd0")
    emit_act(ctx, cdec, cdec, AF.Exp, tag="ab.cdec")
    for h in range(16):
        lm = ctx.LMT.v(128 * h, 128)
        emit_act(ctx, lm, ctx.const["DIFFT"], AF.Exp,
                 scale=U(64 + h, 1), bias=U(80 + h, 1), tag="ab.lmt")
        emit_tt(ctx, lm, lm, trilt, OP.mult, tag="ab.lmm")
        dr = ctx.DRT.v(128 * h, 128)
        emit_act(ctx, dr, ctx.const["REVT"], AF.Exp,
                 scale=U(64 + h, 1), bias=U(80 + h, 1), tag="ab.drt")
        do = ctx.DOT.v(128 * h, 128)
        emit_act(ctx, do, ctx.const["DECT"], AF.Exp, scale=U(64 + h, 1), tag="ab.dot")
        emit_act(ctx, XS.v(1144 + h, 1), ctx.const["REVC"], AF.Exp,
                 scale=U(64 + h, 1), bias=U(80 + h, 1), tag="ab.drtc")

    for tc in range(4):
        t0 = 512 * tc
        emit_rms_chunk(ctx, XM, 0, t0, NB, "ab.n1")
        for g4 in range(4):
            wt = emit_wload_g(ctx, rp_gate, g4, 4096 * (g4 % 2), tag="ab.wg")
            for mj in range(4):
                m = 4 * g4 + mj
                ps = PSA.v(512 * (m % 2), 512)
                for k in range(8):
                    emit_mm(ctx, ps, wt(k, mj), NB.v(512 * k, 512), k == 0, k == 7, tag="ab.g")
                emit_act(ctx, SG.v(512 * (m % 4), 512), ps, AF.Silu, tag="ab.sgev")
                if m % 4 != 3:
                    continue
                dst = View(sp_sg.ap[tc, m - 3:m + 1].rearrange("m p c -> p m c"), sp_sg.regs)
                sv = View(SG.t[0:128, SG.off:SG.off + 2048].rearrange("p (m c) -> p m c", c=512),
                          [(SG.name, SG.off, SG.off + 2048)])
                emit_dma_store(ctx, "scalar", dst, sv, tag="ab.sgsp")
    for i in range(16):
        p.op("gpsimd", lambda q, o=VB.v(515 * i, 3).ap: q.memset(o, 0.0),
             writes=[VB.v(515 * i, 3)], tag="ab.cz")
    def value_group(tc, g4, mj, box={}):
        if mj == 0:
            box[g4] = emit_wload_g(ctx, rp_val, g4, 4096 * (g4 % 2), tag="ab.wv")
        wt = box[g4]
        m = 4 * g4 + mj
        ps = ctx.PSC.v(0, 512)
        for k in range(8):
            emit_mm(ctx, ps, wt(k, mj), NB.v(512 * k, 512), k == 0, k == 7, tag="ab.v")
        emit_copy(ctx, "scalar", VB.v(515 * m + 3, 512), ps, tag="ab.vev")

    emit_rms_chunk(ctx, XM, 0, 0, NB, "ab.n1b")
    for g4 in range(4):
        for mj in range(4):
            value_group(0, g4, mj)
    for tc in range(4):
        t0 = 512 * tc
        for g in range(2):
            wtB = emit_wload_g(ctx, rp_B, g, 0, tag="ab.wB")
            wtC = emit_wload_g(ctx, rp_C, g, 4096, tag="ab.wC", queue="scalar")
            for mj in range(4):
                m = 4 * g + mj
                psb_ = PSA.v(512 * (m % 2), 512)
                psc_ = PSB.v(0, 512)
                for k in range(8):
                    xmk = XM.v(2048 * k + t0, 512)
                    emit_mm(ctx, psb_, wtB(k, mj), xmk, k == 0, k == 7, tag="ab.bm")
                    emit_mm(ctx, psc_, wtC(k, mj), xmk, k == 0, k == 7, tag="ab.cm")
                emit_copy(ctx, "vector", BCM.v(1024 * m, 512), psb_, tag="ab.bmev")
                emit_copy(ctx, "vector", BCM.v(1024 * m + 512, 512), psc_, tag="ab.cmev")
        for m in range(16):
            acc = XS.v(0, 512)
            emit_ts(ctx, acc, VB.v(515 * m + 0, 512), cv["ab_conv"].col(4 * m + 0),
                    None, OP.mult, tag="ab.c0")
            for j in range(1, 4):
                emit_stt(ctx, acc, VB.v(515 * m + j, 512), cv["ab_conv"].col(4 * m + j),
                         acc, OP.mult, OP.add, tag=f"ab.c{j}")
            emit_act(ctx, VACT.v(512 * m, 512), acc, AF.Silu,
                     bias=cv["ab_convb"].col(m), tag="ab.silu")
            emit_copy(ctx, "vector", VB.v(515 * m, 3), VB.v(515 * m + 512, 3), tag="ab.carry")
        WS = ctx.WS
        for m in range(8):
            for sc in range(4):
                srcv = BCM.v(1024 * m + 128 * sc, 128)
                dstv = WS.v(512 * m + 128 * sc, 128)
                q = "sync" if (m + sc) % 2 == 0 else "scalar"
                p.op(q, lambda qq, o=dstv.ap, i=srcv.ap:
                     qq.dma_start(out=o, in_=i, transpose=True),
                     reads=[srcv], writes=[dstv], dma=True, tag="ab.bmT")
        fillers = []
        if tc + 1 < 4:
            emit_rms_chunk(ctx, XM, 0, 512 * (tc + 1), NB, "ab.n1b")
            for g4 in range(4):
                for mj in range(4):
                    fillers.append((tc + 1, g4, mj))
        for sc in range(4):
            c0 = 128 * sc
            gchunk = 4 * tc + sc

            def front(h, sc=sc, c0=c0, gchunk=gchunk, tc=tc):
                hb = 64 * (h % 2)
                sgc = SG.v(128 * (h % 4), 128)
                emit_dma_load(ctx, "scalar", sgc,
                              View(sp_sg.ap[tc, h][:, c0:c0 + 128], sp_sg.regs),
                              tag="ab.sgld")
                emit_transpose(ctx, PST.v(128 * (h % 4), 128),
                               VACT.v(512 * h + c0, 128), idb, tag="ab.vT")
                vtok = SCB.v(128 * (h % 4), 128)
                emit_copy(ctx, "scalar", vtok, PST.v(128 * (h % 4), 128), tag="ab.vTe")
                bmv = BCM.v(1024 * (h // 2) + c0, 128, hb, 64)
                cmv = BCM.v(1024 * (h // 2) + 512 + c0, 128, hb, 64)
                psCB = PSA.v(512 * (h % 2), 128)
                emit_mm(ctx, psCB, bmv, cmv, True, True, tag="ab.cb")
                emit_tt(ctx, ctx.ATTN.v(512 + 128 * (h % 2), 128), psCB,
                        ctx.LMT.v(128 * h, 128), OP.mult, tag="ab.cbl")
                emit_ts(ctx, SCB.v(512 + 64 * (h % 2), 64),
                        WS.v(512 * (h // 2) + 128 * sc + hb, 64),
                        XS.v(1144 + h, 1), None, OP.mult, tag="ab.bdrT")
                if gchunk > 0:
                    emit_tt(ctx, ctx.ATTN.v(256 + 128 * (h % 2), 128, hb, 64), cmv,
                            ctx.DOT.v(128 * h, 128, hb, 64), OP.mult, tag="ab.cmd")

            def bundle(h, sc=sc, c0=c0, gchunk=gchunk):
                hb = 64 * (h % 2)
                vtok = SCB.v(128 * (h % 4), 128)
                cbl = ctx.ATTN.v(512 + 128 * (h % 2), 128)
                bdrT = SCB.v(512 + 64 * (h % 2), 64)
                psY = PSB.v(512 * (h % 2), 128)
                emit_mm(ctx, psY, vtok, cbl, True, gchunk == 0, tag="ab.yd")
                if gchunk > 0:
                    cmd = ctx.ATTN.v(256 + 128 * (h % 2), 128, hb, 64)
                    emit_mm(ctx, psY, ctx.HC.v(128 * h, 128, hb, 64), cmd,
                            False, True, tag="ab.yo")
                psSt = PSA.v(1024 + 512 * (h % 2), 128, 0, 64)
                emit_mm(ctx, psSt, bdrT, vtok, True, True, tag="ab.st")
                hcf = ctx.HCF.v(128 * h, 128, hb, 64)
                if gchunk == 0:
                    emit_copy(ctx, "vector", hcf, psSt, tag="ab.hc0")
                else:
                    emit_ts(ctx, hcf, hcf, U(96 + h, 1, 0, 64), None, OP.mult, tag="ab.hcd")
                    emit_tt(ctx, hcf, hcf, psSt, OP.add, tag="ab.hca")
                emit_copy(ctx, "scalar", ctx.HC.v(128 * h, 128, hb, 64), hcf, tag="ab.hccast")
                yb = ctx.ATTN.v(128 * (h % 2), 128)
                emit_stt(ctx, yb, VACT.v(512 * h + c0, 128), cv["ab_D"].col(h), psY,
                         OP.mult, OP.add, tag="ab.ybd")
                emit_tt(ctx, VACT.v(512 * h + c0, 128), yb, SG.v(128 * (h % 4), 128),
                        OP.mult, tag="ab.yb")

            for h in range(16):
                front(h)
                if h >= 2:
                    bundle(h - 2)
                if h % 4 == 1 and fillers:
                    value_group(*fillers.pop(0))
            bundle(14)
            bundle(15)
        for g in range(4):
            wt = emit_wload_g(ctx, rp_op, g, 4096 * (g % 2), tag="ab.wop")
            for fj in range(2):
                f = 2 * g + fj
                ps = PSB.v(0, 512)
                for k in range(16):
                    emit_mm(ctx, ps, wt(k, fj), VACT.v(512 * k, 512),
                            k == 0, k == 15, tag="ab.op")
                xmv = XM.v(2048 * f + t0, 512)
                emit_stt(ctx, xmv, ps, cv["ab_msc"].col(f), xmv, OP.mult, OP.add, tag="ab.opres")
    emit_l1ab_mlp(ctx, rp_mfc, rp_mpr, sp_h, "ab_m2sc", "ab.m2")
    rarow = ctx.ACTF.v(0, 2048, 0, 1)
    rbrow = ctx.HCF.v(0, 2048, 0, 1)
    emit_dma_load(ctx, "scalar", rarow, View(lnr.ap[0:1, :], lnr.regs), tag="ab.lra")
    emit_dma_load(ctx, "scalar", rbrow, View(lnr.ap[1:2, :], lnr.regs), tag="ab.lrb")
    onesr_f = ctx.const["ONESR_F"]
    for c in range(4):
        pra = ctx.PSA.v(0, 512)
        prb = ctx.PSA.v(512, 512)
        emit_mm(ctx, pra, onesr_f, ctx.ACTF.v(512 * c, 512, 0, 1), True, True, tag="ab.bca")
        emit_mm(ctx, prb, onesr_f, ctx.HCF.v(512 * c, 512, 0, 1), True, True, tag="ab.bcb")
        for i in range(8):
            xc = XS.v(512, 512)
            emit_dma_load(ctx, "scalar", xc, xview(ctx.x_src, i, 512 * c, 512), tag="ab.x0l")
            t0v = XS.v(0, 512)
            emit_tt(ctx, t0v, xc, pra, OP.mult, tag="ab.x0m")
            emit_tt(ctx, t0v, t0v, prb, OP.add, tag="ab.x0a")
            t1 = XS.v(1024, 512)
            emit_ts(ctx, t1, t0v, cv["ab_ra1"].col(i), None, OP.mult, tag="ab.xa1")
            emit_stt(ctx, XM.v(2048 * i + 512 * c, 512), XM.v(2048 * i + 512 * c, 512),
                     cv["ab_ra0"].col(i), t1, OP.mult, OP.add, tag="ab.xa")
    emit_l1ab_qkv(ctx, rp_qw, rp_kw, rp_vw, sp_q)
    emit_l1ab_attn(ctx, rp_ow, sp_q)
    emit_l1ab_mlp(ctx, rp_afc, rp_apr, sp_h, "ab_a2sc", "ab.a2")
    for i in range(8):
        for c in range(4):
            emit_xadd(ctx, i, 512 * c, 512, XM.v(2048 * i + 512 * c, 512), tag="l1ab.res")


def emit_l1ab_mlp(ctx, rp_fc, rp_pr, sp_h, sc_name, tag):
    p = ctx.p
    XM, NB, VACT, XS = ctx.XM, ctx.NB, ctx.VACT, ctx.XS
    PSA, PSB = ctx.PSA, ctx.PSB
    cv = ctx.colvecs
    for tc in range(4):
        t0 = 512 * tc
        emit_rms_chunk(ctx, XM, 0, t0, NB, tag + ".n")
        for mg in range(2):
            for g4 in range(4):
                wt = emit_wload_g(ctx, rp_fc, 4 * mg + g4, 4096 * (g4 % 2), tag=tag + ".wf")
                for mj in range(4):
                    m = 4 * g4 + mj
                    ps = PSA.v(512 * (m % 2), 512)
                    for k in range(8):
                        emit_mm(ctx, ps, wt(k, mj), NB.v(512 * k, 512), k == 0, k == 7,
                                tag=tag + ".fc")
                    rh = XS.v(512 * (m % 2), 512)
                    emit_act(ctx, rh, ps, AF.Relu, tag=tag + ".relu")
                    emit_tt(ctx, VACT.v(512 * m, 512), rh, rh, OP.mult, tag=tag + ".sq")
            dst = View(sp_h.ap[tc, 16 * mg:16 * (mg + 1)].rearrange("m p c -> p m c"), sp_h.regs)
            sv = View(ctx.VACT.t[0:128, ctx.VACT.off:ctx.VACT.off + 8192]
                      .rearrange("p (m c) -> p m c", c=512),
                      [(ctx.VACT.name, ctx.VACT.off, ctx.VACT.off + 8192)])
            emit_dma_store(ctx, "sync", dst, sv, tag=tag + ".hsp")
    for tc in range(4):
        t0 = 512 * tc
        hv = [None] * 32
        for k in range(32):
            hv[k] = VACT.v(256 * k, 256) if False else None
        pool2 = [ctx.VACT, ctx.BCM]
        for k in range(32):
            pl = pool2[k // 16]
            hv[k] = pl.v(512 * (k % 16), 512)
            emit_dma_load(ctx, "sync" if k % 2 == 0 else "scalar", hv[k],
                          View(sp_h.ap[tc, k], sp_h.regs), tag=tag + ".hld")
        for fh in range(4):
            wtq = [emit_wload_g(ctx, rp_pr[q], fh, 2048 * q,
                                tag=tag + ".wp", queue="sync" if q % 2 == 0 else "scalar")
                   for q in range(4)]
            for fj in range(2):
                f = 2 * fh + fj
                ps = PSB.v(512 * fj, 512)
                for q in range(4):
                    for kq in range(8):
                        k = 8 * q + kq
                        emit_mm(ctx, ps, wtq[q](kq, fj), hv[k],
                                k == 0, k == 31, tag=tag + ".pr")
                xmv = XM.v(2048 * f + t0, 512)
                emit_stt(ctx, xmv, ps, cv[sc_name].col(f), xmv, OP.mult, OP.add,
                         tag=tag + ".res")


def emit_l1ab_qkv(ctx, rp_qw, rp_kw, rp_vw, sp_q):
    p = ctx.p
    XM, NB, XS = ctx.XM, ctx.NB, ctx.XS
    PSA, PSB, PSC = ctx.PSA, ctx.PSB, ctx.PSC
    B = ctx.BIGB
    cv = ctx.colvecs
    sp_k = p.dram(ctx.name("ab_kt"), [16, 128, 256], BF16)
    for g in range(2):
        src = View(rp_qw.scr.ap[g], rp_qw.scr.regs)
        dv = View(ctx.WB.t[0:128, ctx.WB.off + 4096 * g:ctx.WB.off + 4096 * (g + 1)],
                  [(ctx.WB.name, ctx.WB.off + 4096 * g, ctx.WB.off + 4096 * (g + 1))])
        emit_dma_load(ctx, "sync", dv, src, tag="ab.wq")
    for nm, rp, off in (("k", rp_kw, 28736), ("v", rp_vw, 30784)):
        src = View(rp.scr.ap[0], rp.scr.regs)
        dvv = View(B.t[0:128, B.off + off:B.off + off + 2048],
                   [(B.name, B.off + off, B.off + off + 2048)])
        emit_dma_load(ctx, "scalar", dvv, src, tag="ab.w" + nm)
    ones_dst = View(B.t[0:128, B.off + 24576:B.off + 24576 + 4160]
                    .rearrange("p (j h o) -> p j h o", h=4, o=65)[:, :, :, 64:65],
                    [(B.name, B.off + 24576, B.off + 28736)])
    ones_src = View(ctx.const["ONESC_B"].ap.unsqueeze(1).to_broadcast([128, 16, 4, 1]),
                    ctx.const["ONESC_B"].regs)
    p.op("vector", lambda q, o=ones_dst.ap, i=ones_src.ap: q.tensor_copy(o, i),
         reads=[ones_src], writes=[ones_dst], tag="ab.vones")

    for tc in range(4):
        t0 = 512 * tc
        emit_rms_chunk(ctx, XM, 0, t0, NB, "ab.n3")
        def q_front(tt):
            pp = tt % 2
            for mg in range(2):
                ps = PSA.v(1024 * pp + 512 * mg, 512)
                for k in range(8):
                    rq0 = ctx.WB.off + 4096 * mg + 512 * k
                    rhs = View(ctx.WB.t[0:128, rq0:rq0 + 512],
                               [(ctx.WB.name, rq0, rq0 + 512)])
                    emit_mm(ctx, ps, NB.v(512 * k + 128 * tt, 128), rhs, k == 0, k == 7, tag="ab.q")
            qt = B.v(4096 * pp, 1024)
            emit_copy(ctx, "vector", qt, PSA.v(1024 * pp, 1024), tag="ab.qev")

        def q_tail(tt, tc=tc):
            pp = tt % 2
            gt = 4 * tc + tt
            qb0 = 4096 * pp
            qt = B.v(qb0, 1024)
            sq = XS.v(0, 1024)
            emit_act(ctx, sq, qt, AF.Square, tag="ab.qsq")
            w = 64
            cur = sq
            tgt = [XS.v(1024, 512), XS.v(1536, 256), XS.v(1024, 128),
                   XS.v(1536, 64), XS.v(1024, 32), XS.v(1536, 16)]
            for stp in range(6):
                half = w // 2
                g = View(cur.ap.rearrange("p (g k) -> p g k", g=16), cur.regs)
                dst = tgt[stp]
                emit_tt(ctx, View(dst.ap.rearrange("p (g k) -> p g k", g=16), dst.regs),
                        View(g.ap[:, :, 0:half], cur.regs), View(g.ap[:, :, half:w], cur.regs),
                        OP.add, tag="ab.qss")
                cur = dst
                w = half
            rst = XS.v(1536, 16)
            emit_ts(ctx, rst, cur, 1.0 / 64.0, None, OP.mult, tag="ab.qms")
            emit_act(ctx, rst, rst, AF.Ln, bias=1.1920929e-07, tag="ab.qln")
            emit_act(ctx, rst, rst, AF.Exp, scale=-0.5, tag="ab.qex")
            rb = View(rst.ap.unsqueeze(2).to_broadcast([128, 16, 64]), rst.regs)
            emit_tt(ctx, View(qt.ap.rearrange("p (g k) -> p g k", g=16), qt.regs),
                    View(qt.ap.rearrange("p (g k) -> p g k", g=16), qt.regs), rb,
                    OP.mult, tag="ab.qrn")
            qr = B.v(qb0 + 1024, 1024)
            cosv = View(ctx.const["COSTOK"].ap[:, 32 * gt:32 * (gt + 1)]
                        .unsqueeze(1).unsqueeze(1).to_broadcast([128, 2, 4, 32]),
                        ctx.const["COSTOK"].regs)
            sinv = View(ctx.const["SINTOK"].ap[:, 32 * gt:32 * (gt + 1)]
                        .unsqueeze(1).unsqueeze(1).to_broadcast([128, 2, 4, 32]),
                        ctx.const["SINTOK"].regs)
            for g2 in range(2):
                def xin(off, g2=g2):
                    ap = bass.AP(B.t, B.off + qb0 + 512 * g2 + off,
                                 [[PCOLS(B), 128], [256, 2], [64, 4], [1, 32]])
                    return View(ap, [(B.name, B.off + qb0, B.off + qb0 + 1024)])
                def xout(off, g2=g2):
                    ap = bass.AP(B.t, B.off + qb0 + 1024 + 512 * g2 + off,
                                 [[PCOLS(B), 128], [64, 2], [128, 4], [1, 32]])
                    return View(ap, [(B.name, B.off + qb0 + 1024, B.off + qb0 + 2048)])
                t1n = View(bass.AP(XS.t, XS.off + 1024, [[PCOLS(XS), 128], [128, 2], [32, 4], [1, 32]]),
                           XS.v(1024, 512).regs)
                t2n = View(bass.AP(XS.t, XS.off + 1536, [[PCOLS(XS), 128], [128, 2], [32, 4], [1, 32]]),
                           XS.v(1536, 512).regs)
                emit_tt(ctx, t1n, xin(0), cosv, OP.mult, tag="ab.r1a")
                emit_tt(ctx, t2n, xin(32), sinv, OP.mult, tag="ab.r1b")
                emit_tt(ctx, xout(0), t1n, t2n, OP.add, tag="ab.r1")
                emit_tt(ctx, t1n, xin(32), cosv, OP.mult, tag="ab.r2a")
                emit_tt(ctx, t2n, xin(0), sinv, OP.mult, tag="ab.r2b")
                emit_tt(ctx, xout(32), t1n, t2n, OP.subtract, tag="ab.r2")
            emit_dma_store(ctx, "sync", View(sp_q.ap[gt], sp_q.regs), qr, tag="ab.qsp")

        def kv_part(tt, tc=tc):
            pp = tt % 2
            gt = 4 * tc + tt
            ktb = 4096 * pp + 2048
            for nm, off, spk in (("k", 28736, True), ("v", 30784, False)):
                ps = PSC.v(0, 256)
                for k in range(8):
                    rk0 = B.off + off + 256 * k
                    rhs = View(B.t[0:128, rk0:rk0 + 256],
                               [(B.name, rk0, rk0 + 256)])
                    emit_mm(ctx, ps, NB.v(512 * k + 128 * tt, 128), rhs, k == 0, k == 7,
                            tag="ab." + nm)
                if spk:
                    kt = B.v(ktb, 256)
                    emit_copy(ctx, "vector", kt, ps, tag="ab.kev")
                    sqk = XS.v(0, 256)
                    emit_act(ctx, sqk, kt, AF.Square, tag="ab.ksq")
                    wk = 64
                    curk = sqk
                    tgtk = [XS.v(1024, 128), XS.v(1536, 64), XS.v(1024, 32),
                            XS.v(1536, 16), XS.v(1024, 8), XS.v(1536, 4)]
                    for stp in range(6):
                        half = wk // 2
                        g = View(curk.ap.rearrange("p (g k) -> p g k", g=4), curk.regs)
                        dst = tgtk[stp]
                        emit_tt(ctx, View(dst.ap.rearrange("p (g k) -> p g k", g=4), dst.regs),
                                View(g.ap[:, :, 0:half], curk.regs),
                                View(g.ap[:, :, half:wk], curk.regs), OP.add, tag="ab.kss")
                        curk = dst
                        wk = half
                    rstk = XS.v(1536, 4)
                    emit_ts(ctx, rstk, curk, 1.0 / 64.0, None, OP.mult, tag="ab.kms")
                    emit_act(ctx, rstk, rstk, AF.Ln, bias=1.1920929e-07, tag="ab.kln")
                    emit_act(ctx, rstk, rstk, AF.Exp, scale=-0.5, tag="ab.kex")
                    rbk = View(rstk.ap.unsqueeze(2).to_broadcast([128, 4, 64]), rstk.regs)
                    emit_tt(ctx, View(kt.ap.rearrange("p (g k) -> p g k", g=4), kt.regs),
                            View(kt.ap.rearrange("p (g k) -> p g k", g=4), kt.regs), rbk,
                            OP.mult, tag="ab.krn")
                    kr = B.v(ktb + 256, 256)
                    cosk = View(ctx.const["COSTOK"].ap[:, 32 * gt:32 * (gt + 1)]
                                .unsqueeze(1).to_broadcast([128, 4, 32]), ctx.const["COSTOK"].regs)
                    sink = View(ctx.const["SINTOK"].ap[:, 32 * gt:32 * (gt + 1)]
                                .unsqueeze(1).to_broadcast([128, 4, 32]), ctx.const["SINTOK"].regs)
                    def kin(off2):
                        ap = bass.AP(B.t, B.off + ktb + off2, [[PCOLS(B), 128], [64, 4], [1, 32]])
                        return View(ap, [(B.name, B.off + ktb, B.off + ktb + 256)])
                    def kout(off2):
                        ap = bass.AP(B.t, B.off + ktb + 256 + off2, [[PCOLS(B), 128], [64, 4], [1, 32]])
                        return View(ap, [(B.name, B.off + ktb + 256, B.off + ktb + 512)])
                    tk1 = View(bass.AP(XS.t, XS.off + 1024, [[PCOLS(XS), 128], [32, 4], [1, 32]]),
                               XS.v(1024, 128).regs)
                    tk2 = View(bass.AP(XS.t, XS.off + 1536, [[PCOLS(XS), 128], [32, 4], [1, 32]]),
                               XS.v(1536, 128).regs)
                    emit_tt(ctx, tk1, kin(0), cosk, OP.mult, tag="ab.kr1a")
                    emit_tt(ctx, tk2, kin(32), sink, OP.mult, tag="ab.kr1b")
                    emit_tt(ctx, kout(0), tk1, tk2, OP.add, tag="ab.kr1")
                    emit_tt(ctx, tk1, kin(32), cosk, OP.mult, tag="ab.kr2a")
                    emit_tt(ctx, tk2, kin(0), sink, OP.mult, tag="ab.kr2b")
                    emit_tt(ctx, kout(32), tk1, tk2, OP.subtract, tag="ab.kr2")
                    emit_dma_store(ctx, "scalar", View(sp_k.ap[gt], sp_k.regs), kr, tag="ab.ksp")
                else:
                    dst = View(B.t[0:128, B.off + 24576 + 260 * gt:B.off + 24576 + 260 * gt + 260]
                               .rearrange("p (h o) -> p h o", h=4)[:, :, 0:64],
                               [(B.name, B.off + 24576 + 260 * gt, B.off + 24576 + 260 * (gt + 1))])
                    srcps = View(PSC.t[0:128, 0:256].rearrange("p (h o) -> p h o", h=4),
                                 [(PSC.name, 0, 512)])
                    p.op("vector", lambda q, o=dst.ap, i=srcps.ap: q.tensor_copy(o, i),
                         reads=[View(srcps.ap, [(PSC.name, 0, 512)])], writes=[dst], tag="ab.vev2")

        for tt in range(4):
            q_front(tt)
            if tt > 0:
                q_tail(tt - 1)
                kv_part(tt - 1)
        q_tail(3)
        kv_part(3)
    for j in range(8):
        src = View(sp_q.ap.rearrange("a p c -> (a p) c")[:, 128 * j:128 * (j + 1)], sp_q.regs)
        dst = B.v(4096 + 2048 * j, 2048)
        p.op("sync", lambda q, o=dst.ap, i=src.ap: q.dma_start(out=o, in_=i, transpose=True),
             reads=[src], writes=[dst], dma=True, tag="ab.qT")
    for j in range(2):
        src = View(sp_k.ap.rearrange("a p c -> (a p) c")[:, 128 * j:128 * (j + 1)], sp_k.regs)
        dst = B.v(20480 + 2048 * j, 2048)
        p.op("sync", lambda q, o=dst.ap, i=src.ap: q.dma_start(out=o, in_=i, transpose=True),
             reads=[src], writes=[dst], dma=True, tag="ab.kT")


def emit_l1ab_attn(ctx, rp_ow, sp_q):
    p = ctx.p
    XM, XS = ctx.XM, ctx.XS
    PSA, PSB = ctx.PSA, ctx.PSB
    B = ctx.BIGB
    cv = ctx.colvecs
    cmask = ctx.const["CMASK_B"]
    onesr_f = ctx.const["ONESR_F"]
    for g in range(2):
        src = View(rp_ow.scr.ap[g], rp_ow.scr.regs)
        dv = View(ctx.WB.t[0:128, ctx.WB.off + 4096 * g:ctx.WB.off + 4096 * (g + 1)],
                  [(ctx.WB.name, ctx.WB.off + 4096 * g, ctx.WB.off + 4096 * (g + 1))])
        emit_dma_load(ctx, "sync", dv, src, tag="ab.wo")
    gsc = XS.v(1312, 16)
    emit_ts(ctx, gsc, cv["ab_qg"].cols(0, 16), 0.125, None, OP.mult, tag="ab.gsc")
    gbi = XS.v(1328, 16)
    emit_act(ctx, gbi, cv["ab_qg"].cols(0, 16), AF.Abs, tag="ab.gb0")
    emit_ts(ctx, gbi, gbi, -8.0, None, OP.mult, tag="ab.gbi")

    PI = [2 * (h % 4) + 8 * ((h // 4) // 2) + ((h // 4) % 2) for h in range(16)]
    for c in range(4):
        for hpi in range(8):
            hpair = (2 * hpi, 2 * hpi + 1)
            nj = 4 * c + 4

            def sc_front(j, i, c=c):
                h = hpair[i]
                hk = h // 4
                s = PI[h]
                qv = B.v(4096 + 2048 * (s // 2) + 512 * c, 512, 64 * (s % 2), 64)
                kv = B.v(20480 + 2048 * (hk // 2) + 128 * j, 128, 64 * (hk % 2), 64)
                psS = PSA.v(512 * (2 * i + j % 2), 512)
                emit_mm(ctx, psS, kv, qv, True, True, tag="ab.s")
                atb = (ctx.ATTN.v(512 * (j % 2), 512) if i == 0
                       else B.v(28736 + 512 * (j % 2), 512))
                emit_act(ctx, atb, psS, AF.Exp,
                         scale=View(gsc.ap[:, h:h + 1], gsc.regs),
                         bias=View(gbi.ap[:, h:h + 1], gbi.regs), tag="ab.exp")
                if j >= 4 * c:
                    emit_tt(ctx, atb, atb,
                            View(cmask.ap.tensor[0:128, cmask.regs[0][1] + 512 * (j - 4 * c):
                                 cmask.regs[0][1] + 512 * (j - 4 * c + 1)], cmask.regs),
                            OP.mult, tag="ab.msk")

            def sc_av(j, i, c=c, nj=nj):
                h = hpair[i]
                hk = h // 4
                atb = (ctx.ATTN.v(512 * (j % 2), 512) if i == 0
                       else B.v(28736 + 512 * (j % 2), 512))
                vav = View(B.t[0:128, B.off + 24576 + 260 * j + 65 * hk:
                                 B.off + 24576 + 260 * j + 65 * (hk + 1)],
                           [(B.name, B.off + 24576 + 260 * j, B.off + 24576 + 260 * (j + 1))])
                emit_mm(ctx, PSB.v(512 * i, 512, 0, 65), vav, atb,
                        j == 0, j == nj - 1, tag="ab.av")

            for j in range(nj):
                sc_front(j, 0)
                sc_front(j, 1)
                if j > 0:
                    sc_av(j - 1, 0)
                    sc_av(j - 1, 1)
            sc_av(nj - 1, 0)
            sc_av(nj - 1, 1)
            for i in range(2):
                h = hpair[i]
                rl = XS.v(512 * i, 512, 0, 1)
                p.op("vector", lambda q, o=rl.ap, i_=PSB.t[64:65, 512 * i:512 * i + 512]:
                     q.reciprocal(o, i_),
                     reads=[PSB.v(512 * i, 512, 64, 1)], writes=[rl], tag="ab.rl")
                psb2 = (ctx.PSC.v(0, 512, 0, 64) if i == 0
                        else PSA.v(1024, 512, 0, 64))
                emit_mm(ctx, psb2, View(onesr_f.ap[0:1, 0:64], onesr_f.regs), rl,
                        True, True, tag="ab.bc2")
                rlb = (ctx.ATTN.v(1024, 512, 0, 64) if i == 0
                       else B.v(29760, 512, 0, 64))
                emit_copy(ctx, "scalar", rlb, psb2, tag="ab.rlb")
                yt = B.v(0 + 512 * (h // 2), 512, 64 * (h % 2), 64)
                emit_tt(ctx, yt, PSB.v(512 * i, 512, 0, 64), rlb, OP.mult, tag="ab.y")
        for f in range(8):
            ps = PSA.v(1024 + 512 * (f % 2), 512)
            for k in range(8):
                lo0 = ctx.WB.off + 4096 * (f // 4) + 512 * k + 128 * (f % 4)
                lhsT = View(ctx.WB.t[0:128, lo0:lo0 + 128],
                            [(ctx.WB.name, lo0, lo0 + 128)])
                emit_mm(ctx, ps, lhsT, B.v(512 * k, 512), k == 0, k == 7, tag="ab.o")
            xmv = XM.v(2048 * f + 512 * c, 512)
            emit_stt(ctx, xmv, ps, cv["ab_asc"].col(f), xmv, OP.mult, OP.add, tag="ab.ores")


def build_prog_l1ab():
    global LAST_CTX
    nc = bass.Bass("TRN2", target_bir_lowering=False)
    ctx = make_base_ctx(nc, w_cols=64, ab_cols=0, actf_cols=2048, sm_pool=False,
                        consts=["IDB", "TRILT_B", "CMASK_B", "ONESC_B", "ONESC_F",
                                "ONESR_F", "ONESR_B", "DIFFT", "REVT", "DECT",
                                "REVC", "COSTOK", "SINTOK"], no_x=True)
    LAST_CTX = ctx
    p = ctx.p
    ctx.XM = SubPool(ctx.XN, 0, 16384)
    B = p.sbuf("BIGBp", 34864, BF16)
    ctx.BIGB = B
    ctx.VB = SubPool(B, 0, 8240)
    ctx.VACT = SubPool(B, 8240, 8192)
    ctx.BCM = SubPool(B, 16432, 8192)
    ctx.SG = SubPool(B, 24624, 2048)
    ctx.LMT = SubPool(B, 26672, 2048)
    ctx.DRT = SubPool(B, 28720, 2048)
    ctx.DOT = SubPool(B, 30768, 2048)
    ctx.HC = SubPool(B, 32816, 2048)
    ctx.WB = p.sbuf("WBp", 8192, BF16)
    ctx.WS = p.sbuf("WSp", 4096, BF16)
    ctx.NB = p.sbuf("NBp", 4096, BF16)
    ctx.SCB = p.sbuf("SCBp", 768, BF16)
    ctx.ATTN = p.sbuf("ATTNp", 1536, BF16)
    x_in = p.dram("x_in", [1024, 2048], F32, kind="ExternalInput")
    x_out = p.dram("x_out", [1024, 2048], F32, kind="ExternalOutput")
    ctx.x_in, ctx.x_out = x_in, x_out
    ctx.x_src, ctx.x_dst = x_in, x_out
    declare_weights(ctx, [("L1_attn_b_in_proj_w", (1024, 4096), F32),
                          ("L1_attn_b_B_proj_w", (1024, 1024), F32),
                          ("L1_attn_b_C_proj_w", (1024, 1024), F32),
                          ("L1_attn_b_out_proj_w", (2048, 1024), F32),
                          ("L1_attn_b_m_fc_w", (1024, 4096), F32),
                          ("L1_attn_b_m_proj_w", (4096, 1024), F32),
                          ("L1_attn_b_q_w", (1024, 1024), F32),
                          ("L1_attn_b_k_w", (1024, 256), F32),
                          ("L1_attn_b_v_w", (1024, 256), F32),
                          ("L1_attn_b_o_w", (1024, 1024), F32),
                          ("L1_attn_b_a_fc_w", (1024, 4096), F32),
                          ("L1_attn_b_a_proj_w", (4096, 1024), F32)])
    for nm, n in (("ab_rm0", 8), ("ab_rm1", 8), ("ab_ra0", 8), ("ab_ra1", 8),
                  ("ab_msc", 8), ("ab_m2sc", 8), ("ab_asc", 8), ("ab_a2sc", 8),
                  ("ab_conv", 64), ("ab_convb", 16), ("ab_D", 16),
                  ("ab_dtb", 16), ("ab_alog", 16), ("ab_qg", 16)):
        declare_colvec(ctx, nm, n)
    emit_ln_hbm(ctx, x_in)
    emit_l1ab(ctx)
    p.finalize()
    return nc


def emit_ln_hbm(ctx, x_in):
    p = ctx.p
    XN, ACTF = ctx.XN, ctx.ACTF
    PSA = ctx.PSA
    onescb = ctx.const["ONESC_B"]
    onesr = ctx.const["ONESR_F"]
    XSB, HCF = ctx.XS, ctx.HCF
    for i in range(8):
        xf = XSB.v(0, 2048)
        emit_dma_load(ctx, "sync", xf, xview(x_in, i, 0, 2048), tag="lnh.x")
        xb = XN.v(2048 * i, 2048)
        emit_act(ctx, xb, xf, AF.Copy, tag="lnh.cast")
        for c in range(4):
            emit_mm(ctx, PSA.v(512 * c, 512, 0, 1), onescb, XN.v(2048 * i + 512 * c, 512),
                    i == 0, i == 7, tag="lnh.sum")
    mu = HCF.v(0, 2048, 0, 1)
    emit_ts(ctx, mu, PSA.v(0, 2048, 0, 1), 1.0 / D, None, OP.mult, tag="lnh.mu")
    for i in range(8):
        sq = XN.v(2048 * i, 2048)
        emit_act(ctx, sq, sq, AF.Square, tag="lnh.sq")
        for c in range(4):
            emit_mm(ctx, PSA.v(512 * c, 512, 0, 1), onescb, XN.v(2048 * i + 512 * c, 512),
                    i == 0, i == 7, tag="lnh.ss")
    ms = ACTF.v(0, 2048, 0, 1)
    emit_ts(ctx, ms, PSA.v(0, 2048, 0, 1), 1.0 / D, None, OP.mult, tag="lnh.ms")
    mu2 = XSB.v(0, 2048, 0, 1)
    emit_tt(ctx, mu2, mu, mu, OP.mult, tag="lnh.mu2")
    var = ACTF.v(0, 2048, 0, 1)
    emit_tt(ctx, var, ms, mu2, OP.subtract, tag="lnh.var")
    lnv = XSB.v(0, 2048, 0, 1)
    emit_act(ctx, lnv, var, AF.Ln, bias=1e-5, tag="lnh.lnv")
    ra = ACTF.v(0, 2048, 0, 1)
    emit_act(ctx, ra, lnv, AF.Exp, scale=-0.5, tag="lnh.ra")
    rb = XSB.v(0, 2048, 0, 1)
    emit_stt(ctx, rb, mu, -1.0, ra, OP.mult, OP.mult, tag="lnh.rb")
    for c in range(4):
        pra = PSA.v(0, 512)
        prb = PSA.v(512, 512)
        emit_mm(ctx, pra, onesr, ACTF.v(512 * c, 512, 0, 1), True, True, tag="lnh.bca")
        emit_mm(ctx, prb, onesr, XSB.v(512 * c, 512, 0, 1), True, True, tag="lnh.bcb")
        for i in range(8):
            xb = HCF.v(512, 512)
            emit_dma_load(ctx, "sync", xb, xview(x_in, i, 512 * c, 512), tag="lnh.x2")
            tmp = HCF.v(0, 512)
            emit_tt(ctx, tmp, xb, pra, OP.mult, tag="lnh.mul")
            emit_tt(ctx, XN.v(2048 * i + 512 * c, 512), tmp, prb, OP.add, tag="lnh.add")


def host_l1ab_inputs(inputs):
    out = {k: inputs[k] for k in ("L1_attn_b_in_proj_w", "L1_attn_b_B_proj_w",
                                  "L1_attn_b_C_proj_w", "L1_attn_b_out_proj_w",
                                  "L1_attn_b_m_fc_w", "L1_attn_b_m_proj_w",
                                  "L1_attn_b_q_w", "L1_attn_b_k_w", "L1_attn_b_v_w",
                                  "L1_attn_b_o_w", "L1_attn_b_a_fc_w",
                                  "L1_attn_b_a_proj_w")}
    rs = lambda a, n: np.ascontiguousarray(np.asarray(a).reshape(n, 128).T)
    bc = lambda a: np.ascontiguousarray(np.broadcast_to(np.asarray(a)[None, :], (128, len(a))))
    out["cv_ab_rm0"] = rs(inputs["L1_attn_b_resid_mix_m"][0], 8)
    out["cv_ab_rm1"] = rs(inputs["L1_attn_b_resid_mix_m"][1], 8)
    out["cv_ab_ra0"] = rs(inputs["L1_attn_b_resid_mix_a"][0], 8)
    out["cv_ab_ra1"] = rs(inputs["L1_attn_b_resid_mix_a"][1], 8)
    out["cv_ab_msc"] = rs(inputs["L1_attn_b_mamba_scale"], 8)
    out["cv_ab_m2sc"] = rs(inputs["L1_attn_b_m_mlp_scale"], 8)
    out["cv_ab_asc"] = rs(inputs["L1_attn_b_attn_scale"], 8)
    out["cv_ab_a2sc"] = rs(inputs["L1_attn_b_a_mlp_scale"], 8)
    cw = inputs["L1_attn_b_conv_w"].reshape(16, 128, 4)
    out["cv_ab_conv"] = np.ascontiguousarray(cw.transpose(1, 0, 2).reshape(128, 64))
    out["cv_ab_convb"] = rs(inputs["L1_attn_b_conv_b"], 16)
    out["cv_ab_D"] = bc(inputs["L1_attn_b_D_param"])
    out["cv_ab_dtb"] = bc(inputs["L1_attn_b_dt_bias"])
    out["cv_ab_alog"] = bc(inputs["L1_attn_b_A_log"])
    out["cv_ab_qg"] = bc(inputs["L1_attn_b_q_gain"])
    return out




def build_prog_fused():
    global LAST_CTX
    nc = bass.Bass("TRN2", target_bir_lowering=False)
    p = Prog(nc)
    ctx = Ctx(p)
    LAST_CTX = ctx
    ctx.win = {}
    ctx.colvecs = {}
    ctx.smf_off = 0
    ctx.xr_slot = 0
    ctx.XN = p.sbuf("XN", 16384, BF16)
    ABF = p.sbuf("ABF", 53616, BF16)
    AFX = p.sbuf("AFX", 13696, F32)
    ctx.XR = SubPool(AFX, 0, 1024)
    ctx.XS = SubPool(AFX, 1024, 2048)
    ctx.HCF = SubPool(AFX, 3072, 2048)
    ctx.ACTF = SubPool(AFX, 5120, 2048)
    CARVE = 1024
    ctx.LNSQ = p.sbuf("LNSQf", 512, BF16)
    ctx.SMF = p.sbuf("SMF", 256, F32)
    ctx.PSA = p.psum("PSA", 2048, F32)
    ctx.PSB = p.psum("PSB", 1024, F32)
    ctx.PSC = p.psum("PSC", 512, F32)
    ctx.PST = p.psum("PST", 1024, BF16)
    emit_load_consts(ctx, CONST_NAMES + ["DIFFT", "REVT", "DECT", "REVC",
                                          "COSTOK", "SINTOK"])

    x_in = p.dram("x_in", [1024, 2048], F32, kind="ExternalInput")
    x_out = p.dram("x_out", [1024, 2048], F32, kind="ExternalOutput")
    x_scr = p.dram("x_scr", [1024, 2048], F32)
    ctx.x_in, ctx.x_out = x_in, x_out

    declare_weights(ctx, [("L0_attn_a_qkv_w", (192, 1024), F32),
                          ("L0_attn_a_proj_w", (1024, 1024), F32),
                          ("L0_ffn_w_gate", (3072, 1024), I32),
                          ("L0_ffn_w_up", (3072, 1024), I32),
                          ("L0_ffn_w_down", (1024, 3072), I32),
                          ("L0_attn_b_Wq", (1024, 1024), F32),
                          ("L0_attn_b_Wk", (1024, 1024), F32),
                          ("L0_attn_b_Wv", (1024, 1024), F32),
                          ("L0_attn_b_Wo", (1024, 1024), F32),
                          ("L1_attn_a_gate_w", (8, 1024), F32),
                          ("L1_attn_a_q_w", (1024, 1024), F32),
                          ("L1_attn_a_k_w", (8, 1024, 1024), F32),
                          ("L1_attn_a_v_w", (8, 1024, 1024), F32),
                          ("L1_attn_a_fmq1_w", (128, 128), F32),
                          ("L1_attn_a_fmq2_w", (128, 128), F32),
                          ("L1_attn_a_fmk1_w", (128, 128), F32),
                          ("L1_attn_a_fmk2_w", (128, 128), F32),
                          ("L1_attn_a_o_w", (1024, 1024), F32),
                          ("L1_ffn_Wq", (1024, 2048), F32),
                          ("L1_ffn_sub_key0", (32, 128), F32),
                          ("L1_ffn_sub_key1", (32, 128), F32),
                          ("L1_ffn_expert_down", (1024, 1024), F32),
                          ("L1_ffn_expert_up", (1024, 1024), F32),
                          ("L1_attn_b_in_proj_w", (1024, 4096), F32),
                          ("L1_attn_b_B_proj_w", (1024, 1024), F32),
                          ("L1_attn_b_C_proj_w", (1024, 1024), F32),
                          ("L1_attn_b_out_proj_w", (2048, 1024), F32),
                          ("L1_attn_b_m_fc_w", (1024, 4096), F32),
                          ("L1_attn_b_m_proj_w", (4096, 1024), F32),
                          ("L1_attn_b_q_w", (1024, 1024), F32),
                          ("L1_attn_b_k_w", (1024, 256), F32),
                          ("L1_attn_b_v_w", (1024, 256), F32),
                          ("L1_attn_b_o_w", (1024, 1024), F32),
                          ("L1_attn_b_a_fc_w", (1024, 4096), F32),
                          ("L1_attn_b_a_proj_w", (4096, 1024), F32)])
    declare_colvec(ctx, "l0aa_qkvb", 2)
    declare_colvec(ctx, "l0aa_projb", 8)
    declare_colvec(ctx, "l1aa_nw", 1)
    declare_colvec(ctx, "l1ffn_bq", 16)
    declare_colvec(ctx, "l1ffn_gam", 2)
    declare_colvec(ctx, "l1ffn_bet", 2)
    for nm, n in (("ab_rm0", 8), ("ab_rm1", 8), ("ab_ra0", 8), ("ab_ra1", 8),
                  ("ab_msc", 8), ("ab_m2sc", 8), ("ab_asc", 8), ("ab_a2sc", 8),
                  ("ab_conv", 64), ("ab_convb", 16), ("ab_D", 16),
                  ("ab_dtb", 16), ("ab_alog", 16), ("ab_qg", 16)):
        declare_colvec(ctx, nm, n)

    def a_l0aa():
        ctx.W = SubPool(ABF, 0, 14336)
        ctx.ACTB = SubPool(ABF, 14336, 8704)

    def a_l0ffn():
        ctx.W = SubPool(ABF, 0, 4096)
        ctx.ACTB = SubPool(ABF, 4096, 512)
        ctx.W2 = SubPool(ABF, 4608, 12288)
        ctx.H = SubPool(ABF, 16896, 12288)

    def a_l0ab():
        ctx.W = SubPool(ABF, 0, 16384)
        ctx.ACTB = SubPool(ABF, 16384, 512)
        ctx.QH = SubPool(ABF, 16896, 8192)
        ctx.KH = SubPool(ABF, 25088, 8192)
        ctx.VAUG = SubPool(ABF, 33280, 8320)
        ctx.YH = SubPool(ABF, 47216, 2048)
        ctx.ATTN = SubPool(ABF, 49264, 1536)

    def a_l1aa():
        ctx.W = SubPool(ABF, 0, 4096)
        ctx.QF = SubPool(ABF, 4096, 8192)
        ctx.KF1 = SubPool(ABF, 12288, 8192)
        ctx.KF2 = SubPool(ABF, 20480, 8192)
        ctx.VT = SubPool(ABF, 28672, 8192)
        ctx.STV = SubPool(ABF, 36864, 1024)
        ctx.MTOKB = SubPool(ABF, 37888, 128)
        ctx.FM = SubPool(ABF, 38016, 512)
        ctx.RB = SubPool(ABF, 38528, 2816)
        ctx.KT = SubPool(ABF, 41344, 8192)
        ctx.STF = SubPool(AFX, CARVE, 1024)
        ctx.UN = SubPool(AFX, CARVE + 1024, 6144)
        ctx.MTOKF = SubPool(AFX, CARVE + 7168, 128)

    def a_l1ffn():
        ctx.W = SubPool(ABF, 0, 16384)
        ctx.QB = SubPool(ABF, 16384, 4096)
        ctx.F2 = SubPool(ABF, 20480, 13312)
        ctx.SCOR = SubPool(AFX, CARVE, 8192)
        ctx.SF = SubPool(AFX, CARVE + 8192, 4096)
        ctx.RN = SubPool(AFX, CARVE + 12288, 256)
        ctx.ACC = SubPool(AFX, CARVE + 12544, 128)

    def a_l1ab():
        ctx.W = SubPool(ABF, 0, 64)
        ctx.XM = SubPool(ctx.XN, 0, 16384)
        B = SubPool(ABF, 64, 34864)
        ctx.BIGB = B
        ctx.VB = SubPool(B, 0, 8240)
        ctx.VACT = SubPool(B, 8240, 8192)
        ctx.BCM = SubPool(B, 16432, 8192)
        ctx.SG = SubPool(B, 24624, 2048)
        ctx.LMT = SubPool(B, 26672, 2048)
        ctx.DRT = SubPool(B, 28720, 2048)
        ctx.DOT = SubPool(B, 30768, 2048)
        ctx.HC = SubPool(B, 32816, 2048)
        ctx.WB = SubPool(ABF, 34928, 8192)
        ctx.WS = SubPool(ABF, 43120, 4096)
        ctx.NB = SubPool(ABF, 47216, 4096)
        ctx.SCB = SubPool(ABF, 51312, 768)
        ctx.ATTN = SubPool(ABF, 52080, 1536)

    blocks = [("l0aa", a_l0aa, lambda: emit_l0aa(ctx)),
              ("l0ffn", a_l0ffn, lambda: emit_l0ffn(ctx)),
              ("l0ab", a_l0ab, lambda: emit_l0ab(ctx)),
              ("l0ffn2", a_l0ffn, lambda: emit_l0ffn(ctx)),
              ("l1aa", a_l1aa, lambda: emit_l1aa(ctx)),
              ("l1ffn", a_l1ffn, lambda: emit_l1ffn(ctx, 0)),
              ("l1ab", a_l1ab, lambda: emit_l1ab(ctx)),
              ("l1ffn2", a_l1ffn, lambda: emit_l1ffn(ctx, 1))]
    xs = [x_in] + [x_scr] * 7
    xd = [x_scr] * 7 + [x_out]
    ctx.block_marks = []
    rp_ws = SubPool(ABF, 43120, 4096)
    for bi, (bname, assign, emitfn) in enumerate(blocks):
        ctx.block_marks.append(("ln_" + bname,
                                {e: len(v) for e, v in p.eng_ops.items()}))
        assign()
        ctx.x_src, ctx.x_dst = xs[bi], xd[bi]
        emit_ln_hbm(ctx, ctx.x_src)
        ctx.block_marks.append((bname, {e: len(v) for e, v in p.eng_ops.items()}))
        emitfn()
        if bi == 0:
            emit_l0ffn_phaseW(ctx, SubPool(ABF, 47216, 4096))
        if bi == 1:
            emit_l1ab_repacks(ctx, rp_ws, 0)
        elif bi == 2:
            ctx.rp_store_qs = ("gpsimd", "gpsimd")
            emit_l1ab_repacks(ctx, rp_ws, 1)
            emit_l1ab_repacks(ctx, rp_ws, 2)
            del ctx.rp_store_qs
    ctx.block_marks.append(("end", {e: len(v) for e, v in p.eng_ops.items()}))
    p.finalize()
    return nc


_PROG_CACHE = {}


def _get_prog(name):
    global LAST_CTX
    builders = {"l0aa": build_prog_l0aa, "l0ffn": build_prog_l0ffn,
                "l0ab": build_prog_l0ab, "l1aa": build_prog_l1aa,
                "l1ffn": build_prog_l1ffn, "l1ab": build_prog_l1ab,
                "fused": build_prog_fused}
    if name not in _PROG_CACHE:
        nc = builders[name]()
        _PROG_CACHE[name] = (nc, LAST_CTX.host_fconsts())
    return _PROG_CACHE[name]


def host_fused_inputs(inputs):
    base = {}
    for hostfn in (host_l0aa_inputs, host_l0ffn_inputs, host_l0ab_inputs,
                   host_l1aa_inputs, host_l1ffn_inputs, host_l1ab_inputs):
        base.update(hostfn(inputs))
    base.update(host_const_inputs())
    return base


def kernel(**inputs):
    nc, fc = _get_prog("fused")
    base = host_fused_inputs(inputs)
    base["fconsts"] = fc
    in_maps = [dict(base, x_in=np.ascontiguousarray(
        inputs["L0_attn_a_ids"][c].T.astype(np.float32))) for c in range(NCORES)]
    res = run_bass_kernel_spmd(nc, in_maps, core_ids=list(range(NCORES)))
    return np.ascontiguousarray(
        np.stack([np.asarray(res.results[c]["x_out"]).T
                  for c in range(NCORES)]).astype(np.float32))


# version 27
# speedup vs baseline: 1.0161x; 1.0161x over previous
import numpy as np
import ml_dtypes
from contextlib import ExitStack

import concourse.bass as bass
import concourse.mybir as mybir
from concourse.bass_utils import run_bass_kernel_spmd

F32 = mybir.dt.float32
BF16 = mybir.dt.bfloat16
I32 = mybir.dt.int32
U8 = mybir.dt.uint8
AF = mybir.ActivationFunctionType
OP = mybir.AluOpType
BF = ml_dtypes.bfloat16

ENGS = ("sync", "tensor", "vector", "scalar", "gpsimd")
SELF_SYNC = True
DMAQS = ("sync", "scalar", "gpsimd")

NCORES = 8
T = 2048
D = 1024



class View:
    __slots__ = ("ap", "regs")

    def __init__(self, ap, regs):
        self.ap = ap
        self.regs = regs


class Pool:

    def __init__(self, prog, name, tensor, cols, dtype, parts, is_psum=False):
        self.prog = prog
        self.name = name
        self.t = tensor
        self.cols = cols
        self.dtype = dtype
        self.parts = parts
        self.is_psum = is_psum
        self.off = 0
        self.bank = 2048 // (2 if dtype == BF16 else 4)

    def v(self, col0, ncols, part0=0, nparts=None):
        nparts = nparts if nparts is not None else self.parts
        ap = self.t[part0:part0 + nparts, col0:col0 + ncols]
        lo, hi = col0, col0 + ncols
        if self.is_psum:
            lo = (lo // self.bank) * self.bank
            hi = ceil_div(hi, self.bank) * self.bank
        return View(ap, [(self.name, lo, hi)])

    def all(self):
        return self.v(0, self.cols)


class SubPool:

    def __init__(self, parent, off, cols):
        self.parent = parent
        self.off = parent.off + off
        self.cols = cols
        self.t = parent.t
        self.name = parent.name
        base = parent
        while isinstance(base, SubPool):
            base = base.parent
        self._base = base

    def v(self, col0, ncols, part0=0, nparts=None):
        return self._base.v(self.off + col0, ncols, part0, nparts)


class Prog:
    def __init__(self, nc):
        self.nc = nc
        self.eng_ops = {e: [] for e in ENGS}
        self.dma_n = {q: 0 for q in DMAQS}
        self.cc_n = 0
        self.needed = set()
        self.segs = {}
        self.pools = {}
        self._es = None

    def sbuf(self, name, cols, dtype, parts=128):
        t = self.nc.sbuf_tensor(name, [parts, cols], dtype)
        return self._reg_pool(name, t, cols, dtype, parts)

    def psum(self, name, cols, dtype, parts=128):
        t = self.nc.psum_tensor(name, [parts, cols], dtype)
        return self._reg_pool(name, t, cols, dtype, parts, True)

    def _reg_pool(self, name, tctx, cols, dtype, parts, is_psum=False):
        if self._es is None:
            self._es = ExitStack()
        t = self._es.enter_context(tctx)
        p = Pool(self, name, t, cols, dtype, parts, is_psum)
        self.pools[name] = p
        return p

    def dram(self, name, shape, dtype, kind="Internal"):
        h = self.nc.dram_tensor(name, shape, dtype, kind=kind)
        ap = h.ap()
        return View(ap, [("dram:" + name, 0, 1 << 60)])

    def _dep_read(self, space, lo, hi, deps):
        for s in self.segs.get(space, ()):
            if s[0] < hi and lo < s[1] and s[2] is not None:
                deps.add(s[2])

    def _dep_write(self, space, lo, hi, deps):
        for s in self.segs.get(space, ()):
            if s[0] < hi and lo < s[1]:
                if s[2] is not None:
                    deps.add(s[2])
                deps.update(s[3])

    def _commit_read(self, space, lo, hi, ev):
        for s in self.segs.get(space, ()):
            if s[0] < hi and lo < s[1]:
                s[3].append(ev)

    def _commit_write(self, space, lo, hi, ev):
        segs = self.segs.setdefault(space, [])
        out = []
        for s in segs:
            if s[0] < hi and lo < s[1]:
                if s[0] < lo:
                    out.append([s[0], lo, s[2], list(s[3])])
                if hi < s[1]:
                    out.append([hi, s[1], s[2], list(s[3])])
            else:
                out.append(s)
        out.append([lo, hi, ev, []])
        self.segs[space] = out

    def op(self, eng, fn, reads=(), writes=(), dma=False, cc=False, tag=""):
        deps = set()
        for v in reads:
            for (sp, lo, hi) in v.regs:
                self._dep_read(sp, lo, hi, deps)
        for v in writes:
            for (sp, lo, hi) in v.regs:
                self._dep_write(sp, lo, hi, deps)
        i = len(self.eng_ops[eng])
        if not SELF_SYNC or eng == "tensor":
            deps = {d for d in deps if not (d[0] == "c" and d[1] == eng)}
        if dma:
            q = eng
            assert q in DMAQS, f"dma on engine {eng}"
            self.dma_n[q] += 1
            ev = ("d", q, self.dma_n[q])
        elif cc:
            self.cc_n += 1
            ev = ("cc", self.cc_n)
        else:
            ev = ("c", eng, i)
        for d in deps:
            if d[0] == "c":
                self.needed.add(d)
        self.eng_ops[eng].append(dict(fn=fn, deps=deps, ev=ev, tag=tag))
        for v in reads:
            for (sp, lo, hi) in v.regs:
                self._commit_read(sp, lo, hi, ev)
        for v in writes:
            for (sp, lo, hi) in v.regs:
                self._commit_write(sp, lo, hi, ev)
        return ev

    def barrier_ev(self, eng):
        ops = self.eng_ops[eng]
        if not ops:
            return None
        ev = ops[-1]["ev"]
        if ev[0] == "c":
            self.needed.add(ev)
        return ev

    def finalize(self):
        nc = self.nc
        es = self._es if self._es is not None else ExitStack()
        self._es = es
        sems = {}
        for e in ENGS:
            sems[("c", e)] = es.enter_context(nc.semaphore("sem_" + e))
        for q in DMAQS:
            sems[("d", q)] = es.enter_context(nc.semaphore("dma_" + q))
        sems[("cc",)] = es.enter_context(nc.semaphore("sem_cc"))

        val = {}
        inc_at = {e: {} for e in ENGS}
        for e in ENGS:
            idxs = sorted(i for (k, ee, i) in self.needed if k == "c" and ee == e)
            for rank, i in enumerate(idxs):
                val[("c", e, i)] = rank + 1
                inc_at[e][i] = rank + 1

        def ev_to_wait(ev):
            if ev[0] == "c":
                return (("c", ev[1]), val[ev])
            if ev[0] == "d":
                return (("d", ev[1]), 16 * ev[2])
            return (("cc",), ev[1])

        block = es.enter_context(nc.Block())
        engobjs = {}

        def make_section(e):
            ops = self.eng_ops[e]

            def body(engine):
                done = {}
                for i, o in enumerate(ops):
                    for dep in sorted(o["deps"], key=str):
                        semk, v = ev_to_wait(dep)
                        if done.get(semk, 0) < v:
                            engine.wait_ge(sems[semk], v)
                            done[semk] = v
                    try:
                        inst = o["fn"](engine)
                    except Exception as ex:
                        raise RuntimeError(f"op tag={o['tag']} eng={e} i={i}: {ex}") from ex
                    ev = o["ev"]
                    if ev[0] == "d":
                        inst.then_inc(sems[("d", e)], 16)
                    elif ev[0] == "cc":
                        inst.then_inc(sems[("cc",)])
                    elif i in inc_at[e]:
                        inst.then_inc(sems[("c", e)], 1)
                if e in DMAQS and self.dma_n[e] > 0:
                    engine.wait_ge(sems[("d", e)], 16 * self.dma_n[e])
            return body

        for e in ENGS:
            if not self.eng_ops[e]:
                continue
            dec = getattr(block, e if e != "sync" else "sync")
            dec(make_section(e))
        es.close()
        return nc



def ceil_div(a, b):
    return (a + b - 1) // b


def PCOLS(pool):
    p = pool
    while isinstance(p, SubPool):
        p = p.parent
    return p.cols


class Ctx:

    def __init__(self, prog):
        self.p = prog
        self.const = {}
        self.uid = 0
        self.fconst_vals = []
        self._fc_pool = None

    def name(self, base):
        self.uid += 1
        return f"{base}{self.uid}"

    def fconst(self, value):
        value = float(value)
        if self._fc_pool is None:
            self._fc_dram = self.p.dram("fconsts", [128, 16], F32, kind="ExternalInput")
            self._fc_pool = self.p.sbuf("FC", 16, F32)
            emit_dma_load(self, "sync", self._fc_pool.v(0, 16), self._fc_dram, tag="fconsts")
        if value in self.fconst_vals:
            j = self.fconst_vals.index(value)
        else:
            self.fconst_vals.append(value)
            j = len(self.fconst_vals) - 1
            assert j < 16
        return self._fc_pool.v(j, 1)

    def host_fconsts(self):
        a = np.zeros((128, 16), np.float32)
        for j, v in enumerate(self.fconst_vals):
            a[:, j] = v
        return a


def emit_dma_load(ctx, eng, dst_view, src_ap_view, tag=""):
    p = ctx.p
    return p.op(eng, lambda q, o=dst_view.ap, i=src_ap_view.ap: q.dma_start(out=o, in_=i),
                reads=[src_ap_view], writes=[dst_view], dma=True, tag=tag)


def emit_dma_store(ctx, eng, dst_ap_view, src_view, tag=""):
    p = ctx.p
    return p.op(eng, lambda q, o=dst_ap_view.ap, i=src_view.ap: q.dma_start(out=o, in_=i),
                reads=[src_view], writes=[dst_ap_view], dma=True, tag=tag)


def xview(xt, f, c0, n):
    sp = xt.regs[0][0]
    return View(xt.ap[128 * f:128 * (f + 1), c0:c0 + n],
                [(sp, 2048 * f + c0, 2048 * f + c0 + n)])


def emit_xadd(ctx, f, c0, n, inv, scalar=None, op0=None, tag="xadd"):
    p = ctx.p
    xr = ctx.XR.v(0, n)
    src = xview(ctx.x_src, f, c0, n)
    emit_dma_load(ctx, "scalar", xr, src, tag=tag + ".ld")
    xo = ctx.XR.v(512, n)
    if scalar is None:
        emit_tt(ctx, xo, inv, xr, OP.add, tag=tag + ".add")
    else:
        emit_stt(ctx, xo, inv, scalar, xr, op0, OP.add, tag=tag + ".stt")
    dst = xview(ctx.x_dst, f, c0, n)
    emit_dma_store(ctx, "scalar", dst, xo, tag=tag + ".st")



def host_constants():
    c = {}
    c["IDB"] = np.eye(128).astype(BF)
    c["ID32"] = np.eye(128, dtype=np.float32)
    c["ONESC_B"] = np.ones((128, 1), BF)
    c["ONESC_F"] = np.ones((128, 1), np.float32)
    c["ONESR_B"] = np.ones((1, 128), BF)
    c["ONESR_F"] = np.ones((1, 128), np.float32)
    j = np.arange(128)[:, None]
    t = np.arange(128)[None, :]
    c["TRILT_B"] = (j <= t).astype(BF)
    m = np.zeros((128, 4, 512), np.float32)
    for a in range(4):
        m[:, a, :] = (128 * a + j) <= np.arange(512)[None, :]
    c["CMASK_B"] = m.reshape(128, 2048).astype(BF)
    e16 = np.zeros((128, 16, 16), np.float32)
    for mm_ in range(16):
        e16[:, mm_, mm_] = 1.0
    c["EHS16"] = e16.reshape(128, 256).astype(BF)
    c["DIFFT"] = np.maximum(t - j, 0).astype(np.float32)
    c["REVT"] = np.broadcast_to(127.0 - np.arange(128, dtype=np.float32)[None, :],
                                (128, 128)).copy()
    c["DECT"] = np.broadcast_to(np.arange(128, dtype=np.float32)[None, :] + 1.0,
                                (128, 128)).copy()
    c["REVC"] = (127.0 - np.arange(128, dtype=np.float32))[:, None].copy()
    invf = 1.0 / (10000.0 ** (np.arange(0, 64, 2, dtype=np.float32) / 64.0))
    tt_ = np.arange(2048, dtype=np.float32).reshape(16, 128)
    f_ = tt_[:, :, None] * invf[None, None, :]
    c["COSTOK"] = np.cos(f_).transpose(1, 0, 2).reshape(128, 512).astype(BF)
    c["SINTOK"] = np.sin(f_).transpose(1, 0, 2).reshape(128, 512).astype(BF)
    return c


def emit_load_consts(ctx, names):
    p = ctx.p
    hc = host_constants()
    sel = {n: hc[n] for n in names}
    total = 0
    metas = []
    for n, a in sel.items():
        assert a.shape[0] <= 128
        cols = a.shape[1] * (2 if a.dtype == np.dtype(BF) else 4) // 4
        metas.append((n, a))
    cb = sum(a.shape[1] for n, a in metas if a.dtype == np.dtype(BF))
    cf = sum(a.shape[1] for n, a in metas if a.dtype == np.float32)
    pb = p.sbuf("constB", max(cb, 1), BF16)
    pf = p.sbuf("constF", max(cf, 1), F32)
    ob = of = 0
    host_vals = {}
    for n, a in metas:
        dt = BF16 if a.dtype == np.dtype(BF) else F32
        dv = p.dram("c_" + n, list(a.shape), dt, kind="ExternalInput")
        host_vals["c_" + n] = a
        if dt == BF16:
            v = pb.v(ob, a.shape[1], 0, a.shape[0]); ob += a.shape[1]
        else:
            v = pf.v(of, a.shape[1], 0, a.shape[0]); of += a.shape[1]
        emit_dma_load(ctx, "sync", v, dv, tag="const:" + n)
        ctx.const[n] = v
    ctx.host_const_vals = host_vals
    return host_vals



def emit_mm(ctx, psv, lhsT, rhs, start, stop, tag=""):
    ctx.p.op("tensor",
             lambda t, o=psv.ap, l=lhsT.ap, r=rhs.ap, s0=start, s1=stop:
             t.matmul(o, l, r, start=s0, stop=s1),
             reads=[lhsT, rhs], writes=[psv], tag=tag)


def emit_transpose(ctx, psv, inv, ident, tag=""):
    np_ = inv.ap.shape[0]
    b0 = inv.ap.base_partition() if callable(inv.ap.base_partition) else inv.ap.base_partition
    dap = ident.ap
    if dap.shape[0] != np_ or b0 != 0:
        dap = ident.ap.tensor[b0:b0 + np_, 0:np_]
    ctx.p.op("tensor",
             lambda t, o=psv.ap, i=inv.ap, d=dap: t.transpose(o, i, d),
             reads=[inv, ident], writes=[psv], tag=tag)


def emit_copy(ctx, eng, dst, src, tag=""):
    fn = (lambda q, o=dst.ap, i=src.ap: q.tensor_copy(o, i)) if eng != "scalar" else \
         (lambda q, o=dst.ap, i=src.ap: q.copy(o, i))
    ctx.p.op(eng, fn, reads=[src], writes=[dst], tag=tag)


def emit_tt(ctx, dst, a, b, op, tag="", eng="vector"):
    ctx.p.op(eng, lambda q, o=dst.ap, x=a.ap, y=b.ap, oo=op: q.tensor_tensor(o, x, y, oo),
             reads=[a, b], writes=[dst], tag=tag)


def emit_ts(ctx, dst, a, s1, s2, op0, op1=None, tag="", eng="vector", accum=None):
    reads = [a]
    s1a = s1
    if isinstance(s1, View):
        reads.append(s1); s1a = s1.ap
    s2a = s2
    if isinstance(s2, View):
        reads.append(s2); s2a = s2.ap
    writes = [dst]
    aa = None
    if accum is not None:
        writes.append(accum); aa = accum.ap
    if op1 is None:
        ctx.p.op(eng, lambda q, o=dst.ap, x=a.ap, u=s1a, w=s2a, o0=op0, ac=aa:
                 q.tensor_scalar(o, x, u, w, o0, accum_out=ac),
                 reads=reads, writes=writes, tag=tag)
    else:
        ctx.p.op(eng, lambda q, o=dst.ap, x=a.ap, u=s1a, w=s2a, o0=op0, o1=op1, ac=aa:
                 q.tensor_scalar(o, x, u, w, o0, o1, accum_out=ac),
                 reads=reads, writes=writes, tag=tag)


def emit_stt(ctx, dst, in0, scal, in1, op0, op1, tag=""):
    reads = [in0, in1]
    sa = scal
    if isinstance(scal, View):
        reads.append(scal); sa = scal.ap
    ctx.p.op("vector", lambda q, o=dst.ap, a=in0.ap, s=sa, b=in1.ap, o0=op0, o1=op1:
             q.scalar_tensor_tensor(o, a, s, b, o0, o1),
             reads=reads, writes=[dst], tag=tag)


def emit_act(ctx, dst, src, func, bias=0.0, scale=1.0, accum=None, tag=""):
    if isinstance(bias, float) and bias != 0.0:
        bias = ctx.fconst(bias)
    if isinstance(bias, View) and bias.ap.shape[0] != src.ap.shape[0]:
        b0 = src.ap.base_partition
        b0 = b0() if callable(b0) else b0
        j = bias.regs[0][1]
        bias = View(bias.ap.tensor[b0:b0 + src.ap.shape[0], j:j + 1], bias.regs)
    reads = [src]
    ba = bias
    if isinstance(bias, View):
        reads.append(bias); ba = bias.ap
    sa = scale
    if isinstance(scale, View):
        reads.append(scale); sa = scale.ap
    writes = [dst]
    aa = None
    if accum is not None:
        writes.append(accum); aa = accum.ap
    if aa is None:
        ctx.p.op("scalar", lambda q, o=dst.ap, i=src.ap, f=func, b=ba, s=sa:
                 q.activation(o, i, f, bias=b, scale=s),
                 reads=reads, writes=writes, tag=tag)
    else:
        ctx.p.op("scalar", lambda q, o=dst.ap, i=src.ap, f=func, b=ba, s=sa, ac=aa:
                 q.activation(o, i, f, bias=b, scale=s, accum_out=ac),
                 reads=reads, writes=writes, tag=tag)


def load_weight_bf16(ctx, dram_ap, regs, rows, cols, dst_pool, dst_col0, tag="w"):
    nrt = ceil_div(rows, 128)
    for k in range(nrt):
        r0 = 128 * k
        nr = min(128, rows - r0)
        dv = View(dram_ap[r0:r0 + nr, :], regs)
        sv = dst_pool.v(dst_col0 + k * cols, cols, 0, nr)
        emit_dma_load(ctx, "gpsimd", sv, dv, tag=tag + f".load{k}")
    return nrt


def wtile(pool, col0, cols, k, nr=128, c0=0, ncol=None):
    ncol = cols if ncol is None else ncol
    base = pool.off + col0 + k * cols + c0
    return View(pool.t[0:nr, base: base + ncol],
                [(pool.name, base, base + ncol)])


def emit_transpose_tiles(ctx, src_pool, src_col0, rows, cols, dst_pool, dst_col0,
                         pst, tag="wt", row0=0, nrows=None):
    nrows = rows - row0 if nrows is None else nrows
    nrt = ceil_div(nrows, 128)
    nct = ceil_div(cols, 128)
    idb = ctx.const["IDB"]
    for c in range(nct):
        c0 = 128 * c
        ncol = min(128, cols - c0)
        for k0 in range(0, nrt, 8):
            kn = min(nrt, k0 + 8)
            for k in range(k0, kn):
                nr = min(128, nrows - 128 * k)
                svv = wtile(src_pool, src_col0, cols, k, nr, c0, ncol)
                ps = View(pst.t[0:ncol, 128 * (k - k0):128 * (k - k0) + nr],
                          [(pst.name, 128 * (k - k0), 128 * (k - k0) + nr)])
                emit_transpose(ctx, ps, svv, idb, tag=tag + f".t{c}.{k}")
            w = min(nrows, 128 * kn) - 128 * k0
            pse = View(pst.t[0:ncol, 0:w], [(pst.name, 0, w)])
            b0 = dst_pool.off + dst_col0 + c * rows + row0 + 128 * k0
            dv = View(dst_pool.t[0:ncol, b0:b0 + w], [(dst_pool.name, b0, b0 + w)])
            emit_copy(ctx, "vector", dv, pse, tag=tag + f".ev{c}.{k0}")


def wt_tile(pool, col0, rows, c, ncol=128, r0=0, nr=None):
    nr = rows if nr is None else nr
    base = pool.off + col0 + c * rows + r0
    return View(pool.t[0:ncol, base:base + nr], [(pool.name, base, base + nr)])



class ColVec:
    def __init__(self, pool, col0, n):
        self.pool = pool
        self.col0 = col0
        self.n = n

    def col(self, j, part0=0, nparts=128):
        return self.pool.v(self.col0 + j, 1, part0, nparts)

    def cols(self, j, n, part0=0, nparts=128):
        return self.pool.v(self.col0 + j, n, part0, nparts)


def declare_colvec(ctx, name, ncols, nparts=128):
    p = ctx.p
    dv = p.dram("cv_" + name, [nparts, ncols], F32, kind="ExternalInput")
    col0 = ctx.smf_off
    ctx.smf_off += ncols
    v = ctx.SMF.v(col0, ncols, 0, nparts)
    emit_dma_load(ctx, "sync", v, dv, tag="cv." + name)
    cv = ColVec(ctx.SMF, col0, ncols)
    ctx.colvecs[name] = cv
    return cv




def emit_l0aa(ctx):
    XN = ctx.XN
    PSA, PSB, PSC, PST = ctx.PSA, ctx.PSB, ctx.PSC, ctx.PST
    W, AB = ctx.W, ctx.ACTB
    idb = ctx.const["IDB"]
    trilt = ctx.const["TRILT_B"]
    qkvb = ctx.colvecs["l0aa_qkvb"]
    projb = ctx.colvecs["l0aa_projb"]

    load_weight_bf16(ctx, ctx.win["L0_attn_a_qkv_w"].ap, ctx.win["L0_attn_a_qkv_w"].regs,
                     192, 1024, W, 0, tag="qkvw")
    emit_transpose_tiles(ctx, W, 0, 192, 1024, W, 2048, PST, tag="qkvwT")
    load_weight_bf16(ctx, ctx.win["L0_attn_a_proj_w"].ap, ctx.win["L0_attn_a_proj_w"].regs,
                     1024, 1024, W, 4096, tag="projw")

    for c in range(4):
        for k in range(8):
            lA = wt_tile(W, 2048, 192, k, 128, 0, 128)
            lB = wt_tile(W, 2048, 192, k, 128, 128, 64)
            emit_mm(ctx, PSA.v(0, 512), lA, XN.v(2048 * k + 512 * c, 512), k == 0, k == 7, tag="qkv.A")
            emit_mm(ctx, PSB.v(0, 512, 0, 64), lB, XN.v(2048 * k + 512 * c, 512), k == 0, k == 7, tag="qkv.B")
        emit_ts(ctx, AB.v(512 * c, 512, 0, 64), PSA.v(0, 512, 0, 64),
                qkvb.col(0, 0, 64), None, OP.add, tag="qkv.evq")
        emit_ts(ctx, AB.v(2048 + 512 * c, 512, 0, 64), PSA.v(0, 512, 64, 64),
                qkvb.col(0, 64, 64), None, OP.add, tag="qkv.evk")
        emit_ts(ctx, AB.v(4096 + 512 * c, 512, 0, 64), PSB.v(0, 512, 0, 64),
                qkvb.col(1, 0, 64), None, OP.add, tag="qkv.evv")

    qT = lambda c0, n: AB.v(c0, n, 0, 64)
    kTv = lambda c0, n: AB.v(2048 + c0, n, 0, 64)
    vT = lambda c0, n: AB.v(4096 + c0, n, 0, 64)

    state_ps = PSC.v(0, 64, 0, 64)
    for c in range(16):
        c0 = 128 * c
        ktok = AB.v(8192, 64)
        vtok = AB.v(8256, 64)
        emit_transpose(ctx, PST.v(0, 64), kTv(c0, 128), idb, tag="laa.tk")
        emit_transpose(ctx, PST.v(64, 64), vT(c0, 128), idb, tag="laa.tv")
        emit_copy(ctx, "scalar", ktok, PST.v(0, 64), tag="laa.ek")
        emit_copy(ctx, "scalar", vtok, PST.v(64, 64), tag="laa.ev")
        psS = PSA.v(0, 128)
        emit_mm(ctx, psS, kTv(c0, 128), qT(c0, 128), True, True, tag="laa.S")
        smask = AB.v(8320, 128)
        emit_tt(ctx, smask, psS, trilt, OP.mult, tag="laa.mask")
        psO = PSB.v(512, 128, 0, 64)
        state_sb = AB.v(8448, 64, 0, 64)
        emit_mm(ctx, psO, vtok, smask, True, c == 0, tag="laa.intra")
        if c > 0:
            emit_mm(ctx, psO, state_sb, qT(c0, 128), False, True, tag="laa.inter")
        oT = AB.v(6144 + c0, 128, 0, 64)
        emit_copy(ctx, "vector", oT, psO, tag="laa.eo")
        emit_mm(ctx, state_ps, ktok, vtok, c == 0, c == 15, tag="laa.state")
        emit_copy(ctx, "vector", state_sb, state_ps, tag="laa.es")

    for k in range(8):
        b = 4096 + 1024 * k
        for w in (512, 256, 128, 64):
            a0 = W.v(b, w)
            a1 = W.v(b + w, w)
            emit_tt(ctx, a0, a0, a1, OP.add, tag="weff.sum")
    for k in range(8):
        b = 4096 + 1024 * k
        emit_transpose(ctx, PST.v(0, 128, 0, 64), W.v(b, 64), idb, tag="weff.t")
        emit_copy(ctx, "vector", W.v(12288 + 128 * k, 128, 0, 64), PST.v(0, 128, 0, 64), tag="weff.ev")

    for f in range(8):
        weffT = W.v(12288 + 128 * f, 128, 0, 64)
        for c in range(4):
            ps = PSA.v(512 + 512 * (c % 2), 512)
            emit_mm(ctx, ps, weffT, AB.v(6144 + 512 * c, 512, 0, 64), True, True, tag="out.mm")
            emit_xadd(ctx, f, 512 * c, 512, ps, projb.col(f), OP.add, tag="l0aa.res")



CONST_NAMES = ["IDB", "ID32", "ONESC_B", "ONESC_F", "ONESR_B", "ONESR_F",
               "TRILT_B", "CMASK_B", "EHS16"]


def make_base_ctx(nc, w_cols=14336, ab_cols=8704, actf_cols=2048, consts=None, sm_pool=True, no_x=False):
    p = Prog(nc)
    ctx = Ctx(p)
    ctx.win = {}
    ctx.colvecs = {}
    ctx.smf_off = 0
    ctx.xr_slot = 0
    ctx.XN = p.sbuf("XN", 8 * 2048, BF16)
    ctx.W = p.sbuf("Wp", w_cols, BF16)
    if ab_cols:
        ctx.ACTB = p.sbuf("ACTB", ab_cols, BF16)
    ctx.ACTF = p.sbuf("ACTF", max(actf_cols, 2048), F32)
    ctx.XS = p.sbuf("XSp", 2048, F32)
    ctx.HCF = p.sbuf("HCFp", 2048, F32)
    ctx.XR = p.sbuf("XRp", 1024, F32)
    ctx.LNSQ = p.sbuf("LNSQ", 512, BF16)
    ctx.SMF = p.sbuf("SMF", 256, F32)
    ctx.PSA = p.psum("PSA", 2048, F32)
    ctx.PSB = p.psum("PSB", 1024, F32)
    ctx.PSC = p.psum("PSC", 512, F32)
    ctx.PST = p.psum("PST", 1024, BF16)
    emit_load_consts(ctx, consts if consts is not None else CONST_NAMES)
    return ctx


def declare_weights(ctx, names_shapes_dtypes):
    for name, shape, dt in names_shapes_dtypes:
        ctx.win[name] = ctx.p.dram(name, list(shape), dt, kind="ExternalInput")


LAST_CTX = None


def build_prog_l0aa():
    global LAST_CTX
    nc = bass.Bass("TRN2", target_bir_lowering=False)
    ctx = make_base_ctx(nc)
    LAST_CTX = ctx
    p = ctx.p
    x_in = p.dram("x_in", [1024, 2048], F32, kind="ExternalInput")
    x_out = p.dram("x_out", [1024, 2048], F32, kind="ExternalOutput")
    declare_weights(ctx, [("L0_attn_a_qkv_w", (192, 1024), F32),
                          ("L0_attn_a_proj_w", (1024, 1024), F32)])
    declare_colvec(ctx, "l0aa_qkvb", 2)
    declare_colvec(ctx, "l0aa_projb", 8)
    ctx.x_src, ctx.x_dst = x_in, x_out
    emit_ln_hbm(ctx, x_in)
    emit_l0aa(ctx)
    p.finalize()
    return nc


def host_l0aa_inputs(inputs):
    qkvb = np.zeros((128, 2), np.float32)
    qkvb[0:64, 0] = inputs["L0_attn_a_qkv_b"][0:64]
    qkvb[64:128, 0] = inputs["L0_attn_a_qkv_b"][64:128]
    qkvb[0:64, 1] = inputs["L0_attn_a_qkv_b"][128:192]
    projb = np.ascontiguousarray(inputs["L0_attn_a_proj_b"].reshape(8, 128).T)
    return {"L0_attn_a_qkv_w": inputs["L0_attn_a_qkv_w"],
            "L0_attn_a_proj_w": inputs["L0_attn_a_proj_w"],
            "cv_l0aa_qkvb": qkvb, "cv_l0aa_projb": projb}


def host_const_inputs():
    return {"c_" + n: a for n, a in host_constants().items()}



def emit_l0ffn_phaseW(ctx, W):
    p = ctx.p
    PST = ctx.PST
    idb = ctx.const["IDB"]
    wguT = p.dram("sc_wguT", [24, 128, 2048], BF16)
    wdT = p.dram("sc_wdT", [8, 128, 3072], BF16)
    ctx.ffn_cache = (wguT, wdT)
    for half, dv in ((0, ctx.win["L0_ffn_w_gate"]),
                     (1, ctx.win["L0_ffn_w_up"])):
        for m in range(24):
            raw = W.v(0, 1024)
            src = View(dv.ap[128 * m:128 * (m + 1), :], dv.regs)
            emit_dma_load(ctx, "gpsimd", raw, src, tag=f"ffn.w{half}{m}")
            for c in range(8):
                emit_transpose(ctx, PST.v(128 * c, 128), W.v(128 * c, 128), idb,
                               tag="ffn.wT")
            wt = W.v(1024, 1024)
            emit_copy(ctx, "vector", wt, PST.v(0, 1024), tag="ffn.wev")
            dst = View(wguT.ap[m][:, 1024 * half:1024 * (half + 1)], wguT.regs)
            emit_dma_store(ctx, "sync", dst, wt, tag="ffn.wsp")
    for f in range(8):
        raw = W.v(0, 3072)
        src = View(ctx.win["L0_ffn_w_down"].ap[128 * f:128 * (f + 1), :],
                   ctx.win["L0_ffn_w_down"].regs)
        emit_dma_load(ctx, "gpsimd", raw, src, tag=f"ffn.wd{f}")
        for mb in range(3):
            for j in range(8):
                emit_transpose(ctx, PST.v(128 * j, 128),
                               W.v(128 * (8 * mb + j), 128), idb, tag="ffn.wdT")
            wt = W.v(3072, 1024)
            emit_copy(ctx, "vector", wt, PST.v(0, 1024), tag="ffn.wdev")
            dst = View(wdT.ap[f, :, 1024 * mb:1024 * (mb + 1)], wdT.regs)
            emit_dma_store(ctx, "sync", dst, wt, tag="ffn.wdsp")


def emit_l0ffn(ctx):
    p = ctx.p
    XN = ctx.XN
    PSA, PSB, PST = ctx.PSA, ctx.PSB, ctx.PST
    W, W2, H = ctx.W, ctx.W2, ctx.H
    idb = ctx.const["IDB"]

    if getattr(ctx, "ffn_cache", None) is None:
        emit_l0ffn_phaseW(ctx, W)
    wguT, wdT = ctx.ffn_cache

    for tc in range(4):
        t0 = 512 * tc
        for m in range(24):
            s = m % 3
            vgu = W2.v(2048 * s, 2048)
            emit_dma_load(ctx, "sync", vgu, View(wguT.ap[m], wguT.regs), tag="ffn.lg")
            pg = PSA.v(0, 512)
            pu = PSA.v(512, 512)
            for k in range(8):
                emit_mm(ctx, pg, W2.v(2048 * s + 128 * k, 128),
                        XN.v(2048 * k + t0, 512), k == 0, k == 7, tag="ffn.g")
                emit_mm(ctx, pu, W2.v(2048 * s + 1024 + 128 * k, 128),
                        XN.v(2048 * k + t0, 512), k == 0, k == 7, tag="ffn.u")
            sg = ctx.ACTB.v(0, 512)
            emit_act(ctx, sg, pg, AF.Silu, scale=1.0 / 127.0, tag="ffn.silu")
            emit_tt(ctx, H.v(512 * m, 512), sg, pu, OP.mult, tag="ffn.h")
        for f in range(8):
            s = f % 2
            wv = W2.v(6144 + 3072 * s, 3072)
            emit_dma_load(ctx, "sync", wv, View(wdT.ap[f], wdT.regs), tag="ffn.lwd")
            pd = PSB.v(0, 512)
            for m in range(24):
                lhsT = W2.v(6144 + 3072 * s + 128 * m, 128)
                emit_mm(ctx, pd, lhsT, H.v(512 * m, 512),
                        m == 0, m == 23, tag="ffn.d")
            emit_xadd(ctx, f, t0, 512, pd, 1.0 / (127.0 * 127.0),
                      OP.mult, tag="ffn.res")


def build_prog_l0ffn():
    global LAST_CTX
    nc = bass.Bass("TRN2", target_bir_lowering=False)
    ctx = make_base_ctx(nc, w_cols=4096, ab_cols=512, actf_cols=4096)
    LAST_CTX = ctx
    p = ctx.p
    ctx.W2 = p.sbuf("W2p", 12288, BF16)
    ctx.H = p.sbuf("Hp", 12288, BF16)
    x_in = p.dram("x_in", [1024, 2048], F32, kind="ExternalInput")
    x_out = p.dram("x_out", [1024, 2048], F32, kind="ExternalOutput")
    declare_weights(ctx, [("L0_ffn_w_gate", (3072, 1024), I32),
                          ("L0_ffn_w_up", (3072, 1024), I32),
                          ("L0_ffn_w_down", (1024, 3072), I32)])
    ctx.x_src, ctx.x_dst = x_in, x_out
    emit_ln_hbm(ctx, x_in)
    emit_l0ffn(ctx)
    p.finalize()
    return nc


def host_l0ffn_inputs(inputs):
    return {k: inputs[k] for k in ("L0_ffn_w_gate", "L0_ffn_w_up", "L0_ffn_w_down")}



def emit_l0ab(ctx, M0=16.0):
    p = ctx.p
    XN = ctx.XN
    PSA, PSB, PST = ctx.PSA, ctx.PSB, ctx.PST
    W = ctx.W
    QH, KH, VAUG, YH = ctx.QH, ctx.KH, ctx.VAUG, ctx.YH
    ATTN = ctx.ATTN
    idb = ctx.const["IDB"]
    cmask = ctx.const["CMASK_B"]
    onesr = ctx.const["ONESR_F"]
    onesb = ctx.const["ONESC_B"]

    for g in range(2):
        for wi, (wname, dstpool) in enumerate((("L0_attn_b_Wq", QH), ("L0_attn_b_Wk", KH))):
            sl = (2 * g + wi) % 2
            dv = ctx.win[wname]
            for r in range(4):
                src = View(dv.ap[512 * g + 128 * r:512 * g + 128 * (r + 1), :], dv.regs)
                emit_dma_load(ctx, "gpsimd", W.v(8192 * sl + 1024 * r, 1024), src, tag="ab.wl")
            emit_transpose_tiles(ctx, W, 8192 * sl, 512, 1024, W, 8192 * sl + 4096, PST,
                                 tag="ab.wT")
            for i in range(4):
                for c in range(4):
                    ps = PSA.v(512 * (c % 2), 512)
                    for k in range(8):
                        lhsT = wt_tile(W, 8192 * sl + 4096, 512, k, 128, 128 * i, 128)
                        emit_mm(ctx, ps, lhsT, XN.v(2048 * k + 512 * c, 512),
                                k == 0, k == 7, tag="ab.qk")
                    emit_copy(ctx, "vector", dstpool.v(2048 * i + 512 * c, 512), ps, tag="ab.qkev")
        dv = ctx.win["L0_attn_b_Wv"]
        for r in range(4):
            src = View(dv.ap[512 * g + 128 * r:512 * g + 128 * (r + 1), :], dv.regs)
            emit_dma_load(ctx, "gpsimd", W.v(1024 * r, 1024), src, tag="ab.wvl")
        emit_transpose_tiles(ctx, W, 0, 512, 1024, W, 4096, PST, tag="ab.wvT")
        ones_dst = View(VAUG.t[0:128, VAUG.off:VAUG.off + 520 * 16]
                        .rearrange("p (j h o) -> p j h o", h=8, o=65)[:, :, :, 64:65],
                        [(VAUG.name, VAUG.off, VAUG.off + 520 * 16)])
        ones_src = View(onesb.ap.unsqueeze(1).to_broadcast([128, 16, 8, 1]), onesb.regs)
        p.op("vector", lambda q, o=ones_dst.ap, i=ones_src.ap: q.tensor_copy(o, i),
             reads=[ones_src], writes=[ones_dst], tag="ab.vones")
        for tj in range(16):
            ps = PSA.v(0, 512)
            for k in range(8):
                rhs = wt_tile(W, 4096, 512, k, 128, 0, 512)
                emit_mm(ctx, ps, XN.v(2048 * k + 128 * tj, 128), rhs,
                        k == 0, k == 7, tag="ab.v")
            dst = View(VAUG.t[0:128, VAUG.off + 520 * tj:VAUG.off + 520 * tj + 520]
                       .rearrange("p (h o) -> p h o", h=8)[:, :, 0:64],
                       [(VAUG.name, VAUG.off + 520 * tj, VAUG.off + 520 * tj + 520)])
            srcps = View(PSA.t[0:128, 0:512].rearrange("p (h o) -> p h o", h=8),
                         [(PSA.name, 0, 512)])
            p.op("vector", lambda q, o=dst.ap, i=srcps.ap: q.tensor_copy(o, i),
                 reads=[srcps], writes=[dst], tag="ab.vev")
        dv = ctx.win["L0_attn_b_Wo"]
        for r in range(8):
            src = View(dv.ap[128 * r:128 * (r + 1), 512 * g:512 * (g + 1)], dv.regs)
            emit_dma_load(ctx, "gpsimd", W.v(512 * r, 512), src, tag="ab.wol")
        emit_transpose_tiles(ctx, W, 0, 1024, 512, W, 4096, PST, tag="ab.woT")
        for c in range(4):
            for hp in range(4):
                hpair = (2 * hp, 2 * hp + 1)
                nj = 4 * c + 4

                def atslot(j, i):
                    return (ATTN.v(512 * (j % 2), 512) if i == 0
                            else W.v(512 * (j % 2), 512))

                def a_front(j, i, c=c):
                    hh = hpair[i]
                    kblk, kbase = hh // 2, 64 * (hh % 2)
                    psS = PSA.v(512 * (2 * i + j % 2), 512)
                    emit_mm(ctx, psS,
                            KH.v(2048 * kblk + 128 * j, 128, kbase, 64),
                            QH.v(2048 * kblk + 512 * c, 512, kbase, 64),
                            True, True, tag="ab.s")
                    at = atslot(j, i)
                    emit_act(ctx, at, psS, AF.Exp, bias=-M0, scale=0.125, tag="ab.exp")
                    if j >= 4 * c:
                        emit_tt(ctx, at, at,
                                View(cmask.ap.tensor[0:128, cmask.regs[0][1] + 512 * (j - 4 * c):
                                     cmask.regs[0][1] + 512 * (j - 4 * c + 1)], cmask.regs),
                                OP.mult, tag="ab.mask")

                def a_av(j, i, nj=nj):
                    hh = hpair[i]
                    emit_mm(ctx, PSB.v(512 * i, 512, 0, 65),
                            VAUG.v(520 * j + 65 * hh, 65), atslot(j, i),
                            j == 0, j == nj - 1, tag="ab.av")

                for j in range(nj):
                    a_front(j, 0)
                    a_front(j, 1)
                    if j > 0:
                        a_av(j - 1, 0)
                        a_av(j - 1, 1)
                a_av(nj - 1, 0)
                a_av(nj - 1, 1)
                for i in range(2):
                    hh = hpair[i]
                    kblk, kbase = hh // 2, 64 * (hh % 2)
                    rl = ctx.XS.v(512 * i, 512, 0, 1)
                    p.op("vector", lambda q, o=rl.ap, i_=PSB.t[64:65, 512 * i:512 * i + 512]:
                         q.reciprocal(o, i_),
                         reads=[PSB.v(512 * i, 512, 64, 1)], writes=[rl], tag="ab.rl")
                    psb2 = (ctx.PSC.v(0, 512, 0, 64) if i == 0
                            else PSA.v(0, 512, 0, 64))
                    emit_mm(ctx, psb2, View(onesr.ap[0:1, 0:64], onesr.regs), rl,
                            True, True, tag="ab.bc")
                    rlb = (ATTN.v(1024, 512, 0, 64) if i == 0
                           else W.v(1024, 512, 0, 64))
                    emit_copy(ctx, "scalar", rlb, psb2, tag="ab.rlbev")
                    emit_tt(ctx, YH.v(512 * kblk, 512, kbase, 64),
                            PSB.v(512 * i, 512, 0, 64), rlb, OP.mult, tag="ab.y")
            for f in range(8):
                ps = PSA.v(1024, 512)
                for i in range(4):
                    lhsT = wt_tile(W, 4096, 1024, i, 128, 128 * f, 128)
                    emit_mm(ctx, ps, lhsT, YH.v(512 * i, 512), i == 0, i == 3, tag="ab.o")
                emit_xadd(ctx, f, 512 * c, 512, ps, tag="l0ab.res")


def build_prog_l0ab():
    global LAST_CTX
    nc = bass.Bass("TRN2", target_bir_lowering=False)
    ctx = make_base_ctx(nc, w_cols=16384, ab_cols=512, actf_cols=4096)
    LAST_CTX = ctx
    p = ctx.p
    ctx.QH = p.sbuf("QH", 8192, BF16)
    ctx.KH = p.sbuf("KH", 8192, BF16)
    ctx.VAUG = p.sbuf("VAUGp", 520 * 16, BF16)
    ctx.YH = p.sbuf("YHp", 2048, BF16)
    ctx.ATTN = p.sbuf("ATTNp", 1536, BF16)
    x_in = p.dram("x_in", [1024, 2048], F32, kind="ExternalInput")
    x_out = p.dram("x_out", [1024, 2048], F32, kind="ExternalOutput")
    declare_weights(ctx, [("L0_attn_b_Wq", (1024, 1024), F32),
                          ("L0_attn_b_Wk", (1024, 1024), F32),
                          ("L0_attn_b_Wv", (1024, 1024), F32),
                          ("L0_attn_b_Wo", (1024, 1024), F32)])
    ctx.x_src, ctx.x_dst = x_in, x_out
    emit_ln_hbm(ctx, x_in)
    emit_l0ab(ctx)
    p.finalize()
    return nc


def host_l0ab_inputs(inputs):
    return {k: inputs[k] for k in ("L0_attn_b_Wq", "L0_attn_b_Wk",
                                   "L0_attn_b_Wv", "L0_attn_b_Wo")}



def grouped(v, g=16):
    return View(v.ap.rearrange("p (g k) -> p g k", g=g), v.regs)


def gslice(v, k0, k1, g=16):
    ap = v.ap.rearrange("p (g k) -> p g k", g=g)
    return View(ap[:, :, k0:k1], v.regs)


def emit_l1ffn(ctx, idx):
    p = ctx.p
    XN = ctx.XN
    PSA, PSB, PSC, PST = ctx.PSA, ctx.PSB, ctx.PSC, ctx.PST
    W, SCOR, QB, F2, SF, RN, ACC = (ctx.W, ctx.SCOR, ctx.QB, ctx.F2, ctx.SF,
                                    ctx.RN, ctx.ACC)
    idb = ctx.const["IDB"]
    id32 = ctx.const["ID32"]
    ehs = ctx.const["EHS16"]
    bq = ctx.colvecs["l1ffn_bq"]
    gam = ctx.colvecs["l1ffn_gam"]
    bet = ctx.colvecs["l1ffn_bet"]

    scr = getattr(ctx, "l1ffn_scr", None)
    if scr is None:
        scr = p.dram("sc_qT", [16, 128, 2048], BF16)
        ctx.l1ffn_scr = scr
    cc_in = p.dram(f"cc_in{idx}", [128, 128], F32)
    cc_out = p.dram(f"cc_out{idx}", [128, 128], F32)
    ccv_in = View(cc_in.ap, [(f"dram:cc_in{idx}", 0, 128)])
    ccv_out = View(cc_out.ap, [(f"dram:cc_out{idx}", 0, 128)])

    load_weight_bf16(ctx, ctx.win["L1_ffn_Wq"].ap, ctx.win["L1_ffn_Wq"].regs,
                     1024, 2048, W, 0, tag="ff.wq")
    for m in range(16):
        for ch in range(2):
            for c2 in range(2):
                c = 2 * ch + c2
                ps = PSA.v(1024 * ch + 512 * c2, 512)
                for k in range(8):
                    lhsT = wtile(W, 0, 2048, k, 128, 128 * m, 128)
                    emit_mm(ctx, ps, lhsT, XN.v(2048 * k + 512 * c, 512),
                            k == 0, k == 7, tag="ff.q")
            qb = QB.v(2048 * ch + 1024 * (m % 2), 1024)
            emit_act(ctx, qb, PSA.v(1024 * ch, 1024), AF.Identity, bias=bq.col(m),
                     accum=ACC.v(m + 16 * ch, 1), tag="ff.qev")
            emit_act(ctx, SF.v(1024 * ch, 1024), PSA.v(1024 * ch, 1024), AF.Square,
                     bias=bq.col(m), accum=ACC.v(32 + m + 16 * ch, 1), tag="ff.qsq")
            emit_dma_store(ctx, "sync",
                           View(scr.ap[m][:, 1024 * ch:1024 * (ch + 1)], scr.regs),
                           qb, tag="ff.spill")
    emit_tt(ctx, ACC.v(88, 16), ACC.v(0, 16), ACC.v(16, 16), OP.add, tag="ff.cmb")
    emit_tt(ctx, ACC.v(104, 16), ACC.v(32, 16), ACC.v(48, 16), OP.add, tag="ff.cmb2")
    for j, (base, par) in enumerate(((88, 0), (88, 1), (104, 0), (104, 1))):
        srcap = bass.AP(ACC.t, ACC.off + base + par, [[PCOLS(ACC), 128], [2, 8]])
        srcv = View(srcap, [(ACC.name, ACC.off + base, ACC.off + base + 16)])
        dstv = ACC.v(64 + j, 1)
        p.op("vector", lambda q, o=dstv.ap, i=srcap:
             q.reduce_sum(o, i, mybir.AxisListType.X),
             reads=[srcv], writes=[dstv], tag="ff.red")
    emit_dma_store(ctx, "gpsimd", View(cc_in.ap[:, 0:4], ccv_in.regs),
                   ACC.v(64, 4), tag="ff.ccst")
    p.op("gpsimd", lambda q, a=cc_in.ap, b=cc_out.ap: q.collective_compute(
        "AllReduce", OP.add, replica_groups=[list(range(NCORES))], ins=[a], outs=[b]),
        reads=[ccv_in], writes=[ccv_out], cc=True, tag="ff.cc")
    emit_dma_load(ctx, "gpsimd", ACC.v(68, 4), View(cc_out.ap[:, 0:4], ccv_out.regs), tag="ff.ccld")
    NTOT = 131072.0
    mean = ACC.v(72, 2)
    emit_ts(ctx, mean, ACC.v(68, 2), 1.0 / NTOT, None, OP.mult, tag="ff.mean")
    msq = ACC.v(74, 2)
    emit_tt(ctx, msq, mean, mean, OP.mult, tag="ff.msq")
    var = ACC.v(76, 2)
    emit_ts(ctx, var, ACC.v(70, 2), 1.0 / NTOT, None, OP.mult, tag="ff.var0")
    emit_tt(ctx, var, var, msq, OP.subtract, tag="ff.var")
    sd = ACC.v(78, 2)
    emit_act(ctx, sd, var, AF.Sqrt, bias=1e-5, tag="ff.sd")
    rsd = ACC.v(80, 2)
    p.op("vector", lambda q, o=rsd.ap, i=sd.ap: q.reciprocal(o, i),
         reads=[sd], writes=[rsd], tag="ff.rsd")
    bnsc = ACC.v(82, 2)
    emit_tt(ctx, bnsc, rsd, gam.cols(0, 2), OP.mult, tag="ff.bnsc")
    bnsh = ACC.v(84, 2)
    emit_stt(ctx, bnsh, mean, -1.0, bnsc, OP.mult, OP.mult, tag="ff.bnsh0")
    emit_tt(ctx, bnsh, bnsh, bet.cols(0, 2), OP.add, tag="ff.bnsh")

    for j, nm in enumerate(("L1_ffn_sub_key0", "L1_ffn_sub_key1")):
        kb = F2.v(8704 + 128 * j, 128, 0, 32)
        emit_dma_load(ctx, "gpsimd", kb, ctx.win[nm], tag="ff.sk")
        ksq = SF.v(0, 128, 0, 32)
        emit_act(ctx, ksq, kb, AF.Square, tag="ff.ksq")
        kss = SF.v(128, 1, 0, 32)
        p.op("vector", lambda q, o=kss.ap, i=ksq.ap: q.reduce_sum(o, i, mybir.AxisListType.X),
             reads=[ksq], writes=[kss], tag="ff.kss")
        ksd = SF.v(129, 1, 0, 32)
        emit_act(ctx, ksd, kss, AF.Sqrt, tag="ff.ksd")
        emit_ts(ctx, ksd, ksd, 1e-12, None, OP.max, tag="ff.kmx")
        krn = SF.v(130, 1, 0, 32)
        p.op("vector", lambda q, o=krn.ap, i=ksd.ap: q.reciprocal(o, i),
             reads=[ksd], writes=[krn], tag="ff.krc")
        emit_ts(ctx, kb, kb, krn, None, OP.mult, tag="ff.kn")
        emit_transpose(ctx, PST.v(0, 32), kb, idb, tag="ff.kT")
        emit_copy(ctx, "vector", F2.v(8960 + 32 * j, 32), PST.v(0, 32), tag="ff.kTev")
    k0T = F2.v(8960, 32)
    k1T = F2.v(8992, 32)

    for m in range(16):
        qb = QB.v(0, 2048)
        emit_dma_load(ctx, "sync", qb, View(scr.ap[m], scr.regs), tag="ff.qld")
        par = m % 2
        qbn = QB.v(0, 2048)
        emit_ts(ctx, qbn, qb, ACC.v(82 + par, 1), ACC.v(84 + par, 1), OP.mult, OP.add, tag="ff.bn")
        emit_act(ctx, F2.v(0, 2048), qbn, AF.Square, tag="ff.sq")
        for c in range(4):
            lhsT = View(ehs.ap.tensor[0:128, ehs.regs[0][1] + 16 * m:
                                      ehs.regs[0][1] + 16 * (m + 1)], ehs.regs)
            emit_mm(ctx, PSA.v(512 * c, 512, 0, 16), lhsT, F2.v(512 * c, 512),
                    m == 0, m == 15, tag="ff.l2")
        for tc in range(16):
            b = 512 * (tc % 2)
            sv = View(PSB.t[0:128, b:b + 32], [(PSB.name, b, b + 512)])
            emit_mm(ctx, sv, QB.v(128 * tc, 128), (k0T if par == 0 else k1T),
                    True, True, tag="ff.s")
            emit_copy(ctx, "vector", SCOR.v(512 * tc + 32 * m, 32), sv, tag="ff.sev")
    emit_copy(ctx, "vector", SF.v(0, 2048, 0, 16), PSA.v(0, 2048, 0, 16), tag="ff.ss")
    emit_act(ctx, SF.v(0, 2048, 0, 16), SF.v(0, 2048, 0, 16), AF.Sqrt, tag="ff.ssd")
    emit_ts(ctx, SF.v(0, 2048, 0, 16), SF.v(0, 2048, 0, 16), 1e-12, None, OP.max, tag="ff.smx")
    p.op("vector", lambda q, o=SF.v(2048, 2048, 0, 16).ap, i=SF.v(0, 2048, 0, 16).ap:
         q.reciprocal(o, i), reads=[SF.v(0, 2048, 0, 16)],
         writes=[SF.v(2048, 2048, 0, 16)], tag="ff.rn")
    for tc in range(16):
        ps = PSC.v(0, 16)
        emit_transpose(ctx, ps, SF.v(2048 + 128 * tc, 128, 0, 16), id32, tag="ff.rT")
        emit_copy(ctx, "vector", RN.v(16 * tc, 16), ps, tag="ff.rTev")

    emit_l1ffn_phaseB(ctx)


def emit_l1ffn_phaseB(ctx):
    p = ctx.p
    XN = ctx.XN
    PSA, PSB, PSC, PST = ctx.PSA, ctx.PSB, ctx.PSC, ctx.PST
    W, SCOR, F2, SF, RN = ctx.W, ctx.SCOR, ctx.F2, ctx.SF, ctx.RN
    idb = ctx.const["IDB"]

    load_weight_bf16(ctx, ctx.win["L1_ffn_expert_down"].ap,
                     ctx.win["L1_ffn_expert_down"].regs, 1024, 1024, W, 8192, tag="ff.dn")
    emit_transpose_tiles(ctx, W, 8192, 1024, 1024, W, 0, PST, tag="ff.dnT")
    load_weight_bf16(ctx, ctx.win["L1_ffn_expert_up"].ap,
                     ctx.win["L1_ffn_expert_up"].regs, 1024, 1024, W, 8192, tag="ff.up")

    def ffb_front(tc):
        t0 = 128 * tc
        o2 = 4608 * (tc % 2)
        for dc in range(2):
            ps = PSA.v(1024 * (tc % 2) + 512 * dc, 512)
            for k in range(8):
                rhs = wt_tile(W, 0, 1024, k, 128, 512 * dc, 512)
                emit_mm(ctx, ps, XN.v(2048 * k + t0, 128), rhs, k == 0, k == 7, tag="fb.eh")
            emit_act(ctx, F2.v(o2 + 2560 + 512 * dc, 512), ps, AF.Gelu, tag="fb.gelu")

    def ffb_chain(tc):
        o2 = 4608 * (tc % 2)
        s = SF.v(0, 512)
        emit_copy(ctx, "vector", s, SCOR.v(512 * tc, 512), tag="fb.s")
        wk = SF.v(512, 512)
        emit_copy(ctx, "vector", wk, SCOR.v(512 * tc, 512), tag="fb.wk")
        m1 = SF.v(1536, 16)
        for r in range(4):
            cur, curw = wk, 32
            dsts = [SF.v(1024, 256), SF.v(1280, 128), SF.v(1024, 64), SF.v(1280, 32),
                    (m1 if r == 3 else SF.v(1024, 16))]
            for step in range(5):
                half = curw // 2
                dst = dsts[step]
                emit_tt(ctx, grouped(dst), gslice(cur, 0, half), gslice(cur, half, curw),
                        OP.max, tag="fb.tm")
                cur, curw = dst, half
            if r < 3:
                mb = View(cur.ap.unsqueeze(2).to_broadcast([128, 16, 32]), cur.regs)
                eq = SF.v(1552, 512)
                emit_tt(ctx, grouped(eq), grouped(wk), mb, OP.is_ge, tag="fb.eq")
                pen = SF.v(2064, 512)
                emit_ts(ctx, pen, eq, 1e30, None, OP.mult, tag="fb.pen")
                emit_tt(ctx, wk, wk, pen, OP.subtract, tag="fb.msk")
        t4b = View(m1.ap.unsqueeze(2).to_broadcast([128, 16, 32]), m1.regs)
        ge = SF.v(1552, 512)
        emit_tt(ctx, grouped(ge), grouped(s), t4b, OP.is_ge, tag="fb.ge")
        for m in range(16):
            emit_act(ctx, SF.v(2576 + 32 * m, 32), SF.v(32 * m, 32), AF.Exp,
                     scale=RN.v(16 * tc + m, 1), tag="fb.exp")
        e = SF.v(2576, 512)
        z = SF.v(3088, 512)
        emit_tt(ctx, z, e, ge, OP.mult, tag="fb.z")
        cur, curw = z, 32
        for step, dst in enumerate([SF.v(3600, 256), SF.v(3856, 128), SF.v(3600, 64),
                                    SF.v(3856, 32), SF.v(3600, 16)]):
            half = curw // 2
            emit_tt(ctx, grouped(dst), gslice(cur, 0, half), gslice(cur, half, curw),
                    OP.add, tag="fb.zs")
            cur, curw = dst, half
        rz = SF.v(3984, 16)
        p.op("vector", lambda q, o=rz.ap, i=cur.ap: q.reciprocal(o, i),
             reads=[cur], writes=[rz], tag="fb.rz")
        pb = z
        rzb = View(rz.ap.unsqueeze(2).to_broadcast([128, 16, 32]), rz.regs)
        emit_tt(ctx, grouped(pb), grouped(z), rzb, OP.mult, tag="fb.p")
        Wb = SF.v(1024, 1024)
        tmp = SF.v(2064, 1024)
        for h in range(8):
            p0 = View(bass.AP(SF.t, SF.off + 3088 + 64 * h,
                              [[PCOLS(SF), 128], [1, 32], [0, 32]]), pb.regs)
            p1 = View(bass.AP(SF.t, SF.off + 3088 + 64 * h + 32,
                              [[PCOLS(SF), 128], [0, 32], [1, 32]]), pb.regs)
            dst = Wb if h == 0 else tmp
            emit_tt(ctx, View(dst.ap.rearrange("p (a b) -> p a b", a=32), dst.regs),
                    p0, p1, OP.mult, tag="fb.w")
            if h > 0:
                emit_tt(ctx, Wb, Wb, tmp, OP.add, tag="fb.wa")
        cb = F2.v(o2 + 3584, 1024)
        emit_tt(ctx, cb, Wb, F2.v(o2 + 2560, 1024), OP.mult, tag="fb.c")

    def ffb_tail(tc):
        o2 = 4608 * (tc % 2)
        cq = tc % 4
        for e8 in range(8):
            emit_transpose(ctx, PST.v(128 * e8, 128), F2.v(o2 + 3584 + 128 * e8, 128),
                           idb, tag="fb.cT")
        for e8 in range(8):
            emit_copy(ctx, "vector", F2.v(9216 + 512 * e8 + 128 * cq, 128),
                      PST.v(128 * e8, 128), tag="fb.cTev")
        if cq == 3:
            tg0 = 512 * (tc // 4)
            for f in range(8):
                ps = PSB.v(0, 512)
                for k in range(8):
                    lhsT = wtile(W, 8192, 1024, k, 128, 128 * f, 128)
                    emit_mm(ctx, ps, lhsT, F2.v(9216 + 512 * k, 512), k == 0, k == 7,
                            tag="fb.up")
                emit_xadd(ctx, f, tg0, 512, ps, tag="l1ffn.res")

    for tc in range(16):
        ffb_front(tc)
        if tc > 0:
            ffb_tail(tc - 1)
        ffb_chain(tc)
    ffb_tail(15)


def build_prog_l1ffn(debug=False):
    global LAST_CTX
    nc = bass.Bass("TRN2", target_bir_lowering=False)
    ctx = make_base_ctx(nc, w_cols=16384, ab_cols=0, actf_cols=0, sm_pool=False,
                        consts=["IDB", "ID32", "ONESC_B", "ONESC_F", "ONESR_F", "EHS16"])
    LAST_CTX = ctx
    p = ctx.p
    ctx.SCOR = p.sbuf("SCOR", 8192, F32)
    ctx.QB = p.sbuf("QBp", 4096, BF16)
    ctx.F2 = p.sbuf("F2p", 13312, BF16)
    ctx.SF = p.sbuf("SFp", 4096, F32)
    ctx.RN = p.sbuf("RNp", 256, F32)
    ctx.ACC = p.sbuf("ACCp", 128, F32)
    x_in = p.dram("x_in", [1024, 2048], F32, kind="ExternalInput")
    x_out = p.dram("x_out", [1024, 2048], F32, kind="ExternalOutput")
    declare_weights(ctx, [("L1_ffn_Wq", (1024, 2048), F32),
                          ("L1_ffn_sub_key0", (32, 128), F32),
                          ("L1_ffn_sub_key1", (32, 128), F32),
                          ("L1_ffn_expert_down", (1024, 1024), F32),
                          ("L1_ffn_expert_up", (1024, 1024), F32)])
    declare_colvec(ctx, "l1ffn_bq", 16)
    declare_colvec(ctx, "l1ffn_gam", 2)
    declare_colvec(ctx, "l1ffn_bet", 2)
    ctx.x_src, ctx.x_dst = x_in, x_out
    emit_ln_hbm(ctx, x_in)
    emit_l1ffn(ctx, 0)
    if debug:
        dbg1 = p.dram("dbg_acc", [128, 96], F32, kind="ExternalOutput")
        emit_dma_store(ctx, "scalar", dbg1, ctx.ACC.v(0, 96), tag="dbg")
        dbg2 = p.dram("dbg_scor", [128, 8192], F32, kind="ExternalOutput")
        emit_dma_store(ctx, "scalar", dbg2, ctx.SCOR.v(0, 8192), tag="dbg")
        dbg3 = p.dram("dbg_rn", [128, 256], F32, kind="ExternalOutput")
        emit_dma_store(ctx, "scalar", dbg3, ctx.RN.v(0, 256), tag="dbg")
        dbg4 = p.dram("dbg_xn", [128, 2048], BF16, kind="ExternalOutput")
        emit_dma_store(ctx, "scalar", dbg4, ctx.XN.v(0, 2048), tag="dbg")
    p.finalize()
    return nc


def host_l1ffn_inputs(inputs):
    out = {k: inputs[k] for k in ("L1_ffn_Wq", "L1_ffn_sub_key0", "L1_ffn_sub_key1",
                                  "L1_ffn_expert_down", "L1_ffn_expert_up")}
    out["cv_l1ffn_bq"] = np.ascontiguousarray(inputs["L1_ffn_bq"].reshape(16, 128).T)
    out["cv_l1ffn_gam"] = np.ascontiguousarray(inputs["L1_ffn_bn_gamma"].reshape(2, 128).T)
    out["cv_l1ffn_bet"] = np.ascontiguousarray(inputs["L1_ffn_bn_beta"].reshape(2, 128).T)
    return out



def emit_l1aa(ctx):
    p = ctx.p
    XN = ctx.XN
    PSA, PSB, PSC, PST = ctx.PSA, ctx.PSB, ctx.PSC, ctx.PST
    W, QF, KF1, KF2, VT, STV, STF = (ctx.W, ctx.QF, ctx.KF1, ctx.KF2, ctx.VT,
                                     ctx.STV, ctx.STF)
    UN, MTOKB, MTOKF, FM, RB = ctx.UN, ctx.MTOKB, ctx.MTOKF, ctx.FM, ctx.RB
    idb = ctx.const["IDB"]
    trilt = ctx.const["TRILT_B"]
    onesr_b = ctx.const["ONESR_B"]
    onesc_f = ctx.const["ONESC_F"]
    nw = ctx.colvecs["l1aa_nw"]

    hbm_o = p.dram(ctx.name("aa_o"), [8, 128, 2048], F32)
    hbm_st = p.dram(ctx.name("aa_st"), [8, 128, 1024], F32)
    ov_o = View(hbm_o.ap, hbm_o.regs)

    for j, nm in enumerate(("L1_attn_a_fmq1_w", "L1_attn_a_fmq2_w",
                            "L1_attn_a_fmk1_w", "L1_attn_a_fmk2_w")):
        raw = W.v(0, 128)
        emit_dma_load(ctx, "gpsimd", raw, ctx.win[nm], tag="aa.fml")
        emit_transpose(ctx, PST.v(0, 128), raw, idb, tag="aa.fmT")
        emit_copy(ctx, "vector", FM.v(128 * j, 128), PST.v(0, 128), tag="aa.fmev")
    emit_ts(ctx, FM.v(0, 128), FM.v(0, 128), float(128.0 ** -0.5), None, OP.mult, tag="aa.fms")

    raw = W.v(0, 1024, 0, 8)
    emit_dma_load(ctx, "gpsimd", raw, ctx.win["L1_attn_a_gate_w"], tag="aa.gl")
    for k in range(8):
        emit_transpose(ctx, PST.v(8 * k, 8), W.v(128 * k, 128, 0, 8), idb, tag="aa.gT")
    emit_copy(ctx, "vector", W.v(1024, 64), PST.v(0, 64), tag="aa.gev")
    for tc in range(16):
        ps = PSC.v(0, 8)
        for k in range(8):
            emit_mm(ctx, ps, XN.v(2048 * k + 128 * tc, 128), W.v(1024 + 8 * k, 8),
                    k == 0, k == 7, tag="aa.logit")
        s = UN.v(0, 8)
        emit_copy(ctx, "vector", s, ps, tag="aa.lev")
        emit_tt(ctx, UN.v(8, 4), View(s.ap[:, 0:4], s.regs), View(s.ap[:, 4:8], s.regs), OP.max, tag="aa.m")
        emit_tt(ctx, UN.v(12, 2), UN.v(8, 2), UN.v(10, 2), OP.max, tag="aa.m")
        m1 = UN.v(16, 1)
        emit_tt(ctx, m1, UN.v(12, 1), UN.v(13, 1), OP.max, tag="aa.m")
        eq1 = UN.v(24, 8)
        emit_ts(ctx, eq1, s, m1, None, OP.is_ge, tag="aa.eq1")
        pen = UN.v(32, 8)
        emit_ts(ctx, pen, eq1, 1e30, None, OP.mult, tag="aa.pen")
        s2 = UN.v(40, 8)
        emit_tt(ctx, s2, s, pen, OP.subtract, tag="aa.s2")
        emit_tt(ctx, UN.v(8, 4), View(s2.ap[:, 0:4], s2.regs), View(s2.ap[:, 4:8], s2.regs), OP.max, tag="aa.m2")
        emit_tt(ctx, UN.v(12, 2), UN.v(8, 2), UN.v(10, 2), OP.max, tag="aa.m2")
        m2 = UN.v(17, 1)
        emit_tt(ctx, m2, UN.v(12, 1), UN.v(13, 1), OP.max, tag="aa.m2")
        eq2 = UN.v(32, 8)
        emit_ts(ctx, eq2, s2, m2, None, OP.is_ge, tag="aa.eq2")
        d12 = UN.v(18, 1)
        emit_tt(ctx, d12, m1, m2, OP.subtract, tag="aa.d")
        rw1 = UN.v(19, 1)
        emit_act(ctx, rw1, d12, AF.Sigmoid, tag="aa.rw1")
        rw2 = UN.v(20, 1)
        emit_act(ctx, rw2, d12, AF.Sigmoid, scale=-1.0, tag="aa.rw2")
        t1 = UN.v(40, 8)
        emit_ts(ctx, t1, eq1, rw1, None, OP.mult, tag="aa.t1")
        t2 = UN.v(48, 8)
        emit_ts(ctx, t2, eq2, rw2, None, OP.mult, tag="aa.t2")
        emit_tt(ctx, MTOKB.v(8 * tc, 8), t1, t2, OP.add, tag="aa.rw")
        emit_tt(ctx, MTOKF.v(8 * tc, 8), eq1, eq2, OP.add, tag="aa.mk")

    for g in range(2):
        t0 = 1024 * g
        for h in range(8):
            qwr = W.v(0, 1024)
            src = View(ctx.win["L1_attn_a_q_w"].ap[128 * h:128 * (h + 1), :],
                       ctx.win["L1_attn_a_q_w"].regs)
            emit_dma_load(ctx, "gpsimd", qwr, src, tag="aa.qwl")
            for fj in range(2):
                for dt in range(8):
                    psn = PSA.v(512 * fj + 128 * (dt % 4), 128)
                    emit_mm(ctx, psn, W.v(128 * dt, 128), FM.v(128 * fj, 128),
                            dt % 4 == 0, dt % 4 == 3, tag="aa.qc")
                    if dt % 4 == 3:
                        emit_copy(ctx, "vector", W.v(1024 + 1024 * fj + 128 * (dt - 3), 512),
                                  PSA.v(512 * fj, 512), tag="aa.qcev")
            for c in range(2):
                for fj in range(2):
                    ps = PSB.v(512 * fj, 512)
                    for k in range(8):
                        emit_mm(ctx, ps, W.v(1024 + 1024 * fj + 128 * k, 128),
                                XN.v(2048 * k + t0 + 512 * c, 512), k == 0, k == 7, tag="aa.qf")
                tmp = RB.v(1280, 512)
                emit_copy(ctx, "scalar", tmp, PSB.v(0, 512), tag="aa.qp1")
                emit_tt(ctx, QF.v(1024 * h + 512 * c, 512), tmp, PSB.v(512, 512),
                        OP.mult, tag="aa.qprod")
        for e in range(8):
            for h in range(8):
                kwr = W.v(0, 1024)
                src = View(ctx.win["L1_attn_a_k_w"].ap[e, 128 * h:128 * (h + 1), :],
                           ctx.win["L1_attn_a_k_w"].regs)
                emit_dma_load(ctx, "gpsimd", kwr, src, tag="aa.kwl")
                for fj in range(2):
                    for dt in range(8):
                        psn = PSA.v(512 * fj + 128 * (dt % 4), 128)
                        emit_mm(ctx, psn, W.v(128 * dt, 128), FM.v(256 + 128 * fj, 128),
                                dt % 4 == 0, dt % 4 == 3, tag="aa.kc")
                        if dt % 4 == 3:
                            emit_copy(ctx, "vector", W.v(1024 + 1024 * fj + 128 * (dt - 3), 512),
                                      PSA.v(512 * fj, 512), tag="aa.kcev")
                for c in range(2):
                    for fj in range(2):
                        ps = PSB.v(512 * fj, 512)
                        for k in range(8):
                            emit_mm(ctx, ps, W.v(1024 + 1024 * fj + 128 * k, 128),
                                    XN.v(2048 * k + t0 + 512 * c, 512), k == 0, k == 7, tag="aa.kf")
                    tmp = RB.v(1280, 512)
                    emit_copy(ctx, "scalar", tmp, PSB.v(0, 512), tag="aa.kp1")
                    emit_tt(ctx, KF1.v(1024 * h + 512 * c, 512), tmp, PSB.v(512, 512),
                            OP.mult, tag="aa.kprod")
            for cc in range(8):
                tcg = 8 * g + cc
                psr = View(PSA.t[0:1, 128 * cc:128 * (cc + 1)],
                           [(PSA.name, 512 * (cc // 4), 512 * (cc // 4) + 512)])
                emit_mm(ctx, psr, MTOKB.v(8 * tcg + e, 1), idb,
                        cc % 4 == 0, cc % 4 == 3, tag="aa.rrow")
            rrow = RB.v(1792, 1024, 0, 1)
            emit_copy(ctx, "vector", rrow, PSA.v(0, 1024, 0, 1), tag="aa.rrev")
            for c in range(2):
                psb = PSA.v(512 * (c % 2), 512)
                emit_mm(ctx, psb, onesr_b, RB.v(1792 + 512 * c, 512, 0, 1), True, True, tag="aa.rwb")
                emit_copy(ctx, "scalar", RB.v(512 * c, 512), psb, tag="aa.rwbev")
            for half in range(2):
                for r in range(4):
                    src = View(ctx.win["L1_attn_a_v_w"].ap[e, 512 * half + 128 * r:512 * half + 128 * (r + 1), :],
                               ctx.win["L1_attn_a_v_w"].regs)
                    emit_dma_load(ctx, "gpsimd", W.v(1024 * r, 1024), src, tag="aa.vwl")
                emit_transpose_tiles(ctx, W, 0, 1024, 1024, KF2, 0, PST,
                                     tag="aa.vwT", row0=512 * half, nrows=512)
            for tj in range(8):
                for n2 in range(2):
                    ps = PSB.v(512 * n2, 512)
                    for k in range(8):
                        rhs = wt_tile(KF2, 0, 1024, k, 128, 512 * n2, 512)
                        emit_mm(ctx, ps, XN.v(2048 * k + t0 + 128 * tj, 128), rhs,
                                k == 0, k == 7, tag="aa.v")
                    emit_ts(ctx, VT.v(1024 * tj + 512 * n2, 512), ps,
                            MTOKF.v(8 * (8 * g + tj) + e, 1), None, OP.mult, tag="aa.vev")
            if g == 1:
                emit_dma_load(ctx, "sync", STF.v(0, 1024),
                              View(hbm_st.ap[e], hbm_st.regs), tag="aa.stld")
                emit_copy(ctx, "vector", STV.v(0, 1024), STF.v(0, 1024), tag="aa.stcast0")
            KT = ctx.KT
            for h in range(8):
                for c8 in range(8):
                    srcv = KF1.v(1024 * h + 128 * c8, 128)
                    dstv = KT.v(1024 * h + 128 * c8, 128)
                    q = "sync" if (h + c8) % 2 == 0 else "scalar"
                    p.op(q, lambda qq, o=dstv.ap, i=srcv.ap:
                         qq.dma_start(out=o, in_=i, transpose=True),
                         reads=[srcv], writes=[dstv], dma=True, tag="aa.ktd")
            for grp in range(2):
                for ci in range(4):
                    c = 4 * grp + ci
                    c0 = 128 * c
                    first = (g == 0 and c == 0)

                    def bundle(hp, first=first, c=c, c0=c0, ci=ci, grp=grp):
                        stv = STV.v(128 * hp, 128)
                        stf = STF.v(128 * hp, 128)
                        ov = PSA.v(1024 + 512 * (hp % 2), 128)
                        vchunk = VT.v(1024 * c + 128 * hp, 128)
                        smask = RB.v(1024 + 128 * (hp % 4), 128)
                        emit_mm(ctx, ov, vchunk, smask, True, first, tag="aa.intra")
                        if not first:
                            emit_mm(ctx, ov, stv, QF.v(1024 * hp + c0, 128),
                                    False, True, tag="aa.inter")
                        stv_ps = PSB.v(512 * (hp % 2), 128)
                        emit_mm(ctx, stv_ps, KT.v(1024 * hp + c0, 128), vchunk,
                                True, True, tag="aa.state")
                        if first:
                            emit_copy(ctx, "vector", stf, stv_ps, tag="aa.stcp")
                        else:
                            emit_tt(ctx, stf, stf, stv_ps, OP.add, tag="aa.stadd")
                        emit_copy(ctx, "scalar", stv, stf, tag="aa.stcast")
                        emit_tt(ctx, UN.v(512 * hp + 128 * ci, 128), ov,
                                RB.v(512 * grp + 128 * ci, 128), OP.mult, tag="aa.ob")

                    for h in range(8):
                        psS = PSA.v(512 * (h % 2), 128)
                        emit_mm(ctx, psS, KF1.v(1024 * h + c0, 128), QF.v(1024 * h + c0, 128),
                                True, True, tag="aa.S")
                        emit_tt(ctx, RB.v(1024 + 128 * (h % 4), 128), psS, trilt,
                                OP.mult, tag="aa.smask")
                        if h >= 2:
                            bundle(h - 2)
                    bundle(6)
                    bundle(7)
                dst = View(hbm_o.ap[:, :, t0 + 512 * grp:t0 + 512 * (grp + 1)].rearrange("h p t -> p h t"),
                           hbm_o.regs)
                srcv = View(UN.t[0:128, UN.off:UN.off + 4096].rearrange("p (h t) -> p h t", h=8),
                            [(UN.name, UN.off, UN.off + 4096)])
                if e == 0:
                    p.op("gpsimd", lambda q, o=dst.ap, i=srcv.ap: q.dma_start(out=o, in_=i),
                         reads=[srcv], writes=[dst], dma=True, tag="aa.ost0")
                else:
                    p.op("gpsimd", lambda q, o=dst.ap, i=srcv.ap:
                         q.dma_start(out=o, in_=i, accum_op=OP.add),
                         reads=[srcv], writes=[dst], dma=True, tag="aa.ost")
            if g == 0:
                emit_dma_store(ctx, "sync", View(hbm_st.ap[e], hbm_st.regs),
                               STF.v(0, 1024), tag="aa.stsp")
        for r in range(2):
            for rr_ in range(4):
                src = View(ctx.win["L1_attn_a_o_w"].ap[512 * r + 128 * rr_:512 * r + 128 * (rr_ + 1), :],
                           ctx.win["L1_attn_a_o_w"].regs)
                emit_dma_load(ctx, "gpsimd", W.v(1024 * rr_, 1024), src, tag="aa.owl")
            emit_transpose_tiles(ctx, W, 0, 1024, 1024, KF2, 0, PST,
                                 tag="aa.owT", row0=512 * r, nrows=512)
        for h in range(8):
            ot = UN.v(0, 1024)
            emit_dma_load(ctx, "sync", ot,
                          View(hbm_o.ap[h, :, t0:t0 + 1024], hbm_o.regs), tag="aa.old")
            sq = UN.v(1024, 1024)
            emit_act(ctx, sq, ot, AF.Square, tag="aa.osq")
            for c in range(2):
                ps = View(PSA.t[0:1, 512 * c:512 * (c + 1)], [(PSA.name, 512 * c, 512 * (c + 1))])
                emit_mm(ctx, ps, onesc_f, UN.v(1024 + 512 * c, 512), True, True, tag="aa.oss")
            rr2 = UN.v(2048, 1024, 0, 1)
            emit_copy(ctx, "vector", rr2, PSA.v(0, 1024, 0, 1), tag="aa.ossev")
            emit_act(ctx, rr2, rr2, AF.Ln, scale=1.0 / 128.0, bias=1e-5, tag="aa.oln")
            emit_act(ctx, rr2, rr2, AF.Exp, scale=-0.5, tag="aa.orst")
            for c in range(2):
                psb = PSA.v(1024 + 512 * (c % 2), 512)
                emit_mm(ctx, psb, ctx.const["ONESR_F"], UN.v(2048 + 512 * c, 512, 0, 1),
                        True, True, tag="aa.rbc")
                emit_stt(ctx, QF.v(1024 * h + 512 * c, 512), UN.v(512 * c, 512),
                         nw.col(0), psb, OP.mult, OP.mult, tag="aa.onrm")
        for f in range(8):
            for c in range(2):
                ps = PSB.v(0, 512)
                for k in range(8):
                    lhsT = wt_tile(KF2, 0, 1024, k, 128, 128 * f, 128)
                    emit_mm(ctx, ps, lhsT, QF.v(1024 * k + 512 * c, 512), k == 0, k == 7, tag="aa.ow")
                emit_xadd(ctx, f, t0 + 512 * c, 512, ps, tag="l1aa.res")


def build_prog_l1aa():
    global LAST_CTX
    nc = bass.Bass("TRN2", target_bir_lowering=False)
    ctx = make_base_ctx(nc, w_cols=4096, ab_cols=0, actf_cols=0, sm_pool=False,
                        consts=["IDB", "TRILT_B", "ONESR_B", "ONESC_B", "ONESC_F", "ONESR_F"])
    LAST_CTX = ctx
    p = ctx.p
    ctx.QF = p.sbuf("QFp", 8192, BF16)
    ctx.KF1 = p.sbuf("KF1p", 8192, BF16)
    ctx.KF2 = p.sbuf("KF2p", 8192, BF16)
    ctx.VT = p.sbuf("VTp", 8192, BF16)
    ctx.STV = p.sbuf("STVp", 1024, BF16)
    ctx.STF = p.sbuf("STFp", 1024, F32)
    ctx.UN = p.sbuf("UNp", 6144, F32)
    ctx.MTOKB = p.sbuf("MTOKBp", 128, BF16)
    ctx.MTOKF = p.sbuf("MTOKFp", 128, F32)
    ctx.FM = p.sbuf("FMp", 512, BF16)
    ctx.RB = p.sbuf("RBp", 2816, BF16)
    ctx.KT = p.sbuf("KTp", 8192, BF16)

    class _Shift:
        def __init__(self, pool, off):
            self.pool, self.off = pool, off
        def v(self, c0, n, p0=0, np_=None):
            return self.pool.v(self.off + c0, n, p0, np_ if np_ is not None else 128)
    ctx.ACTF = _Shift(ctx.UN, 4096)
    x_in = p.dram("x_in", [1024, 2048], F32, kind="ExternalInput")
    x_out = p.dram("x_out", [1024, 2048], F32, kind="ExternalOutput")
    declare_weights(ctx, [("L1_attn_a_gate_w", (8, 1024), F32),
                          ("L1_attn_a_q_w", (1024, 1024), F32),
                          ("L1_attn_a_k_w", (8, 1024, 1024), F32),
                          ("L1_attn_a_v_w", (8, 1024, 1024), F32),
                          ("L1_attn_a_fmq1_w", (128, 128), F32),
                          ("L1_attn_a_fmq2_w", (128, 128), F32),
                          ("L1_attn_a_fmk1_w", (128, 128), F32),
                          ("L1_attn_a_fmk2_w", (128, 128), F32),
                          ("L1_attn_a_o_w", (1024, 1024), F32)])
    declare_colvec(ctx, "l1aa_nw", 1)
    ctx.x_src, ctx.x_dst = x_in, x_out
    emit_ln_hbm(ctx, x_in)
    emit_l1aa(ctx)
    p.finalize()
    return nc


def host_l1aa_inputs(inputs):
    out = {k: inputs[k] for k in ("L1_attn_a_gate_w", "L1_attn_a_q_w",
                                  "L1_attn_a_k_w", "L1_attn_a_v_w",
                                  "L1_attn_a_fmq1_w", "L1_attn_a_fmq2_w",
                                  "L1_attn_a_fmk1_w", "L1_attn_a_fmk2_w",
                                  "L1_attn_a_o_w")}
    out["cv_l1aa_nw"] = np.ascontiguousarray(inputs["L1_attn_a_norm_w"].reshape(128, 1))
    return out



class RePack:

    def __init__(self, scr, S, nk, ng):
        self.scr, self.S, self.nk, self.ng = scr, S, nk, ng


def emit_repack(ctx, dram_view, rows, cols, col0, ncols, tag, S=4, row0=0, nrows=None,
                qs=("scalar", "gpsimd")):
    p = ctx.p
    nrows = rows if nrows is None else nrows
    nk = nrows // 128
    nm = ncols // 128
    S = min(S, nm)
    assert nm % S == 0
    ng = nm // S
    CW = 128 * S
    scr = p.dram(ctx.name("rp"), [ng, 128, nk * CW], BF16)
    for k in range(nk):
        src = View(dram_view.ap[row0 + 128 * k:row0 + 128 * (k + 1),
                                col0:col0 + ncols], dram_view.regs)
        buf = ctx.WS.v(0, ncols)
        emit_dma_load(ctx, "gpsimd", buf, src, tag=tag + ".l")
        splits = np.linspace(0, ng, 3).astype(int)
        for qi in range(2):
            g0, g1 = int(splits[qi]), int(splits[qi + 1])
            if g0 == g1:
                continue
            dst = View(scr.ap[g0:g1, :, CW * k:CW * (k + 1)].rearrange("g p c -> p g c"),
                       scr.regs)
            sv = View(ctx.WS.t[0:128, ctx.WS.off + CW * g0:ctx.WS.off + CW * g1]
                      .rearrange("p (g c) -> p g c", c=CW),
                      [(ctx.WS.name, ctx.WS.off + CW * g0, ctx.WS.off + CW * g1)])
            emit_dma_store(ctx, qs[qi], dst, sv, tag=tag + ".s")
    return RePack(scr, S, nk, ng)


def emit_wload_g(ctx, rp, g, slot_col, tag="wl", queue="sync"):
    w = rp.nk * 128 * rp.S
    wb0 = ctx.WB.off + slot_col
    dv = View(ctx.WB.t[0:128, wb0:wb0 + w], [(ctx.WB.name, wb0, wb0 + w)])
    emit_dma_load(ctx, queue, dv, View(rp.scr.ap[g], rp.scr.regs), tag=tag)
    S = rp.S

    def tile(k, mj=0, ncol=128):
        c0 = wb0 + 128 * S * k + 128 * mj
        return View(ctx.WB.t[0:128, c0:c0 + ncol], [(ctx.WB.name, c0, c0 + ncol)])
    return tile


def emit_rms_chunk(ctx, src_pool, src_base, t0, dst_pool, tag):
    p = ctx.p
    onescb = ctx.const["ONESC_B"]
    onesr = ctx.const["ONESR_F"]
    XS = ctx.XS
    psr = View(ctx.PSC.t[0:1, 0:512], [(ctx.PSC.name, 0, 512)])
    for i in range(8):
        sv = src_pool.v(src_base + 2048 * i + t0, 512)
        emit_act(ctx, ctx.LNSQ.v(0, 512), sv, AF.Square, tag=tag + ".sq")
        emit_mm(ctx, psr, onescb, ctx.LNSQ.v(0, 512), i == 0, i == 7, tag=tag + ".ss")
    row = XS.v(512, 512, 0, 1)
    emit_ts(ctx, row, psr, 1.0 / 1024.0, None, OP.mult, tag=tag + ".ms")
    emit_act(ctx, row, row, AF.Ln, bias=1.1920929e-07, tag=tag + ".ln")
    emit_act(ctx, row, row, AF.Exp, scale=-0.5, tag=tag + ".ex")
    bc = ctx.PSC.v(0, 512)
    emit_mm(ctx, bc, onesr, row, True, True, tag=tag + ".bc")
    for i in range(8):
        sv = src_pool.v(src_base + 2048 * i + t0, 512)
        emit_tt(ctx, dst_pool.v(512 * i, 512), sv, bc, OP.mult, tag=tag + ".ap")


def emit_l1ab_repacks(ctx, ws_pool, batch):
    win = ctx.win
    saved_ws = getattr(ctx, "WS", None)
    ctx.WS = ws_pool
    if not hasattr(ctx, "ab_rps"):
        ctx.ab_rps = [None] * 13
    r = ctx.ab_rps
    if batch == 0:
        q0 = ("sync", "gpsimd")
        r[0] = emit_repack(ctx, win["L1_attn_b_in_proj_w"], 1024, 4096, 2048, 2048,
                           "ab.rg", qs=q0)
        r[1] = emit_repack(ctx, win["L1_attn_b_in_proj_w"], 1024, 4096, 0, 2048,
                           "ab.rv", qs=q0)
        r[2] = emit_repack(ctx, win["L1_attn_b_B_proj_w"], 1024, 1024, 0, 1024,
                           "ab.rB", qs=q0)
        r[3] = emit_repack(ctx, win["L1_attn_b_C_proj_w"], 1024, 1024, 0, 1024,
                           "ab.rC", qs=q0)
    elif batch == 1:
        qg = getattr(ctx, "rp_store_qs", ("scalar", "gpsimd"))
        r[4] = emit_repack(ctx, win["L1_attn_b_out_proj_w"], 2048, 1024, 0, 1024,
                           "ab.rop", S=2, qs=qg)
        r[5] = emit_repack(ctx, win["L1_attn_b_m_fc_w"], 1024, 4096, 0, 4096, "ab.rmf", qs=qg)
        r[6] = [emit_repack(ctx, win["L1_attn_b_m_proj_w"], 4096, 1024, 0, 1024,
                            f"ab.rmp{q}", S=2, row0=1024 * q, nrows=1024, qs=qg)
                for q in range(4)]
    else:
        qg = getattr(ctx, "rp_store_qs", ("scalar", "gpsimd"))
        r[7] = emit_repack(ctx, win["L1_attn_b_a_fc_w"], 1024, 4096, 0, 4096, "ab.raf", qs=qg)
        r[8] = [emit_repack(ctx, win["L1_attn_b_a_proj_w"], 4096, 1024, 0, 1024,
                            f"ab.rap{q}", S=2, row0=1024 * q, nrows=1024, qs=qg)
                for q in range(4)]
        r[9] = emit_repack(ctx, win["L1_attn_b_q_w"], 1024, 1024, 0, 1024, "ab.rq", qs=qg)
        r[10] = emit_repack(ctx, win["L1_attn_b_k_w"], 1024, 256, 0, 256, "ab.rk", S=2, qs=qg)
        r[11] = emit_repack(ctx, win["L1_attn_b_v_w"], 1024, 256, 0, 256, "ab.rvw", S=2, qs=qg)
        r[12] = emit_repack(ctx, win["L1_attn_b_o_w"], 1024, 1024, 0, 1024, "ab.ro", qs=qg)
    if saved_ws is not None:
        ctx.WS = saved_ws


def emit_l1ab(ctx):
    p = ctx.p
    XN, XM = ctx.XN, ctx.XM
    PSA, PSB, PSC, PST = ctx.PSA, ctx.PSB, ctx.PSC, ctx.PST
    NB, VB, VACT, SG, BCM, XS, SCB = (ctx.NB, ctx.VB, ctx.VACT, ctx.SG,
                                      ctx.BCM, ctx.XS, ctx.SCB)
    idb = ctx.const["IDB"]
    trilt = ctx.const["TRILT_B"]
    cv = ctx.colvecs
    win = ctx.win

    lnr = p.dram(ctx.name("ab_lnr"), [2, 2048], F32)
    emit_dma_store(ctx, "scalar", View(lnr.ap[0:1, :], lnr.regs),
                   ctx.ACTF.v(0, 2048, 0, 1), tag="ab.lnra")
    emit_dma_store(ctx, "scalar", View(lnr.ap[1:2, :], lnr.regs),
                   XS.v(0, 2048, 0, 1), tag="ab.lnrb")
    sp_sg = p.dram(ctx.name("ab_sg"), [4, 16, 128, 512], BF16)
    sp_h = p.dram(ctx.name("ab_h"), [4, 32, 128, 512], BF16)
    sp_q = p.dram(ctx.name("ab_q"), [16, 128, 1024], BF16)

    if not hasattr(ctx, "ab_rps"):
        emit_l1ab_repacks(ctx, ctx.WS, 0)
        emit_l1ab_repacks(ctx, ctx.WS, 1)
        emit_l1ab_repacks(ctx, ctx.WS, 2)
    (rp_gate, rp_val, rp_B, rp_C, rp_op, rp_mfc, rp_mpr, rp_afc, rp_apr,
     rp_qw, rp_kw, rp_vw, rp_ow) = ctx.ab_rps

    rmc = XS.v(1024, 8, 0, 128)
    emit_tt(ctx, rmc, cv["ab_rm0"].cols(0, 8), cv["ab_rm1"].cols(0, 8), OP.add, tag="ab.rm")
    for i in range(8):
        emit_ts(ctx, XM.v(2048 * i, 2048), XN.v(2048 * i, 2048),
                View(rmc.ap[:, i:i + 1], rmc.regs), None, OP.mult, tag="ab.xm")

    U = lambda c, n, p0=0, np_=128: XS.v(1032 + c, n, p0, np_)
    dtb = cv["ab_dtb"].cols(0, 16)
    alog = cv["ab_alog"].cols(0, 16)
    ax = U(0, 16)
    emit_act(ctx, ax, dtb, AF.Abs, tag="ab.abs")
    emit_act(ctx, ax, ax, AF.Exp, scale=-1.0, tag="ab.e")
    emit_act(ctx, ax, ax, AF.Ln, bias=1.0, tag="ab.l")
    mx = U(16, 16)
    emit_ts(ctx, mx, dtb, 0.0, None, OP.max, tag="ab.mx")
    dt = U(32, 16)
    emit_tt(ctx, dt, mx, ax, OP.add, tag="ab.dt")
    a_ = U(48, 16)
    emit_act(ctx, a_, alog, AF.Exp, tag="ab.A0")
    emit_ts(ctx, a_, a_, -1.0, None, OP.mult, tag="ab.A")
    adt = U(64, 16)
    emit_tt(ctx, adt, a_, dt, OP.mult, tag="ab.adt")
    lndt = U(80, 16)
    emit_act(ctx, lndt, dt, AF.Ln, tag="ab.lndt")
    cdec = U(96, 16)
    emit_ts(ctx, cdec, adt, 128.0, None, OP.mult, tag="ab.cd0")
    emit_act(ctx, cdec, cdec, AF.Exp, tag="ab.cdec")
    for h in range(16):
        lm = ctx.LMT.v(128 * h, 128)
        emit_act(ctx, lm, ctx.const["DIFFT"], AF.Exp,
                 scale=U(64 + h, 1), bias=U(80 + h, 1), tag="ab.lmt")
        emit_tt(ctx, lm, lm, trilt, OP.mult, tag="ab.lmm")
        dr = ctx.DRT.v(128 * h, 128)
        emit_act(ctx, dr, ctx.const["REVT"], AF.Exp,
                 scale=U(64 + h, 1), bias=U(80 + h, 1), tag="ab.drt")
        do = ctx.DOT.v(128 * h, 128)
        emit_act(ctx, do, ctx.const["DECT"], AF.Exp, scale=U(64 + h, 1), tag="ab.dot")
        emit_act(ctx, XS.v(1144 + h, 1), ctx.const["REVC"], AF.Exp,
                 scale=U(64 + h, 1), bias=U(80 + h, 1), tag="ab.drtc")

    for tc in range(4):
        t0 = 512 * tc
        emit_rms_chunk(ctx, XM, 0, t0, NB, "ab.n1")
        for g4 in range(4):
            wt = emit_wload_g(ctx, rp_gate, g4, 4096 * (g4 % 2), tag="ab.wg")
            for mj in range(4):
                m = 4 * g4 + mj
                ps = PSA.v(512 * (m % 2), 512)
                for k in range(8):
                    emit_mm(ctx, ps, wt(k, mj), NB.v(512 * k, 512), k == 0, k == 7, tag="ab.g")
                emit_act(ctx, SG.v(512 * (m % 4), 512), ps, AF.Silu, tag="ab.sgev")
                if m % 4 != 3:
                    continue
                dst = View(sp_sg.ap[tc, m - 3:m + 1].rearrange("m p c -> p m c"), sp_sg.regs)
                sv = View(SG.t[0:128, SG.off:SG.off + 2048].rearrange("p (m c) -> p m c", c=512),
                          [(SG.name, SG.off, SG.off + 2048)])
                emit_dma_store(ctx, "scalar", dst, sv, tag="ab.sgsp")
    for i in range(16):
        p.op("gpsimd", lambda q, o=VB.v(515 * i, 3).ap: q.memset(o, 0.0),
             writes=[VB.v(515 * i, 3)], tag="ab.cz")
    def value_group(tc, g4, mj, box={}):
        if mj == 0:
            box[g4] = emit_wload_g(ctx, rp_val, g4, 4096 * (g4 % 2), tag="ab.wv")
        wt = box[g4]
        m = 4 * g4 + mj
        ps = ctx.PSC.v(0, 512)
        for k in range(8):
            emit_mm(ctx, ps, wt(k, mj), NB.v(512 * k, 512), k == 0, k == 7, tag="ab.v")
        emit_copy(ctx, "scalar", VB.v(515 * m + 3, 512), ps, tag="ab.vev")

    emit_rms_chunk(ctx, XM, 0, 0, NB, "ab.n1b")
    for g4 in range(4):
        for mj in range(4):
            value_group(0, g4, mj)
    for tc in range(4):
        t0 = 512 * tc
        for g in range(2):
            wtB = emit_wload_g(ctx, rp_B, g, 0, tag="ab.wB")
            wtC = emit_wload_g(ctx, rp_C, g, 4096, tag="ab.wC", queue="scalar")
            for mj in range(4):
                m = 4 * g + mj
                psb_ = PSA.v(512 * (m % 2), 512)
                psc_ = PSB.v(0, 512)
                for k in range(8):
                    xmk = XM.v(2048 * k + t0, 512)
                    emit_mm(ctx, psb_, wtB(k, mj), xmk, k == 0, k == 7, tag="ab.bm")
                    emit_mm(ctx, psc_, wtC(k, mj), xmk, k == 0, k == 7, tag="ab.cm")
                emit_copy(ctx, "vector", BCM.v(1024 * m, 512), psb_, tag="ab.bmev")
                emit_copy(ctx, "vector", BCM.v(1024 * m + 512, 512), psc_, tag="ab.cmev")
        for m in range(16):
            acc = XS.v(0, 512)
            emit_ts(ctx, acc, VB.v(515 * m + 0, 512), cv["ab_conv"].col(4 * m + 0),
                    None, OP.mult, tag="ab.c0")
            for j in range(1, 4):
                emit_stt(ctx, acc, VB.v(515 * m + j, 512), cv["ab_conv"].col(4 * m + j),
                         acc, OP.mult, OP.add, tag=f"ab.c{j}")
            emit_act(ctx, VACT.v(512 * m, 512), acc, AF.Silu,
                     bias=cv["ab_convb"].col(m), tag="ab.silu")
            emit_copy(ctx, "vector", VB.v(515 * m, 3), VB.v(515 * m + 512, 3), tag="ab.carry")
        WS = ctx.WS
        for m in range(8):
            for sc in range(4):
                srcv = BCM.v(1024 * m + 128 * sc, 128)
                dstv = WS.v(512 * m + 128 * sc, 128)
                q = "sync" if (m + sc) % 2 == 0 else "scalar"
                p.op(q, lambda qq, o=dstv.ap, i=srcv.ap:
                     qq.dma_start(out=o, in_=i, transpose=True),
                     reads=[srcv], writes=[dstv], dma=True, tag="ab.bmT")
        fillers = []
        if tc + 1 < 4:
            emit_rms_chunk(ctx, XM, 0, 512 * (tc + 1), NB, "ab.n1b")
            for g4 in range(4):
                for mj in range(4):
                    fillers.append((tc + 1, g4, mj))
        for sc in range(4):
            c0 = 128 * sc
            gchunk = 4 * tc + sc

            def front(h, sc=sc, c0=c0, gchunk=gchunk, tc=tc):
                hb = 64 * (h % 2)
                sgc = SG.v(128 * (h % 4), 128)
                emit_dma_load(ctx, "scalar", sgc,
                              View(sp_sg.ap[tc, h][:, c0:c0 + 128], sp_sg.regs),
                              tag="ab.sgld")
                emit_transpose(ctx, PST.v(128 * (h % 4), 128),
                               VACT.v(512 * h + c0, 128), idb, tag="ab.vT")
                vtok = SCB.v(128 * (h % 4), 128)
                emit_copy(ctx, "scalar", vtok, PST.v(128 * (h % 4), 128), tag="ab.vTe")
                bmv = BCM.v(1024 * (h // 2) + c0, 128, hb, 64)
                cmv = BCM.v(1024 * (h // 2) + 512 + c0, 128, hb, 64)
                psCB = PSA.v(512 * (h % 2), 128)
                emit_mm(ctx, psCB, bmv, cmv, True, True, tag="ab.cb")
                emit_tt(ctx, ctx.ATTN.v(512 + 128 * (h % 2), 128), psCB,
                        ctx.LMT.v(128 * h, 128), OP.mult, tag="ab.cbl")
                emit_ts(ctx, SCB.v(512 + 64 * (h % 2), 64),
                        WS.v(512 * (h // 2) + 128 * sc + hb, 64),
                        XS.v(1144 + h, 1), None, OP.mult, tag="ab.bdrT")
                if gchunk > 0:
                    emit_tt(ctx, ctx.ATTN.v(256 + 128 * (h % 2), 128, hb, 64), cmv,
                            ctx.DOT.v(128 * h, 128, hb, 64), OP.mult, tag="ab.cmd")

            def bundle(h, sc=sc, c0=c0, gchunk=gchunk):
                hb = 64 * (h % 2)
                vtok = SCB.v(128 * (h % 4), 128)
                cbl = ctx.ATTN.v(512 + 128 * (h % 2), 128)
                bdrT = SCB.v(512 + 64 * (h % 2), 64)
                psY = PSB.v(512 * (h % 2), 128)
                emit_mm(ctx, psY, vtok, cbl, True, gchunk == 0, tag="ab.yd")
                if gchunk > 0:
                    cmd = ctx.ATTN.v(256 + 128 * (h % 2), 128, hb, 64)
                    emit_mm(ctx, psY, ctx.HC.v(128 * h, 128, hb, 64), cmd,
                            False, True, tag="ab.yo")
                psSt = PSA.v(1024 + 512 * (h % 2), 128, 0, 64)
                emit_mm(ctx, psSt, bdrT, vtok, True, True, tag="ab.st")
                hcf = ctx.HCF.v(128 * h, 128, hb, 64)
                if gchunk == 0:
                    emit_copy(ctx, "vector", hcf, psSt, tag="ab.hc0")
                else:
                    emit_ts(ctx, hcf, hcf, U(96 + h, 1, 0, 64), None, OP.mult, tag="ab.hcd")
                    emit_tt(ctx, hcf, hcf, psSt, OP.add, tag="ab.hca")
                emit_copy(ctx, "scalar", ctx.HC.v(128 * h, 128, hb, 64), hcf, tag="ab.hccast")
                yb = ctx.ATTN.v(128 * (h % 2), 128)
                emit_stt(ctx, yb, VACT.v(512 * h + c0, 128), cv["ab_D"].col(h), psY,
                         OP.mult, OP.add, tag="ab.ybd")
                emit_tt(ctx, VACT.v(512 * h + c0, 128), yb, SG.v(128 * (h % 4), 128),
                        OP.mult, tag="ab.yb")

            for h in range(16):
                front(h)
                if h >= 2:
                    bundle(h - 2)
                if h % 4 == 1 and fillers:
                    value_group(*fillers.pop(0))
            bundle(14)
            bundle(15)
        for g in range(4):
            wt = emit_wload_g(ctx, rp_op, g, 4096 * (g % 2), tag="ab.wop")
            for fj in range(2):
                f = 2 * g + fj
                ps = PSB.v(0, 512)
                for k in range(16):
                    emit_mm(ctx, ps, wt(k, fj), VACT.v(512 * k, 512),
                            k == 0, k == 15, tag="ab.op")
                xmv = XM.v(2048 * f + t0, 512)
                emit_stt(ctx, xmv, ps, cv["ab_msc"].col(f), xmv, OP.mult, OP.add, tag="ab.opres")
    emit_l1ab_mlp(ctx, rp_mfc, rp_mpr, sp_h, "ab_m2sc", "ab.m2")
    rarow = ctx.ACTF.v(0, 2048, 0, 1)
    rbrow = ctx.HCF.v(0, 2048, 0, 1)
    emit_dma_load(ctx, "scalar", rarow, View(lnr.ap[0:1, :], lnr.regs), tag="ab.lra")
    emit_dma_load(ctx, "scalar", rbrow, View(lnr.ap[1:2, :], lnr.regs), tag="ab.lrb")
    onesr_f = ctx.const["ONESR_F"]
    for c in range(4):
        pra = ctx.PSA.v(0, 512)
        prb = ctx.PSA.v(512, 512)
        emit_mm(ctx, pra, onesr_f, ctx.ACTF.v(512 * c, 512, 0, 1), True, True, tag="ab.bca")
        emit_mm(ctx, prb, onesr_f, ctx.HCF.v(512 * c, 512, 0, 1), True, True, tag="ab.bcb")
        for i in range(8):
            xc = XS.v(512, 512)
            emit_dma_load(ctx, "scalar", xc, xview(ctx.x_src, i, 512 * c, 512), tag="ab.x0l")
            t0v = XS.v(0, 512)
            emit_tt(ctx, t0v, xc, pra, OP.mult, tag="ab.x0m")
            emit_tt(ctx, t0v, t0v, prb, OP.add, tag="ab.x0a")
            t1 = XS.v(1024, 512)
            emit_ts(ctx, t1, t0v, cv["ab_ra1"].col(i), None, OP.mult, tag="ab.xa1")
            emit_stt(ctx, XM.v(2048 * i + 512 * c, 512), XM.v(2048 * i + 512 * c, 512),
                     cv["ab_ra0"].col(i), t1, OP.mult, OP.add, tag="ab.xa")
    emit_l1ab_qkv(ctx, rp_qw, rp_kw, rp_vw, sp_q)
    emit_l1ab_attn(ctx, rp_ow, sp_q)
    emit_l1ab_mlp(ctx, rp_afc, rp_apr, sp_h, "ab_a2sc", "ab.a2")
    for i in range(8):
        for c in range(4):
            emit_xadd(ctx, i, 512 * c, 512, XM.v(2048 * i + 512 * c, 512), tag="l1ab.res")


def emit_l1ab_mlp(ctx, rp_fc, rp_pr, sp_h, sc_name, tag):
    p = ctx.p
    XM, NB, VACT, XS = ctx.XM, ctx.NB, ctx.VACT, ctx.XS
    PSA, PSB = ctx.PSA, ctx.PSB
    cv = ctx.colvecs
    for tc in range(4):
        t0 = 512 * tc
        emit_rms_chunk(ctx, XM, 0, t0, NB, tag + ".n")
        pool2 = [ctx.VACT, ctx.BCM]
        for mg in range(2):
            for g4 in range(4):
                wt = emit_wload_g(ctx, rp_fc, 4 * mg + g4, 4096 * (g4 % 2), tag=tag + ".wf")
                for mj in range(4):
                    m = 4 * g4 + mj
                    ps = PSA.v(512 * (m % 2), 512)
                    for k in range(8):
                        emit_mm(ctx, ps, wt(k, mj), NB.v(512 * k, 512), k == 0, k == 7,
                                tag=tag + ".fc")
                    rh = XS.v(512 * (m % 2), 512)
                    emit_act(ctx, rh, ps, AF.Relu, tag=tag + ".relu")
                    emit_tt(ctx, pool2[mg].v(512 * m, 512), rh, rh, OP.mult, tag=tag + ".sq")
        hv = [pool2[k // 16].v(512 * (k % 16), 512) for k in range(32)]
        for fh in range(4):
            wtq = [emit_wload_g(ctx, rp_pr[q], fh, 2048 * q,
                                tag=tag + ".wp", queue="sync" if q % 2 == 0 else "scalar")
                   for q in range(4)]
            for fj in range(2):
                f = 2 * fh + fj
                ps = PSB.v(512 * fj, 512)
                for q in range(4):
                    for kq in range(8):
                        k = 8 * q + kq
                        emit_mm(ctx, ps, wtq[q](kq, fj), hv[k],
                                k == 0, k == 31, tag=tag + ".pr")
                xmv = XM.v(2048 * f + t0, 512)
                emit_stt(ctx, xmv, ps, cv[sc_name].col(f), xmv, OP.mult, OP.add,
                         tag=tag + ".res")


def emit_l1ab_qkv(ctx, rp_qw, rp_kw, rp_vw, sp_q):
    p = ctx.p
    XM, NB, XS = ctx.XM, ctx.NB, ctx.XS
    PSA, PSB, PSC = ctx.PSA, ctx.PSB, ctx.PSC
    B = ctx.BIGB
    cv = ctx.colvecs
    sp_k = p.dram(ctx.name("ab_kt"), [16, 128, 256], BF16)
    for g in range(2):
        src = View(rp_qw.scr.ap[g], rp_qw.scr.regs)
        dv = View(ctx.WB.t[0:128, ctx.WB.off + 4096 * g:ctx.WB.off + 4096 * (g + 1)],
                  [(ctx.WB.name, ctx.WB.off + 4096 * g, ctx.WB.off + 4096 * (g + 1))])
        emit_dma_load(ctx, "sync", dv, src, tag="ab.wq")
    for nm, rp, off in (("k", rp_kw, 28736), ("v", rp_vw, 30784)):
        src = View(rp.scr.ap[0], rp.scr.regs)
        dvv = View(B.t[0:128, B.off + off:B.off + off + 2048],
                   [(B.name, B.off + off, B.off + off + 2048)])
        emit_dma_load(ctx, "scalar", dvv, src, tag="ab.w" + nm)
    ones_dst = View(B.t[0:128, B.off + 24576:B.off + 24576 + 4160]
                    .rearrange("p (j h o) -> p j h o", h=4, o=65)[:, :, :, 64:65],
                    [(B.name, B.off + 24576, B.off + 28736)])
    ones_src = View(ctx.const["ONESC_B"].ap.unsqueeze(1).to_broadcast([128, 16, 4, 1]),
                    ctx.const["ONESC_B"].regs)
    p.op("vector", lambda q, o=ones_dst.ap, i=ones_src.ap: q.tensor_copy(o, i),
         reads=[ones_src], writes=[ones_dst], tag="ab.vones")

    for tc in range(4):
        t0 = 512 * tc
        emit_rms_chunk(ctx, XM, 0, t0, NB, "ab.n3")
        def q_front(tt):
            pp = tt % 2
            for mg in range(2):
                ps = PSA.v(1024 * pp + 512 * mg, 512)
                for k in range(8):
                    rq0 = ctx.WB.off + 4096 * mg + 512 * k
                    rhs = View(ctx.WB.t[0:128, rq0:rq0 + 512],
                               [(ctx.WB.name, rq0, rq0 + 512)])
                    emit_mm(ctx, ps, NB.v(512 * k + 128 * tt, 128), rhs, k == 0, k == 7, tag="ab.q")
            qt = B.v(4096 * pp, 1024)
            emit_copy(ctx, "vector", qt, PSA.v(1024 * pp, 1024), tag="ab.qev")

        def q_tail(tt, tc=tc):
            pp = tt % 2
            gt = 4 * tc + tt
            qb0 = 4096 * pp
            qt = B.v(qb0, 1024)
            sq = XS.v(0, 1024)
            emit_act(ctx, sq, qt, AF.Square, tag="ab.qsq")
            w = 64
            cur = sq
            tgt = [XS.v(1024, 512), XS.v(1536, 256), XS.v(1024, 128),
                   XS.v(1536, 64), XS.v(1024, 32), XS.v(1536, 16)]
            for stp in range(6):
                half = w // 2
                g = View(cur.ap.rearrange("p (g k) -> p g k", g=16), cur.regs)
                dst = tgt[stp]
                emit_tt(ctx, View(dst.ap.rearrange("p (g k) -> p g k", g=16), dst.regs),
                        View(g.ap[:, :, 0:half], cur.regs), View(g.ap[:, :, half:w], cur.regs),
                        OP.add, tag="ab.qss")
                cur = dst
                w = half
            rst = XS.v(1536, 16)
            emit_ts(ctx, rst, cur, 1.0 / 64.0, None, OP.mult, tag="ab.qms")
            emit_act(ctx, rst, rst, AF.Ln, bias=1.1920929e-07, tag="ab.qln")
            emit_act(ctx, rst, rst, AF.Exp, scale=-0.5, tag="ab.qex")
            rb = View(rst.ap.unsqueeze(2).to_broadcast([128, 16, 64]), rst.regs)
            emit_tt(ctx, View(qt.ap.rearrange("p (g k) -> p g k", g=16), qt.regs),
                    View(qt.ap.rearrange("p (g k) -> p g k", g=16), qt.regs), rb,
                    OP.mult, tag="ab.qrn")
            qr = B.v(qb0 + 1024, 1024)
            cosv = View(ctx.const["COSTOK"].ap[:, 32 * gt:32 * (gt + 1)]
                        .unsqueeze(1).unsqueeze(1).to_broadcast([128, 2, 4, 32]),
                        ctx.const["COSTOK"].regs)
            sinv = View(ctx.const["SINTOK"].ap[:, 32 * gt:32 * (gt + 1)]
                        .unsqueeze(1).unsqueeze(1).to_broadcast([128, 2, 4, 32]),
                        ctx.const["SINTOK"].regs)
            for g2 in range(2):
                def xin(off, g2=g2):
                    ap = bass.AP(B.t, B.off + qb0 + 512 * g2 + off,
                                 [[PCOLS(B), 128], [256, 2], [64, 4], [1, 32]])
                    return View(ap, [(B.name, B.off + qb0, B.off + qb0 + 1024)])
                def xout(off, g2=g2):
                    ap = bass.AP(B.t, B.off + qb0 + 1024 + 512 * g2 + off,
                                 [[PCOLS(B), 128], [64, 2], [128, 4], [1, 32]])
                    return View(ap, [(B.name, B.off + qb0 + 1024, B.off + qb0 + 2048)])
                t1n = View(bass.AP(XS.t, XS.off + 1024, [[PCOLS(XS), 128], [128, 2], [32, 4], [1, 32]]),
                           XS.v(1024, 512).regs)
                t2n = View(bass.AP(XS.t, XS.off + 1536, [[PCOLS(XS), 128], [128, 2], [32, 4], [1, 32]]),
                           XS.v(1536, 512).regs)
                emit_tt(ctx, t1n, xin(0), cosv, OP.mult, tag="ab.r1a")
                emit_tt(ctx, t2n, xin(32), sinv, OP.mult, tag="ab.r1b")
                emit_tt(ctx, xout(0), t1n, t2n, OP.add, tag="ab.r1")
                emit_tt(ctx, t1n, xin(32), cosv, OP.mult, tag="ab.r2a")
                emit_tt(ctx, t2n, xin(0), sinv, OP.mult, tag="ab.r2b")
                emit_tt(ctx, xout(32), t1n, t2n, OP.subtract, tag="ab.r2")
            emit_dma_store(ctx, "sync", View(sp_q.ap[gt], sp_q.regs), qr, tag="ab.qsp")

        def kv_part(tt, tc=tc):
            pp = tt % 2
            gt = 4 * tc + tt
            ktb = 4096 * pp + 2048
            for nm, off, spk in (("k", 28736, True), ("v", 30784, False)):
                ps = PSC.v(0, 256)
                for k in range(8):
                    rk0 = B.off + off + 256 * k
                    rhs = View(B.t[0:128, rk0:rk0 + 256],
                               [(B.name, rk0, rk0 + 256)])
                    emit_mm(ctx, ps, NB.v(512 * k + 128 * tt, 128), rhs, k == 0, k == 7,
                            tag="ab." + nm)
                if spk:
                    kt = B.v(ktb, 256)
                    emit_copy(ctx, "vector", kt, ps, tag="ab.kev")
                    sqk = XS.v(0, 256)
                    emit_act(ctx, sqk, kt, AF.Square, tag="ab.ksq")
                    wk = 64
                    curk = sqk
                    tgtk = [XS.v(1024, 128), XS.v(1536, 64), XS.v(1024, 32),
                            XS.v(1536, 16), XS.v(1024, 8), XS.v(1536, 4)]
                    for stp in range(6):
                        half = wk // 2
                        g = View(curk.ap.rearrange("p (g k) -> p g k", g=4), curk.regs)
                        dst = tgtk[stp]
                        emit_tt(ctx, View(dst.ap.rearrange("p (g k) -> p g k", g=4), dst.regs),
                                View(g.ap[:, :, 0:half], curk.regs),
                                View(g.ap[:, :, half:wk], curk.regs), OP.add, tag="ab.kss")
                        curk = dst
                        wk = half
                    rstk = XS.v(1536, 4)
                    emit_ts(ctx, rstk, curk, 1.0 / 64.0, None, OP.mult, tag="ab.kms")
                    emit_act(ctx, rstk, rstk, AF.Ln, bias=1.1920929e-07, tag="ab.kln")
                    emit_act(ctx, rstk, rstk, AF.Exp, scale=-0.5, tag="ab.kex")
                    rbk = View(rstk.ap.unsqueeze(2).to_broadcast([128, 4, 64]), rstk.regs)
                    emit_tt(ctx, View(kt.ap.rearrange("p (g k) -> p g k", g=4), kt.regs),
                            View(kt.ap.rearrange("p (g k) -> p g k", g=4), kt.regs), rbk,
                            OP.mult, tag="ab.krn")
                    kr = B.v(ktb + 256, 256)
                    cosk = View(ctx.const["COSTOK"].ap[:, 32 * gt:32 * (gt + 1)]
                                .unsqueeze(1).to_broadcast([128, 4, 32]), ctx.const["COSTOK"].regs)
                    sink = View(ctx.const["SINTOK"].ap[:, 32 * gt:32 * (gt + 1)]
                                .unsqueeze(1).to_broadcast([128, 4, 32]), ctx.const["SINTOK"].regs)
                    def kin(off2):
                        ap = bass.AP(B.t, B.off + ktb + off2, [[PCOLS(B), 128], [64, 4], [1, 32]])
                        return View(ap, [(B.name, B.off + ktb, B.off + ktb + 256)])
                    def kout(off2):
                        ap = bass.AP(B.t, B.off + ktb + 256 + off2, [[PCOLS(B), 128], [64, 4], [1, 32]])
                        return View(ap, [(B.name, B.off + ktb + 256, B.off + ktb + 512)])
                    tk1 = View(bass.AP(XS.t, XS.off + 1024, [[PCOLS(XS), 128], [32, 4], [1, 32]]),
                               XS.v(1024, 128).regs)
                    tk2 = View(bass.AP(XS.t, XS.off + 1536, [[PCOLS(XS), 128], [32, 4], [1, 32]]),
                               XS.v(1536, 128).regs)
                    emit_tt(ctx, tk1, kin(0), cosk, OP.mult, tag="ab.kr1a")
                    emit_tt(ctx, tk2, kin(32), sink, OP.mult, tag="ab.kr1b")
                    emit_tt(ctx, kout(0), tk1, tk2, OP.add, tag="ab.kr1")
                    emit_tt(ctx, tk1, kin(32), cosk, OP.mult, tag="ab.kr2a")
                    emit_tt(ctx, tk2, kin(0), sink, OP.mult, tag="ab.kr2b")
                    emit_tt(ctx, kout(32), tk1, tk2, OP.subtract, tag="ab.kr2")
                    emit_dma_store(ctx, "scalar", View(sp_k.ap[gt], sp_k.regs), kr, tag="ab.ksp")
                else:
                    dst = View(B.t[0:128, B.off + 24576 + 260 * gt:B.off + 24576 + 260 * gt + 260]
                               .rearrange("p (h o) -> p h o", h=4)[:, :, 0:64],
                               [(B.name, B.off + 24576 + 260 * gt, B.off + 24576 + 260 * (gt + 1))])
                    srcps = View(PSC.t[0:128, 0:256].rearrange("p (h o) -> p h o", h=4),
                                 [(PSC.name, 0, 512)])
                    p.op("vector", lambda q, o=dst.ap, i=srcps.ap: q.tensor_copy(o, i),
                         reads=[View(srcps.ap, [(PSC.name, 0, 512)])], writes=[dst], tag="ab.vev2")

        for tt in range(4):
            q_front(tt)
            if tt > 0:
                q_tail(tt - 1)
                kv_part(tt - 1)
        q_tail(3)
        kv_part(3)
    for j in range(8):
        src = View(sp_q.ap.rearrange("a p c -> (a p) c")[:, 128 * j:128 * (j + 1)], sp_q.regs)
        dst = B.v(4096 + 2048 * j, 2048)
        p.op("sync", lambda q, o=dst.ap, i=src.ap: q.dma_start(out=o, in_=i, transpose=True),
             reads=[src], writes=[dst], dma=True, tag="ab.qT")
    for j in range(2):
        src = View(sp_k.ap.rearrange("a p c -> (a p) c")[:, 128 * j:128 * (j + 1)], sp_k.regs)
        dst = B.v(20480 + 2048 * j, 2048)
        p.op("sync", lambda q, o=dst.ap, i=src.ap: q.dma_start(out=o, in_=i, transpose=True),
             reads=[src], writes=[dst], dma=True, tag="ab.kT")


def emit_l1ab_attn(ctx, rp_ow, sp_q):
    p = ctx.p
    XM, XS = ctx.XM, ctx.XS
    PSA, PSB = ctx.PSA, ctx.PSB
    B = ctx.BIGB
    cv = ctx.colvecs
    cmask = ctx.const["CMASK_B"]
    onesr_f = ctx.const["ONESR_F"]
    for g in range(2):
        src = View(rp_ow.scr.ap[g], rp_ow.scr.regs)
        dv = View(ctx.WB.t[0:128, ctx.WB.off + 4096 * g:ctx.WB.off + 4096 * (g + 1)],
                  [(ctx.WB.name, ctx.WB.off + 4096 * g, ctx.WB.off + 4096 * (g + 1))])
        emit_dma_load(ctx, "sync", dv, src, tag="ab.wo")
    gsc = XS.v(1312, 16)
    emit_ts(ctx, gsc, cv["ab_qg"].cols(0, 16), 0.125, None, OP.mult, tag="ab.gsc")
    gbi = XS.v(1328, 16)
    emit_act(ctx, gbi, cv["ab_qg"].cols(0, 16), AF.Abs, tag="ab.gb0")
    emit_ts(ctx, gbi, gbi, -8.0, None, OP.mult, tag="ab.gbi")

    PI = [2 * (h % 4) + 8 * ((h // 4) // 2) + ((h // 4) % 2) for h in range(16)]
    for c in range(4):
        for hpi in range(8):
            hpair = (2 * hpi, 2 * hpi + 1)
            nj = 4 * c + 4

            def sc_front(j, i, c=c):
                h = hpair[i]
                hk = h // 4
                s = PI[h]
                qv = B.v(4096 + 2048 * (s // 2) + 512 * c, 512, 64 * (s % 2), 64)
                kv = B.v(20480 + 2048 * (hk // 2) + 128 * j, 128, 64 * (hk % 2), 64)
                psS = PSA.v(512 * (2 * i + j % 2), 512)
                emit_mm(ctx, psS, kv, qv, True, True, tag="ab.s")
                atb = (ctx.ATTN.v(512 * (j % 2), 512) if i == 0
                       else B.v(28736 + 512 * (j % 2), 512))
                emit_act(ctx, atb, psS, AF.Exp,
                         scale=View(gsc.ap[:, h:h + 1], gsc.regs),
                         bias=View(gbi.ap[:, h:h + 1], gbi.regs), tag="ab.exp")
                if j >= 4 * c:
                    emit_tt(ctx, atb, atb,
                            View(cmask.ap.tensor[0:128, cmask.regs[0][1] + 512 * (j - 4 * c):
                                 cmask.regs[0][1] + 512 * (j - 4 * c + 1)], cmask.regs),
                            OP.mult, tag="ab.msk")

            def sc_av(j, i, c=c, nj=nj):
                h = hpair[i]
                hk = h // 4
                atb = (ctx.ATTN.v(512 * (j % 2), 512) if i == 0
                       else B.v(28736 + 512 * (j % 2), 512))
                vav = View(B.t[0:128, B.off + 24576 + 260 * j + 65 * hk:
                                 B.off + 24576 + 260 * j + 65 * (hk + 1)],
                           [(B.name, B.off + 24576 + 260 * j, B.off + 24576 + 260 * (j + 1))])
                emit_mm(ctx, PSB.v(512 * i, 512, 0, 65), vav, atb,
                        j == 0, j == nj - 1, tag="ab.av")

            for j in range(nj):
                sc_front(j, 0)
                sc_front(j, 1)
                if j > 0:
                    sc_av(j - 1, 0)
                    sc_av(j - 1, 1)
            sc_av(nj - 1, 0)
            sc_av(nj - 1, 1)
            for i in range(2):
                h = hpair[i]
                rl = XS.v(512 * i, 512, 0, 1)
                p.op("vector", lambda q, o=rl.ap, i_=PSB.t[64:65, 512 * i:512 * i + 512]:
                     q.reciprocal(o, i_),
                     reads=[PSB.v(512 * i, 512, 64, 1)], writes=[rl], tag="ab.rl")
                psb2 = (ctx.PSC.v(0, 512, 0, 64) if i == 0
                        else PSA.v(1024, 512, 0, 64))
                emit_mm(ctx, psb2, View(onesr_f.ap[0:1, 0:64], onesr_f.regs), rl,
                        True, True, tag="ab.bc2")
                rlb = (ctx.ATTN.v(1024, 512, 0, 64) if i == 0
                       else B.v(29760, 512, 0, 64))
                emit_copy(ctx, "scalar", rlb, psb2, tag="ab.rlb")
                yt = B.v(0 + 512 * (h // 2), 512, 64 * (h % 2), 64)
                emit_tt(ctx, yt, PSB.v(512 * i, 512, 0, 64), rlb, OP.mult, tag="ab.y")
        for f in range(8):
            ps = PSA.v(1024 + 512 * (f % 2), 512)
            for k in range(8):
                lo0 = ctx.WB.off + 4096 * (f // 4) + 512 * k + 128 * (f % 4)
                lhsT = View(ctx.WB.t[0:128, lo0:lo0 + 128],
                            [(ctx.WB.name, lo0, lo0 + 128)])
                emit_mm(ctx, ps, lhsT, B.v(512 * k, 512), k == 0, k == 7, tag="ab.o")
            xmv = XM.v(2048 * f + 512 * c, 512)
            emit_stt(ctx, xmv, ps, cv["ab_asc"].col(f), xmv, OP.mult, OP.add, tag="ab.ores")


def build_prog_l1ab():
    global LAST_CTX
    nc = bass.Bass("TRN2", target_bir_lowering=False)
    ctx = make_base_ctx(nc, w_cols=64, ab_cols=0, actf_cols=2048, sm_pool=False,
                        consts=["IDB", "TRILT_B", "CMASK_B", "ONESC_B", "ONESC_F",
                                "ONESR_F", "ONESR_B", "DIFFT", "REVT", "DECT",
                                "REVC", "COSTOK", "SINTOK"], no_x=True)
    LAST_CTX = ctx
    p = ctx.p
    ctx.XM = SubPool(ctx.XN, 0, 16384)
    B = p.sbuf("BIGBp", 34864, BF16)
    ctx.BIGB = B
    ctx.VB = SubPool(B, 0, 8240)
    ctx.VACT = SubPool(B, 8240, 8192)
    ctx.BCM = SubPool(B, 16432, 8192)
    ctx.SG = SubPool(B, 24624, 2048)
    ctx.LMT = SubPool(B, 26672, 2048)
    ctx.DRT = SubPool(B, 28720, 2048)
    ctx.DOT = SubPool(B, 30768, 2048)
    ctx.HC = SubPool(B, 32816, 2048)
    ctx.WB = p.sbuf("WBp", 8192, BF16)
    ctx.WS = p.sbuf("WSp", 4096, BF16)
    ctx.NB = p.sbuf("NBp", 4096, BF16)
    ctx.SCB = p.sbuf("SCBp", 768, BF16)
    ctx.ATTN = p.sbuf("ATTNp", 1536, BF16)
    x_in = p.dram("x_in", [1024, 2048], F32, kind="ExternalInput")
    x_out = p.dram("x_out", [1024, 2048], F32, kind="ExternalOutput")
    ctx.x_in, ctx.x_out = x_in, x_out
    ctx.x_src, ctx.x_dst = x_in, x_out
    declare_weights(ctx, [("L1_attn_b_in_proj_w", (1024, 4096), F32),
                          ("L1_attn_b_B_proj_w", (1024, 1024), F32),
                          ("L1_attn_b_C_proj_w", (1024, 1024), F32),
                          ("L1_attn_b_out_proj_w", (2048, 1024), F32),
                          ("L1_attn_b_m_fc_w", (1024, 4096), F32),
                          ("L1_attn_b_m_proj_w", (4096, 1024), F32),
                          ("L1_attn_b_q_w", (1024, 1024), F32),
                          ("L1_attn_b_k_w", (1024, 256), F32),
                          ("L1_attn_b_v_w", (1024, 256), F32),
                          ("L1_attn_b_o_w", (1024, 1024), F32),
                          ("L1_attn_b_a_fc_w", (1024, 4096), F32),
                          ("L1_attn_b_a_proj_w", (4096, 1024), F32)])
    for nm, n in (("ab_rm0", 8), ("ab_rm1", 8), ("ab_ra0", 8), ("ab_ra1", 8),
                  ("ab_msc", 8), ("ab_m2sc", 8), ("ab_asc", 8), ("ab_a2sc", 8),
                  ("ab_conv", 64), ("ab_convb", 16), ("ab_D", 16),
                  ("ab_dtb", 16), ("ab_alog", 16), ("ab_qg", 16)):
        declare_colvec(ctx, nm, n)
    emit_ln_hbm(ctx, x_in)
    emit_l1ab(ctx)
    p.finalize()
    return nc


def emit_ln_hbm(ctx, x_in):
    p = ctx.p
    XN, ACTF = ctx.XN, ctx.ACTF
    PSA = ctx.PSA
    onescb = ctx.const["ONESC_B"]
    onesr = ctx.const["ONESR_F"]
    XSB, HCF = ctx.XS, ctx.HCF
    for i in range(8):
        xf = XSB.v(0, 2048)
        emit_dma_load(ctx, "sync", xf, xview(x_in, i, 0, 2048), tag="lnh.x")
        xb = XN.v(2048 * i, 2048)
        emit_act(ctx, xb, xf, AF.Copy, tag="lnh.cast")
        for c in range(4):
            emit_mm(ctx, PSA.v(512 * c, 512, 0, 1), onescb, XN.v(2048 * i + 512 * c, 512),
                    i == 0, i == 7, tag="lnh.sum")
    mu = HCF.v(0, 2048, 0, 1)
    emit_ts(ctx, mu, PSA.v(0, 2048, 0, 1), 1.0 / D, None, OP.mult, tag="lnh.mu")
    for i in range(8):
        sq = XN.v(2048 * i, 2048)
        emit_act(ctx, sq, sq, AF.Square, tag="lnh.sq")
        for c in range(4):
            emit_mm(ctx, PSA.v(512 * c, 512, 0, 1), onescb, XN.v(2048 * i + 512 * c, 512),
                    i == 0, i == 7, tag="lnh.ss")
    ms = ACTF.v(0, 2048, 0, 1)
    emit_ts(ctx, ms, PSA.v(0, 2048, 0, 1), 1.0 / D, None, OP.mult, tag="lnh.ms")
    mu2 = XSB.v(0, 2048, 0, 1)
    emit_tt(ctx, mu2, mu, mu, OP.mult, tag="lnh.mu2")
    var = ACTF.v(0, 2048, 0, 1)
    emit_tt(ctx, var, ms, mu2, OP.subtract, tag="lnh.var")
    lnv = XSB.v(0, 2048, 0, 1)
    emit_act(ctx, lnv, var, AF.Ln, bias=1e-5, tag="lnh.lnv")
    ra = ACTF.v(0, 2048, 0, 1)
    emit_act(ctx, ra, lnv, AF.Exp, scale=-0.5, tag="lnh.ra")
    rb = XSB.v(0, 2048, 0, 1)
    emit_stt(ctx, rb, mu, -1.0, ra, OP.mult, OP.mult, tag="lnh.rb")
    for c in range(4):
        pra = PSA.v(0, 512)
        prb = PSA.v(512, 512)
        emit_mm(ctx, pra, onesr, ACTF.v(512 * c, 512, 0, 1), True, True, tag="lnh.bca")
        emit_mm(ctx, prb, onesr, XSB.v(512 * c, 512, 0, 1), True, True, tag="lnh.bcb")
        for i in range(8):
            xb = HCF.v(512, 512)
            emit_dma_load(ctx, "sync", xb, xview(x_in, i, 512 * c, 512), tag="lnh.x2")
            tmp = HCF.v(0, 512)
            emit_tt(ctx, tmp, xb, pra, OP.mult, tag="lnh.mul")
            emit_tt(ctx, XN.v(2048 * i + 512 * c, 512), tmp, prb, OP.add, tag="lnh.add")


def host_l1ab_inputs(inputs):
    out = {k: inputs[k] for k in ("L1_attn_b_in_proj_w", "L1_attn_b_B_proj_w",
                                  "L1_attn_b_C_proj_w", "L1_attn_b_out_proj_w",
                                  "L1_attn_b_m_fc_w", "L1_attn_b_m_proj_w",
                                  "L1_attn_b_q_w", "L1_attn_b_k_w", "L1_attn_b_v_w",
                                  "L1_attn_b_o_w", "L1_attn_b_a_fc_w",
                                  "L1_attn_b_a_proj_w")}
    rs = lambda a, n: np.ascontiguousarray(np.asarray(a).reshape(n, 128).T)
    bc = lambda a: np.ascontiguousarray(np.broadcast_to(np.asarray(a)[None, :], (128, len(a))))
    out["cv_ab_rm0"] = rs(inputs["L1_attn_b_resid_mix_m"][0], 8)
    out["cv_ab_rm1"] = rs(inputs["L1_attn_b_resid_mix_m"][1], 8)
    out["cv_ab_ra0"] = rs(inputs["L1_attn_b_resid_mix_a"][0], 8)
    out["cv_ab_ra1"] = rs(inputs["L1_attn_b_resid_mix_a"][1], 8)
    out["cv_ab_msc"] = rs(inputs["L1_attn_b_mamba_scale"], 8)
    out["cv_ab_m2sc"] = rs(inputs["L1_attn_b_m_mlp_scale"], 8)
    out["cv_ab_asc"] = rs(inputs["L1_attn_b_attn_scale"], 8)
    out["cv_ab_a2sc"] = rs(inputs["L1_attn_b_a_mlp_scale"], 8)
    cw = inputs["L1_attn_b_conv_w"].reshape(16, 128, 4)
    out["cv_ab_conv"] = np.ascontiguousarray(cw.transpose(1, 0, 2).reshape(128, 64))
    out["cv_ab_convb"] = rs(inputs["L1_attn_b_conv_b"], 16)
    out["cv_ab_D"] = bc(inputs["L1_attn_b_D_param"])
    out["cv_ab_dtb"] = bc(inputs["L1_attn_b_dt_bias"])
    out["cv_ab_alog"] = bc(inputs["L1_attn_b_A_log"])
    out["cv_ab_qg"] = bc(inputs["L1_attn_b_q_gain"])
    return out




def build_prog_fused():
    global LAST_CTX
    nc = bass.Bass("TRN2", target_bir_lowering=False)
    p = Prog(nc)
    ctx = Ctx(p)
    LAST_CTX = ctx
    ctx.win = {}
    ctx.colvecs = {}
    ctx.smf_off = 0
    ctx.xr_slot = 0
    ctx.XN = p.sbuf("XN", 16384, BF16)
    ABF = p.sbuf("ABF", 53616, BF16)
    AFX = p.sbuf("AFX", 13696, F32)
    ctx.XR = SubPool(AFX, 0, 1024)
    ctx.XS = SubPool(AFX, 1024, 2048)
    ctx.HCF = SubPool(AFX, 3072, 2048)
    ctx.ACTF = SubPool(AFX, 5120, 2048)
    CARVE = 1024
    ctx.LNSQ = p.sbuf("LNSQf", 512, BF16)
    ctx.SMF = p.sbuf("SMF", 256, F32)
    ctx.PSA = p.psum("PSA", 2048, F32)
    ctx.PSB = p.psum("PSB", 1024, F32)
    ctx.PSC = p.psum("PSC", 512, F32)
    ctx.PST = p.psum("PST", 1024, BF16)
    emit_load_consts(ctx, CONST_NAMES + ["DIFFT", "REVT", "DECT", "REVC",
                                          "COSTOK", "SINTOK"])

    x_in = p.dram("x_in", [1024, 2048], F32, kind="ExternalInput")
    x_out = p.dram("x_out", [1024, 2048], F32, kind="ExternalOutput")
    x_scr = p.dram("x_scr", [1024, 2048], F32)
    ctx.x_in, ctx.x_out = x_in, x_out

    declare_weights(ctx, [("L0_attn_a_qkv_w", (192, 1024), F32),
                          ("L0_attn_a_proj_w", (1024, 1024), F32),
                          ("L0_ffn_w_gate", (3072, 1024), I32),
                          ("L0_ffn_w_up", (3072, 1024), I32),
                          ("L0_ffn_w_down", (1024, 3072), I32),
                          ("L0_attn_b_Wq", (1024, 1024), F32),
                          ("L0_attn_b_Wk", (1024, 1024), F32),
                          ("L0_attn_b_Wv", (1024, 1024), F32),
                          ("L0_attn_b_Wo", (1024, 1024), F32),
                          ("L1_attn_a_gate_w", (8, 1024), F32),
                          ("L1_attn_a_q_w", (1024, 1024), F32),
                          ("L1_attn_a_k_w", (8, 1024, 1024), F32),
                          ("L1_attn_a_v_w", (8, 1024, 1024), F32),
                          ("L1_attn_a_fmq1_w", (128, 128), F32),
                          ("L1_attn_a_fmq2_w", (128, 128), F32),
                          ("L1_attn_a_fmk1_w", (128, 128), F32),
                          ("L1_attn_a_fmk2_w", (128, 128), F32),
                          ("L1_attn_a_o_w", (1024, 1024), F32),
                          ("L1_ffn_Wq", (1024, 2048), F32),
                          ("L1_ffn_sub_key0", (32, 128), F32),
                          ("L1_ffn_sub_key1", (32, 128), F32),
                          ("L1_ffn_expert_down", (1024, 1024), F32),
                          ("L1_ffn_expert_up", (1024, 1024), F32),
                          ("L1_attn_b_in_proj_w", (1024, 4096), F32),
                          ("L1_attn_b_B_proj_w", (1024, 1024), F32),
                          ("L1_attn_b_C_proj_w", (1024, 1024), F32),
                          ("L1_attn_b_out_proj_w", (2048, 1024), F32),
                          ("L1_attn_b_m_fc_w", (1024, 4096), F32),
                          ("L1_attn_b_m_proj_w", (4096, 1024), F32),
                          ("L1_attn_b_q_w", (1024, 1024), F32),
                          ("L1_attn_b_k_w", (1024, 256), F32),
                          ("L1_attn_b_v_w", (1024, 256), F32),
                          ("L1_attn_b_o_w", (1024, 1024), F32),
                          ("L1_attn_b_a_fc_w", (1024, 4096), F32),
                          ("L1_attn_b_a_proj_w", (4096, 1024), F32)])
    declare_colvec(ctx, "l0aa_qkvb", 2)
    declare_colvec(ctx, "l0aa_projb", 8)
    declare_colvec(ctx, "l1aa_nw", 1)
    declare_colvec(ctx, "l1ffn_bq", 16)
    declare_colvec(ctx, "l1ffn_gam", 2)
    declare_colvec(ctx, "l1ffn_bet", 2)
    for nm, n in (("ab_rm0", 8), ("ab_rm1", 8), ("ab_ra0", 8), ("ab_ra1", 8),
                  ("ab_msc", 8), ("ab_m2sc", 8), ("ab_asc", 8), ("ab_a2sc", 8),
                  ("ab_conv", 64), ("ab_convb", 16), ("ab_D", 16),
                  ("ab_dtb", 16), ("ab_alog", 16), ("ab_qg", 16)):
        declare_colvec(ctx, nm, n)

    def a_l0aa():
        ctx.W = SubPool(ABF, 0, 14336)
        ctx.ACTB = SubPool(ABF, 14336, 8704)

    def a_l0ffn():
        ctx.W = SubPool(ABF, 0, 4096)
        ctx.ACTB = SubPool(ABF, 4096, 512)
        ctx.W2 = SubPool(ABF, 4608, 12288)
        ctx.H = SubPool(ABF, 16896, 12288)

    def a_l0ab():
        ctx.W = SubPool(ABF, 0, 16384)
        ctx.ACTB = SubPool(ABF, 16384, 512)
        ctx.QH = SubPool(ABF, 16896, 8192)
        ctx.KH = SubPool(ABF, 25088, 8192)
        ctx.VAUG = SubPool(ABF, 33280, 8320)
        ctx.YH = SubPool(ABF, 47216, 2048)
        ctx.ATTN = SubPool(ABF, 49264, 1536)

    def a_l1aa():
        ctx.W = SubPool(ABF, 0, 4096)
        ctx.QF = SubPool(ABF, 4096, 8192)
        ctx.KF1 = SubPool(ABF, 12288, 8192)
        ctx.KF2 = SubPool(ABF, 20480, 8192)
        ctx.VT = SubPool(ABF, 28672, 8192)
        ctx.STV = SubPool(ABF, 36864, 1024)
        ctx.MTOKB = SubPool(ABF, 37888, 128)
        ctx.FM = SubPool(ABF, 38016, 512)
        ctx.RB = SubPool(ABF, 38528, 2816)
        ctx.KT = SubPool(ABF, 41344, 8192)
        ctx.STF = SubPool(AFX, CARVE, 1024)
        ctx.UN = SubPool(AFX, CARVE + 1024, 6144)
        ctx.MTOKF = SubPool(AFX, CARVE + 7168, 128)

    def a_l1ffn():
        ctx.W = SubPool(ABF, 0, 16384)
        ctx.QB = SubPool(ABF, 16384, 4096)
        ctx.F2 = SubPool(ABF, 20480, 13312)
        ctx.SCOR = SubPool(AFX, CARVE, 8192)
        ctx.SF = SubPool(AFX, CARVE + 8192, 4096)
        ctx.RN = SubPool(AFX, CARVE + 12288, 256)
        ctx.ACC = SubPool(AFX, CARVE + 12544, 128)

    def a_l1ab():
        ctx.W = SubPool(ABF, 0, 64)
        ctx.XM = SubPool(ctx.XN, 0, 16384)
        B = SubPool(ABF, 64, 34864)
        ctx.BIGB = B
        ctx.VB = SubPool(B, 0, 8240)
        ctx.VACT = SubPool(B, 8240, 8192)
        ctx.BCM = SubPool(B, 16432, 8192)
        ctx.SG = SubPool(B, 24624, 2048)
        ctx.LMT = SubPool(B, 26672, 2048)
        ctx.DRT = SubPool(B, 28720, 2048)
        ctx.DOT = SubPool(B, 30768, 2048)
        ctx.HC = SubPool(B, 32816, 2048)
        ctx.WB = SubPool(ABF, 34928, 8192)
        ctx.WS = SubPool(ABF, 43120, 4096)
        ctx.NB = SubPool(ABF, 47216, 4096)
        ctx.SCB = SubPool(ABF, 51312, 768)
        ctx.ATTN = SubPool(ABF, 52080, 1536)

    blocks = [("l0aa", a_l0aa, lambda: emit_l0aa(ctx)),
              ("l0ffn", a_l0ffn, lambda: emit_l0ffn(ctx)),
              ("l0ab", a_l0ab, lambda: emit_l0ab(ctx)),
              ("l0ffn2", a_l0ffn, lambda: emit_l0ffn(ctx)),
              ("l1aa", a_l1aa, lambda: emit_l1aa(ctx)),
              ("l1ffn", a_l1ffn, lambda: emit_l1ffn(ctx, 0)),
              ("l1ab", a_l1ab, lambda: emit_l1ab(ctx)),
              ("l1ffn2", a_l1ffn, lambda: emit_l1ffn(ctx, 1))]
    xs = [x_in] + [x_scr] * 7
    xd = [x_scr] * 7 + [x_out]
    ctx.block_marks = []
    rp_ws = SubPool(ABF, 43120, 4096)
    for bi, (bname, assign, emitfn) in enumerate(blocks):
        ctx.block_marks.append(("ln_" + bname,
                                {e: len(v) for e, v in p.eng_ops.items()}))
        assign()
        ctx.x_src, ctx.x_dst = xs[bi], xd[bi]
        emit_ln_hbm(ctx, ctx.x_src)
        ctx.block_marks.append((bname, {e: len(v) for e, v in p.eng_ops.items()}))
        emitfn()
        if bi == 0:
            emit_l0ffn_phaseW(ctx, SubPool(ABF, 47216, 4096))
        if bi == 1:
            emit_l1ab_repacks(ctx, rp_ws, 0)
        elif bi == 2:
            ctx.rp_store_qs = ("gpsimd", "gpsimd")
            emit_l1ab_repacks(ctx, rp_ws, 1)
            emit_l1ab_repacks(ctx, rp_ws, 2)
            del ctx.rp_store_qs
    ctx.block_marks.append(("end", {e: len(v) for e, v in p.eng_ops.items()}))
    p.finalize()
    return nc


_PROG_CACHE = {}


def _get_prog(name):
    global LAST_CTX
    builders = {"l0aa": build_prog_l0aa, "l0ffn": build_prog_l0ffn,
                "l0ab": build_prog_l0ab, "l1aa": build_prog_l1aa,
                "l1ffn": build_prog_l1ffn, "l1ab": build_prog_l1ab,
                "fused": build_prog_fused}
    if name not in _PROG_CACHE:
        nc = builders[name]()
        _PROG_CACHE[name] = (nc, LAST_CTX.host_fconsts())
    return _PROG_CACHE[name]


def host_fused_inputs(inputs):
    base = {}
    for hostfn in (host_l0aa_inputs, host_l0ffn_inputs, host_l0ab_inputs,
                   host_l1aa_inputs, host_l1ffn_inputs, host_l1ab_inputs):
        base.update(hostfn(inputs))
    base.update(host_const_inputs())
    return base


def kernel(**inputs):
    nc, fc = _get_prog("fused")
    base = host_fused_inputs(inputs)
    base["fconsts"] = fc
    in_maps = [dict(base, x_in=np.ascontiguousarray(
        inputs["L0_attn_a_ids"][c].T.astype(np.float32))) for c in range(NCORES)]
    res = run_bass_kernel_spmd(nc, in_maps, core_ids=list(range(NCORES)))
    return np.ascontiguousarray(
        np.stack([np.asarray(res.results[c]["x_out"]).T
                  for c in range(NCORES)]).astype(np.float32))


# version 28 (final)
# speedup vs baseline: 1.0349x; 1.0184x over previous
import numpy as np
import ml_dtypes
from contextlib import ExitStack

import concourse.bass as bass
import concourse.mybir as mybir
from concourse.bass_utils import run_bass_kernel_spmd

F32 = mybir.dt.float32
BF16 = mybir.dt.bfloat16
I32 = mybir.dt.int32
U8 = mybir.dt.uint8
AF = mybir.ActivationFunctionType
OP = mybir.AluOpType
BF = ml_dtypes.bfloat16

ENGS = ("sync", "tensor", "vector", "scalar", "gpsimd")
SELF_SYNC = True
DMAQS = ("sync", "scalar", "gpsimd")

NCORES = 8
T = 2048
D = 1024



class View:
    __slots__ = ("ap", "regs")

    def __init__(self, ap, regs):
        self.ap = ap
        self.regs = regs


class Pool:

    def __init__(self, prog, name, tensor, cols, dtype, parts, is_psum=False):
        self.prog = prog
        self.name = name
        self.t = tensor
        self.cols = cols
        self.dtype = dtype
        self.parts = parts
        self.is_psum = is_psum
        self.off = 0
        self.bank = 2048 // (2 if dtype == BF16 else 4)

    def v(self, col0, ncols, part0=0, nparts=None):
        nparts = nparts if nparts is not None else self.parts
        ap = self.t[part0:part0 + nparts, col0:col0 + ncols]
        lo, hi = col0, col0 + ncols
        if self.is_psum:
            lo = (lo // self.bank) * self.bank
            hi = ceil_div(hi, self.bank) * self.bank
        return View(ap, [(self.name, lo, hi)])

    def all(self):
        return self.v(0, self.cols)


class SubPool:

    def __init__(self, parent, off, cols):
        self.parent = parent
        self.off = parent.off + off
        self.cols = cols
        self.t = parent.t
        self.name = parent.name
        base = parent
        while isinstance(base, SubPool):
            base = base.parent
        self._base = base

    def v(self, col0, ncols, part0=0, nparts=None):
        return self._base.v(self.off + col0, ncols, part0, nparts)


class Prog:
    def __init__(self, nc):
        self.nc = nc
        self.eng_ops = {e: [] for e in ENGS}
        self.dma_n = {q: 0 for q in DMAQS}
        self.cc_n = 0
        self.needed = set()
        self.segs = {}
        self.pools = {}
        self._es = None

    def sbuf(self, name, cols, dtype, parts=128):
        t = self.nc.sbuf_tensor(name, [parts, cols], dtype)
        return self._reg_pool(name, t, cols, dtype, parts)

    def psum(self, name, cols, dtype, parts=128):
        t = self.nc.psum_tensor(name, [parts, cols], dtype)
        return self._reg_pool(name, t, cols, dtype, parts, True)

    def _reg_pool(self, name, tctx, cols, dtype, parts, is_psum=False):
        if self._es is None:
            self._es = ExitStack()
        t = self._es.enter_context(tctx)
        p = Pool(self, name, t, cols, dtype, parts, is_psum)
        self.pools[name] = p
        return p

    def dram(self, name, shape, dtype, kind="Internal"):
        h = self.nc.dram_tensor(name, shape, dtype, kind=kind)
        ap = h.ap()
        return View(ap, [("dram:" + name, 0, 1 << 60)])

    def _dep_read(self, space, lo, hi, deps):
        for s in self.segs.get(space, ()):
            if s[0] < hi and lo < s[1] and s[2] is not None:
                deps.add(s[2])

    def _dep_write(self, space, lo, hi, deps):
        for s in self.segs.get(space, ()):
            if s[0] < hi and lo < s[1]:
                if s[2] is not None:
                    deps.add(s[2])
                deps.update(s[3])

    def _commit_read(self, space, lo, hi, ev):
        for s in self.segs.get(space, ()):
            if s[0] < hi and lo < s[1]:
                s[3].append(ev)

    def _commit_write(self, space, lo, hi, ev):
        segs = self.segs.setdefault(space, [])
        out = []
        for s in segs:
            if s[0] < hi and lo < s[1]:
                if s[0] < lo:
                    out.append([s[0], lo, s[2], list(s[3])])
                if hi < s[1]:
                    out.append([hi, s[1], s[2], list(s[3])])
            else:
                out.append(s)
        out.append([lo, hi, ev, []])
        self.segs[space] = out

    def op(self, eng, fn, reads=(), writes=(), dma=False, cc=False, tag=""):
        deps = set()
        for v in reads:
            for (sp, lo, hi) in v.regs:
                self._dep_read(sp, lo, hi, deps)
        for v in writes:
            for (sp, lo, hi) in v.regs:
                self._dep_write(sp, lo, hi, deps)
        i = len(self.eng_ops[eng])
        if not SELF_SYNC or eng == "tensor":
            deps = {d for d in deps if not (d[0] == "c" and d[1] == eng)}
        if dma:
            q = eng
            assert q in DMAQS, f"dma on engine {eng}"
            self.dma_n[q] += 1
            ev = ("d", q, self.dma_n[q])
        elif cc:
            self.cc_n += 1
            ev = ("cc", self.cc_n)
        else:
            ev = ("c", eng, i)
        for d in deps:
            if d[0] == "c":
                self.needed.add(d)
        self.eng_ops[eng].append(dict(fn=fn, deps=deps, ev=ev, tag=tag))
        for v in reads:
            for (sp, lo, hi) in v.regs:
                self._commit_read(sp, lo, hi, ev)
        for v in writes:
            for (sp, lo, hi) in v.regs:
                self._commit_write(sp, lo, hi, ev)
        return ev

    def barrier_ev(self, eng):
        ops = self.eng_ops[eng]
        if not ops:
            return None
        ev = ops[-1]["ev"]
        if ev[0] == "c":
            self.needed.add(ev)
        return ev

    def finalize(self):
        nc = self.nc
        es = self._es if self._es is not None else ExitStack()
        self._es = es
        sems = {}
        for e in ENGS:
            sems[("c", e)] = es.enter_context(nc.semaphore("sem_" + e))
        for q in DMAQS:
            sems[("d", q)] = es.enter_context(nc.semaphore("dma_" + q))
        sems[("cc",)] = es.enter_context(nc.semaphore("sem_cc"))

        val = {}
        inc_at = {e: {} for e in ENGS}
        for e in ENGS:
            idxs = sorted(i for (k, ee, i) in self.needed if k == "c" and ee == e)
            for rank, i in enumerate(idxs):
                val[("c", e, i)] = rank + 1
                inc_at[e][i] = rank + 1

        def ev_to_wait(ev):
            if ev[0] == "c":
                return (("c", ev[1]), val[ev])
            if ev[0] == "d":
                return (("d", ev[1]), 16 * ev[2])
            return (("cc",), ev[1])

        block = es.enter_context(nc.Block())
        engobjs = {}

        def make_section(e):
            ops = self.eng_ops[e]

            def body(engine):
                done = {}
                for i, o in enumerate(ops):
                    for dep in sorted(o["deps"], key=str):
                        semk, v = ev_to_wait(dep)
                        if done.get(semk, 0) < v:
                            engine.wait_ge(sems[semk], v)
                            done[semk] = v
                    try:
                        inst = o["fn"](engine)
                    except Exception as ex:
                        raise RuntimeError(f"op tag={o['tag']} eng={e} i={i}: {ex}") from ex
                    ev = o["ev"]
                    if ev[0] == "d":
                        inst.then_inc(sems[("d", e)], 16)
                    elif ev[0] == "cc":
                        inst.then_inc(sems[("cc",)])
                    elif i in inc_at[e]:
                        inst.then_inc(sems[("c", e)], 1)
                if e in DMAQS and self.dma_n[e] > 0:
                    engine.wait_ge(sems[("d", e)], 16 * self.dma_n[e])
            return body

        for e in ENGS:
            if not self.eng_ops[e]:
                continue
            dec = getattr(block, e if e != "sync" else "sync")
            dec(make_section(e))
        es.close()
        return nc



def ceil_div(a, b):
    return (a + b - 1) // b


def PCOLS(pool):
    p = pool
    while isinstance(p, SubPool):
        p = p.parent
    return p.cols


class Ctx:

    def __init__(self, prog):
        self.p = prog
        self.const = {}
        self.uid = 0
        self.fconst_vals = []
        self._fc_pool = None

    def name(self, base):
        self.uid += 1
        return f"{base}{self.uid}"

    def fconst(self, value):
        value = float(value)
        if self._fc_pool is None:
            self._fc_dram = self.p.dram("fconsts", [128, 16], F32, kind="ExternalInput")
            self._fc_pool = self.p.sbuf("FC", 16, F32)
            emit_dma_load(self, "sync", self._fc_pool.v(0, 16), self._fc_dram, tag="fconsts")
        if value in self.fconst_vals:
            j = self.fconst_vals.index(value)
        else:
            self.fconst_vals.append(value)
            j = len(self.fconst_vals) - 1
            assert j < 16
        return self._fc_pool.v(j, 1)

    def host_fconsts(self):
        a = np.zeros((128, 16), np.float32)
        for j, v in enumerate(self.fconst_vals):
            a[:, j] = v
        return a


def emit_dma_load(ctx, eng, dst_view, src_ap_view, tag=""):
    p = ctx.p
    return p.op(eng, lambda q, o=dst_view.ap, i=src_ap_view.ap: q.dma_start(out=o, in_=i),
                reads=[src_ap_view], writes=[dst_view], dma=True, tag=tag)


def emit_dma_store(ctx, eng, dst_ap_view, src_view, tag=""):
    p = ctx.p
    return p.op(eng, lambda q, o=dst_ap_view.ap, i=src_view.ap: q.dma_start(out=o, in_=i),
                reads=[src_view], writes=[dst_ap_view], dma=True, tag=tag)


def xview(xt, f, c0, n):
    sp = xt.regs[0][0]
    return View(xt.ap[128 * f:128 * (f + 1), c0:c0 + n],
                [(sp, 2048 * f + c0, 2048 * f + c0 + n)])


def emit_xadd(ctx, f, c0, n, inv, scalar=None, op0=None, tag="xadd"):
    p = ctx.p
    xr = ctx.XR.v(0, n)
    src = xview(ctx.x_src, f, c0, n)
    emit_dma_load(ctx, "scalar", xr, src, tag=tag + ".ld")
    xo = ctx.XR.v(512, n)
    if scalar is None:
        emit_tt(ctx, xo, inv, xr, OP.add, tag=tag + ".add")
    else:
        emit_stt(ctx, xo, inv, scalar, xr, op0, OP.add, tag=tag + ".stt")
    dst = xview(ctx.x_dst, f, c0, n)
    emit_dma_store(ctx, "scalar", dst, xo, tag=tag + ".st")



def host_constants():
    c = {}
    c["IDB"] = np.eye(128).astype(BF)
    c["ID32"] = np.eye(128, dtype=np.float32)
    c["ONESC_B"] = np.ones((128, 1), BF)
    c["ONESC_F"] = np.ones((128, 1), np.float32)
    c["ONESR_B"] = np.ones((1, 128), BF)
    c["ONESR_F"] = np.ones((1, 128), np.float32)
    j = np.arange(128)[:, None]
    t = np.arange(128)[None, :]
    c["TRILT_B"] = (j <= t).astype(BF)
    m = np.zeros((128, 4, 512), np.float32)
    for a in range(4):
        m[:, a, :] = (128 * a + j) <= np.arange(512)[None, :]
    c["CMASK_B"] = m.reshape(128, 2048).astype(BF)
    e16 = np.zeros((128, 16, 16), np.float32)
    for mm_ in range(16):
        e16[:, mm_, mm_] = 1.0
    c["EHS16"] = e16.reshape(128, 256).astype(BF)
    c["DIFFT"] = np.maximum(t - j, 0).astype(np.float32)
    c["REVT"] = np.broadcast_to(127.0 - np.arange(128, dtype=np.float32)[None, :],
                                (128, 128)).copy()
    c["DECT"] = np.broadcast_to(np.arange(128, dtype=np.float32)[None, :] + 1.0,
                                (128, 128)).copy()
    c["REVC"] = (127.0 - np.arange(128, dtype=np.float32))[:, None].copy()
    invf = 1.0 / (10000.0 ** (np.arange(0, 64, 2, dtype=np.float32) / 64.0))
    tt_ = np.arange(2048, dtype=np.float32).reshape(16, 128)
    f_ = tt_[:, :, None] * invf[None, None, :]
    c["COSTOK"] = np.cos(f_).transpose(1, 0, 2).reshape(128, 512).astype(BF)
    c["SINTOK"] = np.sin(f_).transpose(1, 0, 2).reshape(128, 512).astype(BF)
    return c


def emit_load_consts(ctx, names):
    p = ctx.p
    hc = host_constants()
    sel = {n: hc[n] for n in names}
    total = 0
    metas = []
    for n, a in sel.items():
        assert a.shape[0] <= 128
        cols = a.shape[1] * (2 if a.dtype == np.dtype(BF) else 4) // 4
        metas.append((n, a))
    cb = sum(a.shape[1] for n, a in metas if a.dtype == np.dtype(BF))
    cf = sum(a.shape[1] for n, a in metas if a.dtype == np.float32)
    pb = p.sbuf("constB", max(cb, 1), BF16)
    pf = p.sbuf("constF", max(cf, 1), F32)
    ob = of = 0
    host_vals = {}
    for n, a in metas:
        dt = BF16 if a.dtype == np.dtype(BF) else F32
        dv = p.dram("c_" + n, list(a.shape), dt, kind="ExternalInput")
        host_vals["c_" + n] = a
        if dt == BF16:
            v = pb.v(ob, a.shape[1], 0, a.shape[0]); ob += a.shape[1]
        else:
            v = pf.v(of, a.shape[1], 0, a.shape[0]); of += a.shape[1]
        emit_dma_load(ctx, "sync", v, dv, tag="const:" + n)
        ctx.const[n] = v
    ctx.host_const_vals = host_vals
    return host_vals



def emit_mm(ctx, psv, lhsT, rhs, start, stop, tag=""):
    ctx.p.op("tensor",
             lambda t, o=psv.ap, l=lhsT.ap, r=rhs.ap, s0=start, s1=stop:
             t.matmul(o, l, r, start=s0, stop=s1),
             reads=[lhsT, rhs], writes=[psv], tag=tag)


def emit_transpose(ctx, psv, inv, ident, tag=""):
    np_ = inv.ap.shape[0]
    b0 = inv.ap.base_partition() if callable(inv.ap.base_partition) else inv.ap.base_partition
    dap = ident.ap
    if dap.shape[0] != np_ or b0 != 0:
        dap = ident.ap.tensor[b0:b0 + np_, 0:np_]
    ctx.p.op("tensor",
             lambda t, o=psv.ap, i=inv.ap, d=dap: t.transpose(o, i, d),
             reads=[inv, ident], writes=[psv], tag=tag)


def emit_copy(ctx, eng, dst, src, tag=""):
    fn = (lambda q, o=dst.ap, i=src.ap: q.tensor_copy(o, i)) if eng != "scalar" else \
         (lambda q, o=dst.ap, i=src.ap: q.copy(o, i))
    ctx.p.op(eng, fn, reads=[src], writes=[dst], tag=tag)


def emit_tt(ctx, dst, a, b, op, tag="", eng="vector"):
    ctx.p.op(eng, lambda q, o=dst.ap, x=a.ap, y=b.ap, oo=op: q.tensor_tensor(o, x, y, oo),
             reads=[a, b], writes=[dst], tag=tag)


def emit_ts(ctx, dst, a, s1, s2, op0, op1=None, tag="", eng="vector", accum=None):
    reads = [a]
    s1a = s1
    if isinstance(s1, View):
        reads.append(s1); s1a = s1.ap
    s2a = s2
    if isinstance(s2, View):
        reads.append(s2); s2a = s2.ap
    writes = [dst]
    aa = None
    if accum is not None:
        writes.append(accum); aa = accum.ap
    if op1 is None:
        ctx.p.op(eng, lambda q, o=dst.ap, x=a.ap, u=s1a, w=s2a, o0=op0, ac=aa:
                 q.tensor_scalar(o, x, u, w, o0, accum_out=ac),
                 reads=reads, writes=writes, tag=tag)
    else:
        ctx.p.op(eng, lambda q, o=dst.ap, x=a.ap, u=s1a, w=s2a, o0=op0, o1=op1, ac=aa:
                 q.tensor_scalar(o, x, u, w, o0, o1, accum_out=ac),
                 reads=reads, writes=writes, tag=tag)


def emit_stt(ctx, dst, in0, scal, in1, op0, op1, tag=""):
    reads = [in0, in1]
    sa = scal
    if isinstance(scal, View):
        reads.append(scal); sa = scal.ap
    ctx.p.op("vector", lambda q, o=dst.ap, a=in0.ap, s=sa, b=in1.ap, o0=op0, o1=op1:
             q.scalar_tensor_tensor(o, a, s, b, o0, o1),
             reads=reads, writes=[dst], tag=tag)


def emit_act(ctx, dst, src, func, bias=0.0, scale=1.0, accum=None, tag=""):
    if isinstance(bias, float) and bias != 0.0:
        bias = ctx.fconst(bias)
    if isinstance(bias, View) and bias.ap.shape[0] != src.ap.shape[0]:
        b0 = src.ap.base_partition
        b0 = b0() if callable(b0) else b0
        j = bias.regs[0][1]
        bias = View(bias.ap.tensor[b0:b0 + src.ap.shape[0], j:j + 1], bias.regs)
    reads = [src]
    ba = bias
    if isinstance(bias, View):
        reads.append(bias); ba = bias.ap
    sa = scale
    if isinstance(scale, View):
        reads.append(scale); sa = scale.ap
    writes = [dst]
    aa = None
    if accum is not None:
        writes.append(accum); aa = accum.ap
    if aa is None:
        ctx.p.op("scalar", lambda q, o=dst.ap, i=src.ap, f=func, b=ba, s=sa:
                 q.activation(o, i, f, bias=b, scale=s),
                 reads=reads, writes=writes, tag=tag)
    else:
        ctx.p.op("scalar", lambda q, o=dst.ap, i=src.ap, f=func, b=ba, s=sa, ac=aa:
                 q.activation(o, i, f, bias=b, scale=s, accum_out=ac),
                 reads=reads, writes=writes, tag=tag)


def load_weight_bf16(ctx, dram_ap, regs, rows, cols, dst_pool, dst_col0, tag="w"):
    nrt = ceil_div(rows, 128)
    for k in range(nrt):
        r0 = 128 * k
        nr = min(128, rows - r0)
        dv = View(dram_ap[r0:r0 + nr, :], regs)
        sv = dst_pool.v(dst_col0 + k * cols, cols, 0, nr)
        emit_dma_load(ctx, "gpsimd", sv, dv, tag=tag + f".load{k}")
    return nrt


def wtile(pool, col0, cols, k, nr=128, c0=0, ncol=None):
    ncol = cols if ncol is None else ncol
    base = pool.off + col0 + k * cols + c0
    return View(pool.t[0:nr, base: base + ncol],
                [(pool.name, base, base + ncol)])


def emit_transpose_tiles(ctx, src_pool, src_col0, rows, cols, dst_pool, dst_col0,
                         pst, tag="wt", row0=0, nrows=None):
    nrows = rows - row0 if nrows is None else nrows
    nrt = ceil_div(nrows, 128)
    nct = ceil_div(cols, 128)
    idb = ctx.const["IDB"]
    for c in range(nct):
        c0 = 128 * c
        ncol = min(128, cols - c0)
        for k0 in range(0, nrt, 8):
            kn = min(nrt, k0 + 8)
            for k in range(k0, kn):
                nr = min(128, nrows - 128 * k)
                svv = wtile(src_pool, src_col0, cols, k, nr, c0, ncol)
                ps = View(pst.t[0:ncol, 128 * (k - k0):128 * (k - k0) + nr],
                          [(pst.name, 128 * (k - k0), 128 * (k - k0) + nr)])
                emit_transpose(ctx, ps, svv, idb, tag=tag + f".t{c}.{k}")
            w = min(nrows, 128 * kn) - 128 * k0
            pse = View(pst.t[0:ncol, 0:w], [(pst.name, 0, w)])
            b0 = dst_pool.off + dst_col0 + c * rows + row0 + 128 * k0
            dv = View(dst_pool.t[0:ncol, b0:b0 + w], [(dst_pool.name, b0, b0 + w)])
            emit_copy(ctx, "vector", dv, pse, tag=tag + f".ev{c}.{k0}")


def wt_tile(pool, col0, rows, c, ncol=128, r0=0, nr=None):
    nr = rows if nr is None else nr
    base = pool.off + col0 + c * rows + r0
    return View(pool.t[0:ncol, base:base + nr], [(pool.name, base, base + nr)])



class ColVec:
    def __init__(self, pool, col0, n):
        self.pool = pool
        self.col0 = col0
        self.n = n

    def col(self, j, part0=0, nparts=128):
        return self.pool.v(self.col0 + j, 1, part0, nparts)

    def cols(self, j, n, part0=0, nparts=128):
        return self.pool.v(self.col0 + j, n, part0, nparts)


def declare_colvec(ctx, name, ncols, nparts=128):
    p = ctx.p
    dv = p.dram("cv_" + name, [nparts, ncols], F32, kind="ExternalInput")
    col0 = ctx.smf_off
    ctx.smf_off += ncols
    v = ctx.SMF.v(col0, ncols, 0, nparts)
    emit_dma_load(ctx, "sync", v, dv, tag="cv." + name)
    cv = ColVec(ctx.SMF, col0, ncols)
    ctx.colvecs[name] = cv
    return cv




def emit_l0aa(ctx):
    XN = ctx.XN
    PSA, PSB, PSC, PST = ctx.PSA, ctx.PSB, ctx.PSC, ctx.PST
    W, AB = ctx.W, ctx.ACTB
    idb = ctx.const["IDB"]
    trilt = ctx.const["TRILT_B"]
    qkvb = ctx.colvecs["l0aa_qkvb"]
    projb = ctx.colvecs["l0aa_projb"]

    load_weight_bf16(ctx, ctx.win["L0_attn_a_qkv_w"].ap, ctx.win["L0_attn_a_qkv_w"].regs,
                     192, 1024, W, 0, tag="qkvw")
    emit_transpose_tiles(ctx, W, 0, 192, 1024, W, 2048, PST, tag="qkvwT")
    load_weight_bf16(ctx, ctx.win["L0_attn_a_proj_w"].ap, ctx.win["L0_attn_a_proj_w"].regs,
                     1024, 1024, W, 4096, tag="projw")

    for c in range(4):
        for k in range(8):
            lA = wt_tile(W, 2048, 192, k, 128, 0, 128)
            lB = wt_tile(W, 2048, 192, k, 128, 128, 64)
            emit_mm(ctx, PSA.v(0, 512), lA, XN.v(2048 * k + 512 * c, 512), k == 0, k == 7, tag="qkv.A")
            emit_mm(ctx, PSB.v(0, 512, 0, 64), lB, XN.v(2048 * k + 512 * c, 512), k == 0, k == 7, tag="qkv.B")
        emit_ts(ctx, AB.v(512 * c, 512, 0, 64), PSA.v(0, 512, 0, 64),
                qkvb.col(0, 0, 64), None, OP.add, tag="qkv.evq")
        emit_ts(ctx, AB.v(2048 + 512 * c, 512, 0, 64), PSA.v(0, 512, 64, 64),
                qkvb.col(0, 64, 64), None, OP.add, tag="qkv.evk")
        emit_ts(ctx, AB.v(4096 + 512 * c, 512, 0, 64), PSB.v(0, 512, 0, 64),
                qkvb.col(1, 0, 64), None, OP.add, tag="qkv.evv")

    qT = lambda c0, n: AB.v(c0, n, 0, 64)
    kTv = lambda c0, n: AB.v(2048 + c0, n, 0, 64)
    vT = lambda c0, n: AB.v(4096 + c0, n, 0, 64)

    state_ps = PSC.v(0, 64, 0, 64)
    for c in range(16):
        c0 = 128 * c
        ktok = AB.v(8192, 64)
        vtok = AB.v(8256, 64)
        emit_transpose(ctx, PST.v(0, 64), kTv(c0, 128), idb, tag="laa.tk")
        emit_transpose(ctx, PST.v(64, 64), vT(c0, 128), idb, tag="laa.tv")
        emit_copy(ctx, "scalar", ktok, PST.v(0, 64), tag="laa.ek")
        emit_copy(ctx, "scalar", vtok, PST.v(64, 64), tag="laa.ev")
        psS = PSA.v(0, 128)
        emit_mm(ctx, psS, kTv(c0, 128), qT(c0, 128), True, True, tag="laa.S")
        smask = AB.v(8320, 128)
        emit_tt(ctx, smask, psS, trilt, OP.mult, tag="laa.mask")
        psO = PSB.v(512, 128, 0, 64)
        state_sb = AB.v(8448, 64, 0, 64)
        emit_mm(ctx, psO, vtok, smask, True, c == 0, tag="laa.intra")
        if c > 0:
            emit_mm(ctx, psO, state_sb, qT(c0, 128), False, True, tag="laa.inter")
        oT = AB.v(6144 + c0, 128, 0, 64)
        emit_copy(ctx, "vector", oT, psO, tag="laa.eo")
        emit_mm(ctx, state_ps, ktok, vtok, c == 0, c == 15, tag="laa.state")
        emit_copy(ctx, "vector", state_sb, state_ps, tag="laa.es")

    for k in range(8):
        b = 4096 + 1024 * k
        for w in (512, 256, 128, 64):
            a0 = W.v(b, w)
            a1 = W.v(b + w, w)
            emit_tt(ctx, a0, a0, a1, OP.add, tag="weff.sum")
    for k in range(8):
        b = 4096 + 1024 * k
        emit_transpose(ctx, PST.v(0, 128, 0, 64), W.v(b, 64), idb, tag="weff.t")
        emit_copy(ctx, "vector", W.v(12288 + 128 * k, 128, 0, 64), PST.v(0, 128, 0, 64), tag="weff.ev")

    for f in range(8):
        weffT = W.v(12288 + 128 * f, 128, 0, 64)
        for c in range(4):
            ps = PSA.v(512 + 512 * (c % 2), 512)
            emit_mm(ctx, ps, weffT, AB.v(6144 + 512 * c, 512, 0, 64), True, True, tag="out.mm")
            emit_xadd(ctx, f, 512 * c, 512, ps, projb.col(f), OP.add, tag="l0aa.res")



CONST_NAMES = ["IDB", "ID32", "ONESC_B", "ONESC_F", "ONESR_B", "ONESR_F",
               "TRILT_B", "CMASK_B", "EHS16"]


def make_base_ctx(nc, w_cols=14336, ab_cols=8704, actf_cols=2048, consts=None, sm_pool=True, no_x=False):
    p = Prog(nc)
    ctx = Ctx(p)
    ctx.win = {}
    ctx.colvecs = {}
    ctx.smf_off = 0
    ctx.xr_slot = 0
    ctx.XN = p.sbuf("XN", 8 * 2048, BF16)
    ctx.W = p.sbuf("Wp", w_cols, BF16)
    if ab_cols:
        ctx.ACTB = p.sbuf("ACTB", ab_cols, BF16)
    ctx.ACTF = p.sbuf("ACTF", max(actf_cols, 2048), F32)
    ctx.XS = p.sbuf("XSp", 2048, F32)
    ctx.HCF = p.sbuf("HCFp", 2048, F32)
    ctx.XR = p.sbuf("XRp", 1024, F32)
    ctx.LNSQ = p.sbuf("LNSQ", 512, BF16)
    ctx.SMF = p.sbuf("SMF", 256, F32)
    ctx.PSA = p.psum("PSA", 2048, F32)
    ctx.PSB = p.psum("PSB", 1024, F32)
    ctx.PSC = p.psum("PSC", 512, F32)
    ctx.PST = p.psum("PST", 1024, BF16)
    emit_load_consts(ctx, consts if consts is not None else CONST_NAMES)
    return ctx


def declare_weights(ctx, names_shapes_dtypes):
    for name, shape, dt in names_shapes_dtypes:
        ctx.win[name] = ctx.p.dram(name, list(shape), dt, kind="ExternalInput")


LAST_CTX = None


def build_prog_l0aa():
    global LAST_CTX
    nc = bass.Bass("TRN2", target_bir_lowering=False)
    ctx = make_base_ctx(nc)
    LAST_CTX = ctx
    p = ctx.p
    x_in = p.dram("x_in", [1024, 2048], F32, kind="ExternalInput")
    x_out = p.dram("x_out", [1024, 2048], F32, kind="ExternalOutput")
    declare_weights(ctx, [("L0_attn_a_qkv_w", (192, 1024), F32),
                          ("L0_attn_a_proj_w", (1024, 1024), F32)])
    declare_colvec(ctx, "l0aa_qkvb", 2)
    declare_colvec(ctx, "l0aa_projb", 8)
    ctx.x_src, ctx.x_dst = x_in, x_out
    emit_ln_hbm(ctx, x_in)
    emit_l0aa(ctx)
    p.finalize()
    return nc


def host_l0aa_inputs(inputs):
    qkvb = np.zeros((128, 2), np.float32)
    qkvb[0:64, 0] = inputs["L0_attn_a_qkv_b"][0:64]
    qkvb[64:128, 0] = inputs["L0_attn_a_qkv_b"][64:128]
    qkvb[0:64, 1] = inputs["L0_attn_a_qkv_b"][128:192]
    projb = np.ascontiguousarray(inputs["L0_attn_a_proj_b"].reshape(8, 128).T)
    return {"L0_attn_a_qkv_w": inputs["L0_attn_a_qkv_w"],
            "L0_attn_a_proj_w": inputs["L0_attn_a_proj_w"],
            "cv_l0aa_qkvb": qkvb, "cv_l0aa_projb": projb}


def host_const_inputs():
    return {"c_" + n: a for n, a in host_constants().items()}



def emit_l0ffn_phaseW(ctx, W):
    p = ctx.p
    PST = ctx.PST
    idb = ctx.const["IDB"]
    wguT = p.dram("sc_wguT", [24, 128, 2048], BF16)
    wdT = p.dram("sc_wdT", [8, 128, 3072], BF16)
    ctx.ffn_cache = (wguT, wdT)
    for half, dv in ((0, ctx.win["L0_ffn_w_gate"]),
                     (1, ctx.win["L0_ffn_w_up"])):
        for m in range(24):
            raw = W.v(0, 1024)
            src = View(dv.ap[128 * m:128 * (m + 1), :], dv.regs)
            emit_dma_load(ctx, "gpsimd", raw, src, tag=f"ffn.w{half}{m}")
            for c in range(8):
                emit_transpose(ctx, PST.v(128 * c, 128), W.v(128 * c, 128), idb,
                               tag="ffn.wT")
            wt = W.v(1024, 1024)
            emit_copy(ctx, "vector", wt, PST.v(0, 1024), tag="ffn.wev")
            dst = View(wguT.ap[m][:, 1024 * half:1024 * (half + 1)], wguT.regs)
            emit_dma_store(ctx, "sync", dst, wt, tag="ffn.wsp")
    for f in range(8):
        raw = W.v(0, 3072)
        src = View(ctx.win["L0_ffn_w_down"].ap[128 * f:128 * (f + 1), :],
                   ctx.win["L0_ffn_w_down"].regs)
        emit_dma_load(ctx, "gpsimd", raw, src, tag=f"ffn.wd{f}")
        for mb in range(3):
            for j in range(8):
                emit_transpose(ctx, PST.v(128 * j, 128),
                               W.v(128 * (8 * mb + j), 128), idb, tag="ffn.wdT")
            wt = W.v(3072, 1024)
            emit_copy(ctx, "vector", wt, PST.v(0, 1024), tag="ffn.wdev")
            dst = View(wdT.ap[f, :, 1024 * mb:1024 * (mb + 1)], wdT.regs)
            emit_dma_store(ctx, "sync", dst, wt, tag="ffn.wdsp")


def emit_l0ffn(ctx):
    p = ctx.p
    XN = ctx.XN
    PSA, PSB, PST = ctx.PSA, ctx.PSB, ctx.PST
    W, W2, H = ctx.W, ctx.W2, ctx.H
    idb = ctx.const["IDB"]

    if getattr(ctx, "ffn_cache", None) is None:
        emit_l0ffn_phaseW(ctx, W)
    wguT, wdT = ctx.ffn_cache

    for tc in range(4):
        t0 = 512 * tc
        for m in range(24):
            s = m % 3
            vgu = W2.v(2048 * s, 2048)
            emit_dma_load(ctx, "sync", vgu, View(wguT.ap[m], wguT.regs), tag="ffn.lg")
            pg = PSA.v(0, 512)
            pu = PSA.v(512, 512)
            for k in range(8):
                emit_mm(ctx, pg, W2.v(2048 * s + 128 * k, 128),
                        XN.v(2048 * k + t0, 512), k == 0, k == 7, tag="ffn.g")
                emit_mm(ctx, pu, W2.v(2048 * s + 1024 + 128 * k, 128),
                        XN.v(2048 * k + t0, 512), k == 0, k == 7, tag="ffn.u")
            sg = ctx.ACTB.v(0, 512)
            emit_act(ctx, sg, pg, AF.Silu, scale=1.0 / 127.0, tag="ffn.silu")
            emit_tt(ctx, H.v(512 * m, 512), sg, pu, OP.mult, tag="ffn.h")
        for f in range(8):
            s = f % 2
            wv = W2.v(6144 + 3072 * s, 3072)
            emit_dma_load(ctx, "sync", wv, View(wdT.ap[f], wdT.regs), tag="ffn.lwd")
            pd = PSB.v(0, 512)
            for m in range(24):
                lhsT = W2.v(6144 + 3072 * s + 128 * m, 128)
                emit_mm(ctx, pd, lhsT, H.v(512 * m, 512),
                        m == 0, m == 23, tag="ffn.d")
            emit_xadd(ctx, f, t0, 512, pd, 1.0 / (127.0 * 127.0),
                      OP.mult, tag="ffn.res")


def build_prog_l0ffn():
    global LAST_CTX
    nc = bass.Bass("TRN2", target_bir_lowering=False)
    ctx = make_base_ctx(nc, w_cols=4096, ab_cols=512, actf_cols=4096)
    LAST_CTX = ctx
    p = ctx.p
    ctx.W2 = p.sbuf("W2p", 12288, BF16)
    ctx.H = p.sbuf("Hp", 12288, BF16)
    x_in = p.dram("x_in", [1024, 2048], F32, kind="ExternalInput")
    x_out = p.dram("x_out", [1024, 2048], F32, kind="ExternalOutput")
    declare_weights(ctx, [("L0_ffn_w_gate", (3072, 1024), I32),
                          ("L0_ffn_w_up", (3072, 1024), I32),
                          ("L0_ffn_w_down", (1024, 3072), I32)])
    ctx.x_src, ctx.x_dst = x_in, x_out
    emit_ln_hbm(ctx, x_in)
    emit_l0ffn(ctx)
    p.finalize()
    return nc


def host_l0ffn_inputs(inputs):
    return {k: inputs[k] for k in ("L0_ffn_w_gate", "L0_ffn_w_up", "L0_ffn_w_down")}



def emit_l0ab(ctx, M0=16.0):
    p = ctx.p
    XN = ctx.XN
    PSA, PSB, PST = ctx.PSA, ctx.PSB, ctx.PST
    W = ctx.W
    QH, KH, VAUG, YH = ctx.QH, ctx.KH, ctx.VAUG, ctx.YH
    ATTN = ctx.ATTN
    idb = ctx.const["IDB"]
    cmask = ctx.const["CMASK_B"]
    onesr = ctx.const["ONESR_F"]
    onesb = ctx.const["ONESC_B"]

    for g in range(2):
        for wi, (wname, dstpool) in enumerate((("L0_attn_b_Wq", QH), ("L0_attn_b_Wk", KH))):
            sl = (2 * g + wi) % 2
            dv = ctx.win[wname]
            for r in range(4):
                src = View(dv.ap[512 * g + 128 * r:512 * g + 128 * (r + 1), :], dv.regs)
                emit_dma_load(ctx, "gpsimd", W.v(8192 * sl + 1024 * r, 1024), src, tag="ab.wl")
            emit_transpose_tiles(ctx, W, 8192 * sl, 512, 1024, W, 8192 * sl + 4096, PST,
                                 tag="ab.wT")
            for i in range(4):
                for c in range(4):
                    ps = PSA.v(512 * (c % 2), 512)
                    for k in range(8):
                        lhsT = wt_tile(W, 8192 * sl + 4096, 512, k, 128, 128 * i, 128)
                        emit_mm(ctx, ps, lhsT, XN.v(2048 * k + 512 * c, 512),
                                k == 0, k == 7, tag="ab.qk")
                    emit_copy(ctx, "vector", dstpool.v(2048 * i + 512 * c, 512), ps, tag="ab.qkev")
        dv = ctx.win["L0_attn_b_Wv"]
        for r in range(4):
            src = View(dv.ap[512 * g + 128 * r:512 * g + 128 * (r + 1), :], dv.regs)
            emit_dma_load(ctx, "gpsimd", W.v(1024 * r, 1024), src, tag="ab.wvl")
        emit_transpose_tiles(ctx, W, 0, 512, 1024, W, 4096, PST, tag="ab.wvT")
        ones_dst = View(VAUG.t[0:128, VAUG.off:VAUG.off + 520 * 16]
                        .rearrange("p (j h o) -> p j h o", h=8, o=65)[:, :, :, 64:65],
                        [(VAUG.name, VAUG.off, VAUG.off + 520 * 16)])
        ones_src = View(onesb.ap.unsqueeze(1).to_broadcast([128, 16, 8, 1]), onesb.regs)
        p.op("vector", lambda q, o=ones_dst.ap, i=ones_src.ap: q.tensor_copy(o, i),
             reads=[ones_src], writes=[ones_dst], tag="ab.vones")
        for tj in range(16):
            ps = PSA.v(0, 512)
            for k in range(8):
                rhs = wt_tile(W, 4096, 512, k, 128, 0, 512)
                emit_mm(ctx, ps, XN.v(2048 * k + 128 * tj, 128), rhs,
                        k == 0, k == 7, tag="ab.v")
            dst = View(VAUG.t[0:128, VAUG.off + 520 * tj:VAUG.off + 520 * tj + 520]
                       .rearrange("p (h o) -> p h o", h=8)[:, :, 0:64],
                       [(VAUG.name, VAUG.off + 520 * tj, VAUG.off + 520 * tj + 520)])
            srcps = View(PSA.t[0:128, 0:512].rearrange("p (h o) -> p h o", h=8),
                         [(PSA.name, 0, 512)])
            p.op("vector", lambda q, o=dst.ap, i=srcps.ap: q.tensor_copy(o, i),
                 reads=[srcps], writes=[dst], tag="ab.vev")
        dv = ctx.win["L0_attn_b_Wo"]
        for r in range(8):
            src = View(dv.ap[128 * r:128 * (r + 1), 512 * g:512 * (g + 1)], dv.regs)
            emit_dma_load(ctx, "gpsimd", W.v(512 * r, 512), src, tag="ab.wol")
        emit_transpose_tiles(ctx, W, 0, 1024, 512, W, 4096, PST, tag="ab.woT")
        for c in range(4):
            for hp in range(4):
                hpair = (2 * hp, 2 * hp + 1)
                nj = 4 * c + 4

                def atslot(j, i):
                    return (ATTN.v(512 * (j % 2), 512) if i == 0
                            else W.v(512 * (j % 2), 512))

                def a_front(j, i, c=c):
                    hh = hpair[i]
                    kblk, kbase = hh // 2, 64 * (hh % 2)
                    psS = PSA.v(512 * (2 * i + j % 2), 512)
                    emit_mm(ctx, psS,
                            KH.v(2048 * kblk + 128 * j, 128, kbase, 64),
                            QH.v(2048 * kblk + 512 * c, 512, kbase, 64),
                            True, True, tag="ab.s")
                    at = atslot(j, i)
                    emit_act(ctx, at, psS, AF.Exp, bias=-M0, scale=0.125, tag="ab.exp")
                    if j >= 4 * c:
                        emit_tt(ctx, at, at,
                                View(cmask.ap.tensor[0:128, cmask.regs[0][1] + 512 * (j - 4 * c):
                                     cmask.regs[0][1] + 512 * (j - 4 * c + 1)], cmask.regs),
                                OP.mult, tag="ab.mask")

                def a_av(j, i, nj=nj):
                    hh = hpair[i]
                    emit_mm(ctx, PSB.v(512 * i, 512, 0, 65),
                            VAUG.v(520 * j + 65 * hh, 65), atslot(j, i),
                            j == 0, j == nj - 1, tag="ab.av")

                for j in range(nj):
                    a_front(j, 0)
                    a_front(j, 1)
                    if j > 0:
                        a_av(j - 1, 0)
                        a_av(j - 1, 1)
                a_av(nj - 1, 0)
                a_av(nj - 1, 1)
                for i in range(2):
                    hh = hpair[i]
                    kblk, kbase = hh // 2, 64 * (hh % 2)
                    rl = ctx.XS.v(512 * i, 512, 0, 1)
                    p.op("vector", lambda q, o=rl.ap, i_=PSB.t[64:65, 512 * i:512 * i + 512]:
                         q.reciprocal(o, i_),
                         reads=[PSB.v(512 * i, 512, 64, 1)], writes=[rl], tag="ab.rl")
                    psb2 = (ctx.PSC.v(0, 512, 0, 64) if i == 0
                            else PSA.v(0, 512, 0, 64))
                    emit_mm(ctx, psb2, View(onesr.ap[0:1, 0:64], onesr.regs), rl,
                            True, True, tag="ab.bc")
                    rlb = (ATTN.v(1024, 512, 0, 64) if i == 0
                           else W.v(1024, 512, 0, 64))
                    emit_copy(ctx, "scalar", rlb, psb2, tag="ab.rlbev")
                    emit_tt(ctx, YH.v(512 * kblk, 512, kbase, 64),
                            PSB.v(512 * i, 512, 0, 64), rlb, OP.mult, tag="ab.y")
            for f in range(8):
                ps = PSA.v(1024, 512)
                for i in range(4):
                    lhsT = wt_tile(W, 4096, 1024, i, 128, 128 * f, 128)
                    emit_mm(ctx, ps, lhsT, YH.v(512 * i, 512), i == 0, i == 3, tag="ab.o")
                emit_xadd(ctx, f, 512 * c, 512, ps, tag="l0ab.res")


def build_prog_l0ab():
    global LAST_CTX
    nc = bass.Bass("TRN2", target_bir_lowering=False)
    ctx = make_base_ctx(nc, w_cols=16384, ab_cols=512, actf_cols=4096)
    LAST_CTX = ctx
    p = ctx.p
    ctx.QH = p.sbuf("QH", 8192, BF16)
    ctx.KH = p.sbuf("KH", 8192, BF16)
    ctx.VAUG = p.sbuf("VAUGp", 520 * 16, BF16)
    ctx.YH = p.sbuf("YHp", 2048, BF16)
    ctx.ATTN = p.sbuf("ATTNp", 1536, BF16)
    x_in = p.dram("x_in", [1024, 2048], F32, kind="ExternalInput")
    x_out = p.dram("x_out", [1024, 2048], F32, kind="ExternalOutput")
    declare_weights(ctx, [("L0_attn_b_Wq", (1024, 1024), F32),
                          ("L0_attn_b_Wk", (1024, 1024), F32),
                          ("L0_attn_b_Wv", (1024, 1024), F32),
                          ("L0_attn_b_Wo", (1024, 1024), F32)])
    ctx.x_src, ctx.x_dst = x_in, x_out
    emit_ln_hbm(ctx, x_in)
    emit_l0ab(ctx)
    p.finalize()
    return nc


def host_l0ab_inputs(inputs):
    return {k: inputs[k] for k in ("L0_attn_b_Wq", "L0_attn_b_Wk",
                                   "L0_attn_b_Wv", "L0_attn_b_Wo")}



def grouped(v, g=16):
    return View(v.ap.rearrange("p (g k) -> p g k", g=g), v.regs)


def gslice(v, k0, k1, g=16):
    ap = v.ap.rearrange("p (g k) -> p g k", g=g)
    return View(ap[:, :, k0:k1], v.regs)


def emit_l1ffn(ctx, idx):
    p = ctx.p
    XN = ctx.XN
    PSA, PSB, PSC, PST = ctx.PSA, ctx.PSB, ctx.PSC, ctx.PST
    W, SCOR, QB, F2, SF, RN, ACC = (ctx.W, ctx.SCOR, ctx.QB, ctx.F2, ctx.SF,
                                    ctx.RN, ctx.ACC)
    idb = ctx.const["IDB"]
    id32 = ctx.const["ID32"]
    ehs = ctx.const["EHS16"]
    bq = ctx.colvecs["l1ffn_bq"]
    gam = ctx.colvecs["l1ffn_gam"]
    bet = ctx.colvecs["l1ffn_bet"]

    scr = getattr(ctx, "l1ffn_scr", None)
    if scr is None:
        scr = p.dram("sc_qT", [16, 128, 2048], BF16)
        ctx.l1ffn_scr = scr
    cc_in = p.dram(f"cc_in{idx}", [128, 128], F32)
    cc_out = p.dram(f"cc_out{idx}", [128, 128], F32)
    ccv_in = View(cc_in.ap, [(f"dram:cc_in{idx}", 0, 128)])
    ccv_out = View(cc_out.ap, [(f"dram:cc_out{idx}", 0, 128)])

    load_weight_bf16(ctx, ctx.win["L1_ffn_Wq"].ap, ctx.win["L1_ffn_Wq"].regs,
                     1024, 2048, W, 0, tag="ff.wq")
    for m in range(16):
        for ch in range(2):
            for c2 in range(2):
                c = 2 * ch + c2
                ps = PSA.v(1024 * ch + 512 * c2, 512)
                for k in range(8):
                    lhsT = wtile(W, 0, 2048, k, 128, 128 * m, 128)
                    emit_mm(ctx, ps, lhsT, XN.v(2048 * k + 512 * c, 512),
                            k == 0, k == 7, tag="ff.q")
            qb = QB.v(2048 * ch + 1024 * (m % 2), 1024)
            emit_act(ctx, qb, PSA.v(1024 * ch, 1024), AF.Identity, bias=bq.col(m),
                     accum=ACC.v(m + 16 * ch, 1), tag="ff.qev")
            emit_act(ctx, SF.v(1024 * ch, 1024), PSA.v(1024 * ch, 1024), AF.Square,
                     bias=bq.col(m), accum=ACC.v(32 + m + 16 * ch, 1), tag="ff.qsq")
            emit_dma_store(ctx, "sync",
                           View(scr.ap[m][:, 1024 * ch:1024 * (ch + 1)], scr.regs),
                           qb, tag="ff.spill")
    emit_tt(ctx, ACC.v(88, 16), ACC.v(0, 16), ACC.v(16, 16), OP.add, tag="ff.cmb")
    emit_tt(ctx, ACC.v(104, 16), ACC.v(32, 16), ACC.v(48, 16), OP.add, tag="ff.cmb2")
    for j, (base, par) in enumerate(((88, 0), (88, 1), (104, 0), (104, 1))):
        srcap = bass.AP(ACC.t, ACC.off + base + par, [[PCOLS(ACC), 128], [2, 8]])
        srcv = View(srcap, [(ACC.name, ACC.off + base, ACC.off + base + 16)])
        dstv = ACC.v(64 + j, 1)
        p.op("vector", lambda q, o=dstv.ap, i=srcap:
             q.reduce_sum(o, i, mybir.AxisListType.X),
             reads=[srcv], writes=[dstv], tag="ff.red")
    emit_dma_store(ctx, "gpsimd", View(cc_in.ap[:, 0:4], ccv_in.regs),
                   ACC.v(64, 4), tag="ff.ccst")
    p.op("gpsimd", lambda q, a=cc_in.ap, b=cc_out.ap: q.collective_compute(
        "AllReduce", OP.add, replica_groups=[list(range(NCORES))], ins=[a], outs=[b]),
        reads=[ccv_in], writes=[ccv_out], cc=True, tag="ff.cc")
    emit_dma_load(ctx, "gpsimd", ACC.v(68, 4), View(cc_out.ap[:, 0:4], ccv_out.regs), tag="ff.ccld")
    NTOT = 131072.0
    mean = ACC.v(72, 2)
    emit_ts(ctx, mean, ACC.v(68, 2), 1.0 / NTOT, None, OP.mult, tag="ff.mean")
    msq = ACC.v(74, 2)
    emit_tt(ctx, msq, mean, mean, OP.mult, tag="ff.msq")
    var = ACC.v(76, 2)
    emit_ts(ctx, var, ACC.v(70, 2), 1.0 / NTOT, None, OP.mult, tag="ff.var0")
    emit_tt(ctx, var, var, msq, OP.subtract, tag="ff.var")
    sd = ACC.v(78, 2)
    emit_act(ctx, sd, var, AF.Sqrt, bias=1e-5, tag="ff.sd")
    rsd = ACC.v(80, 2)
    p.op("vector", lambda q, o=rsd.ap, i=sd.ap: q.reciprocal(o, i),
         reads=[sd], writes=[rsd], tag="ff.rsd")
    bnsc = ACC.v(82, 2)
    emit_tt(ctx, bnsc, rsd, gam.cols(0, 2), OP.mult, tag="ff.bnsc")
    bnsh = ACC.v(84, 2)
    emit_stt(ctx, bnsh, mean, -1.0, bnsc, OP.mult, OP.mult, tag="ff.bnsh0")
    emit_tt(ctx, bnsh, bnsh, bet.cols(0, 2), OP.add, tag="ff.bnsh")

    for j, nm in enumerate(("L1_ffn_sub_key0", "L1_ffn_sub_key1")):
        kb = F2.v(8704 + 128 * j, 128, 0, 32)
        emit_dma_load(ctx, "gpsimd", kb, ctx.win[nm], tag="ff.sk")
        ksq = SF.v(0, 128, 0, 32)
        emit_act(ctx, ksq, kb, AF.Square, tag="ff.ksq")
        kss = SF.v(128, 1, 0, 32)
        p.op("vector", lambda q, o=kss.ap, i=ksq.ap: q.reduce_sum(o, i, mybir.AxisListType.X),
             reads=[ksq], writes=[kss], tag="ff.kss")
        ksd = SF.v(129, 1, 0, 32)
        emit_act(ctx, ksd, kss, AF.Sqrt, tag="ff.ksd")
        emit_ts(ctx, ksd, ksd, 1e-12, None, OP.max, tag="ff.kmx")
        krn = SF.v(130, 1, 0, 32)
        p.op("vector", lambda q, o=krn.ap, i=ksd.ap: q.reciprocal(o, i),
             reads=[ksd], writes=[krn], tag="ff.krc")
        emit_ts(ctx, kb, kb, krn, None, OP.mult, tag="ff.kn")
        emit_transpose(ctx, PST.v(0, 32), kb, idb, tag="ff.kT")
        emit_copy(ctx, "vector", F2.v(8960 + 32 * j, 32), PST.v(0, 32), tag="ff.kTev")
    k0T = F2.v(8960, 32)
    k1T = F2.v(8992, 32)

    for m in range(16):
        qb = QB.v(0, 2048)
        emit_dma_load(ctx, "sync", qb, View(scr.ap[m], scr.regs), tag="ff.qld")
        par = m % 2
        qbn = QB.v(0, 2048)
        emit_ts(ctx, qbn, qb, ACC.v(82 + par, 1), ACC.v(84 + par, 1), OP.mult, OP.add, tag="ff.bn")
        emit_act(ctx, F2.v(0, 2048), qbn, AF.Square, tag="ff.sq")
        for c in range(4):
            lhsT = View(ehs.ap.tensor[0:128, ehs.regs[0][1] + 16 * m:
                                      ehs.regs[0][1] + 16 * (m + 1)], ehs.regs)
            emit_mm(ctx, PSA.v(512 * c, 512, 0, 16), lhsT, F2.v(512 * c, 512),
                    m == 0, m == 15, tag="ff.l2")
        for tc in range(16):
            b = 512 * (tc % 2)
            sv = View(PSB.t[0:128, b:b + 32], [(PSB.name, b, b + 512)])
            emit_mm(ctx, sv, QB.v(128 * tc, 128), (k0T if par == 0 else k1T),
                    True, True, tag="ff.s")
            emit_copy(ctx, "vector", SCOR.v(512 * tc + 32 * m, 32), sv, tag="ff.sev")
    emit_copy(ctx, "vector", SF.v(0, 2048, 0, 16), PSA.v(0, 2048, 0, 16), tag="ff.ss")
    emit_act(ctx, SF.v(0, 2048, 0, 16), SF.v(0, 2048, 0, 16), AF.Sqrt, tag="ff.ssd")
    emit_ts(ctx, SF.v(0, 2048, 0, 16), SF.v(0, 2048, 0, 16), 1e-12, None, OP.max, tag="ff.smx")
    p.op("vector", lambda q, o=SF.v(2048, 2048, 0, 16).ap, i=SF.v(0, 2048, 0, 16).ap:
         q.reciprocal(o, i), reads=[SF.v(0, 2048, 0, 16)],
         writes=[SF.v(2048, 2048, 0, 16)], tag="ff.rn")
    for tc in range(16):
        ps = PSC.v(0, 16)
        emit_transpose(ctx, ps, SF.v(2048 + 128 * tc, 128, 0, 16), id32, tag="ff.rT")
        emit_copy(ctx, "vector", RN.v(16 * tc, 16), ps, tag="ff.rTev")

    emit_l1ffn_phaseB(ctx)


def emit_l1ffn_phaseB(ctx):
    p = ctx.p
    XN = ctx.XN
    PSA, PSB, PSC, PST = ctx.PSA, ctx.PSB, ctx.PSC, ctx.PST
    W, SCOR, F2, SF, RN = ctx.W, ctx.SCOR, ctx.F2, ctx.SF, ctx.RN
    idb = ctx.const["IDB"]

    load_weight_bf16(ctx, ctx.win["L1_ffn_expert_down"].ap,
                     ctx.win["L1_ffn_expert_down"].regs, 1024, 1024, W, 8192, tag="ff.dn")
    emit_transpose_tiles(ctx, W, 8192, 1024, 1024, W, 0, PST, tag="ff.dnT")
    load_weight_bf16(ctx, ctx.win["L1_ffn_expert_up"].ap,
                     ctx.win["L1_ffn_expert_up"].regs, 1024, 1024, W, 8192, tag="ff.up")

    def ffb_front(tc):
        t0 = 128 * tc
        o2 = 4608 * (tc % 2)
        for dc in range(2):
            ps = PSA.v(1024 * (tc % 2) + 512 * dc, 512)
            for k in range(8):
                rhs = wt_tile(W, 0, 1024, k, 128, 512 * dc, 512)
                emit_mm(ctx, ps, XN.v(2048 * k + t0, 128), rhs, k == 0, k == 7, tag="fb.eh")
            emit_act(ctx, F2.v(o2 + 2560 + 512 * dc, 512), ps, AF.Gelu, tag="fb.gelu")

    def ffb_chain(tc):
        o2 = 4608 * (tc % 2)
        s = SF.v(0, 512)
        emit_copy(ctx, "vector", s, SCOR.v(512 * tc, 512), tag="fb.s")
        wk = SF.v(512, 512)
        emit_copy(ctx, "vector", wk, SCOR.v(512 * tc, 512), tag="fb.wk")
        m1 = SF.v(1536, 16)
        for r in range(4):
            cur, curw = wk, 32
            dsts = [SF.v(1024, 256), SF.v(1280, 128), SF.v(1024, 64), SF.v(1280, 32),
                    (m1 if r == 3 else SF.v(1024, 16))]
            for step in range(5):
                half = curw // 2
                dst = dsts[step]
                emit_tt(ctx, grouped(dst), gslice(cur, 0, half), gslice(cur, half, curw),
                        OP.max, tag="fb.tm")
                cur, curw = dst, half
            if r < 3:
                mb = View(cur.ap.unsqueeze(2).to_broadcast([128, 16, 32]), cur.regs)
                eq = SF.v(1552, 512)
                emit_tt(ctx, grouped(eq), grouped(wk), mb, OP.is_ge, tag="fb.eq")
                pen = SF.v(2064, 512)
                emit_ts(ctx, pen, eq, 1e30, None, OP.mult, tag="fb.pen")
                emit_tt(ctx, wk, wk, pen, OP.subtract, tag="fb.msk")
        t4b = View(m1.ap.unsqueeze(2).to_broadcast([128, 16, 32]), m1.regs)
        ge = SF.v(1552, 512)
        emit_tt(ctx, grouped(ge), grouped(s), t4b, OP.is_ge, tag="fb.ge")
        for m in range(16):
            emit_act(ctx, SF.v(2576 + 32 * m, 32), SF.v(32 * m, 32), AF.Exp,
                     scale=RN.v(16 * tc + m, 1), tag="fb.exp")
        e = SF.v(2576, 512)
        z = SF.v(3088, 512)
        emit_tt(ctx, z, e, ge, OP.mult, tag="fb.z")
        cur, curw = z, 32
        for step, dst in enumerate([SF.v(3600, 256), SF.v(3856, 128), SF.v(3600, 64),
                                    SF.v(3856, 32), SF.v(3600, 16)]):
            half = curw // 2
            emit_tt(ctx, grouped(dst), gslice(cur, 0, half), gslice(cur, half, curw),
                    OP.add, tag="fb.zs")
            cur, curw = dst, half
        rz = SF.v(3984, 16)
        p.op("vector", lambda q, o=rz.ap, i=cur.ap: q.reciprocal(o, i),
             reads=[cur], writes=[rz], tag="fb.rz")
        pb = z
        rzb = View(rz.ap.unsqueeze(2).to_broadcast([128, 16, 32]), rz.regs)
        emit_tt(ctx, grouped(pb), grouped(z), rzb, OP.mult, tag="fb.p")
        Wb = SF.v(1024, 1024)
        tmp = SF.v(2064, 1024)
        for h in range(8):
            p0 = View(bass.AP(SF.t, SF.off + 3088 + 64 * h,
                              [[PCOLS(SF), 128], [1, 32], [0, 32]]), pb.regs)
            p1 = View(bass.AP(SF.t, SF.off + 3088 + 64 * h + 32,
                              [[PCOLS(SF), 128], [0, 32], [1, 32]]), pb.regs)
            dst = Wb if h == 0 else tmp
            emit_tt(ctx, View(dst.ap.rearrange("p (a b) -> p a b", a=32), dst.regs),
                    p0, p1, OP.mult, tag="fb.w")
            if h > 0:
                emit_tt(ctx, Wb, Wb, tmp, OP.add, tag="fb.wa")
        cb = F2.v(o2 + 3584, 1024)
        emit_tt(ctx, cb, Wb, F2.v(o2 + 2560, 1024), OP.mult, tag="fb.c")

    def ffb_tail(tc):
        o2 = 4608 * (tc % 2)
        cq = tc % 4
        for e8 in range(8):
            emit_transpose(ctx, PST.v(128 * e8, 128), F2.v(o2 + 3584 + 128 * e8, 128),
                           idb, tag="fb.cT")
        for e8 in range(8):
            emit_copy(ctx, "vector", F2.v(9216 + 512 * e8 + 128 * cq, 128),
                      PST.v(128 * e8, 128), tag="fb.cTev")
        if cq == 3:
            tg0 = 512 * (tc // 4)
            for f in range(8):
                ps = PSB.v(0, 512)
                for k in range(8):
                    lhsT = wtile(W, 8192, 1024, k, 128, 128 * f, 128)
                    emit_mm(ctx, ps, lhsT, F2.v(9216 + 512 * k, 512), k == 0, k == 7,
                            tag="fb.up")
                emit_xadd(ctx, f, tg0, 512, ps, tag="l1ffn.res")

    for tc in range(16):
        ffb_front(tc)
        if tc > 0:
            ffb_tail(tc - 1)
        ffb_chain(tc)
    ffb_tail(15)


def build_prog_l1ffn(debug=False):
    global LAST_CTX
    nc = bass.Bass("TRN2", target_bir_lowering=False)
    ctx = make_base_ctx(nc, w_cols=16384, ab_cols=0, actf_cols=0, sm_pool=False,
                        consts=["IDB", "ID32", "ONESC_B", "ONESC_F", "ONESR_F", "EHS16"])
    LAST_CTX = ctx
    p = ctx.p
    ctx.SCOR = p.sbuf("SCOR", 8192, F32)
    ctx.QB = p.sbuf("QBp", 4096, BF16)
    ctx.F2 = p.sbuf("F2p", 13312, BF16)
    ctx.SF = p.sbuf("SFp", 4096, F32)
    ctx.RN = p.sbuf("RNp", 256, F32)
    ctx.ACC = p.sbuf("ACCp", 128, F32)
    x_in = p.dram("x_in", [1024, 2048], F32, kind="ExternalInput")
    x_out = p.dram("x_out", [1024, 2048], F32, kind="ExternalOutput")
    declare_weights(ctx, [("L1_ffn_Wq", (1024, 2048), F32),
                          ("L1_ffn_sub_key0", (32, 128), F32),
                          ("L1_ffn_sub_key1", (32, 128), F32),
                          ("L1_ffn_expert_down", (1024, 1024), F32),
                          ("L1_ffn_expert_up", (1024, 1024), F32)])
    declare_colvec(ctx, "l1ffn_bq", 16)
    declare_colvec(ctx, "l1ffn_gam", 2)
    declare_colvec(ctx, "l1ffn_bet", 2)
    ctx.x_src, ctx.x_dst = x_in, x_out
    emit_ln_hbm(ctx, x_in)
    emit_l1ffn(ctx, 0)
    if debug:
        dbg1 = p.dram("dbg_acc", [128, 96], F32, kind="ExternalOutput")
        emit_dma_store(ctx, "scalar", dbg1, ctx.ACC.v(0, 96), tag="dbg")
        dbg2 = p.dram("dbg_scor", [128, 8192], F32, kind="ExternalOutput")
        emit_dma_store(ctx, "scalar", dbg2, ctx.SCOR.v(0, 8192), tag="dbg")
        dbg3 = p.dram("dbg_rn", [128, 256], F32, kind="ExternalOutput")
        emit_dma_store(ctx, "scalar", dbg3, ctx.RN.v(0, 256), tag="dbg")
        dbg4 = p.dram("dbg_xn", [128, 2048], BF16, kind="ExternalOutput")
        emit_dma_store(ctx, "scalar", dbg4, ctx.XN.v(0, 2048), tag="dbg")
    p.finalize()
    return nc


def host_l1ffn_inputs(inputs):
    out = {k: inputs[k] for k in ("L1_ffn_Wq", "L1_ffn_sub_key0", "L1_ffn_sub_key1",
                                  "L1_ffn_expert_down", "L1_ffn_expert_up")}
    out["cv_l1ffn_bq"] = np.ascontiguousarray(inputs["L1_ffn_bq"].reshape(16, 128).T)
    out["cv_l1ffn_gam"] = np.ascontiguousarray(inputs["L1_ffn_bn_gamma"].reshape(2, 128).T)
    out["cv_l1ffn_bet"] = np.ascontiguousarray(inputs["L1_ffn_bn_beta"].reshape(2, 128).T)
    return out



def emit_l1aa(ctx):
    p = ctx.p
    XN = ctx.XN
    PSA, PSB, PSC, PST = ctx.PSA, ctx.PSB, ctx.PSC, ctx.PST
    W, QF, KF1, KF2, VT, STV, STF = (ctx.W, ctx.QF, ctx.KF1, ctx.KF2, ctx.VT,
                                     ctx.STV, ctx.STF)
    UN, MTOKB, MTOKF, FM, RB = ctx.UN, ctx.MTOKB, ctx.MTOKF, ctx.FM, ctx.RB
    idb = ctx.const["IDB"]
    trilt = ctx.const["TRILT_B"]
    onesr_b = ctx.const["ONESR_B"]
    onesc_f = ctx.const["ONESC_F"]
    nw = ctx.colvecs["l1aa_nw"]

    hbm_o = p.dram(ctx.name("aa_o"), [8, 128, 2048], F32)
    hbm_st = p.dram(ctx.name("aa_st"), [8, 128, 1024], F32)
    ov_o = View(hbm_o.ap, hbm_o.regs)

    for j, nm in enumerate(("L1_attn_a_fmq1_w", "L1_attn_a_fmq2_w",
                            "L1_attn_a_fmk1_w", "L1_attn_a_fmk2_w")):
        raw = W.v(0, 128)
        emit_dma_load(ctx, "gpsimd", raw, ctx.win[nm], tag="aa.fml")
        emit_transpose(ctx, PST.v(0, 128), raw, idb, tag="aa.fmT")
        emit_copy(ctx, "vector", FM.v(128 * j, 128), PST.v(0, 128), tag="aa.fmev")
    emit_ts(ctx, FM.v(0, 128), FM.v(0, 128), float(128.0 ** -0.5), None, OP.mult, tag="aa.fms")

    raw = W.v(0, 1024, 0, 8)
    emit_dma_load(ctx, "gpsimd", raw, ctx.win["L1_attn_a_gate_w"], tag="aa.gl")
    for k in range(8):
        emit_transpose(ctx, PST.v(8 * k, 8), W.v(128 * k, 128, 0, 8), idb, tag="aa.gT")
    emit_copy(ctx, "vector", W.v(1024, 64), PST.v(0, 64), tag="aa.gev")
    for tc in range(16):
        ps = PSC.v(0, 8)
        for k in range(8):
            emit_mm(ctx, ps, XN.v(2048 * k + 128 * tc, 128), W.v(1024 + 8 * k, 8),
                    k == 0, k == 7, tag="aa.logit")
        s = UN.v(0, 8)
        emit_copy(ctx, "vector", s, ps, tag="aa.lev")
        emit_tt(ctx, UN.v(8, 4), View(s.ap[:, 0:4], s.regs), View(s.ap[:, 4:8], s.regs), OP.max, tag="aa.m")
        emit_tt(ctx, UN.v(12, 2), UN.v(8, 2), UN.v(10, 2), OP.max, tag="aa.m")
        m1 = UN.v(16, 1)
        emit_tt(ctx, m1, UN.v(12, 1), UN.v(13, 1), OP.max, tag="aa.m")
        eq1 = UN.v(24, 8)
        emit_ts(ctx, eq1, s, m1, None, OP.is_ge, tag="aa.eq1")
        pen = UN.v(32, 8)
        emit_ts(ctx, pen, eq1, 1e30, None, OP.mult, tag="aa.pen")
        s2 = UN.v(40, 8)
        emit_tt(ctx, s2, s, pen, OP.subtract, tag="aa.s2")
        emit_tt(ctx, UN.v(8, 4), View(s2.ap[:, 0:4], s2.regs), View(s2.ap[:, 4:8], s2.regs), OP.max, tag="aa.m2")
        emit_tt(ctx, UN.v(12, 2), UN.v(8, 2), UN.v(10, 2), OP.max, tag="aa.m2")
        m2 = UN.v(17, 1)
        emit_tt(ctx, m2, UN.v(12, 1), UN.v(13, 1), OP.max, tag="aa.m2")
        eq2 = UN.v(32, 8)
        emit_ts(ctx, eq2, s2, m2, None, OP.is_ge, tag="aa.eq2")
        d12 = UN.v(18, 1)
        emit_tt(ctx, d12, m1, m2, OP.subtract, tag="aa.d")
        rw1 = UN.v(19, 1)
        emit_act(ctx, rw1, d12, AF.Sigmoid, tag="aa.rw1")
        rw2 = UN.v(20, 1)
        emit_act(ctx, rw2, d12, AF.Sigmoid, scale=-1.0, tag="aa.rw2")
        t1 = UN.v(40, 8)
        emit_ts(ctx, t1, eq1, rw1, None, OP.mult, tag="aa.t1")
        t2 = UN.v(48, 8)
        emit_ts(ctx, t2, eq2, rw2, None, OP.mult, tag="aa.t2")
        emit_tt(ctx, MTOKB.v(8 * tc, 8), t1, t2, OP.add, tag="aa.rw")
        emit_tt(ctx, MTOKF.v(8 * tc, 8), eq1, eq2, OP.add, tag="aa.mk")

    for g in range(2):
        t0 = 1024 * g
        for h in range(8):
            qwr = W.v(0, 1024)
            src = View(ctx.win["L1_attn_a_q_w"].ap[128 * h:128 * (h + 1), :],
                       ctx.win["L1_attn_a_q_w"].regs)
            emit_dma_load(ctx, "gpsimd", qwr, src, tag="aa.qwl")
            for fj in range(2):
                for dt in range(8):
                    psn = PSA.v(512 * fj + 128 * (dt % 4), 128)
                    emit_mm(ctx, psn, W.v(128 * dt, 128), FM.v(128 * fj, 128),
                            dt % 4 == 0, dt % 4 == 3, tag="aa.qc")
                    if dt % 4 == 3:
                        emit_copy(ctx, "vector", W.v(1024 + 1024 * fj + 128 * (dt - 3), 512),
                                  PSA.v(512 * fj, 512), tag="aa.qcev")
            for c in range(2):
                for fj in range(2):
                    ps = PSB.v(512 * fj, 512)
                    for k in range(8):
                        emit_mm(ctx, ps, W.v(1024 + 1024 * fj + 128 * k, 128),
                                XN.v(2048 * k + t0 + 512 * c, 512), k == 0, k == 7, tag="aa.qf")
                tmp = RB.v(1280, 512)
                emit_copy(ctx, "scalar", tmp, PSB.v(0, 512), tag="aa.qp1")
                emit_tt(ctx, QF.v(1024 * h + 512 * c, 512), tmp, PSB.v(512, 512),
                        OP.mult, tag="aa.qprod")
        for e in range(8):
            for h in range(8):
                kwr = W.v(0, 1024)
                src = View(ctx.win["L1_attn_a_k_w"].ap[e, 128 * h:128 * (h + 1), :],
                           ctx.win["L1_attn_a_k_w"].regs)
                emit_dma_load(ctx, "gpsimd", kwr, src, tag="aa.kwl")
                for fj in range(2):
                    for dt in range(8):
                        psn = PSA.v(512 * fj + 128 * (dt % 4), 128)
                        emit_mm(ctx, psn, W.v(128 * dt, 128), FM.v(256 + 128 * fj, 128),
                                dt % 4 == 0, dt % 4 == 3, tag="aa.kc")
                        if dt % 4 == 3:
                            emit_copy(ctx, "vector", W.v(1024 + 1024 * fj + 128 * (dt - 3), 512),
                                      PSA.v(512 * fj, 512), tag="aa.kcev")
                for c in range(2):
                    for fj in range(2):
                        ps = PSB.v(512 * fj, 512)
                        for k in range(8):
                            emit_mm(ctx, ps, W.v(1024 + 1024 * fj + 128 * k, 128),
                                    XN.v(2048 * k + t0 + 512 * c, 512), k == 0, k == 7, tag="aa.kf")
                    tmp = RB.v(1280, 512)
                    emit_copy(ctx, "scalar", tmp, PSB.v(0, 512), tag="aa.kp1")
                    emit_tt(ctx, KF1.v(1024 * h + 512 * c, 512), tmp, PSB.v(512, 512),
                            OP.mult, tag="aa.kprod")
            for cc in range(8):
                tcg = 8 * g + cc
                psr = View(PSA.t[0:1, 128 * cc:128 * (cc + 1)],
                           [(PSA.name, 512 * (cc // 4), 512 * (cc // 4) + 512)])
                emit_mm(ctx, psr, MTOKB.v(8 * tcg + e, 1), idb,
                        cc % 4 == 0, cc % 4 == 3, tag="aa.rrow")
            rrow = RB.v(1792, 1024, 0, 1)
            emit_copy(ctx, "vector", rrow, PSA.v(0, 1024, 0, 1), tag="aa.rrev")
            for c in range(2):
                psb = PSA.v(512 * (c % 2), 512)
                emit_mm(ctx, psb, onesr_b, RB.v(1792 + 512 * c, 512, 0, 1), True, True, tag="aa.rwb")
                emit_copy(ctx, "scalar", RB.v(512 * c, 512), psb, tag="aa.rwbev")
            for half in range(2):
                for r in range(4):
                    src = View(ctx.win["L1_attn_a_v_w"].ap[e, 512 * half + 128 * r:512 * half + 128 * (r + 1), :],
                               ctx.win["L1_attn_a_v_w"].regs)
                    emit_dma_load(ctx, "gpsimd", W.v(1024 * r, 1024), src, tag="aa.vwl")
                emit_transpose_tiles(ctx, W, 0, 1024, 1024, KF2, 0, PST,
                                     tag="aa.vwT", row0=512 * half, nrows=512)
            for tj in range(8):
                for n2 in range(2):
                    ps = PSB.v(512 * n2, 512)
                    for k in range(8):
                        rhs = wt_tile(KF2, 0, 1024, k, 128, 512 * n2, 512)
                        emit_mm(ctx, ps, XN.v(2048 * k + t0 + 128 * tj, 128), rhs,
                                k == 0, k == 7, tag="aa.v")
                    emit_ts(ctx, VT.v(1024 * tj + 512 * n2, 512), ps,
                            MTOKF.v(8 * (8 * g + tj) + e, 1), None, OP.mult, tag="aa.vev")
            if g == 1:
                emit_dma_load(ctx, "sync", STF.v(0, 1024),
                              View(hbm_st.ap[e], hbm_st.regs), tag="aa.stld")
                emit_copy(ctx, "vector", STV.v(0, 1024), STF.v(0, 1024), tag="aa.stcast0")
            KT = ctx.KT
            for h in range(8):
                for c8 in range(8):
                    srcv = KF1.v(1024 * h + 128 * c8, 128)
                    dstv = KT.v(1024 * h + 128 * c8, 128)
                    q = "sync" if (h + c8) % 2 == 0 else "scalar"
                    p.op(q, lambda qq, o=dstv.ap, i=srcv.ap:
                         qq.dma_start(out=o, in_=i, transpose=True),
                         reads=[srcv], writes=[dstv], dma=True, tag="aa.ktd")
            for grp in range(2):
                for ci in range(4):
                    c = 4 * grp + ci
                    c0 = 128 * c
                    first = (g == 0 and c == 0)

                    def bundle(hp, first=first, c=c, c0=c0, ci=ci, grp=grp):
                        stv = STV.v(128 * hp, 128)
                        stf = STF.v(128 * hp, 128)
                        ov = PSA.v(1024 + 512 * (hp % 2), 128)
                        vchunk = VT.v(1024 * c + 128 * hp, 128)
                        smask = RB.v(1024 + 128 * (hp % 4), 128)
                        emit_mm(ctx, ov, vchunk, smask, True, first, tag="aa.intra")
                        if not first:
                            emit_mm(ctx, ov, stv, QF.v(1024 * hp + c0, 128),
                                    False, True, tag="aa.inter")
                        stv_ps = PSB.v(512 * (hp % 2), 128)
                        emit_mm(ctx, stv_ps, KT.v(1024 * hp + c0, 128), vchunk,
                                True, True, tag="aa.state")
                        if first:
                            emit_copy(ctx, "vector", stf, stv_ps, tag="aa.stcp")
                        else:
                            emit_tt(ctx, stf, stf, stv_ps, OP.add, tag="aa.stadd")
                        emit_copy(ctx, "scalar", stv, stf, tag="aa.stcast")
                        emit_tt(ctx, UN.v(512 * hp + 128 * ci, 128), ov,
                                RB.v(512 * grp + 128 * ci, 128), OP.mult, tag="aa.ob")

                    for h in range(8):
                        psS = PSA.v(512 * (h % 2), 128)
                        emit_mm(ctx, psS, KF1.v(1024 * h + c0, 128), QF.v(1024 * h + c0, 128),
                                True, True, tag="aa.S")
                        emit_tt(ctx, RB.v(1024 + 128 * (h % 4), 128), psS, trilt,
                                OP.mult, tag="aa.smask")
                        if h >= 2:
                            bundle(h - 2)
                    bundle(6)
                    bundle(7)
                dst = View(hbm_o.ap[:, :, t0 + 512 * grp:t0 + 512 * (grp + 1)].rearrange("h p t -> p h t"),
                           hbm_o.regs)
                srcv = View(UN.t[0:128, UN.off:UN.off + 4096].rearrange("p (h t) -> p h t", h=8),
                            [(UN.name, UN.off, UN.off + 4096)])
                if e == 0:
                    p.op("gpsimd", lambda q, o=dst.ap, i=srcv.ap: q.dma_start(out=o, in_=i),
                         reads=[srcv], writes=[dst], dma=True, tag="aa.ost0")
                else:
                    p.op("gpsimd", lambda q, o=dst.ap, i=srcv.ap:
                         q.dma_start(out=o, in_=i, accum_op=OP.add),
                         reads=[srcv], writes=[dst], dma=True, tag="aa.ost")
            if g == 0:
                emit_dma_store(ctx, "sync", View(hbm_st.ap[e], hbm_st.regs),
                               STF.v(0, 1024), tag="aa.stsp")
        for r in range(2):
            for rr_ in range(4):
                src = View(ctx.win["L1_attn_a_o_w"].ap[512 * r + 128 * rr_:512 * r + 128 * (rr_ + 1), :],
                           ctx.win["L1_attn_a_o_w"].regs)
                emit_dma_load(ctx, "gpsimd", W.v(1024 * rr_, 1024), src, tag="aa.owl")
            emit_transpose_tiles(ctx, W, 0, 1024, 1024, KF2, 0, PST,
                                 tag="aa.owT", row0=512 * r, nrows=512)
        for h in range(8):
            ot = UN.v(0, 1024)
            emit_dma_load(ctx, "sync", ot,
                          View(hbm_o.ap[h, :, t0:t0 + 1024], hbm_o.regs), tag="aa.old")
            sq = UN.v(1024, 1024)
            emit_act(ctx, sq, ot, AF.Square, tag="aa.osq")
            for c in range(2):
                ps = View(PSA.t[0:1, 512 * c:512 * (c + 1)], [(PSA.name, 512 * c, 512 * (c + 1))])
                emit_mm(ctx, ps, onesc_f, UN.v(1024 + 512 * c, 512), True, True, tag="aa.oss")
            rr2 = UN.v(2048, 1024, 0, 1)
            emit_copy(ctx, "vector", rr2, PSA.v(0, 1024, 0, 1), tag="aa.ossev")
            emit_act(ctx, rr2, rr2, AF.Ln, scale=1.0 / 128.0, bias=1e-5, tag="aa.oln")
            emit_act(ctx, rr2, rr2, AF.Exp, scale=-0.5, tag="aa.orst")
            for c in range(2):
                psb = PSA.v(1024 + 512 * (c % 2), 512)
                emit_mm(ctx, psb, ctx.const["ONESR_F"], UN.v(2048 + 512 * c, 512, 0, 1),
                        True, True, tag="aa.rbc")
                emit_stt(ctx, QF.v(1024 * h + 512 * c, 512), UN.v(512 * c, 512),
                         nw.col(0), psb, OP.mult, OP.mult, tag="aa.onrm")
        for f in range(8):
            for c in range(2):
                ps = PSB.v(0, 512)
                for k in range(8):
                    lhsT = wt_tile(KF2, 0, 1024, k, 128, 128 * f, 128)
                    emit_mm(ctx, ps, lhsT, QF.v(1024 * k + 512 * c, 512), k == 0, k == 7, tag="aa.ow")
                emit_xadd(ctx, f, t0 + 512 * c, 512, ps, tag="l1aa.res")


def build_prog_l1aa():
    global LAST_CTX
    nc = bass.Bass("TRN2", target_bir_lowering=False)
    ctx = make_base_ctx(nc, w_cols=4096, ab_cols=0, actf_cols=0, sm_pool=False,
                        consts=["IDB", "TRILT_B", "ONESR_B", "ONESC_B", "ONESC_F", "ONESR_F"])
    LAST_CTX = ctx
    p = ctx.p
    ctx.QF = p.sbuf("QFp", 8192, BF16)
    ctx.KF1 = p.sbuf("KF1p", 8192, BF16)
    ctx.KF2 = p.sbuf("KF2p", 8192, BF16)
    ctx.VT = p.sbuf("VTp", 8192, BF16)
    ctx.STV = p.sbuf("STVp", 1024, BF16)
    ctx.STF = p.sbuf("STFp", 1024, F32)
    ctx.UN = p.sbuf("UNp", 6144, F32)
    ctx.MTOKB = p.sbuf("MTOKBp", 128, BF16)
    ctx.MTOKF = p.sbuf("MTOKFp", 128, F32)
    ctx.FM = p.sbuf("FMp", 512, BF16)
    ctx.RB = p.sbuf("RBp", 2816, BF16)
    ctx.KT = p.sbuf("KTp", 8192, BF16)

    class _Shift:
        def __init__(self, pool, off):
            self.pool, self.off = pool, off
        def v(self, c0, n, p0=0, np_=None):
            return self.pool.v(self.off + c0, n, p0, np_ if np_ is not None else 128)
    ctx.ACTF = _Shift(ctx.UN, 4096)
    x_in = p.dram("x_in", [1024, 2048], F32, kind="ExternalInput")
    x_out = p.dram("x_out", [1024, 2048], F32, kind="ExternalOutput")
    declare_weights(ctx, [("L1_attn_a_gate_w", (8, 1024), F32),
                          ("L1_attn_a_q_w", (1024, 1024), F32),
                          ("L1_attn_a_k_w", (8, 1024, 1024), F32),
                          ("L1_attn_a_v_w", (8, 1024, 1024), F32),
                          ("L1_attn_a_fmq1_w", (128, 128), F32),
                          ("L1_attn_a_fmq2_w", (128, 128), F32),
                          ("L1_attn_a_fmk1_w", (128, 128), F32),
                          ("L1_attn_a_fmk2_w", (128, 128), F32),
                          ("L1_attn_a_o_w", (1024, 1024), F32)])
    declare_colvec(ctx, "l1aa_nw", 1)
    ctx.x_src, ctx.x_dst = x_in, x_out
    emit_ln_hbm(ctx, x_in)
    emit_l1aa(ctx)
    p.finalize()
    return nc


def host_l1aa_inputs(inputs):
    out = {k: inputs[k] for k in ("L1_attn_a_gate_w", "L1_attn_a_q_w",
                                  "L1_attn_a_k_w", "L1_attn_a_v_w",
                                  "L1_attn_a_fmq1_w", "L1_attn_a_fmq2_w",
                                  "L1_attn_a_fmk1_w", "L1_attn_a_fmk2_w",
                                  "L1_attn_a_o_w")}
    out["cv_l1aa_nw"] = np.ascontiguousarray(inputs["L1_attn_a_norm_w"].reshape(128, 1))
    return out



class RePack:

    def __init__(self, scr, S, nk, ng):
        self.scr, self.S, self.nk, self.ng = scr, S, nk, ng


def emit_repack(ctx, dram_view, rows, cols, col0, ncols, tag, S=4, row0=0, nrows=None,
                qs=("scalar", "gpsimd")):
    p = ctx.p
    nrows = rows if nrows is None else nrows
    nk = nrows // 128
    nm = ncols // 128
    S = min(S, nm)
    assert nm % S == 0
    ng = nm // S
    CW = 128 * S
    scr = p.dram(ctx.name("rp"), [ng, 128, nk * CW], BF16)
    for k in range(nk):
        src = View(dram_view.ap[row0 + 128 * k:row0 + 128 * (k + 1),
                                col0:col0 + ncols], dram_view.regs)
        buf = ctx.WS.v(0, ncols)
        emit_dma_load(ctx, "gpsimd", buf, src, tag=tag + ".l")
        splits = np.linspace(0, ng, 3).astype(int)
        for qi in range(2):
            g0, g1 = int(splits[qi]), int(splits[qi + 1])
            if g0 == g1:
                continue
            dst = View(scr.ap[g0:g1, :, CW * k:CW * (k + 1)].rearrange("g p c -> p g c"),
                       scr.regs)
            sv = View(ctx.WS.t[0:128, ctx.WS.off + CW * g0:ctx.WS.off + CW * g1]
                      .rearrange("p (g c) -> p g c", c=CW),
                      [(ctx.WS.name, ctx.WS.off + CW * g0, ctx.WS.off + CW * g1)])
            emit_dma_store(ctx, qs[qi], dst, sv, tag=tag + ".s")
    return RePack(scr, S, nk, ng)


def emit_wload_g(ctx, rp, g, slot_col, tag="wl", queue="sync"):
    w = rp.nk * 128 * rp.S
    wb0 = ctx.WB.off + slot_col
    dv = View(ctx.WB.t[0:128, wb0:wb0 + w], [(ctx.WB.name, wb0, wb0 + w)])
    emit_dma_load(ctx, queue, dv, View(rp.scr.ap[g], rp.scr.regs), tag=tag)
    S = rp.S

    def tile(k, mj=0, ncol=128):
        c0 = wb0 + 128 * S * k + 128 * mj
        return View(ctx.WB.t[0:128, c0:c0 + ncol], [(ctx.WB.name, c0, c0 + ncol)])
    return tile


def emit_rms_chunk(ctx, src_pool, src_base, t0, dst_pool, tag):
    p = ctx.p
    onescb = ctx.const["ONESC_B"]
    onesr = ctx.const["ONESR_F"]
    XS = ctx.XS
    psr = View(ctx.PSC.t[0:1, 0:512], [(ctx.PSC.name, 0, 512)])
    for i in range(8):
        sv = src_pool.v(src_base + 2048 * i + t0, 512)
        emit_act(ctx, ctx.LNSQ.v(0, 512), sv, AF.Square, tag=tag + ".sq")
        emit_mm(ctx, psr, onescb, ctx.LNSQ.v(0, 512), i == 0, i == 7, tag=tag + ".ss")
    row = XS.v(512, 512, 0, 1)
    emit_ts(ctx, row, psr, 1.0 / 1024.0, None, OP.mult, tag=tag + ".ms")
    emit_act(ctx, row, row, AF.Ln, bias=1.1920929e-07, tag=tag + ".ln")
    emit_act(ctx, row, row, AF.Exp, scale=-0.5, tag=tag + ".ex")
    bc = ctx.PSC.v(0, 512)
    emit_mm(ctx, bc, onesr, row, True, True, tag=tag + ".bc")
    for i in range(8):
        sv = src_pool.v(src_base + 2048 * i + t0, 512)
        emit_tt(ctx, dst_pool.v(512 * i, 512), sv, bc, OP.mult, tag=tag + ".ap")


def emit_l1ab_repacks(ctx, ws_pool, batch):
    win = ctx.win
    saved_ws = getattr(ctx, "WS", None)
    ctx.WS = ws_pool
    if not hasattr(ctx, "ab_rps"):
        ctx.ab_rps = [None] * 13
    r = ctx.ab_rps
    if batch == 0:
        q0 = ("sync", "gpsimd")
        r[0] = emit_repack(ctx, win["L1_attn_b_in_proj_w"], 1024, 4096, 2048, 2048,
                           "ab.rg", qs=q0)
        r[1] = emit_repack(ctx, win["L1_attn_b_in_proj_w"], 1024, 4096, 0, 2048,
                           "ab.rv", qs=q0)
        r[2] = emit_repack(ctx, win["L1_attn_b_B_proj_w"], 1024, 1024, 0, 1024,
                           "ab.rB", qs=q0)
        r[3] = emit_repack(ctx, win["L1_attn_b_C_proj_w"], 1024, 1024, 0, 1024,
                           "ab.rC", qs=q0)
    elif batch == 1:
        qg = getattr(ctx, "rp_store_qs", ("scalar", "gpsimd"))
        r[4] = emit_repack(ctx, win["L1_attn_b_out_proj_w"], 2048, 1024, 0, 1024,
                           "ab.rop", S=2, qs=qg)
        r[5] = emit_repack(ctx, win["L1_attn_b_m_fc_w"], 1024, 4096, 0, 4096, "ab.rmf", qs=qg)
        r[6] = [emit_repack(ctx, win["L1_attn_b_m_proj_w"], 4096, 1024, 0, 1024,
                            f"ab.rmp{q}", S=2, row0=1024 * q, nrows=1024, qs=qg)
                for q in range(4)]
    else:
        qg = getattr(ctx, "rp_store_qs", ("scalar", "gpsimd"))
        r[7] = emit_repack(ctx, win["L1_attn_b_a_fc_w"], 1024, 4096, 0, 4096, "ab.raf", qs=qg)
        r[8] = [emit_repack(ctx, win["L1_attn_b_a_proj_w"], 4096, 1024, 0, 1024,
                            f"ab.rap{q}", S=2, row0=1024 * q, nrows=1024, qs=qg)
                for q in range(4)]
        r[9] = emit_repack(ctx, win["L1_attn_b_q_w"], 1024, 1024, 0, 1024, "ab.rq", qs=qg)
        r[10] = emit_repack(ctx, win["L1_attn_b_k_w"], 1024, 256, 0, 256, "ab.rk", S=2, qs=qg)
        r[11] = emit_repack(ctx, win["L1_attn_b_v_w"], 1024, 256, 0, 256, "ab.rvw", S=2, qs=qg)
        r[12] = emit_repack(ctx, win["L1_attn_b_o_w"], 1024, 1024, 0, 1024, "ab.ro", qs=qg)
    if saved_ws is not None:
        ctx.WS = saved_ws


def emit_l1ab(ctx):
    p = ctx.p
    XN, XM = ctx.XN, ctx.XM
    PSA, PSB, PSC, PST = ctx.PSA, ctx.PSB, ctx.PSC, ctx.PST
    NB, VB, VACT, SG, BCM, XS, SCB = (ctx.NB, ctx.VB, ctx.VACT, ctx.SG,
                                      ctx.BCM, ctx.XS, ctx.SCB)
    idb = ctx.const["IDB"]
    trilt = ctx.const["TRILT_B"]
    cv = ctx.colvecs
    win = ctx.win

    lnr = p.dram(ctx.name("ab_lnr"), [2, 2048], F32)
    emit_dma_store(ctx, "scalar", View(lnr.ap[0:1, :], lnr.regs),
                   ctx.ACTF.v(0, 2048, 0, 1), tag="ab.lnra")
    emit_dma_store(ctx, "scalar", View(lnr.ap[1:2, :], lnr.regs),
                   XS.v(0, 2048, 0, 1), tag="ab.lnrb")
    sp_sg = p.dram(ctx.name("ab_sg"), [4, 16, 128, 512], BF16)
    sp_h = p.dram(ctx.name("ab_h"), [4, 32, 128, 512], BF16)
    sp_q = p.dram(ctx.name("ab_q"), [16, 128, 1024], BF16)

    if not hasattr(ctx, "ab_rps"):
        emit_l1ab_repacks(ctx, ctx.WS, 0)
        emit_l1ab_repacks(ctx, ctx.WS, 1)
        emit_l1ab_repacks(ctx, ctx.WS, 2)
    (rp_gate, rp_val, rp_B, rp_C, rp_op, rp_mfc, rp_mpr, rp_afc, rp_apr,
     rp_qw, rp_kw, rp_vw, rp_ow) = ctx.ab_rps

    rmc = XS.v(1024, 8, 0, 128)
    emit_tt(ctx, rmc, cv["ab_rm0"].cols(0, 8), cv["ab_rm1"].cols(0, 8), OP.add, tag="ab.rm")
    for i in range(8):
        emit_ts(ctx, XM.v(2048 * i, 2048), XN.v(2048 * i, 2048),
                View(rmc.ap[:, i:i + 1], rmc.regs), None, OP.mult, tag="ab.xm")

    U = lambda c, n, p0=0, np_=128: XS.v(1032 + c, n, p0, np_)
    dtb = cv["ab_dtb"].cols(0, 16)
    alog = cv["ab_alog"].cols(0, 16)
    ax = U(0, 16)
    emit_act(ctx, ax, dtb, AF.Abs, tag="ab.abs")
    emit_act(ctx, ax, ax, AF.Exp, scale=-1.0, tag="ab.e")
    emit_act(ctx, ax, ax, AF.Ln, bias=1.0, tag="ab.l")
    mx = U(16, 16)
    emit_ts(ctx, mx, dtb, 0.0, None, OP.max, tag="ab.mx")
    dt = U(32, 16)
    emit_tt(ctx, dt, mx, ax, OP.add, tag="ab.dt")
    a_ = U(48, 16)
    emit_act(ctx, a_, alog, AF.Exp, tag="ab.A0")
    emit_ts(ctx, a_, a_, -1.0, None, OP.mult, tag="ab.A")
    adt = U(64, 16)
    emit_tt(ctx, adt, a_, dt, OP.mult, tag="ab.adt")
    lndt = U(80, 16)
    emit_act(ctx, lndt, dt, AF.Ln, tag="ab.lndt")
    cdec = U(96, 16)
    emit_ts(ctx, cdec, adt, 128.0, None, OP.mult, tag="ab.cd0")
    emit_act(ctx, cdec, cdec, AF.Exp, tag="ab.cdec")
    for h in range(16):
        lm = ctx.LMT.v(128 * h, 128)
        emit_act(ctx, lm, ctx.const["DIFFT"], AF.Exp,
                 scale=U(64 + h, 1), bias=U(80 + h, 1), tag="ab.lmt")
        emit_tt(ctx, lm, lm, trilt, OP.mult, tag="ab.lmm")
        dr = ctx.DRT.v(128 * h, 128)
        emit_act(ctx, dr, ctx.const["REVT"], AF.Exp,
                 scale=U(64 + h, 1), bias=U(80 + h, 1), tag="ab.drt")
        do = ctx.DOT.v(128 * h, 128)
        emit_act(ctx, do, ctx.const["DECT"], AF.Exp, scale=U(64 + h, 1), tag="ab.dot")
        emit_act(ctx, XS.v(1144 + h, 1), ctx.const["REVC"], AF.Exp,
                 scale=U(64 + h, 1), bias=U(80 + h, 1), tag="ab.drtc")

    for tc in range(4):
        t0 = 512 * tc
        emit_rms_chunk(ctx, XM, 0, t0, NB, "ab.n1")
        for g4 in range(4):
            wt = emit_wload_g(ctx, rp_gate, g4, 4096 * (g4 % 2), tag="ab.wg")
            for mj in range(4):
                m = 4 * g4 + mj
                ps = PSA.v(512 * (m % 2), 512)
                for k in range(8):
                    emit_mm(ctx, ps, wt(k, mj), NB.v(512 * k, 512), k == 0, k == 7, tag="ab.g")
                emit_act(ctx, SG.v(512 * (m % 4), 512), ps, AF.Silu, tag="ab.sgev")
                if m % 4 != 3:
                    continue
                dst = View(sp_sg.ap[tc, m - 3:m + 1].rearrange("m p c -> p m c"), sp_sg.regs)
                sv = View(SG.t[0:128, SG.off:SG.off + 2048].rearrange("p (m c) -> p m c", c=512),
                          [(SG.name, SG.off, SG.off + 2048)])
                emit_dma_store(ctx, "scalar", dst, sv, tag="ab.sgsp")
    for i in range(16):
        p.op("gpsimd", lambda q, o=VB.v(515 * i, 3).ap: q.memset(o, 0.0),
             writes=[VB.v(515 * i, 3)], tag="ab.cz")
    def value_group(tc, g4, mj, box={}):
        if mj == 0:
            box[g4] = emit_wload_g(ctx, rp_val, g4, 4096 * (g4 % 2), tag="ab.wv")
        wt = box[g4]
        m = 4 * g4 + mj
        ps = ctx.PSC.v(0, 512)
        for k in range(8):
            emit_mm(ctx, ps, wt(k, mj), NB.v(512 * k, 512), k == 0, k == 7, tag="ab.v")
        emit_copy(ctx, "scalar", VB.v(515 * m + 3, 512), ps, tag="ab.vev")

    emit_rms_chunk(ctx, XM, 0, 0, NB, "ab.n1b")
    for g4 in range(4):
        for mj in range(4):
            value_group(0, g4, mj)
    for tc in range(4):
        t0 = 512 * tc
        for g in range(2):
            wtB = emit_wload_g(ctx, rp_B, g, 0, tag="ab.wB")
            wtC = emit_wload_g(ctx, rp_C, g, 4096, tag="ab.wC", queue="scalar")
            for mj in range(4):
                m = 4 * g + mj
                psb_ = PSA.v(512 * (m % 2), 512)
                psc_ = PSB.v(0, 512)
                for k in range(8):
                    xmk = XM.v(2048 * k + t0, 512)
                    emit_mm(ctx, psb_, wtB(k, mj), xmk, k == 0, k == 7, tag="ab.bm")
                    emit_mm(ctx, psc_, wtC(k, mj), xmk, k == 0, k == 7, tag="ab.cm")
                emit_copy(ctx, "vector", BCM.v(1024 * m, 512), psb_, tag="ab.bmev")
                emit_copy(ctx, "vector", BCM.v(1024 * m + 512, 512), psc_, tag="ab.cmev")
        for m in range(16):
            acc = XS.v(0, 512)
            emit_ts(ctx, acc, VB.v(515 * m + 0, 512), cv["ab_conv"].col(4 * m + 0),
                    None, OP.mult, tag="ab.c0")
            for j in range(1, 4):
                emit_stt(ctx, acc, VB.v(515 * m + j, 512), cv["ab_conv"].col(4 * m + j),
                         acc, OP.mult, OP.add, tag=f"ab.c{j}")
            emit_act(ctx, VACT.v(512 * m, 512), acc, AF.Silu,
                     bias=cv["ab_convb"].col(m), tag="ab.silu")
            emit_copy(ctx, "vector", VB.v(515 * m, 3), VB.v(515 * m + 512, 3), tag="ab.carry")
        WS = ctx.WS
        for m in range(8):
            for sc in range(4):
                srcv = BCM.v(1024 * m + 128 * sc, 128)
                dstv = WS.v(512 * m + 128 * sc, 128)
                q = "sync" if (m + sc) % 2 == 0 else "scalar"
                p.op(q, lambda qq, o=dstv.ap, i=srcv.ap:
                     qq.dma_start(out=o, in_=i, transpose=True),
                     reads=[srcv], writes=[dstv], dma=True, tag="ab.bmT")
        fillers = []
        if tc + 1 < 4:
            emit_rms_chunk(ctx, XM, 0, 512 * (tc + 1), NB, "ab.n1b")
            for g4 in range(4):
                for mj in range(4):
                    fillers.append((tc + 1, g4, mj))
        for sc in range(4):
            c0 = 128 * sc
            gchunk = 4 * tc + sc

            def front(h, sc=sc, c0=c0, gchunk=gchunk, tc=tc):
                hb = 64 * (h % 2)
                sgc = SG.v(128 * (h % 4), 128)
                emit_dma_load(ctx, "scalar", sgc,
                              View(sp_sg.ap[tc, h][:, c0:c0 + 128], sp_sg.regs),
                              tag="ab.sgld")
                emit_transpose(ctx, PST.v(128 * (h % 4), 128),
                               VACT.v(512 * h + c0, 128), idb, tag="ab.vT")
                vtok = SCB.v(128 * (h % 4), 128)
                emit_copy(ctx, "scalar", vtok, PST.v(128 * (h % 4), 128), tag="ab.vTe")
                bmv = BCM.v(1024 * (h // 2) + c0, 128, hb, 64)
                cmv = BCM.v(1024 * (h // 2) + 512 + c0, 128, hb, 64)
                psCB = PSA.v(512 * (h % 2), 128)
                emit_mm(ctx, psCB, bmv, cmv, True, True, tag="ab.cb")
                emit_tt(ctx, ctx.ATTN.v(512 + 128 * (h % 2), 128), psCB,
                        ctx.LMT.v(128 * h, 128), OP.mult, tag="ab.cbl")
                emit_ts(ctx, SCB.v(512 + 64 * (h % 2), 64),
                        WS.v(512 * (h // 2) + 128 * sc + hb, 64),
                        XS.v(1144 + h, 1), None, OP.mult, tag="ab.bdrT")
                if gchunk > 0:
                    emit_tt(ctx, ctx.ATTN.v(256 + 128 * (h % 2), 128, hb, 64), cmv,
                            ctx.DOT.v(128 * h, 128, hb, 64), OP.mult, tag="ab.cmd")

            def bundle(h, sc=sc, c0=c0, gchunk=gchunk):
                hb = 64 * (h % 2)
                vtok = SCB.v(128 * (h % 4), 128)
                cbl = ctx.ATTN.v(512 + 128 * (h % 2), 128)
                bdrT = SCB.v(512 + 64 * (h % 2), 64)
                psY = PSB.v(512 * (h % 2), 128)
                emit_mm(ctx, psY, vtok, cbl, True, gchunk == 0, tag="ab.yd")
                if gchunk > 0:
                    cmd = ctx.ATTN.v(256 + 128 * (h % 2), 128, hb, 64)
                    emit_mm(ctx, psY, ctx.HC.v(128 * h, 128, hb, 64), cmd,
                            False, True, tag="ab.yo")
                psSt = PSA.v(1024 + 512 * (h % 2), 128, 0, 64)
                emit_mm(ctx, psSt, bdrT, vtok, True, True, tag="ab.st")
                hcf = ctx.HCF.v(128 * h, 128, hb, 64)
                if gchunk == 0:
                    emit_copy(ctx, "vector", hcf, psSt, tag="ab.hc0")
                else:
                    emit_ts(ctx, hcf, hcf, U(96 + h, 1, 0, 64), None, OP.mult, tag="ab.hcd")
                    emit_tt(ctx, hcf, hcf, psSt, OP.add, tag="ab.hca")
                emit_copy(ctx, "scalar", ctx.HC.v(128 * h, 128, hb, 64), hcf, tag="ab.hccast")
                yb = ctx.ATTN.v(128 * (h % 2), 128)
                emit_stt(ctx, yb, VACT.v(512 * h + c0, 128), cv["ab_D"].col(h), psY,
                         OP.mult, OP.add, tag="ab.ybd")
                emit_tt(ctx, VACT.v(512 * h + c0, 128), yb, SG.v(128 * (h % 4), 128),
                        OP.mult, tag="ab.yb")

            for h in range(16):
                front(h)
                if h >= 2:
                    bundle(h - 2)
                if h % 4 == 1 and fillers:
                    value_group(*fillers.pop(0))
            bundle(14)
            bundle(15)
        for g in range(4):
            wt = emit_wload_g(ctx, rp_op, g, 4096 * (g % 2), tag="ab.wop")
            for fj in range(2):
                f = 2 * g + fj
                ps = PSB.v(0, 512)
                for k in range(16):
                    emit_mm(ctx, ps, wt(k, fj), VACT.v(512 * k, 512),
                            k == 0, k == 15, tag="ab.op")
                xmv = XM.v(2048 * f + t0, 512)
                emit_stt(ctx, xmv, ps, cv["ab_msc"].col(f), xmv, OP.mult, OP.add, tag="ab.opres")
    emit_l1ab_mlp(ctx, rp_mfc, rp_mpr, sp_h, "ab_m2sc", "ab.m2")
    rarow = ctx.ACTF.v(0, 2048, 0, 1)
    rbrow = ctx.HCF.v(0, 2048, 0, 1)
    emit_dma_load(ctx, "scalar", rarow, View(lnr.ap[0:1, :], lnr.regs), tag="ab.lra")
    emit_dma_load(ctx, "scalar", rbrow, View(lnr.ap[1:2, :], lnr.regs), tag="ab.lrb")
    onesr_f = ctx.const["ONESR_F"]
    for c in range(4):
        pra = ctx.PSA.v(0, 512)
        prb = ctx.PSA.v(512, 512)
        emit_mm(ctx, pra, onesr_f, ctx.ACTF.v(512 * c, 512, 0, 1), True, True, tag="ab.bca")
        emit_mm(ctx, prb, onesr_f, ctx.HCF.v(512 * c, 512, 0, 1), True, True, tag="ab.bcb")
        for i in range(8):
            xc = XS.v(512, 512)
            emit_dma_load(ctx, "scalar", xc, xview(ctx.x_src, i, 512 * c, 512), tag="ab.x0l")
            t0v = XS.v(0, 512)
            emit_tt(ctx, t0v, xc, pra, OP.mult, tag="ab.x0m")
            emit_tt(ctx, t0v, t0v, prb, OP.add, tag="ab.x0a")
            t1 = XS.v(1024, 512)
            emit_ts(ctx, t1, t0v, cv["ab_ra1"].col(i), None, OP.mult, tag="ab.xa1")
            emit_stt(ctx, XM.v(2048 * i + 512 * c, 512), XM.v(2048 * i + 512 * c, 512),
                     cv["ab_ra0"].col(i), t1, OP.mult, OP.add, tag="ab.xa")
    emit_l1ab_qkv(ctx, rp_qw, rp_kw, rp_vw, sp_q)
    emit_l1ab_attn(ctx, rp_ow, sp_q)
    emit_l1ab_mlp(ctx, rp_afc, rp_apr, sp_h, "ab_a2sc", "ab.a2")
    for i in range(8):
        for c in range(4):
            emit_xadd(ctx, i, 512 * c, 512, XM.v(2048 * i + 512 * c, 512), tag="l1ab.res")


def emit_l1ab_mlp(ctx, rp_fc, rp_pr, sp_h, sc_name, tag):
    p = ctx.p
    XM, NB, VACT, XS = ctx.XM, ctx.NB, ctx.VACT, ctx.XS
    PSA, PSB = ctx.PSA, ctx.PSB
    cv = ctx.colvecs
    for tc in range(4):
        t0 = 512 * tc
        emit_rms_chunk(ctx, XM, 0, t0, NB, tag + ".n")
        pool2 = [ctx.VACT, ctx.BCM]
        for mg in range(2):
            for g4 in range(4):
                wt = emit_wload_g(ctx, rp_fc, 4 * mg + g4, 4096 * (g4 % 2), tag=tag + ".wf")
                for mj in range(4):
                    m = 4 * g4 + mj
                    ps = PSA.v(512 * (m % 2), 512)
                    for k in range(8):
                        emit_mm(ctx, ps, wt(k, mj), NB.v(512 * k, 512), k == 0, k == 7,
                                tag=tag + ".fc")
                    rh = XS.v(512 * (m % 2), 512)
                    emit_act(ctx, rh, ps, AF.Relu, tag=tag + ".relu")
                    emit_tt(ctx, pool2[mg].v(512 * m, 512), rh, rh, OP.mult, tag=tag + ".sq")
        hv = [pool2[k // 16].v(512 * (k % 16), 512) for k in range(32)]
        for fh in range(4):
            wtq = [emit_wload_g(ctx, rp_pr[q], fh, 2048 * q,
                                tag=tag + ".wp", queue="sync" if q % 2 == 0 else "scalar")
                   for q in range(4)]
            for fj in range(2):
                f = 2 * fh + fj
                ps = PSB.v(512 * fj, 512)
                for q in range(4):
                    for kq in range(8):
                        k = 8 * q + kq
                        emit_mm(ctx, ps, wtq[q](kq, fj), hv[k],
                                k == 0, k == 31, tag=tag + ".pr")
                xmv = XM.v(2048 * f + t0, 512)
                emit_stt(ctx, xmv, ps, cv[sc_name].col(f), xmv, OP.mult, OP.add,
                         tag=tag + ".res")


def emit_l1ab_qkv(ctx, rp_qw, rp_kw, rp_vw, sp_q):
    p = ctx.p
    XM, NB, XS = ctx.XM, ctx.NB, ctx.XS
    PSA, PSB, PSC = ctx.PSA, ctx.PSB, ctx.PSC
    B = ctx.BIGB
    cv = ctx.colvecs
    sp_k = p.dram(ctx.name("ab_kt"), [16, 128, 256], BF16)
    for g in range(2):
        src = View(rp_qw.scr.ap[g], rp_qw.scr.regs)
        dv = View(ctx.WB.t[0:128, ctx.WB.off + 4096 * g:ctx.WB.off + 4096 * (g + 1)],
                  [(ctx.WB.name, ctx.WB.off + 4096 * g, ctx.WB.off + 4096 * (g + 1))])
        emit_dma_load(ctx, "sync", dv, src, tag="ab.wq")
    for nm, rp, off in (("k", rp_kw, 28736), ("v", rp_vw, 30784)):
        src = View(rp.scr.ap[0], rp.scr.regs)
        dvv = View(B.t[0:128, B.off + off:B.off + off + 2048],
                   [(B.name, B.off + off, B.off + off + 2048)])
        emit_dma_load(ctx, "scalar", dvv, src, tag="ab.w" + nm)
    ones_dst = View(B.t[0:128, B.off + 24576:B.off + 24576 + 4160]
                    .rearrange("p (j h o) -> p j h o", h=4, o=65)[:, :, :, 64:65],
                    [(B.name, B.off + 24576, B.off + 28736)])
    ones_src = View(ctx.const["ONESC_B"].ap.unsqueeze(1).to_broadcast([128, 16, 4, 1]),
                    ctx.const["ONESC_B"].regs)
    p.op("vector", lambda q, o=ones_dst.ap, i=ones_src.ap: q.tensor_copy(o, i),
         reads=[ones_src], writes=[ones_dst], tag="ab.vones")

    for tc in range(4):
        t0 = 512 * tc
        emit_rms_chunk(ctx, XM, 0, t0, NB, "ab.n3")
        def q_front(tt):
            pp = tt % 2
            for mg in range(2):
                ps = PSA.v(1024 * pp + 512 * mg, 512)
                for k in range(8):
                    rq0 = ctx.WB.off + 4096 * mg + 512 * k
                    rhs = View(ctx.WB.t[0:128, rq0:rq0 + 512],
                               [(ctx.WB.name, rq0, rq0 + 512)])
                    emit_mm(ctx, ps, NB.v(512 * k + 128 * tt, 128), rhs, k == 0, k == 7, tag="ab.q")
            qt = B.v(4096 * pp, 1024)
            emit_copy(ctx, "vector", qt, PSA.v(1024 * pp, 1024), tag="ab.qev")

        def q_tail(tt, tc=tc):
            pp = tt % 2
            gt = 4 * tc + tt
            qb0 = 4096 * pp
            qt = B.v(qb0, 1024)
            sq = XS.v(0, 1024)
            emit_act(ctx, sq, qt, AF.Square, tag="ab.qsq")
            w = 64
            cur = sq
            tgt = [XS.v(1024, 512), XS.v(1536, 256), XS.v(1024, 128),
                   XS.v(1536, 64), XS.v(1024, 32), XS.v(1536, 16)]
            for stp in range(6):
                half = w // 2
                g = View(cur.ap.rearrange("p (g k) -> p g k", g=16), cur.regs)
                dst = tgt[stp]
                emit_tt(ctx, View(dst.ap.rearrange("p (g k) -> p g k", g=16), dst.regs),
                        View(g.ap[:, :, 0:half], cur.regs), View(g.ap[:, :, half:w], cur.regs),
                        OP.add, tag="ab.qss")
                cur = dst
                w = half
            rst = XS.v(1536, 16)
            emit_ts(ctx, rst, cur, 1.0 / 64.0, None, OP.mult, tag="ab.qms")
            emit_act(ctx, rst, rst, AF.Ln, bias=1.1920929e-07, tag="ab.qln")
            emit_act(ctx, rst, rst, AF.Exp, scale=-0.5, tag="ab.qex")
            rb = View(rst.ap.unsqueeze(2).to_broadcast([128, 16, 64]), rst.regs)
            emit_tt(ctx, View(qt.ap.rearrange("p (g k) -> p g k", g=16), qt.regs),
                    View(qt.ap.rearrange("p (g k) -> p g k", g=16), qt.regs), rb,
                    OP.mult, tag="ab.qrn")
            qr = B.v(qb0 + 1024, 1024)
            cosv = View(ctx.const["COSTOK"].ap[:, 32 * gt:32 * (gt + 1)]
                        .unsqueeze(1).unsqueeze(1).to_broadcast([128, 2, 4, 32]),
                        ctx.const["COSTOK"].regs)
            sinv = View(ctx.const["SINTOK"].ap[:, 32 * gt:32 * (gt + 1)]
                        .unsqueeze(1).unsqueeze(1).to_broadcast([128, 2, 4, 32]),
                        ctx.const["SINTOK"].regs)
            for g2 in range(2):
                def xin(off, g2=g2):
                    ap = bass.AP(B.t, B.off + qb0 + 512 * g2 + off,
                                 [[PCOLS(B), 128], [256, 2], [64, 4], [1, 32]])
                    return View(ap, [(B.name, B.off + qb0, B.off + qb0 + 1024)])
                def xout(off, g2=g2):
                    ap = bass.AP(B.t, B.off + qb0 + 1024 + 512 * g2 + off,
                                 [[PCOLS(B), 128], [64, 2], [128, 4], [1, 32]])
                    return View(ap, [(B.name, B.off + qb0 + 1024, B.off + qb0 + 2048)])
                t1n = View(bass.AP(XS.t, XS.off + 1024, [[PCOLS(XS), 128], [128, 2], [32, 4], [1, 32]]),
                           XS.v(1024, 512).regs)
                t2n = View(bass.AP(XS.t, XS.off + 1536, [[PCOLS(XS), 128], [128, 2], [32, 4], [1, 32]]),
                           XS.v(1536, 512).regs)
                emit_tt(ctx, t1n, xin(0), cosv, OP.mult, tag="ab.r1a")
                emit_tt(ctx, t2n, xin(32), sinv, OP.mult, tag="ab.r1b")
                emit_tt(ctx, xout(0), t1n, t2n, OP.add, tag="ab.r1")
                emit_tt(ctx, t1n, xin(32), cosv, OP.mult, tag="ab.r2a")
                emit_tt(ctx, t2n, xin(0), sinv, OP.mult, tag="ab.r2b")
                emit_tt(ctx, xout(32), t1n, t2n, OP.subtract, tag="ab.r2")
            emit_dma_store(ctx, "sync", View(sp_q.ap[gt], sp_q.regs), qr, tag="ab.qsp")

        def kv_part(tt, tc=tc):
            pp = tt % 2
            gt = 4 * tc + tt
            ktb = 4096 * pp + 2048
            for nm, off, spk in (("k", 28736, True), ("v", 30784, False)):
                ps = PSC.v(0, 256)
                for k in range(8):
                    rk0 = B.off + off + 256 * k
                    rhs = View(B.t[0:128, rk0:rk0 + 256],
                               [(B.name, rk0, rk0 + 256)])
                    emit_mm(ctx, ps, NB.v(512 * k + 128 * tt, 128), rhs, k == 0, k == 7,
                            tag="ab." + nm)
                if spk:
                    kt = B.v(ktb, 256)
                    emit_copy(ctx, "vector", kt, ps, tag="ab.kev")
                    sqk = XS.v(0, 256)
                    emit_act(ctx, sqk, kt, AF.Square, tag="ab.ksq")
                    wk = 64
                    curk = sqk
                    tgtk = [XS.v(1024, 128), XS.v(1536, 64), XS.v(1024, 32),
                            XS.v(1536, 16), XS.v(1024, 8), XS.v(1536, 4)]
                    for stp in range(6):
                        half = wk // 2
                        g = View(curk.ap.rearrange("p (g k) -> p g k", g=4), curk.regs)
                        dst = tgtk[stp]
                        emit_tt(ctx, View(dst.ap.rearrange("p (g k) -> p g k", g=4), dst.regs),
                                View(g.ap[:, :, 0:half], curk.regs),
                                View(g.ap[:, :, half:wk], curk.regs), OP.add, tag="ab.kss")
                        curk = dst
                        wk = half
                    rstk = XS.v(1536, 4)
                    emit_ts(ctx, rstk, curk, 1.0 / 64.0, None, OP.mult, tag="ab.kms")
                    emit_act(ctx, rstk, rstk, AF.Ln, bias=1.1920929e-07, tag="ab.kln")
                    emit_act(ctx, rstk, rstk, AF.Exp, scale=-0.5, tag="ab.kex")
                    rbk = View(rstk.ap.unsqueeze(2).to_broadcast([128, 4, 64]), rstk.regs)
                    emit_tt(ctx, View(kt.ap.rearrange("p (g k) -> p g k", g=4), kt.regs),
                            View(kt.ap.rearrange("p (g k) -> p g k", g=4), kt.regs), rbk,
                            OP.mult, tag="ab.krn")
                    kr = B.v(ktb + 256, 256)
                    cosk = View(ctx.const["COSTOK"].ap[:, 32 * gt:32 * (gt + 1)]
                                .unsqueeze(1).to_broadcast([128, 4, 32]), ctx.const["COSTOK"].regs)
                    sink = View(ctx.const["SINTOK"].ap[:, 32 * gt:32 * (gt + 1)]
                                .unsqueeze(1).to_broadcast([128, 4, 32]), ctx.const["SINTOK"].regs)
                    def kin(off2):
                        ap = bass.AP(B.t, B.off + ktb + off2, [[PCOLS(B), 128], [64, 4], [1, 32]])
                        return View(ap, [(B.name, B.off + ktb, B.off + ktb + 256)])
                    def kout(off2):
                        ap = bass.AP(B.t, B.off + ktb + 256 + off2, [[PCOLS(B), 128], [64, 4], [1, 32]])
                        return View(ap, [(B.name, B.off + ktb + 256, B.off + ktb + 512)])
                    tk1 = View(bass.AP(XS.t, XS.off + 1024, [[PCOLS(XS), 128], [32, 4], [1, 32]]),
                               XS.v(1024, 128).regs)
                    tk2 = View(bass.AP(XS.t, XS.off + 1536, [[PCOLS(XS), 128], [32, 4], [1, 32]]),
                               XS.v(1536, 128).regs)
                    emit_tt(ctx, tk1, kin(0), cosk, OP.mult, tag="ab.kr1a")
                    emit_tt(ctx, tk2, kin(32), sink, OP.mult, tag="ab.kr1b")
                    emit_tt(ctx, kout(0), tk1, tk2, OP.add, tag="ab.kr1")
                    emit_tt(ctx, tk1, kin(32), cosk, OP.mult, tag="ab.kr2a")
                    emit_tt(ctx, tk2, kin(0), sink, OP.mult, tag="ab.kr2b")
                    emit_tt(ctx, kout(32), tk1, tk2, OP.subtract, tag="ab.kr2")
                    emit_dma_store(ctx, "scalar", View(sp_k.ap[gt], sp_k.regs), kr, tag="ab.ksp")
                else:
                    dst = View(B.t[0:128, B.off + 24576 + 260 * gt:B.off + 24576 + 260 * gt + 260]
                               .rearrange("p (h o) -> p h o", h=4)[:, :, 0:64],
                               [(B.name, B.off + 24576 + 260 * gt, B.off + 24576 + 260 * (gt + 1))])
                    srcps = View(PSC.t[0:128, 0:256].rearrange("p (h o) -> p h o", h=4),
                                 [(PSC.name, 0, 512)])
                    p.op("vector", lambda q, o=dst.ap, i=srcps.ap: q.tensor_copy(o, i),
                         reads=[View(srcps.ap, [(PSC.name, 0, 512)])], writes=[dst], tag="ab.vev2")

        for tt in range(4):
            q_front(tt)
            if tt > 0:
                q_tail(tt - 1)
                kv_part(tt - 1)
        q_tail(3)
        kv_part(3)
    for j in range(8):
        src = View(sp_q.ap.rearrange("a p c -> (a p) c")[:, 128 * j:128 * (j + 1)], sp_q.regs)
        dst = B.v(4096 + 2048 * j, 2048)
        p.op("sync", lambda q, o=dst.ap, i=src.ap: q.dma_start(out=o, in_=i, transpose=True),
             reads=[src], writes=[dst], dma=True, tag="ab.qT")
    for j in range(2):
        src = View(sp_k.ap.rearrange("a p c -> (a p) c")[:, 128 * j:128 * (j + 1)], sp_k.regs)
        dst = B.v(20480 + 2048 * j, 2048)
        p.op("sync", lambda q, o=dst.ap, i=src.ap: q.dma_start(out=o, in_=i, transpose=True),
             reads=[src], writes=[dst], dma=True, tag="ab.kT")


def emit_l1ab_attn(ctx, rp_ow, sp_q):
    p = ctx.p
    XM, XS = ctx.XM, ctx.XS
    PSA, PSB = ctx.PSA, ctx.PSB
    B = ctx.BIGB
    cv = ctx.colvecs
    cmask = ctx.const["CMASK_B"]
    onesr_f = ctx.const["ONESR_F"]
    for g in range(2):
        src = View(rp_ow.scr.ap[g], rp_ow.scr.regs)
        dv = View(ctx.WB.t[0:128, ctx.WB.off + 4096 * g:ctx.WB.off + 4096 * (g + 1)],
                  [(ctx.WB.name, ctx.WB.off + 4096 * g, ctx.WB.off + 4096 * (g + 1))])
        emit_dma_load(ctx, "sync", dv, src, tag="ab.wo")
    gsc = XS.v(1312, 16)
    emit_ts(ctx, gsc, cv["ab_qg"].cols(0, 16), 0.125, None, OP.mult, tag="ab.gsc")
    gbi = XS.v(1328, 16)
    emit_act(ctx, gbi, cv["ab_qg"].cols(0, 16), AF.Abs, tag="ab.gb0")
    emit_ts(ctx, gbi, gbi, -8.0, None, OP.mult, tag="ab.gbi")

    PI = [2 * (h % 4) + 8 * ((h // 4) // 2) + ((h // 4) % 2) for h in range(16)]
    for c in range(4):
        for hpi in range(8):
            hpair = (2 * hpi, 2 * hpi + 1)
            nj = 4 * c + 4

            def sc_front(j, i, c=c):
                h = hpair[i]
                hk = h // 4
                s = PI[h]
                qv = B.v(4096 + 2048 * (s // 2) + 512 * c, 512, 64 * (s % 2), 64)
                kv = B.v(20480 + 2048 * (hk // 2) + 128 * j, 128, 64 * (hk % 2), 64)
                psS = PSA.v(512 * (2 * i + j % 2), 512)
                emit_mm(ctx, psS, kv, qv, True, True, tag="ab.s")
                atb = (ctx.ATTN.v(512 * (j % 2), 512) if i == 0
                       else B.v(28736 + 512 * (j % 2), 512))
                emit_act(ctx, atb, psS, AF.Exp,
                         scale=View(gsc.ap[:, h:h + 1], gsc.regs),
                         bias=View(gbi.ap[:, h:h + 1], gbi.regs), tag="ab.exp")
                if j >= 4 * c:
                    emit_tt(ctx, atb, atb,
                            View(cmask.ap.tensor[0:128, cmask.regs[0][1] + 512 * (j - 4 * c):
                                 cmask.regs[0][1] + 512 * (j - 4 * c + 1)], cmask.regs),
                            OP.mult, tag="ab.msk")

            def sc_av(j, i, c=c, nj=nj):
                h = hpair[i]
                hk = h // 4
                atb = (ctx.ATTN.v(512 * (j % 2), 512) if i == 0
                       else B.v(28736 + 512 * (j % 2), 512))
                vav = View(B.t[0:128, B.off + 24576 + 260 * j + 65 * hk:
                                 B.off + 24576 + 260 * j + 65 * (hk + 1)],
                           [(B.name, B.off + 24576 + 260 * j, B.off + 24576 + 260 * (j + 1))])
                emit_mm(ctx, PSB.v(512 * i, 512, 0, 65), vav, atb,
                        j == 0, j == nj - 1, tag="ab.av")

            for j in range(nj):
                sc_front(j, 0)
                sc_front(j, 1)
                if j > 0:
                    sc_av(j - 1, 0)
                    sc_av(j - 1, 1)
            sc_av(nj - 1, 0)
            sc_av(nj - 1, 1)
            for i in range(2):
                h = hpair[i]
                rl = XS.v(512 * i, 512, 0, 1)
                p.op("vector", lambda q, o=rl.ap, i_=PSB.t[64:65, 512 * i:512 * i + 512]:
                     q.reciprocal(o, i_),
                     reads=[PSB.v(512 * i, 512, 64, 1)], writes=[rl], tag="ab.rl")
                psb2 = (ctx.PSC.v(0, 512, 0, 64) if i == 0
                        else PSA.v(1024, 512, 0, 64))
                emit_mm(ctx, psb2, View(onesr_f.ap[0:1, 0:64], onesr_f.regs), rl,
                        True, True, tag="ab.bc2")
                rlb = (ctx.ATTN.v(1024, 512, 0, 64) if i == 0
                       else B.v(29760, 512, 0, 64))
                emit_copy(ctx, "scalar", rlb, psb2, tag="ab.rlb")
                yt = B.v(0 + 512 * (h // 2), 512, 64 * (h % 2), 64)
                emit_tt(ctx, yt, PSB.v(512 * i, 512, 0, 64), rlb, OP.mult, tag="ab.y")
        for f in range(8):
            ps = PSA.v(1024 + 512 * (f % 2), 512)
            for k in range(8):
                lo0 = ctx.WB.off + 4096 * (f // 4) + 512 * k + 128 * (f % 4)
                lhsT = View(ctx.WB.t[0:128, lo0:lo0 + 128],
                            [(ctx.WB.name, lo0, lo0 + 128)])
                emit_mm(ctx, ps, lhsT, B.v(512 * k, 512), k == 0, k == 7, tag="ab.o")
            xmv = XM.v(2048 * f + 512 * c, 512)
            emit_stt(ctx, xmv, ps, cv["ab_asc"].col(f), xmv, OP.mult, OP.add, tag="ab.ores")


def build_prog_l1ab():
    global LAST_CTX
    nc = bass.Bass("TRN2", target_bir_lowering=False)
    ctx = make_base_ctx(nc, w_cols=64, ab_cols=0, actf_cols=2048, sm_pool=False,
                        consts=["IDB", "TRILT_B", "CMASK_B", "ONESC_B", "ONESC_F",
                                "ONESR_F", "ONESR_B", "DIFFT", "REVT", "DECT",
                                "REVC", "COSTOK", "SINTOK"], no_x=True)
    LAST_CTX = ctx
    p = ctx.p
    ctx.XM = SubPool(ctx.XN, 0, 16384)
    B = p.sbuf("BIGBp", 34864, BF16)
    ctx.BIGB = B
    ctx.VB = SubPool(B, 0, 8240)
    ctx.VACT = SubPool(B, 8240, 8192)
    ctx.BCM = SubPool(B, 16432, 8192)
    ctx.SG = SubPool(B, 24624, 2048)
    ctx.LMT = SubPool(B, 26672, 2048)
    ctx.DRT = SubPool(B, 28720, 2048)
    ctx.DOT = SubPool(B, 30768, 2048)
    ctx.HC = SubPool(B, 32816, 2048)
    ctx.WB = p.sbuf("WBp", 8192, BF16)
    ctx.WS = p.sbuf("WSp", 4096, BF16)
    ctx.NB = p.sbuf("NBp", 4096, BF16)
    ctx.SCB = p.sbuf("SCBp", 768, BF16)
    ctx.ATTN = p.sbuf("ATTNp", 1536, BF16)
    x_in = p.dram("x_in", [1024, 2048], F32, kind="ExternalInput")
    x_out = p.dram("x_out", [1024, 2048], F32, kind="ExternalOutput")
    ctx.x_in, ctx.x_out = x_in, x_out
    ctx.x_src, ctx.x_dst = x_in, x_out
    declare_weights(ctx, [("L1_attn_b_in_proj_w", (1024, 4096), F32),
                          ("L1_attn_b_B_proj_w", (1024, 1024), F32),
                          ("L1_attn_b_C_proj_w", (1024, 1024), F32),
                          ("L1_attn_b_out_proj_w", (2048, 1024), F32),
                          ("L1_attn_b_m_fc_w", (1024, 4096), F32),
                          ("L1_attn_b_m_proj_w", (4096, 1024), F32),
                          ("L1_attn_b_q_w", (1024, 1024), F32),
                          ("L1_attn_b_k_w", (1024, 256), F32),
                          ("L1_attn_b_v_w", (1024, 256), F32),
                          ("L1_attn_b_o_w", (1024, 1024), F32),
                          ("L1_attn_b_a_fc_w", (1024, 4096), F32),
                          ("L1_attn_b_a_proj_w", (4096, 1024), F32)])
    for nm, n in (("ab_rm0", 8), ("ab_rm1", 8), ("ab_ra0", 8), ("ab_ra1", 8),
                  ("ab_msc", 8), ("ab_m2sc", 8), ("ab_asc", 8), ("ab_a2sc", 8),
                  ("ab_conv", 64), ("ab_convb", 16), ("ab_D", 16),
                  ("ab_dtb", 16), ("ab_alog", 16), ("ab_qg", 16)):
        declare_colvec(ctx, nm, n)
    emit_ln_hbm(ctx, x_in)
    emit_l1ab(ctx)
    p.finalize()
    return nc


def emit_ln_hbm(ctx, x_in):
    p = ctx.p
    XN, ACTF = ctx.XN, ctx.ACTF
    PSA = ctx.PSA
    onescb = ctx.const["ONESC_B"]
    onesr = ctx.const["ONESR_F"]
    XSB, HCF = ctx.XS, ctx.HCF
    for i in range(8):
        xf = XSB.v(0, 2048)
        emit_dma_load(ctx, "sync" if i % 2 == 0 else "scalar", xf,
                      xview(x_in, i, 0, 2048), tag="lnh.x")
        xb = XN.v(2048 * i, 2048)
        emit_act(ctx, xb, xf, AF.Copy, tag="lnh.cast")
        for c in range(4):
            emit_mm(ctx, PSA.v(512 * c, 512, 0, 1), onescb, XN.v(2048 * i + 512 * c, 512),
                    i == 0, i == 7, tag="lnh.sum")
    mu = HCF.v(0, 2048, 0, 1)
    emit_ts(ctx, mu, PSA.v(0, 2048, 0, 1), 1.0 / D, None, OP.mult, tag="lnh.mu")
    for i in range(8):
        sq = XN.v(2048 * i, 2048)
        emit_act(ctx, sq, sq, AF.Square, tag="lnh.sq")
        for c in range(4):
            emit_mm(ctx, PSA.v(512 * c, 512, 0, 1), onescb, XN.v(2048 * i + 512 * c, 512),
                    i == 0, i == 7, tag="lnh.ss")
    ms = ACTF.v(0, 2048, 0, 1)
    emit_ts(ctx, ms, PSA.v(0, 2048, 0, 1), 1.0 / D, None, OP.mult, tag="lnh.ms")
    mu2 = XSB.v(0, 2048, 0, 1)
    emit_tt(ctx, mu2, mu, mu, OP.mult, tag="lnh.mu2")
    var = ACTF.v(0, 2048, 0, 1)
    emit_tt(ctx, var, ms, mu2, OP.subtract, tag="lnh.var")
    lnv = XSB.v(0, 2048, 0, 1)
    emit_act(ctx, lnv, var, AF.Ln, bias=1e-5, tag="lnh.lnv")
    ra = ACTF.v(0, 2048, 0, 1)
    emit_act(ctx, ra, lnv, AF.Exp, scale=-0.5, tag="lnh.ra")
    rb = XSB.v(0, 2048, 0, 1)
    emit_stt(ctx, rb, mu, -1.0, ra, OP.mult, OP.mult, tag="lnh.rb")
    for c in range(4):
        pra = PSA.v(0, 512)
        prb = PSA.v(512, 512)
        emit_mm(ctx, pra, onesr, ACTF.v(512 * c, 512, 0, 1), True, True, tag="lnh.bca")
        emit_mm(ctx, prb, onesr, XSB.v(512 * c, 512, 0, 1), True, True, tag="lnh.bcb")
        emit_copy(ctx, "vector", HCF.v(512 * c, 512), pra, tag="lnh.rab")
        emit_copy(ctx, "scalar", ACTF.v(512 * c, 512), prb, tag="lnh.rbb")
    for i in range(8):
        xb = XSB.v(0, 2048)
        emit_dma_load(ctx, "sync" if i % 2 == 0 else "scalar", xb,
                      xview(x_in, i, 0, 2048), tag="lnh.x2")
        for c in range(4):
            tmp = ctx.XR.v(0, 512)
            xn = XN.v(2048 * i + 512 * c, 512)
            emit_tt(ctx, tmp, XSB.v(512 * c, 512), HCF.v(512 * c, 512), OP.mult, tag="lnh.mul")
            emit_tt(ctx, xn, tmp, ACTF.v(512 * c, 512), OP.add, tag="lnh.add")


def host_l1ab_inputs(inputs):
    out = {k: inputs[k] for k in ("L1_attn_b_in_proj_w", "L1_attn_b_B_proj_w",
                                  "L1_attn_b_C_proj_w", "L1_attn_b_out_proj_w",
                                  "L1_attn_b_m_fc_w", "L1_attn_b_m_proj_w",
                                  "L1_attn_b_q_w", "L1_attn_b_k_w", "L1_attn_b_v_w",
                                  "L1_attn_b_o_w", "L1_attn_b_a_fc_w",
                                  "L1_attn_b_a_proj_w")}
    rs = lambda a, n: np.ascontiguousarray(np.asarray(a).reshape(n, 128).T)
    bc = lambda a: np.ascontiguousarray(np.broadcast_to(np.asarray(a)[None, :], (128, len(a))))
    out["cv_ab_rm0"] = rs(inputs["L1_attn_b_resid_mix_m"][0], 8)
    out["cv_ab_rm1"] = rs(inputs["L1_attn_b_resid_mix_m"][1], 8)
    out["cv_ab_ra0"] = rs(inputs["L1_attn_b_resid_mix_a"][0], 8)
    out["cv_ab_ra1"] = rs(inputs["L1_attn_b_resid_mix_a"][1], 8)
    out["cv_ab_msc"] = rs(inputs["L1_attn_b_mamba_scale"], 8)
    out["cv_ab_m2sc"] = rs(inputs["L1_attn_b_m_mlp_scale"], 8)
    out["cv_ab_asc"] = rs(inputs["L1_attn_b_attn_scale"], 8)
    out["cv_ab_a2sc"] = rs(inputs["L1_attn_b_a_mlp_scale"], 8)
    cw = inputs["L1_attn_b_conv_w"].reshape(16, 128, 4)
    out["cv_ab_conv"] = np.ascontiguousarray(cw.transpose(1, 0, 2).reshape(128, 64))
    out["cv_ab_convb"] = rs(inputs["L1_attn_b_conv_b"], 16)
    out["cv_ab_D"] = bc(inputs["L1_attn_b_D_param"])
    out["cv_ab_dtb"] = bc(inputs["L1_attn_b_dt_bias"])
    out["cv_ab_alog"] = bc(inputs["L1_attn_b_A_log"])
    out["cv_ab_qg"] = bc(inputs["L1_attn_b_q_gain"])
    return out




def build_prog_fused():
    global LAST_CTX
    nc = bass.Bass("TRN2", target_bir_lowering=False)
    p = Prog(nc)
    ctx = Ctx(p)
    LAST_CTX = ctx
    ctx.win = {}
    ctx.colvecs = {}
    ctx.smf_off = 0
    ctx.xr_slot = 0
    ctx.XN = p.sbuf("XN", 16384, BF16)
    ABF = p.sbuf("ABF", 53616, BF16)
    AFX = p.sbuf("AFX", 13696, F32)
    ctx.XR = SubPool(AFX, 0, 1024)
    ctx.XS = SubPool(AFX, 1024, 2048)
    ctx.HCF = SubPool(AFX, 3072, 2048)
    ctx.ACTF = SubPool(AFX, 5120, 2048)
    CARVE = 1024
    ctx.LNSQ = p.sbuf("LNSQf", 512, BF16)
    ctx.SMF = p.sbuf("SMF", 256, F32)
    ctx.PSA = p.psum("PSA", 2048, F32)
    ctx.PSB = p.psum("PSB", 1024, F32)
    ctx.PSC = p.psum("PSC", 512, F32)
    ctx.PST = p.psum("PST", 1024, BF16)
    emit_load_consts(ctx, CONST_NAMES + ["DIFFT", "REVT", "DECT", "REVC",
                                          "COSTOK", "SINTOK"])

    x_in = p.dram("x_in", [1024, 2048], F32, kind="ExternalInput")
    x_out = p.dram("x_out", [1024, 2048], F32, kind="ExternalOutput")
    x_scr = p.dram("x_scr", [1024, 2048], F32)
    ctx.x_in, ctx.x_out = x_in, x_out

    declare_weights(ctx, [("L0_attn_a_qkv_w", (192, 1024), F32),
                          ("L0_attn_a_proj_w", (1024, 1024), F32),
                          ("L0_ffn_w_gate", (3072, 1024), I32),
                          ("L0_ffn_w_up", (3072, 1024), I32),
                          ("L0_ffn_w_down", (1024, 3072), I32),
                          ("L0_attn_b_Wq", (1024, 1024), F32),
                          ("L0_attn_b_Wk", (1024, 1024), F32),
                          ("L0_attn_b_Wv", (1024, 1024), F32),
                          ("L0_attn_b_Wo", (1024, 1024), F32),
                          ("L1_attn_a_gate_w", (8, 1024), F32),
                          ("L1_attn_a_q_w", (1024, 1024), F32),
                          ("L1_attn_a_k_w", (8, 1024, 1024), F32),
                          ("L1_attn_a_v_w", (8, 1024, 1024), F32),
                          ("L1_attn_a_fmq1_w", (128, 128), F32),
                          ("L1_attn_a_fmq2_w", (128, 128), F32),
                          ("L1_attn_a_fmk1_w", (128, 128), F32),
                          ("L1_attn_a_fmk2_w", (128, 128), F32),
                          ("L1_attn_a_o_w", (1024, 1024), F32),
                          ("L1_ffn_Wq", (1024, 2048), F32),
                          ("L1_ffn_sub_key0", (32, 128), F32),
                          ("L1_ffn_sub_key1", (32, 128), F32),
                          ("L1_ffn_expert_down", (1024, 1024), F32),
                          ("L1_ffn_expert_up", (1024, 1024), F32),
                          ("L1_attn_b_in_proj_w", (1024, 4096), F32),
                          ("L1_attn_b_B_proj_w", (1024, 1024), F32),
                          ("L1_attn_b_C_proj_w", (1024, 1024), F32),
                          ("L1_attn_b_out_proj_w", (2048, 1024), F32),
                          ("L1_attn_b_m_fc_w", (1024, 4096), F32),
                          ("L1_attn_b_m_proj_w", (4096, 1024), F32),
                          ("L1_attn_b_q_w", (1024, 1024), F32),
                          ("L1_attn_b_k_w", (1024, 256), F32),
                          ("L1_attn_b_v_w", (1024, 256), F32),
                          ("L1_attn_b_o_w", (1024, 1024), F32),
                          ("L1_attn_b_a_fc_w", (1024, 4096), F32),
                          ("L1_attn_b_a_proj_w", (4096, 1024), F32)])
    declare_colvec(ctx, "l0aa_qkvb", 2)
    declare_colvec(ctx, "l0aa_projb", 8)
    declare_colvec(ctx, "l1aa_nw", 1)
    declare_colvec(ctx, "l1ffn_bq", 16)
    declare_colvec(ctx, "l1ffn_gam", 2)
    declare_colvec(ctx, "l1ffn_bet", 2)
    for nm, n in (("ab_rm0", 8), ("ab_rm1", 8), ("ab_ra0", 8), ("ab_ra1", 8),
                  ("ab_msc", 8), ("ab_m2sc", 8), ("ab_asc", 8), ("ab_a2sc", 8),
                  ("ab_conv", 64), ("ab_convb", 16), ("ab_D", 16),
                  ("ab_dtb", 16), ("ab_alog", 16), ("ab_qg", 16)):
        declare_colvec(ctx, nm, n)

    def a_l0aa():
        ctx.W = SubPool(ABF, 0, 14336)
        ctx.ACTB = SubPool(ABF, 14336, 8704)

    def a_l0ffn():
        ctx.W = SubPool(ABF, 0, 4096)
        ctx.ACTB = SubPool(ABF, 4096, 512)
        ctx.W2 = SubPool(ABF, 4608, 12288)
        ctx.H = SubPool(ABF, 16896, 12288)

    def a_l0ab():
        ctx.W = SubPool(ABF, 0, 16384)
        ctx.ACTB = SubPool(ABF, 16384, 512)
        ctx.QH = SubPool(ABF, 16896, 8192)
        ctx.KH = SubPool(ABF, 25088, 8192)
        ctx.VAUG = SubPool(ABF, 33280, 8320)
        ctx.YH = SubPool(ABF, 47216, 2048)
        ctx.ATTN = SubPool(ABF, 49264, 1536)

    def a_l1aa():
        ctx.W = SubPool(ABF, 0, 4096)
        ctx.QF = SubPool(ABF, 4096, 8192)
        ctx.KF1 = SubPool(ABF, 12288, 8192)
        ctx.KF2 = SubPool(ABF, 20480, 8192)
        ctx.VT = SubPool(ABF, 28672, 8192)
        ctx.STV = SubPool(ABF, 36864, 1024)
        ctx.MTOKB = SubPool(ABF, 37888, 128)
        ctx.FM = SubPool(ABF, 38016, 512)
        ctx.RB = SubPool(ABF, 38528, 2816)
        ctx.KT = SubPool(ABF, 41344, 8192)
        ctx.STF = SubPool(AFX, CARVE, 1024)
        ctx.UN = SubPool(AFX, CARVE + 1024, 6144)
        ctx.MTOKF = SubPool(AFX, CARVE + 7168, 128)

    def a_l1ffn():
        ctx.W = SubPool(ABF, 0, 16384)
        ctx.QB = SubPool(ABF, 16384, 4096)
        ctx.F2 = SubPool(ABF, 20480, 13312)
        ctx.SCOR = SubPool(AFX, CARVE, 8192)
        ctx.SF = SubPool(AFX, CARVE + 8192, 4096)
        ctx.RN = SubPool(AFX, CARVE + 12288, 256)
        ctx.ACC = SubPool(AFX, CARVE + 12544, 128)

    def a_l1ab():
        ctx.W = SubPool(ABF, 0, 64)
        ctx.XM = SubPool(ctx.XN, 0, 16384)
        B = SubPool(ABF, 64, 34864)
        ctx.BIGB = B
        ctx.VB = SubPool(B, 0, 8240)
        ctx.VACT = SubPool(B, 8240, 8192)
        ctx.BCM = SubPool(B, 16432, 8192)
        ctx.SG = SubPool(B, 24624, 2048)
        ctx.LMT = SubPool(B, 26672, 2048)
        ctx.DRT = SubPool(B, 28720, 2048)
        ctx.DOT = SubPool(B, 30768, 2048)
        ctx.HC = SubPool(B, 32816, 2048)
        ctx.WB = SubPool(ABF, 34928, 8192)
        ctx.WS = SubPool(ABF, 43120, 4096)
        ctx.NB = SubPool(ABF, 47216, 4096)
        ctx.SCB = SubPool(ABF, 51312, 768)
        ctx.ATTN = SubPool(ABF, 52080, 1536)

    blocks = [("l0aa", a_l0aa, lambda: emit_l0aa(ctx)),
              ("l0ffn", a_l0ffn, lambda: emit_l0ffn(ctx)),
              ("l0ab", a_l0ab, lambda: emit_l0ab(ctx)),
              ("l0ffn2", a_l0ffn, lambda: emit_l0ffn(ctx)),
              ("l1aa", a_l1aa, lambda: emit_l1aa(ctx)),
              ("l1ffn", a_l1ffn, lambda: emit_l1ffn(ctx, 0)),
              ("l1ab", a_l1ab, lambda: emit_l1ab(ctx)),
              ("l1ffn2", a_l1ffn, lambda: emit_l1ffn(ctx, 1))]
    xs = [x_in] + [x_scr] * 7
    xd = [x_scr] * 7 + [x_out]
    ctx.block_marks = []
    rp_ws = SubPool(ABF, 43120, 4096)
    for bi, (bname, assign, emitfn) in enumerate(blocks):
        ctx.block_marks.append(("ln_" + bname,
                                {e: len(v) for e, v in p.eng_ops.items()}))
        assign()
        ctx.x_src, ctx.x_dst = xs[bi], xd[bi]
        emit_ln_hbm(ctx, ctx.x_src)
        ctx.block_marks.append((bname, {e: len(v) for e, v in p.eng_ops.items()}))
        emitfn()
        if bi == 0:
            emit_l0ffn_phaseW(ctx, SubPool(ABF, 47216, 4096))
        if bi == 1:
            emit_l1ab_repacks(ctx, rp_ws, 0)
        elif bi == 2:
            ctx.rp_store_qs = ("gpsimd", "gpsimd")
            emit_l1ab_repacks(ctx, rp_ws, 1)
            emit_l1ab_repacks(ctx, rp_ws, 2)
            del ctx.rp_store_qs
    ctx.block_marks.append(("end", {e: len(v) for e, v in p.eng_ops.items()}))
    p.finalize()
    return nc


_PROG_CACHE = {}


def _get_prog(name):
    global LAST_CTX
    builders = {"l0aa": build_prog_l0aa, "l0ffn": build_prog_l0ffn,
                "l0ab": build_prog_l0ab, "l1aa": build_prog_l1aa,
                "l1ffn": build_prog_l1ffn, "l1ab": build_prog_l1ab,
                "fused": build_prog_fused}
    if name not in _PROG_CACHE:
        nc = builders[name]()
        _PROG_CACHE[name] = (nc, LAST_CTX.host_fconsts())
    return _PROG_CACHE[name]


def host_fused_inputs(inputs):
    base = {}
    for hostfn in (host_l0aa_inputs, host_l0ffn_inputs, host_l0ab_inputs,
                   host_l1aa_inputs, host_l1ffn_inputs, host_l1ab_inputs):
        base.update(hostfn(inputs))
    base.update(host_const_inputs())
    return base


def kernel(**inputs):
    nc, fc = _get_prog("fused")
    base = host_fused_inputs(inputs)
    base["fconsts"] = fc
    in_maps = [dict(base, x_in=np.ascontiguousarray(
        inputs["L0_attn_a_ids"][c].T.astype(np.float32))) for c in range(NCORES)]
    res = run_bass_kernel_spmd(nc, in_maps, core_ids=list(range(NCORES)))
    return np.ascontiguousarray(
        np.stack([np.asarray(res.results[c]["x_out"]).T
                  for c in range(NCORES)]).astype(np.float32))
